# version 08
# baseline (speedup 1.0000x reference)
import numpy as np
from contextlib import ExitStack

import concourse.bass as bass
import concourse.mybir as mybir
from concourse.bass_utils import run_bass_kernel_spmd

F32 = mybir.dt.float32
BF16 = mybir.dt.bfloat16
ALU = mybir.AluOpType
AX = mybir.AxisListType
AF = mybir.ActivationFunctionType

D = 2048
SEQ = 8192
TOK = 2048
NTC_ALL = 64
NTC_OWN = 16
NSLAB_ALL = 32
NSLAB_OWN = 8
ATT_EPS = 1e-6
LN_EPS = 1e-5
WO_CH = 8
WB_CH = 8
N_P2W = WO_CH + WB_CH

peq_q, peq_T, peq_qkv = {}, {}, {}
_c = 0
for _u in range(NTC_OWN):
    _c += 1
    peq_q[_u] = _c
    if _u >= 1:
        _c += 1
        peq_T[_u - 1] = _c
        _c += 1
        peq_qkv[_u - 1] = _c
_c += 1
peq_T[NTC_OWN - 1] = _c
_c += 1
peq_qkv[NTC_OWN - 1] = _c

pe2_mm3 = {}
pe2_y1t = {}
pe2_misc, pe2_hmt, pe2_base = {}, {}, {}


def _plan_pe2():
    global _c
    c = 0

    def ffn(u):
        nonlocal c
        for r in range(4):
            c += 1
            pe2_y1t[(u, r)] = c
        c += 1
        pe2_misc[u] = c
        c += 1
        pe2_hmt[u] = c
        c += 1
        pe2_base[u] = c

    for u in range(NTC_OWN):
        c += 1
        pe2_mm3[u] = c
        if u >= 2:
            ffn(u - 2)
    ffn(NTC_OWN - 2)
    ffn(NTC_OWN - 1)


_plan_pe2()

DQ_N = 10
def dq_sqhat(u):
    return DQ_N * u + 10

_DA_OPS = ["s", "b0", "b1", "b2", "b3", "aggr", "negmean", "veps", "rv", "y1"]
_DB_OPS = ["lh", "rowmax", "mask8", "hmask", "s2", "c0", "c1", "c2", "c3",
           "aggr2", "negmean2", "veps2", "rv2", "y2"]
d2abs = {}
_c2 = 0
def _seg(u, ops):
    global _c2
    for name in ops:
        _c2 += 1
        d2abs[(u, name)] = _c2
def _p2_sched():
    order = []
    for u in range(NTC_OWN):
        order.append(("A", u))
        if u >= 2:
            order.append(("B", u - 2))
    order.append(("B", NTC_OWN - 2))
    order.append(("B", NTC_OWN - 1))
    return order
P2_ORDER = _p2_sched()
for _kind, _u in P2_ORDER:
    _seg(_u, _DA_OPS if _kind == "A" else _DB_OPS)
a2abs = {}
_c3 = 0
for _kind, _u in P2_ORDER:
    if _kind == "A":
        _c3 += 1
        a2abs[(_u, "sqrt1")] = _c3
    else:
        for _r in range(4):
            _c3 += 1
            a2abs[(_u, "ey1t%d" % _r)] = _c3
        _c3 += 1
        a2abs[(_u, "ehmt")] = _c3
        _c3 += 1
        a2abs[(_u, "sqrt2")] = _c3

AQ_N = 4


def build_nc():
    nc = bass.Bass("TRN2", target_bir_lowering=False)

    xt_full = nc.dram_tensor("xt_full", [D, SEQ], F32, kind="ExternalInput").ap()
    xt_own = nc.dram_tensor("xt_own", [D, TOK], F32, kind="ExternalInput").ap()
    x_own = nc.dram_tensor("x_own", [TOK, D], F32, kind="ExternalInput").ap()
    wkv_t = nc.dram_tensor("wkv_t", [D, 1024], F32, kind="ExternalInput").ap()
    wq_t = nc.dram_tensor("wq_t", [D, D], F32, kind="ExternalInput").ap()
    wo_t = nc.dram_tensor("wo_t", [D, D], F32, kind="ExternalInput").ap()
    wbase_t = nc.dram_tensor("wbase_t", [D, D], F32, kind="ExternalInput").ap()
    ra_t = nc.dram_tensor("ra_t", [D, 136], F32, kind="ExternalInput").ap()
    bcat_t = nc.dram_tensor("bcat_t", [128, D], F32, kind="ExternalInput").ap()
    brows = nc.dram_tensor("brows", [1, 5120], F32, kind="ExternalInput").ap()
    ident = nc.dram_tensor("ident", [128, 128], F32, kind="ExternalInput").ap()
    y_out = nc.dram_tensor("y", [TOK, D], F32, kind="ExternalOutput").ap()
    attn_dram = nc.dram_tensor("attn_scratch", [NTC_OWN, 128, D], BF16, kind="Internal").ap()

    with ExitStack() as ctx:
        sb = lambda name, shape, dt: ctx.enter_context(nc.sbuf_tensor(name, shape, dt))
        sem = lambda name: ctx.enter_context(nc.semaphore(name))

        U0 = sb("u0", [128, 16, 2048], BF16)
        U1 = sb("u1", [128, 32768], BF16)
        wkv_sb = U1[:, 0:16384].rearrange("p (c n) -> p c n", c=16)
        slab = [U1[:, 16384:20480].rearrange("p (c n) -> p c n", c=16),
                U1[:, 20480:24576].rearrange("p (c n) -> p c n", c=16)]
        sk_t = [U1[:, 24576:25088], U1[:, 25088:25600]]
        vb_t = [U1[:, 25600:26112], U1[:, 26112:26624]]
        sq_p1 = U1[:, 26624:28672]
        scr1 = U1[:, 28672:30720].bitcast(F32)
        scr2 = U1[:, 30720:32768].bitcast(F32)
        wbase_sb = U1.rearrange("p (c n) -> p c n", c=16)

        fpA = sb("fpA", [128, D], F32)
        fpB = sb("fpB", [128, D], F32)
        prodb = fpB[:, :].bitcast(BF16)[:, 0:2048]
        kvb = sb("kvb", [128, 512], BF16)
        kvb_v = kvb[:, :].rearrange("p (h e) -> p h e", h=4)
        ksr512 = sb("ksr512", [1, 512], F32)
        ksum_bc = sb("ksum_bc", [128, D], BF16)
        I_bf = sb("I_bf", [128, 128], BF16)
        onesf = sb("onesf", [1, 128], F32)
        onesb = sb("onesb", [128, 1], BF16)
        onesr = sb("onesr", [1, 128], BF16)
        ra_sb = sb("ra_sb", [128, 16, 136], BF16)
        bcat_sb = sb("bcat_sb", [128, D], BF16)

        xt_p2 = [sb("xt_p2a", [128, D], F32), sb("xt_p2b", [128, D], F32),
                 sb("xt_p2c", [128, D], F32)]
        ksrow = xt_p2[0][0:1, :]
        sq_p2 = sb("sq_p2", [128, D], BF16)
        sq_p2b = sb("sq_p2b", [128, D], BF16)
        sqp = [sq_p2, sq_p2b]
        sqh = sb("sqh", [128, D], BF16)
        sqhT = sb("sqhT", [128, D], BF16)
        y1b = sb("y1b", [128, D], BF16)
        rowkv = sq_p2[0:1, 0:1024]
        rowq = y1b[0:1, 0:2048]
        rowbb = sqh[0:1, 0:2048]
        lh = sb("lh", [128, 136], F32)
        hmask = sb("hmask", [128, 128], BF16)
        hmaskT = sb("hmaskT", [128, 128], BF16)
        norm16 = sb("norm16", [128, 16], F32)
        normeps = sb("normeps", [128, 16], F32)
        rnorm = sb("rnorm", [128, 16], F32)
        rnbf = sb("rnbf", [128, 16], BF16)
        rowmax = sb("rowmax", [128, 1], F32)
        mask8 = sb("mask8", [128, 8], F32)
        negmean = [sb("negmean%d" % i, [128, 1], F32) for i in range(4)]
        veps = [sb("veps%d" % i, [128, 1], F32) for i in range(4)]
        stdv = [sb("stdv%d" % i, [128, 1], F32) for i in range(4)]
        rstd = [sb("rstd%d" % i, [128, 1], F32) for i in range(4)]
        bnstA = sb("bnstA", [128, 4, 6], F32)
        bnstB = sb("bnstB", [128, 4, 6], F32)
        mvA = sb("mvA", [128, 2], F32)
        mvB = sb("mvB", [128, 2], F32)

        psA = ctx.enter_context(nc.psum_tensor("psA", [128, 2048], F32))
        psB = ctx.enter_context(nc.psum_tensor("psB", [128, 2048], F32))
        psK = [psA[:, 0:512], psA[:, 512:1024]]
        psV = [psA[:, 1024:1536], psA[:, 1536:2048]]
        psKV = psB[:, 0:512]
        psKS = psB[0:1, 512:1024]
        psQ = [psA, psB]

        sWLa = sem("sWLa")
        sWLb = sem("sWLb")
        sXTe = sem("sXTe")
        sXTo = sem("sXTo")
        sWO = sem("sWO")
        sWB = sem("sWB")
        sStp = sem("sStp")
        sKVm = sem("sKVm")
        sAcc = sem("sAcc")
        sDVE1 = sem("sDVE1")
        sACT1 = sem("sACT1")
        sFin = sem("sFin")
        sPEq = sem("sPEq")
        sDVEq = sem("sDVEq")
        sACTq = sem("sACTq")
        sSQW = sem("sSQW")
        sXa = sem("sXa")
        sXb = sem("sXb")
        sXc = sem("sXc")
        sSQRa = sem("sSQRa")
        sSQRb = sem("sSQRb")
        sPE2 = sem("sPE2")
        sDVE2 = sem("sDVE2")
        sACT2 = sem("sACT2")
        sY1B = sem("sY1B")
        sOutA = sem("sOutA")
        sOutB = sem("sOutB")
        sOutC = sem("sOutC")

        sX3 = [sXa, sXb, sXc]
        sOut3 = [sOutA, sOutB, sOutC]
        block = ctx.enter_context(nc.Block())

        def slab_wait(eng, g):
            s = sXTe if g % 2 == 0 else sXTo
            eng.wait_ge(s, 16 * (g // 2 + 1))

        @block.tensor
        def _(tensor):
            tensor.wait_ge(sStp, 1)
            tensor.wait_ge(sWLa, 32)

            for t in range(NTC_ALL):
                par = t % 2
                s_idx = t // 2
                toff = (t - 2 * s_idx) * 128
                slab_wait(tensor, s_idx)
                if t >= 2:
                    tensor.wait_ge(sDVE1, 3 * (t - 1))
                for dc in range(16):
                    lhs = slab[s_idx % 2][:, dc, toff:toff + 128]
                    tensor.matmul(psK[par], lhs, wkv_sb[:, dc, 0:512],
                                  start=(dc == 0), stop=False,
                                  skip_group_check=True)
                    tensor.matmul(psV[par], lhs, wkv_sb[:, dc, 512:1024],
                                  start=(dc == 0), stop=False,
                                  skip_group_check=True)
                tensor.matmul(psK[par], onesr[:, :], rowkv[:, 0:512],
                              start=False, stop=True, skip_group_check=True)
                tensor.matmul(psV[par], onesr[:, :], rowkv[:, 512:1024],
                              start=False, stop=True, skip_group_check=True
                              ).then_inc(sKVm, 1)
                if t >= 1:
                    tp = t - 1
                    tensor.wait_ge(sDVE1, 3 * t)
                    for h4 in range(4):
                        tensor.matmul(psKV[:, h4 * 128:(h4 + 1) * 128],
                                      sk_t[tp % 2][:, h4 * 128:(h4 + 1) * 128],
                                      vb_t[tp % 2][:, h4 * 128:(h4 + 1) * 128],
                                      start=(tp == 0 and h4 == 0),
                                      stop=(tp == NTC_ALL - 1 and h4 == 3),
                                      skip_group_check=True)
                    tensor.matmul(psKS, onesb[:, :], sk_t[tp % 2][:, :],
                                  start=(tp == 0), stop=(tp == NTC_ALL - 1),
                                  skip_group_check=True).then_inc(sAcc, 1)
            tp = NTC_ALL - 1
            tensor.wait_ge(sDVE1, 3 * NTC_ALL)
            for h4 in range(4):
                tensor.matmul(psKV[:, h4 * 128:(h4 + 1) * 128],
                              sk_t[tp % 2][:, h4 * 128:(h4 + 1) * 128],
                              vb_t[tp % 2][:, h4 * 128:(h4 + 1) * 128],
                              start=False, stop=(h4 == 3), skip_group_check=True)
            tensor.matmul(psKS, onesb[:, :], sk_t[tp % 2][:, :],
                          start=False, stop=True, skip_group_check=True
                          ).then_inc(sAcc, 1)

            tensor.wait_ge(sFin, 3)
            for j in range(4):
                mm = tensor.matmul(psA[:, j * 512:(j + 1) * 512], onesf[:, :],
                                   ksrow[:, j * 512:(j + 1) * 512], start=True, stop=True)
            mm.then_inc(sFin, 1)

            tensor.wait_ge(sWLb, 16 * 12)
            tensor.wait_ge(sFin, 5)

            def q_group(u):
                par = u % 2
                s_idx = u // 2
                toff = (u - 2 * s_idx) * 128
                slab_wait(tensor, NSLAB_ALL + s_idx)
                if u >= 2:
                    tensor.wait_ge(sACTq, AQ_N * (u - 2) + 4)
                for dc in range(16):
                    lhs = slab[s_idx % 2][:, dc, toff:toff + 128]
                    for j in range(4):
                        tensor.matmul(psQ[par][:, j * 512:(j + 1) * 512], lhs,
                                      U0[:, dc, j * 512:(j + 1) * 512],
                                      start=(dc == 0), stop=False,
                                      skip_group_check=True)
                for j in range(4):
                    mm = tensor.matmul(psQ[par][:, j * 512:(j + 1) * 512],
                                       onesr[:, :],
                                       rowq[:, j * 512:(j + 1) * 512],
                                       start=False, stop=True, skip_group_check=True)
                mm.then_inc(sPEq, 1)

            def t_group(u):
                par = u % 2
                tensor.wait_ge(sDVEq, dq_sqhat(u))
                for h in range(16):
                    mm = tensor.matmul(psQ[par][:, h * 128:(h + 1) * 128],
                                       sqh[:, h * 128:(h + 1) * 128], I_bf[:, :],
                                       start=True, stop=True)
                mm.then_inc(sPEq, 1)

            def qkv_group(u):
                par = u % 2
                tensor.wait_ge(sACTq, AQ_N * u + 3)
                for h in range(16):
                    mm = tensor.matmul(psQ[par][:, h * 128:(h + 1) * 128],
                                       kvb_v[:, h // 4, :],
                                       sqhT[:, h * 128:(h + 1) * 128],
                                       start=True, stop=True)
                mm.then_inc(sPEq, 1)

            for u in range(NTC_OWN):
                q_group(u)
                if u >= 1:
                    t_group(u - 1)
                    qkv_group(u - 1)
            t_group(NTC_OWN - 1)
            qkv_group(NTC_OWN - 1)

            tensor.wait_ge(sACTq, AQ_N * NTC_OWN)

            def mm3_group(u):
                tensor.wait_ge(sSQRa if u % 2 == 0 else sSQRb, 16 * (u // 2 + 1))
                if u >= 1:
                    tensor.wait_ge(sDVE2, d2abs[(u - 1, "s")])
                if u >= 3:
                    tensor.wait_ge(sDVE2, d2abs[(u - 3, "lh")])
                    tensor.wait_ge(sACT2, a2abs[(u - 3, "ehmt")])
                if u == 0:
                    tensor.wait_ge(sWO, 16 * WO_CH)
                sq = sqp[u % 2]
                for h in range(16):
                    for j in range(4):
                        mm = tensor.matmul(psA[:, j * 512:(j + 1) * 512],
                                           sq[:, h * 128:(h + 1) * 128],
                                           U0[:, h, j * 512:(j + 1) * 512],
                                           start=(h == 0), stop=(h == 15))
                mm.then_inc(sPE2, 1)

            def ffn_group(u):
                for r in range(4):
                    if r == 0:
                        tensor.wait_ge(sY1B, u + 1)
                        if u >= 1:
                            tensor.wait_ge(sDVE2, d2abs[(u - 1, "s2")])
                    if r >= 2:
                        tensor.wait_ge(sACT2, a2abs[(u, "ey1t%d" % (r - 2))])
                    for k in range(4):
                        dc = 4 * r + k
                        base_col = (r % 2) * 512 + k * 128
                        mm = tensor.matmul(psB[:, base_col:base_col + 128],
                                           y1b[:, dc * 128:(dc + 1) * 128], I_bf[:, :],
                                           start=True, stop=True)
                    mm.then_inc(sPE2, 1)
                tensor.wait_ge(sACT2, a2abs[(u, "ey1t3")])
                tensor.wait_ge(sDVE2, d2abs[(min(u + 2, NTC_OWN - 1), "s")])
                for dc in range(16):
                    mm = tensor.matmul(psB[:, 0:136],
                                       sqhT[:, dc * 128:(dc + 1) * 128],
                                       ra_sb[:, dc, :],
                                       start=(dc == 0), stop=(dc == 15))
                mm.then_inc(sPE2, 1)
                tensor.wait_ge(sDVE2, d2abs[(u, "hmask")])
                tensor.matmul(psB[:, 512:640], hmask[:, :], I_bf[:, :],
                              start=True, stop=True).then_inc(sPE2, 1)
                tensor.wait_ge(sDVE2, d2abs[(u, "lh")])
                tensor.wait_ge(sACT2, a2abs[(u, "ehmt")])
                if u == 0:
                    tensor.wait_ge(sWB, 16 * (WB_CH + 1))
                for dc in range(16):
                    for j in range(4):
                        tensor.matmul(psB[:, j * 512:(j + 1) * 512],
                                      sqhT[:, dc * 128:(dc + 1) * 128],
                                      wbase_sb[:, dc, j * 512:(j + 1) * 512],
                                      start=(dc == 0), stop=False,
                                      skip_group_check=True)
                for j in range(4):
                    tensor.matmul(psB[:, j * 512:(j + 1) * 512], onesr[:, :],
                                  rowbb[:, j * 512:(j + 1) * 512],
                                  start=False, stop=False, skip_group_check=True)
                for j in range(4):
                    mm = tensor.matmul(psB[:, j * 512:(j + 1) * 512], hmaskT[:, :],
                                       bcat_sb[:, j * 512:(j + 1) * 512],
                                       start=False, stop=True,
                                       skip_group_check=True)
                mm.then_inc(sPE2, 1)

            for u in range(NTC_OWN):
                mm3_group(u)
                if u >= 2:
                    ffn_group(u - 2)
            ffn_group(NTC_OWN - 2)
            ffn_group(NTC_OWN - 1)

        @block.vector
        def _(vector):
            vector.memset(onesf[:, :], 1.0)
            vector.memset(onesb[:, :], 1.0)
            vector.memset(onesr[:, :], 1.0).then_inc(sStp, 1)

            dcnt = [0]

            def op(ins):
                ins.then_inc(sDVE1, 1)
                dcnt[0] += 1
                vector.wait_ge(sDVE1, dcnt[0])
                return ins

            for t in range(NTC_ALL):
                par = t % 2
                vector.wait_ge(sKVm, t + 1)
                if t >= 1:
                    vector.wait_ge(sACT1, t)
                op(vector.tensor_scalar(scr1[:, 0:512], psK[par], 0.0, None, ALU.min))
                vector.wait_ge(sACT1, t + 1)
                if t >= 2:
                    vector.wait_ge(sAcc, t - 1)
                op(vector.scalar_tensor_tensor(sk_t[par][:, :], psK[par], 0.0,
                                               scr2[:, 0:512], ALU.max, ALU.add))
                op(vector.tensor_copy(vb_t[par][:, :], psV[par]))

            vector.wait_ge(sAcc, NTC_ALL)
            vector.tensor_copy(kvb[:, :], psKV).then_inc(sFin, 1)
            vector.tensor_copy(ksr512[:, :], psKS).then_inc(sFin, 1)
            vector.wait_ge(sFin, 2)
            ks_in = ksr512[:, :].rearrange("a (h d) -> a h d", h=4).unsqueeze(2) \
                .broadcast_to([1, 4, 4, 128])
            vector.tensor_copy(ksrow.rearrange("a (h r d) -> a h r d", h=4, r=4),
                               ks_in).then_inc(sFin, 1)
            vector.wait_ge(sFin, 4)
            vector.tensor_copy(ksum_bc[:, :], psA[:, :]).then_inc(sFin, 1)

            dq = [0]

            def opq(ins):
                ins.then_inc(sDVEq, 1)
                dq[0] += 1
                vector.wait_ge(sDVEq, dq[0])
                return ins

            sq3 = sq_p1.rearrange("p (h d) -> p h d", h=16)
            prod3 = prodb.rearrange("p (h d) -> p h d", h=16)
            ks3 = ksum_bc[:, :].rearrange("p (h d) -> p h d", h=16)
            sqh3 = sqh[:, :].rearrange("p (h d) -> p h d", h=16)
            rn3 = rnbf[:, :].unsqueeze(2).broadcast_to([128, 16, 128])
            for u in range(NTC_OWN):
                par = u % 2
                vector.wait_ge(sPEq, peq_q[u])
                if u >= 1:
                    vector.wait_ge(sSQW, 16 * u)
                for j in range(2):
                    sl = slice(j * 1024, (j + 1) * 1024)
                    if u + j >= 1:
                        vector.wait_ge(sACTq, AQ_N * u + j)
                    opq(vector.tensor_scalar(scr1[:, :], psQ[par][:, sl], 0.0,
                                             None, ALU.min))
                    vector.wait_ge(sACTq, AQ_N * u + j + 1)
                    opq(vector.scalar_tensor_tensor(sq_p1[:, sl], psQ[par][:, sl], 0.0,
                                                    scr2[:, :], ALU.max, ALU.add))
                opq(vector.tensor_tensor(prod3, sq3, ks3, ALU.mult))
                opq(vector.tensor_reduce(norm16[:, :], prod3, AX.X, ALU.add))
                opq(vector.tensor_scalar(normeps[:, :], norm16[:, :], ATT_EPS,
                                         None, ALU.add))
                opq(vector.reciprocal(rnorm[:, :], normeps[:, :]))
                opq(vector.tensor_copy(rnbf[:, :], rnorm[:, :]))
                if u >= 1:
                    vector.wait_ge(sPEq, peq_T[u - 1])
                opq(vector.tensor_tensor(sqh3, sq3, rn3, ALU.mult))

            dd = [0]

            def op2(ins):
                ins.then_inc(sDVE2, 1)
                dd[0] += 1
                vector.wait_ge(sDVE2, dd[0])
                return ins

            lh3 = lh[:, 8:136].rearrange("p (e r) -> p e r", e=8)
            hm3 = hmask[:, :].rearrange("p (e r) -> p e r", e=8)
            m8b = mask8[:, :].unsqueeze(2).broadcast_to([128, 8, 16])

            def seg_a(u):
                p = u % 2
                vector.wait_ge(sPE2, pe2_mm3[u])
                vector.wait_ge(sX3[u % 3], 16 * (u // 3 + 1))
                op2(vector.scalar_tensor_tensor(fpA[:, :], psA[:, :], 1.0,
                                                xt_p2[u % 3][:, :], ALU.mult, ALU.add))
                for g in range(4):
                    op2(vector.bn_stats(bnstA[:, g, :], fpA[:, g * 512:(g + 1) * 512]))
                op2(vector.bn_aggr(mvA[:, :], bnstA[:, :, :].rearrange("p a b -> p (a b)")))
                op2(vector.tensor_scalar(negmean[p][:, :], mvA[:, 0:1], -1.0, None, ALU.mult))
                op2(vector.tensor_scalar(veps[p][:, :], mvA[:, 1:2], LN_EPS, None, ALU.add))
                op2(vector.reciprocal(stdv[p][:, :], veps[p][:, :]))
                vector.wait_ge(sACT2, a2abs[(u, "sqrt1")])
                op2(vector.tensor_scalar(xt_p2[u % 3][:, :], fpA[:, :], negmean[p][:, :],
                                         rstd[p][:, :], ALU.add, ALU.mult))

            def seg_b(u):
                p = 2 + u % 2
                vector.wait_ge(sPE2, pe2_misc[u])
                op2(vector.tensor_copy(lh[:, :], psB[:, 0:136]))
                op2(vector.reduce_max(rowmax[:, :], lh[:, 0:8], AX.X))
                op2(vector.tensor_scalar(mask8[:, :], lh[:, 0:8], rowmax[:, :],
                                         2.0, ALU.is_ge, ALU.mult))
                op2(vector.tensor_tensor(hm3, lh3, m8b, ALU.mult))
                vector.wait_ge(sPE2, pe2_base[u])
                vector.wait_ge(sY1B, u + 1)
                op2(vector.scalar_tensor_tensor(fpA[:, :], psB[:, :], 1.0,
                                                xt_p2[u % 3][:, :], ALU.mult, ALU.add))
                for g in range(4):
                    op2(vector.bn_stats(bnstB[:, g, :], fpA[:, g * 512:(g + 1) * 512]))
                op2(vector.bn_aggr(mvB[:, :], bnstB[:, :, :].rearrange("p a b -> p (a b)")))
                op2(vector.tensor_scalar(negmean[p][:, :], mvB[:, 0:1], -1.0, None, ALU.mult))
                op2(vector.tensor_scalar(veps[p][:, :], mvB[:, 1:2], LN_EPS, None, ALU.add))
                op2(vector.reciprocal(stdv[p][:, :], veps[p][:, :]))
                vector.wait_ge(sACT2, a2abs[(u, "sqrt2")])
                op2(vector.tensor_scalar(xt_p2[u % 3][:, :], fpA[:, :], negmean[p][:, :],
                                         rstd[p][:, :], ALU.add, ALU.mult))

            for kind, u in P2_ORDER:
                if kind == "A":
                    seg_a(u)
                else:
                    seg_b(u)

        @block.scalar
        def _(scalar):
            scalar.preload_act_table(AF.Exp)
            for t in range(NTC_ALL):
                scalar.wait_ge(sDVE1, 3 * t + 1)
                scalar.activation(scr2[:, 0:512], scr1[:, 0:512], AF.Exp).then_inc(sACT1, 1)
            for u in range(NTC_OWN):
                par = u % 2
                for j in range(2):
                    scalar.wait_ge(sDVEq, DQ_N * u + 2 * j + 1)
                    scalar.activation(scr2[:, :], scr1[:, :], AF.Exp).then_inc(sACTq, 1)
                scalar.wait_ge(sPEq, peq_T[u])
                scalar.copy(sqhT[:, :], psQ[par][:, :]).then_inc(sACTq, 1)
                scalar.wait_ge(sPEq, peq_qkv[u])
                scalar.wait_ge(sDVEq, dq_sqhat(u))
                scalar.wait_ge(sSQW, 16 * u)
                scalar.copy(sq_p1, psQ[par][:, :]).then_inc(sACTq, 1)
            scalar.preload_act_table(AF.Sqrt)
            for kind, u in P2_ORDER:
                if kind == "A":
                    p = u % 2
                    scalar.wait_ge(sDVE2, d2abs[(u, "rv")])
                    scalar.activation(rstd[p][:, :], stdv[p][:, :], AF.Sqrt).then_inc(sACT2, 1)
                else:
                    p = 2 + u % 2
                    for r in range(4):
                        scalar.wait_ge(sPE2, pe2_y1t[(u, r)])
                        scalar.copy(sqhT[:, r * 512:(r + 1) * 512],
                                    psB[:, (r % 2) * 512:(r % 2) * 512 + 512]).then_inc(sACT2, 1)
                    scalar.wait_ge(sPE2, pe2_hmt[u])
                    scalar.copy(hmaskT[:, :], psB[:, 512:640]).then_inc(sACT2, 1)
                    scalar.wait_ge(sDVE2, d2abs[(u, "rv2")])
                    scalar.activation(rstd[p][:, :], stdv[p][:, :], AF.Sqrt).then_inc(sACT2, 1)
                    scalar.wait_ge(sDVE2, d2abs[(u, "y2")])
                    scalar.dma_start(out=y_out[u * 128:(u + 1) * 128, :],
                                     in_=xt_p2[u % 3][:, :]).then_inc(sOut3[u % 3], 16)

        @block.gpsimd
        def _(gpsimd):
            wq_src = wq_t.rearrange("(c p) n -> p c n", p=128)
            gpsimd.dma_start(out=wkv_sb, in_=wkv_t.rearrange("(c p) n -> p c n", p=128)
                             ).then_inc(sWLa, 16)
            gpsimd.dma_start(out=rowkv, in_=brows[:, 0:1024]).then_inc(sWLa, 16)
            gpsimd.wait_ge(sWLa, 32)
            wlb_small = [
                (rowq, brows[:, 1024:3072]),
                (I_bf[:, :], ident),
                (ra_sb[:, :, :], ra_t.rearrange("(c p) n -> p c n", p=128)),
                (bcat_sb[:, :], bcat_t),
            ]
            wlb_jobs = wlb_small + [
                (U0[:, 2 * i:2 * i + 2, :], wq_src[:, 2 * i:2 * i + 2, :])
                for i in range(8)]
            wlb_i = [0]

            def emit_wlb():
                if wlb_i[0] < len(wlb_jobs):
                    dst, s_ = wlb_jobs[wlb_i[0]]
                    gpsimd.dma_start(out=dst, in_=s_).then_inc(sWLb, 16)
                    wlb_i[0] += 1

            for s in range(NSLAB_ALL):
                if s >= 2:
                    gpsimd.wait_ge(sAcc, 2 * (s - 1))
                src = xt_full.rearrange("(c p) t -> p c t", p=128)[:, :, s * 256:(s + 1) * 256]
                gpsimd.dma_start(out=slab[s % 2], in_=src).then_inc(
                    sXTe if s % 2 == 0 else sXTo, 16)
                if 2 <= s < 6 or (s >= 8 and s % 2 == 0):
                    emit_wlb()
            while wlb_i[0] < len(wlb_jobs):
                emit_wlb()
            wo_src = wo_t.rearrange("(c p) n -> p c n", p=128)
            wb_src = wbase_t.rearrange("(c p) n -> p c n", p=128)
            for s in range(NSLAB_OWN):
                if s < 2:
                    gpsimd.wait_ge(sAcc, NTC_ALL)
                else:
                    gpsimd.wait_ge(sPEq, peq_q[2 * (s - 1)])
                g = NSLAB_ALL + s
                src = xt_own.rearrange("(c p) t -> p c t", p=128)[:, :, s * 256:(s + 1) * 256]
                gpsimd.dma_start(out=slab[g % 2], in_=src).then_inc(
                    sXTe if g % 2 == 0 else sXTo, 16)
                if 2 <= s < 6:
                    i = s - 2
                    if i == 0:
                        gpsimd.wait_ge(sKVm, NTC_ALL)
                    gpsimd.dma_start(out=wbase_sb[:, 2 * i:2 * i + 2, :],
                                     in_=wb_src[:, 2 * i:2 * i + 2, :]).then_inc(sWB, 16)
            gpsimd.wait_ge(sPEq, peq_q[NTC_OWN - 1])
            for i in range(WO_CH):
                gpsimd.dma_start(out=U0[:, 2 * i:2 * i + 2, :],
                                 in_=wo_src[:, 2 * i:2 * i + 2, :]).then_inc(sWO, 16)
            gpsimd.wait_ge(sDVEq, DQ_N * NTC_OWN)
            gpsimd.wait_ge(sACTq, AQ_N * NTC_OWN)
            gpsimd.wait_ge(sSQW, 16 * NTC_OWN)
            gpsimd.wait_ge(sPEq, peq_T[NTC_OWN - 1])
            for i in range(4, WB_CH):
                gpsimd.dma_start(out=wbase_sb[:, 2 * i:2 * i + 2, :],
                                 in_=wb_src[:, 2 * i:2 * i + 2, :]).then_inc(sWB, 16)
            gpsimd.dma_start(out=rowbb, in_=brows[:, 3072:5120]).then_inc(sWB, 16)
            for u in range(NTC_OWN):
                gpsimd.wait_ge(sDVE2, d2abs[(u, "y1")])
                if u >= 1:
                    gpsimd.wait_ge(sPE2, pe2_y1t[(u - 1, 3)])
                gpsimd.tensor_copy(y1b[:, :], xt_p2[u % 3][:, :]).then_inc(sY1B, 1)

        @block.sync
        def _(sync):
            for u in range(NTC_OWN):
                sync.wait_ge(sACTq, AQ_N * (u + 1))
                sync.dma_start(out=attn_dram[u], in_=sq_p1).then_inc(sSQW, 16)
            def x_load(u):
                if u == 0:
                    sync.wait_ge(sFin, 4)
                if u >= 3:
                    sync.wait_ge(sOut3[u % 3], 16 * ((u - 3) // 3 + 1))
                sync.dma_start(out=xt_p2[u % 3][:, :],
                               in_=x_own[u * 128:(u + 1) * 128, :]).then_inc(
                                   sX3[u % 3], 16)

            def sq_load(u):
                if u >= 2:
                    sync.wait_ge(sPE2, pe2_mm3[u - 2])
                sync.dma_start(out=sqp[u % 2][:, :], in_=attn_dram[u]).then_inc(
                    sSQRa if u % 2 == 0 else sSQRb, 16)

            for u in range(NTC_OWN):
                x_load(u)
                sq_load(u)
            for p3 in range(3):
                n_p = len([u for u in range(NTC_OWN) if u % 3 == p3])
                sync.wait_ge(sOut3[p3], 16 * n_p)


    return nc


def kernel(L0_attn_hidden_states, L0_attn_Wq, L0_attn_bq, L0_attn_Wk, L0_attn_bk,
           L0_attn_Wv, L0_attn_bv, L0_attn_Wo, L0_ffn_W_base, L0_ffn_b_base,
           L0_ffn_router_W, L0_ffn_A, L0_ffn_Bw):
    f32 = np.float32
    X = np.ascontiguousarray(np.asarray(L0_attn_hidden_states, dtype=f32))
    Wq = np.asarray(L0_attn_Wq, f32)
    Wk = np.asarray(L0_attn_Wk, f32)
    Wv = np.asarray(L0_attn_Wv, f32)
    Wo = np.asarray(L0_attn_Wo, f32)
    Wb = np.asarray(L0_ffn_W_base, f32)
    Rw = np.asarray(L0_ffn_router_W, f32)
    A = np.asarray(L0_ffn_A, f32)
    Bw = np.asarray(L0_ffn_Bw, f32)

    wkv_t = np.ascontiguousarray(np.concatenate([Wk, Wv], axis=0).T)
    wq_t = np.ascontiguousarray(Wq.T)
    wo_t = np.ascontiguousarray(Wo.T)
    wbase_t = np.ascontiguousarray(Wb.T)
    ra_t = np.ascontiguousarray(
        np.concatenate([Rw, A.reshape(128, D)], axis=0).T)
    bcat_t = np.ascontiguousarray(Bw.transpose(0, 2, 1).reshape(128, D))
    brows = np.concatenate([
        np.asarray(L0_attn_bk, f32), np.asarray(L0_attn_bv, f32),
        np.asarray(L0_attn_bq, f32), np.asarray(L0_ffn_b_base, f32)
    ]).reshape(1, 5120)
    ident = np.eye(128, dtype=f32)

    xt_b = [np.ascontiguousarray(X[b].T) for b in range(2)]

    nc = build_nc()
    in_maps = []
    for c in range(8):
        b, j = c // 4, c % 4
        in_maps.append(dict(
            xt_full=xt_b[b],
            xt_own=np.ascontiguousarray(xt_b[b][:, j * TOK:(j + 1) * TOK]),
            x_own=np.ascontiguousarray(X[b, j * TOK:(j + 1) * TOK, :]),
            wkv_t=wkv_t, wq_t=wq_t, wo_t=wo_t, wbase_t=wbase_t,
            ra_t=ra_t, bcat_t=bcat_t, brows=brows, ident=ident,
        ))
    res = run_bass_kernel_spmd(nc, in_maps, core_ids=list(range(8)))
    y = np.empty((2, SEQ, D), dtype=f32)
    for c in range(8):
        b, j = c // 4, c % 4
        y[b, j * TOK:(j + 1) * TOK, :] = res.results[c]["y"]
    return y


# version 09
# speedup vs baseline: 1.2645x; 1.2645x over previous
import numpy as np
from contextlib import ExitStack

import concourse.bass as bass
import concourse.mybir as mybir
from concourse.bass_utils import run_bass_kernel_spmd

F32 = mybir.dt.float32
BF16 = mybir.dt.bfloat16
ALU = mybir.AluOpType
AX = mybir.AxisListType
AF = mybir.ActivationFunctionType

D = 2048
SEQ = 8192
TOK = 2048
NTC_ALL = 64
NTC_OWN = 16
NSLAB_ALL = 32
NSLAB_OWN = 8
ATT_EPS = 1e-6
LN_EPS = 1e-5
WO_CH = 8
WB_CH = 8
N_P2W = WO_CH + WB_CH

peq_q, peq_T, peq_qkv = {}, {}, {}
_c = 0
for _u in range(NTC_OWN):
    _c += 1
    peq_q[_u] = _c
    if _u >= 1:
        _c += 1
        peq_T[_u - 1] = _c
        _c += 1
        peq_qkv[_u - 1] = _c
_c += 1
peq_T[NTC_OWN - 1] = _c
_c += 1
peq_qkv[NTC_OWN - 1] = _c

pe2_mm3 = {}
pe2_y1t = {}
pe2_misc, pe2_hmt, pe2_base = {}, {}, {}


def _plan_pe2():
    global _c
    c = 0

    def ffn(u):
        nonlocal c
        for r in range(4):
            c += 1
            pe2_y1t[(u, r)] = c
        c += 1
        pe2_misc[u] = c
        c += 1
        pe2_hmt[u] = c
        c += 1
        pe2_base[u] = c

    for u in range(NTC_OWN):
        c += 1
        pe2_mm3[u] = c
        if u >= 2:
            ffn(u - 2)
    ffn(NTC_OWN - 2)
    ffn(NTC_OWN - 1)


_plan_pe2()

DQ_N = 10
def dq_sqhat(u):
    return DQ_N * u + 10

_DA_OPS = ["s", "b0", "b1", "b2", "b3", "aggr", "negmean", "veps", "rv", "y1"]
_DB_OPS = ["lh", "rowmax", "mask8", "hmask", "s2", "c0", "c1", "c2", "c3",
           "aggr2", "negmean2", "veps2", "rv2", "y2"]
d2abs = {}
_c2 = 0
def _seg(u, ops):
    global _c2
    for name in ops:
        _c2 += 1
        d2abs[(u, name)] = _c2
def _p2_sched():
    order = []
    for u in range(NTC_OWN):
        order.append(("A", u))
        if u >= 2:
            order.append(("B", u - 2))
    order.append(("B", NTC_OWN - 2))
    order.append(("B", NTC_OWN - 1))
    return order
P2_ORDER = _p2_sched()
for _kind, _u in P2_ORDER:
    _seg(_u, _DA_OPS if _kind == "A" else _DB_OPS)
a2abs = {}
_c3 = 0
for _kind, _u in P2_ORDER:
    if _kind == "A":
        _c3 += 1
        a2abs[(_u, "sqrt1")] = _c3
    else:
        for _r in range(4):
            _c3 += 1
            a2abs[(_u, "ey1t%d" % _r)] = _c3
        _c3 += 1
        a2abs[(_u, "ehmt")] = _c3
        _c3 += 1
        a2abs[(_u, "sqrt2")] = _c3

AQ_N = 4


def build_nc():
    nc = bass.Bass("TRN2", target_bir_lowering=False)

    xt_full = nc.dram_tensor("xt_full", [D, SEQ], F32, kind="ExternalInput").ap()
    xt_own = nc.dram_tensor("xt_own", [D, TOK], F32, kind="ExternalInput").ap()
    x_own = nc.dram_tensor("x_own", [TOK, D], F32, kind="ExternalInput").ap()
    wkv_t = nc.dram_tensor("wkv_t", [D, 1024], F32, kind="ExternalInput").ap()
    wq_t = nc.dram_tensor("wq_t", [D, D], F32, kind="ExternalInput").ap()
    wo_t = nc.dram_tensor("wo_t", [D, D], F32, kind="ExternalInput").ap()
    wbase_t = nc.dram_tensor("wbase_t", [D, D], F32, kind="ExternalInput").ap()
    ra_t = nc.dram_tensor("ra_t", [D, 136], F32, kind="ExternalInput").ap()
    bcat_t = nc.dram_tensor("bcat_t", [128, D], F32, kind="ExternalInput").ap()
    brows = nc.dram_tensor("brows", [1, 5120], F32, kind="ExternalInput").ap()
    ident = nc.dram_tensor("ident", [128, 128], F32, kind="ExternalInput").ap()
    y_out = nc.dram_tensor("y", [TOK, D], F32, kind="ExternalOutput").ap()
    attn_dram = nc.dram_tensor("attn_scratch", [NTC_OWN, 128, D], BF16, kind="Internal").ap()

    with ExitStack() as ctx:
        sb = lambda name, shape, dt: ctx.enter_context(nc.sbuf_tensor(name, shape, dt))
        sem = lambda name: ctx.enter_context(nc.semaphore(name))

        U0 = sb("u0", [128, 16, 2048], BF16)
        U1 = sb("u1", [128, 32768], BF16)
        wkv_sb = U1[:, 0:16384].rearrange("p (c n) -> p c n", c=16)
        slab = [U1[:, 16384:20480].rearrange("p (c n) -> p c n", c=16),
                U1[:, 20480:24576].rearrange("p (c n) -> p c n", c=16)]
        sk_t = [U1[:, 24576:25088], U1[:, 25088:25600]]
        vb_t = [U1[:, 25600:26112], U1[:, 26112:26624]]
        sq_p1 = U1[:, 26624:28672]
        scr1 = U1[:, 28672:30720].bitcast(F32)
        scr2 = U1[:, 30720:32768].bitcast(F32)
        wbase_sb = U1.rearrange("p (c n) -> p c n", c=16)

        fpA = sb("fpA", [128, D], F32)
        fpB = sb("fpB", [128, D], F32)
        prodb = fpB[:, :].bitcast(BF16)[:, 0:2048]
        kvb = sb("kvb", [128, 512], BF16)
        kvb_v = kvb[:, :].rearrange("p (h e) -> p h e", h=4)
        ksr512 = sb("ksr512", [1, 512], F32)
        ksum_bc = sb("ksum_bc", [128, D], BF16)
        I_bf = sb("I_bf", [128, 128], BF16)
        onesf = sb("onesf", [1, 128], F32)
        onesb = sb("onesb", [128, 1], BF16)
        onesr = sb("onesr", [1, 128], BF16)
        ra_sb = sb("ra_sb", [128, 16, 136], BF16)
        bcat_sb = sb("bcat_sb", [128, D], BF16)

        xt_p2 = [sb("xt_p2a", [128, D], F32), sb("xt_p2b", [128, D], F32),
                 sb("xt_p2c", [128, D], F32)]
        ksrow = xt_p2[0][0:1, :]
        sq_p2 = sb("sq_p2", [128, D], BF16)
        sq_p2b = sb("sq_p2b", [128, D], BF16)
        sqp = [sq_p2, sq_p2b]
        sqh = sb("sqh", [128, D], BF16)
        sqhT = sb("sqhT", [128, D], BF16)
        y1b = sb("y1b", [128, D], BF16)
        rowkv = sq_p2[0:1, 0:1024]
        rowq = y1b[0:1, 0:2048]
        rowbb = sqh[0:1, 0:2048]
        lh = sb("lh", [128, 136], F32)
        hmask = sb("hmask", [128, 128], BF16)
        hmaskT = sb("hmaskT", [128, 128], BF16)
        norm16 = sb("norm16", [128, 16], F32)
        normeps = sb("normeps", [128, 16], F32)
        rnorm = sb("rnorm", [128, 16], F32)
        rnbf = sb("rnbf", [128, 16], BF16)
        rowmax = sb("rowmax", [128, 1], F32)
        mask8 = sb("mask8", [128, 8], F32)
        negmean = [sb("negmean%d" % i, [128, 1], F32) for i in range(4)]
        veps = [sb("veps%d" % i, [128, 1], F32) for i in range(4)]
        stdv = [sb("stdv%d" % i, [128, 1], F32) for i in range(4)]
        rstd = [sb("rstd%d" % i, [128, 1], F32) for i in range(4)]
        bnstA = sb("bnstA", [128, 4, 6], F32)
        bnstB = sb("bnstB", [128, 4, 6], F32)
        mvA = sb("mvA", [128, 2], F32)
        mvB = sb("mvB", [128, 2], F32)

        psA = ctx.enter_context(nc.psum_tensor("psA", [128, 2048], F32))
        psB = ctx.enter_context(nc.psum_tensor("psB", [128, 2048], F32))
        psK = [psA[:, 0:512], psA[:, 512:1024]]
        psV = [psA[:, 1024:1536], psA[:, 1536:2048]]
        psKV = psB[:, 0:512]
        psKS = psB[0:1, 512:1024]
        psQ = [psA, psB]

        sWLa = sem("sWLa")
        sWLb = sem("sWLb")
        sXTe = sem("sXTe")
        sXTo = sem("sXTo")
        sWO = sem("sWO")
        sWB = sem("sWB")
        sStp = sem("sStp")
        sKVm = sem("sKVm")
        sAcc = sem("sAcc")
        sDVE1 = sem("sDVE1")
        sACT1 = sem("sACT1")
        sFin = sem("sFin")
        sPEq = sem("sPEq")
        sDVEq = sem("sDVEq")
        sACTq = sem("sACTq")
        sSQW = sem("sSQW")
        sXa = sem("sXa")
        sXb = sem("sXb")
        sXc = sem("sXc")
        sSQRa = sem("sSQRa")
        sSQRb = sem("sSQRb")
        sPE2 = sem("sPE2")
        sDVE2 = sem("sDVE2")
        sACT2 = sem("sACT2")
        sY1B = sem("sY1B")
        sOutA = sem("sOutA")
        sOutB = sem("sOutB")
        sOutC = sem("sOutC")

        sX3 = [sXa, sXb, sXc]
        sOut3 = [sOutA, sOutB, sOutC]
        block = ctx.enter_context(nc.Block())

        def slab_wait(eng, g):
            s = sXTe if g % 2 == 0 else sXTo
            eng.wait_ge(s, 16 * (g // 2 + 1))

        @block.tensor
        def _(tensor):
            tensor.wait_ge(sStp, 1)
            tensor.wait_ge(sWLa, 32)

            for t in range(NTC_ALL):
                par = t % 2
                s_idx = t // 2
                toff = (t - 2 * s_idx) * 128
                slab_wait(tensor, s_idx)
                if t >= 2:
                    tensor.wait_ge(sDVE1, 3 * (t - 1))
                for dc in range(16):
                    lhs = slab[s_idx % 2][:, dc, toff:toff + 128]
                    tensor.matmul(psK[par], lhs, wkv_sb[:, dc, 0:512],
                                  start=(dc == 0), stop=False,
                                  skip_group_check=True)
                    tensor.matmul(psV[par], lhs, wkv_sb[:, dc, 512:1024],
                                  start=(dc == 0), stop=False,
                                  skip_group_check=True)
                tensor.matmul(psK[par], onesr[:, :], rowkv[:, 0:512],
                              start=False, stop=True, skip_group_check=True)
                tensor.matmul(psV[par], onesr[:, :], rowkv[:, 512:1024],
                              start=False, stop=True, skip_group_check=True
                              ).then_inc(sKVm, 1)
                if t >= 1:
                    tp = t - 1
                    tensor.wait_ge(sDVE1, 3 * t)
                    for h4 in range(4):
                        tensor.matmul(psKV[:, h4 * 128:(h4 + 1) * 128],
                                      sk_t[tp % 2][:, h4 * 128:(h4 + 1) * 128],
                                      vb_t[tp % 2][:, h4 * 128:(h4 + 1) * 128],
                                      start=(tp == 0 and h4 == 0),
                                      stop=(tp == NTC_ALL - 1 and h4 == 3),
                                      skip_group_check=True)
                    tensor.matmul(psKS, onesb[:, :], sk_t[tp % 2][:, :],
                                  start=(tp == 0), stop=(tp == NTC_ALL - 1),
                                  skip_group_check=True).then_inc(sAcc, 1)
            tp = NTC_ALL - 1
            tensor.wait_ge(sDVE1, 3 * NTC_ALL)
            for h4 in range(4):
                tensor.matmul(psKV[:, h4 * 128:(h4 + 1) * 128],
                              sk_t[tp % 2][:, h4 * 128:(h4 + 1) * 128],
                              vb_t[tp % 2][:, h4 * 128:(h4 + 1) * 128],
                              start=False, stop=(h4 == 3), skip_group_check=True)
            tensor.matmul(psKS, onesb[:, :], sk_t[tp % 2][:, :],
                          start=False, stop=True, skip_group_check=True
                          ).then_inc(sAcc, 1)

            tensor.wait_ge(sFin, 3)
            for j in range(4):
                mm = tensor.matmul(psA[:, j * 512:(j + 1) * 512], onesf[:, :],
                                   ksrow[:, j * 512:(j + 1) * 512], start=True, stop=True)
            mm.then_inc(sFin, 1)

            tensor.wait_ge(sWLb, 16 * 12)
            tensor.wait_ge(sFin, 5)

            def q_group(u):
                par = u % 2
                s_idx = u // 2
                toff = (u - 2 * s_idx) * 128
                slab_wait(tensor, NSLAB_ALL + s_idx)
                if u >= 2:
                    tensor.wait_ge(sACTq, AQ_N * (u - 2) + 4)
                for dc in range(16):
                    lhs = slab[s_idx % 2][:, dc, toff:toff + 128]
                    for j in range(4):
                        tensor.matmul(psQ[par][:, j * 512:(j + 1) * 512], lhs,
                                      U0[:, dc, j * 512:(j + 1) * 512],
                                      start=(dc == 0), stop=False,
                                      skip_group_check=True)
                for j in range(4):
                    mm = tensor.matmul(psQ[par][:, j * 512:(j + 1) * 512],
                                       onesr[:, :],
                                       rowq[:, j * 512:(j + 1) * 512],
                                       start=False, stop=True, skip_group_check=True)
                mm.then_inc(sPEq, 1)

            def t_group(u):
                par = u % 2
                tensor.wait_ge(sDVEq, dq_sqhat(u))
                for h in range(16):
                    mm = tensor.matmul(psQ[par][:, h * 128:(h + 1) * 128],
                                       sqh[:, h * 128:(h + 1) * 128], I_bf[:, :],
                                       start=True, stop=True)
                mm.then_inc(sPEq, 1)

            def qkv_group(u):
                par = u % 2
                tensor.wait_ge(sACTq, AQ_N * u + 3)
                for h in range(16):
                    mm = tensor.matmul(psQ[par][:, h * 128:(h + 1) * 128],
                                       kvb_v[:, h // 4, :],
                                       sqhT[:, h * 128:(h + 1) * 128],
                                       start=True, stop=True)
                mm.then_inc(sPEq, 1)

            for u in range(NTC_OWN):
                q_group(u)
                if u >= 1:
                    t_group(u - 1)
                    qkv_group(u - 1)
            t_group(NTC_OWN - 1)
            qkv_group(NTC_OWN - 1)

            tensor.wait_ge(sACTq, AQ_N * NTC_OWN)

            def mm3_group(u):
                tensor.wait_ge(sSQRa if u % 2 == 0 else sSQRb, 16 * (u // 2 + 1))
                if u >= 1:
                    tensor.wait_ge(sDVE2, d2abs[(u - 1, "s")])
                if u >= 3:
                    tensor.wait_ge(sDVE2, d2abs[(u - 3, "lh")])
                    tensor.wait_ge(sACT2, a2abs[(u - 3, "ehmt")])
                if u == 0:
                    tensor.wait_ge(sWO, 16 * WO_CH)
                sq = sqp[u % 2]
                for h in range(16):
                    for j in range(4):
                        mm = tensor.matmul(psA[:, j * 512:(j + 1) * 512],
                                           sq[:, h * 128:(h + 1) * 128],
                                           U0[:, h, j * 512:(j + 1) * 512],
                                           start=(h == 0), stop=(h == 15))
                mm.then_inc(sPE2, 1)

            def ffn_group(u):
                for r in range(4):
                    if r == 0:
                        tensor.wait_ge(sY1B, u + 1)
                        if u >= 1:
                            tensor.wait_ge(sDVE2, d2abs[(u - 1, "s2")])
                    if r >= 2:
                        tensor.wait_ge(sACT2, a2abs[(u, "ey1t%d" % (r - 2))])
                    for k in range(4):
                        dc = 4 * r + k
                        base_col = (r % 2) * 512 + k * 128
                        mm = tensor.matmul(psB[:, base_col:base_col + 128],
                                           y1b[:, dc * 128:(dc + 1) * 128], I_bf[:, :],
                                           start=True, stop=True)
                    mm.then_inc(sPE2, 1)
                tensor.wait_ge(sACT2, a2abs[(u, "ey1t3")])
                tensor.wait_ge(sDVE2, d2abs[(min(u + 2, NTC_OWN - 1), "s")])
                for dc in range(16):
                    mm = tensor.matmul(psB[:, 0:136],
                                       sqhT[:, dc * 128:(dc + 1) * 128],
                                       ra_sb[:, dc, :],
                                       start=(dc == 0), stop=(dc == 15))
                mm.then_inc(sPE2, 1)
                tensor.wait_ge(sDVE2, d2abs[(u, "hmask")])
                tensor.matmul(psB[:, 512:640], hmask[:, :], I_bf[:, :],
                              start=True, stop=True).then_inc(sPE2, 1)
                tensor.wait_ge(sDVE2, d2abs[(u, "lh")])
                tensor.wait_ge(sACT2, a2abs[(u, "ehmt")])
                if u == 0:
                    tensor.wait_ge(sWB, 16 * (WB_CH + 1))
                for dc in range(16):
                    for j in range(4):
                        tensor.matmul(psB[:, j * 512:(j + 1) * 512],
                                      sqhT[:, dc * 128:(dc + 1) * 128],
                                      wbase_sb[:, dc, j * 512:(j + 1) * 512],
                                      start=(dc == 0), stop=False,
                                      skip_group_check=True)
                for j in range(4):
                    tensor.matmul(psB[:, j * 512:(j + 1) * 512], onesr[:, :],
                                  rowbb[:, j * 512:(j + 1) * 512],
                                  start=False, stop=False, skip_group_check=True)
                for j in range(4):
                    mm = tensor.matmul(psB[:, j * 512:(j + 1) * 512], hmaskT[:, :],
                                       bcat_sb[:, j * 512:(j + 1) * 512],
                                       start=False, stop=True,
                                       skip_group_check=True)
                mm.then_inc(sPE2, 1)

            for u in range(NTC_OWN):
                mm3_group(u)
                if u >= 2:
                    ffn_group(u - 2)
            ffn_group(NTC_OWN - 2)
            ffn_group(NTC_OWN - 1)

        @block.vector
        def _(vector):
            vector.memset(onesf[:, :], 1.0)
            vector.memset(onesb[:, :], 1.0)
            vector.memset(onesr[:, :], 1.0).then_inc(sStp, 1)

            dcnt = [0]

            def op(ins):
                ins.then_inc(sDVE1, 1)
                dcnt[0] += 1
                vector.wait_ge(sDVE1, dcnt[0])
                return ins

            for t in range(NTC_ALL):
                par = t % 2
                vector.wait_ge(sKVm, t + 1)
                if t >= 1:
                    vector.wait_ge(sACT1, t)
                op(vector.tensor_scalar(scr1[:, 0:512], psK[par], 0.0, None, ALU.min))
                vector.wait_ge(sACT1, t + 1)
                if t >= 2:
                    vector.wait_ge(sAcc, t - 1)
                op(vector.scalar_tensor_tensor(sk_t[par][:, :], psK[par], 0.0,
                                               scr2[:, 0:512], ALU.max, ALU.add))
                op(vector.tensor_copy(vb_t[par][:, :], psV[par]))

            vector.wait_ge(sAcc, NTC_ALL)
            vector.tensor_copy(kvb[:, :], psKV).then_inc(sFin, 1)
            vector.tensor_copy(ksr512[:, :], psKS).then_inc(sFin, 1)
            vector.wait_ge(sFin, 2)
            ks_in = ksr512[:, :].rearrange("a (h d) -> a h d", h=4).unsqueeze(2) \
                .broadcast_to([1, 4, 4, 128])
            vector.tensor_copy(ksrow.rearrange("a (h r d) -> a h r d", h=4, r=4),
                               ks_in).then_inc(sFin, 1)
            vector.wait_ge(sFin, 4)
            vector.tensor_copy(ksum_bc[:, :], psA[:, :]).then_inc(sFin, 1)

            dq = [0]

            def opq(ins):
                ins.then_inc(sDVEq, 1)
                dq[0] += 1
                vector.wait_ge(sDVEq, dq[0])
                return ins

            sq3 = sq_p1.rearrange("p (h d) -> p h d", h=16)
            prod3 = prodb.rearrange("p (h d) -> p h d", h=16)
            ks3 = ksum_bc[:, :].rearrange("p (h d) -> p h d", h=16)
            sqh3 = sqh[:, :].rearrange("p (h d) -> p h d", h=16)
            rn3 = rnbf[:, :].unsqueeze(2).broadcast_to([128, 16, 128])
            for u in range(NTC_OWN):
                par = u % 2
                vector.wait_ge(sPEq, peq_q[u])
                if u >= 1:
                    vector.wait_ge(sSQW, 16 * u)
                for j in range(2):
                    sl = slice(j * 1024, (j + 1) * 1024)
                    if u + j >= 1:
                        vector.wait_ge(sACTq, AQ_N * u + j)
                    opq(vector.tensor_scalar(scr1[:, :], psQ[par][:, sl], 0.0,
                                             None, ALU.min))
                    vector.wait_ge(sACTq, AQ_N * u + j + 1)
                    opq(vector.scalar_tensor_tensor(sq_p1[:, sl], psQ[par][:, sl], 0.0,
                                                    scr2[:, :], ALU.max, ALU.add))
                opq(vector.tensor_tensor(prod3, sq3, ks3, ALU.mult))
                opq(vector.tensor_reduce(norm16[:, :], prod3, AX.X, ALU.add))
                opq(vector.tensor_scalar(normeps[:, :], norm16[:, :], ATT_EPS,
                                         None, ALU.add))
                opq(vector.reciprocal(rnorm[:, :], normeps[:, :]))
                opq(vector.tensor_copy(rnbf[:, :], rnorm[:, :]))
                if u >= 1:
                    vector.wait_ge(sPEq, peq_T[u - 1])
                opq(vector.tensor_tensor(sqh3, sq3, rn3, ALU.mult))

            dd = [0]

            def op2(ins):
                ins.then_inc(sDVE2, 1)
                dd[0] += 1
                vector.wait_ge(sDVE2, dd[0])
                return ins

            lh3 = lh[:, 8:136].rearrange("p (e r) -> p e r", e=8)
            hm3 = hmask[:, :].rearrange("p (e r) -> p e r", e=8)
            m8b = mask8[:, :].unsqueeze(2).broadcast_to([128, 8, 16])

            def seg_a(u):
                p = u % 2
                vector.wait_ge(sPE2, pe2_mm3[u])
                vector.wait_ge(sX3[u % 3], 16 * (u // 3 + 1))
                op2(vector.scalar_tensor_tensor(fpA[:, :], psA[:, :], 1.0,
                                                xt_p2[u % 3][:, :], ALU.mult, ALU.add))
                for g in range(4):
                    op2(vector.bn_stats(bnstA[:, g, :], fpA[:, g * 512:(g + 1) * 512]))
                op2(vector.bn_aggr(mvA[:, :], bnstA[:, :, :].rearrange("p a b -> p (a b)")))
                op2(vector.tensor_scalar(negmean[p][:, :], mvA[:, 0:1], -1.0, None, ALU.mult))
                op2(vector.tensor_scalar(veps[p][:, :], mvA[:, 1:2], LN_EPS, None, ALU.add))
                op2(vector.reciprocal(stdv[p][:, :], veps[p][:, :]))
                vector.wait_ge(sACT2, a2abs[(u, "sqrt1")])
                op2(vector.tensor_scalar(xt_p2[u % 3][:, :], fpA[:, :], negmean[p][:, :],
                                         rstd[p][:, :], ALU.add, ALU.mult))

            def seg_b(u):
                p = 2 + u % 2
                vector.wait_ge(sPE2, pe2_misc[u])
                op2(vector.tensor_copy(lh[:, :], psB[:, 0:136]))
                op2(vector.reduce_max(rowmax[:, :], lh[:, 0:8], AX.X))
                op2(vector.tensor_scalar(mask8[:, :], lh[:, 0:8], rowmax[:, :],
                                         2.0, ALU.is_ge, ALU.mult))
                op2(vector.tensor_tensor(hm3, lh3, m8b, ALU.mult))
                vector.wait_ge(sPE2, pe2_base[u])
                vector.wait_ge(sY1B, u + 1)
                op2(vector.scalar_tensor_tensor(fpA[:, :], psB[:, :], 1.0,
                                                xt_p2[u % 3][:, :], ALU.mult, ALU.add))
                for g in range(4):
                    op2(vector.bn_stats(bnstB[:, g, :], fpA[:, g * 512:(g + 1) * 512]))
                op2(vector.bn_aggr(mvB[:, :], bnstB[:, :, :].rearrange("p a b -> p (a b)")))
                op2(vector.tensor_scalar(negmean[p][:, :], mvB[:, 0:1], -1.0, None, ALU.mult))
                op2(vector.tensor_scalar(veps[p][:, :], mvB[:, 1:2], LN_EPS, None, ALU.add))
                op2(vector.reciprocal(stdv[p][:, :], veps[p][:, :]))
                vector.wait_ge(sACT2, a2abs[(u, "sqrt2")])
                op2(vector.tensor_scalar(xt_p2[u % 3][:, :], fpA[:, :], negmean[p][:, :],
                                         rstd[p][:, :], ALU.add, ALU.mult))

            for kind, u in P2_ORDER:
                if kind == "A":
                    seg_a(u)
                else:
                    seg_b(u)

        @block.scalar
        def _(scalar):
            scalar.preload_act_table(AF.Exp)
            for t in range(NTC_ALL):
                scalar.wait_ge(sDVE1, 3 * t + 1)
                scalar.activation(scr2[:, 0:512], scr1[:, 0:512], AF.Exp).then_inc(sACT1, 1)
            for u in range(NTC_OWN):
                par = u % 2
                for j in range(2):
                    scalar.wait_ge(sDVEq, DQ_N * u + 2 * j + 1)
                    scalar.activation(scr2[:, :], scr1[:, :], AF.Exp).then_inc(sACTq, 1)
                scalar.wait_ge(sPEq, peq_T[u])
                scalar.copy(sqhT[:, :], psQ[par][:, :]).then_inc(sACTq, 1)
                scalar.wait_ge(sPEq, peq_qkv[u])
                scalar.wait_ge(sDVEq, dq_sqhat(u))
                scalar.wait_ge(sSQW, 16 * u)
                scalar.copy(sq_p1, psQ[par][:, :]).then_inc(sACTq, 1)
            scalar.preload_act_table(AF.Sqrt)
            for kind, u in P2_ORDER:
                if kind == "A":
                    p = u % 2
                    scalar.wait_ge(sDVE2, d2abs[(u, "rv")])
                    scalar.activation(rstd[p][:, :], stdv[p][:, :], AF.Sqrt).then_inc(sACT2, 1)
                else:
                    p = 2 + u % 2
                    for r in range(4):
                        scalar.wait_ge(sPE2, pe2_y1t[(u, r)])
                        scalar.copy(sqhT[:, r * 512:(r + 1) * 512],
                                    psB[:, (r % 2) * 512:(r % 2) * 512 + 512]).then_inc(sACT2, 1)
                    scalar.wait_ge(sPE2, pe2_hmt[u])
                    scalar.copy(hmaskT[:, :], psB[:, 512:640]).then_inc(sACT2, 1)
                    scalar.wait_ge(sDVE2, d2abs[(u, "rv2")])
                    scalar.activation(rstd[p][:, :], stdv[p][:, :], AF.Sqrt).then_inc(sACT2, 1)
                    scalar.wait_ge(sDVE2, d2abs[(u, "y2")])
                    scalar.dma_start(out=y_out[u * 128:(u + 1) * 128, :],
                                     in_=xt_p2[u % 3][:, :]).then_inc(sOut3[u % 3], 16)

        @block.gpsimd
        def _(gpsimd):
            wq_src = wq_t.rearrange("(c p) n -> p c n", p=128)
            gpsimd.dma_start(out=wkv_sb, in_=wkv_t.rearrange("(c p) n -> p c n", p=128)
                             ).then_inc(sWLa, 16)
            gpsimd.dma_start(out=rowkv, in_=brows[:, 0:1024]).then_inc(sWLa, 16)
            gpsimd.wait_ge(sWLa, 32)
            wlb_small = [
                (rowq, brows[:, 1024:3072]),
                (I_bf[:, :], ident),
                (ra_sb[:, :, :], ra_t.rearrange("(c p) n -> p c n", p=128)),
                (bcat_sb[:, :], bcat_t),
            ]
            wlb_jobs = wlb_small + [
                (U0[:, 2 * i:2 * i + 2, :], wq_src[:, 2 * i:2 * i + 2, :])
                for i in range(8)]
            wlb_i = [0]

            def emit_wlb():
                if wlb_i[0] < len(wlb_jobs):
                    dst, s_ = wlb_jobs[wlb_i[0]]
                    gpsimd.dma_start(out=dst, in_=s_).then_inc(sWLb, 16)
                    wlb_i[0] += 1

            for s in range(NSLAB_ALL):
                if s >= 2:
                    gpsimd.wait_ge(sAcc, 2 * (s - 1))
                src = xt_full.rearrange("(c p) t -> p c t", p=128)[:, :, s * 256:(s + 1) * 256]
                gpsimd.dma_start(out=slab[s % 2], in_=src).then_inc(
                    sXTe if s % 2 == 0 else sXTo, 16)
                if 2 <= s < 6 or (s >= 8 and s % 2 == 0):
                    emit_wlb()
            while wlb_i[0] < len(wlb_jobs):
                emit_wlb()
            wo_src = wo_t.rearrange("(c p) n -> p c n", p=128)
            wb_src = wbase_t.rearrange("(c p) n -> p c n", p=128)
            for s in range(NSLAB_OWN):
                if s < 2:
                    gpsimd.wait_ge(sAcc, NTC_ALL)
                else:
                    gpsimd.wait_ge(sPEq, peq_q[2 * (s - 1)])
                g = NSLAB_ALL + s
                src = xt_own.rearrange("(c p) t -> p c t", p=128)[:, :, s * 256:(s + 1) * 256]
                gpsimd.dma_start(out=slab[g % 2], in_=src).then_inc(
                    sXTe if g % 2 == 0 else sXTo, 16)
                if 2 <= s < 6:
                    i = s - 2
                    if i == 0:
                        gpsimd.wait_ge(sKVm, NTC_ALL)
                    gpsimd.dma_start(out=wbase_sb[:, 2 * i:2 * i + 2, :],
                                     in_=wb_src[:, 2 * i:2 * i + 2, :]).then_inc(sWB, 16)
            gpsimd.wait_ge(sPEq, peq_q[NTC_OWN - 1])
            for i in range(WO_CH):
                gpsimd.dma_start(out=U0[:, 2 * i:2 * i + 2, :],
                                 in_=wo_src[:, 2 * i:2 * i + 2, :]).then_inc(sWO, 16)
            gpsimd.wait_ge(sDVEq, DQ_N * NTC_OWN)
            gpsimd.wait_ge(sACTq, AQ_N * NTC_OWN)
            gpsimd.wait_ge(sSQW, 16 * NTC_OWN)
            gpsimd.wait_ge(sPEq, peq_T[NTC_OWN - 1])
            for i in range(4, WB_CH):
                gpsimd.dma_start(out=wbase_sb[:, 2 * i:2 * i + 2, :],
                                 in_=wb_src[:, 2 * i:2 * i + 2, :]).then_inc(sWB, 16)
            gpsimd.dma_start(out=rowbb, in_=brows[:, 3072:5120]).then_inc(sWB, 16)

            def x_load_g(v):
                if v == 0:
                    gpsimd.wait_ge(sFin, 4)
                if v >= 3:
                    gpsimd.wait_ge(sOut3[v % 3], 16 * ((v - 3) // 3 + 1))
                gpsimd.dma_start(out=xt_p2[v % 3][:, :],
                                 in_=x_own[v * 128:(v + 1) * 128, :]).then_inc(
                                     sX3[v % 3], 16)

            x_load_g(0)
            x_load_g(1)
            x_load_g(2)
            for u in range(NTC_OWN):
                gpsimd.wait_ge(sDVE2, d2abs[(u, "y1")])
                if u >= 1:
                    gpsimd.wait_ge(sPE2, pe2_y1t[(u - 1, 3)])
                gpsimd.tensor_copy(y1b[:, :], xt_p2[u % 3][:, :]).then_inc(sY1B, 1)
                if u + 3 < NTC_OWN:
                    x_load_g(u + 3)

        @block.sync
        def _(sync):
            for u in range(NTC_OWN):
                sync.wait_ge(sACTq, AQ_N * (u + 1))
                sync.dma_start(out=attn_dram[u], in_=sq_p1).then_inc(sSQW, 16)
            def sq_load(u):
                if u >= 2:
                    sync.wait_ge(sPE2, pe2_mm3[u - 2])
                sync.dma_start(out=sqp[u % 2][:, :], in_=attn_dram[u]).then_inc(
                    sSQRa if u % 2 == 0 else sSQRb, 16)

            for u in range(NTC_OWN):
                sq_load(u)
            for p3 in range(3):
                n_p = len([u for u in range(NTC_OWN) if u % 3 == p3])
                sync.wait_ge(sOut3[p3], 16 * n_p)


    return nc


def kernel(L0_attn_hidden_states, L0_attn_Wq, L0_attn_bq, L0_attn_Wk, L0_attn_bk,
           L0_attn_Wv, L0_attn_bv, L0_attn_Wo, L0_ffn_W_base, L0_ffn_b_base,
           L0_ffn_router_W, L0_ffn_A, L0_ffn_Bw):
    f32 = np.float32
    X = np.ascontiguousarray(np.asarray(L0_attn_hidden_states, dtype=f32))
    Wq = np.asarray(L0_attn_Wq, f32)
    Wk = np.asarray(L0_attn_Wk, f32)
    Wv = np.asarray(L0_attn_Wv, f32)
    Wo = np.asarray(L0_attn_Wo, f32)
    Wb = np.asarray(L0_ffn_W_base, f32)
    Rw = np.asarray(L0_ffn_router_W, f32)
    A = np.asarray(L0_ffn_A, f32)
    Bw = np.asarray(L0_ffn_Bw, f32)

    wkv_t = np.ascontiguousarray(np.concatenate([Wk, Wv], axis=0).T)
    wq_t = np.ascontiguousarray(Wq.T)
    wo_t = np.ascontiguousarray(Wo.T)
    wbase_t = np.ascontiguousarray(Wb.T)
    ra_t = np.ascontiguousarray(
        np.concatenate([Rw, A.reshape(128, D)], axis=0).T)
    bcat_t = np.ascontiguousarray(Bw.transpose(0, 2, 1).reshape(128, D))
    brows = np.concatenate([
        np.asarray(L0_attn_bk, f32), np.asarray(L0_attn_bv, f32),
        np.asarray(L0_attn_bq, f32), np.asarray(L0_ffn_b_base, f32)
    ]).reshape(1, 5120)
    ident = np.eye(128, dtype=f32)

    xt_b = [np.ascontiguousarray(X[b].T) for b in range(2)]

    nc = build_nc()
    in_maps = []
    for c in range(8):
        b, j = c // 4, c % 4
        in_maps.append(dict(
            xt_full=xt_b[b],
            xt_own=np.ascontiguousarray(xt_b[b][:, j * TOK:(j + 1) * TOK]),
            x_own=np.ascontiguousarray(X[b, j * TOK:(j + 1) * TOK, :]),
            wkv_t=wkv_t, wq_t=wq_t, wo_t=wo_t, wbase_t=wbase_t,
            ra_t=ra_t, bcat_t=bcat_t, brows=brows, ident=ident,
        ))
    res = run_bass_kernel_spmd(nc, in_maps, core_ids=list(range(8)))
    y = np.empty((2, SEQ, D), dtype=f32)
    for c in range(8):
        b, j = c // 4, c % 4
        y[b, j * TOK:(j + 1) * TOK, :] = res.results[c]["y"]
    return y


# version 10
# speedup vs baseline: 1.3259x; 1.0485x over previous
import numpy as np
from contextlib import ExitStack

import concourse.bass as bass
import concourse.mybir as mybir
from concourse.bass_utils import run_bass_kernel_spmd

F32 = mybir.dt.float32
BF16 = mybir.dt.bfloat16
ALU = mybir.AluOpType
AX = mybir.AxisListType
AF = mybir.ActivationFunctionType

D = 2048
SEQ = 8192
TOK = 2048
NTC_ALL = 64
NTC_OWN = 16
NSLAB_ALL = 32
NSLAB_OWN = 8
ATT_EPS = 1e-6
LN_EPS = 1e-5
WO_CH = 8
WB_CH = 8
N_P2W = WO_CH + WB_CH

peq_q, peq_T, peq_qkv = {}, {}, {}
_c = 0
for _u in range(NTC_OWN):
    _c += 1
    peq_q[_u] = _c
    if _u >= 1:
        _c += 1
        peq_T[_u - 1] = _c
        _c += 1
        peq_qkv[_u - 1] = _c
_c += 1
peq_T[NTC_OWN - 1] = _c
_c += 1
peq_qkv[NTC_OWN - 1] = _c

pe2_mm3 = {}
pe2_y1t = {}
pe2_misc, pe2_hmt, pe2_base = {}, {}, {}


def _plan_pe2():
    global _c
    c = 0

    def ffn(u):
        nonlocal c
        for r in range(4):
            c += 1
            pe2_y1t[(u, r)] = c
        c += 1
        pe2_misc[u] = c
        c += 1
        pe2_hmt[u] = c
        c += 1
        pe2_base[u] = c

    for u in range(NTC_OWN):
        c += 1
        pe2_mm3[u] = c
        if u >= 2:
            ffn(u - 2)
    ffn(NTC_OWN - 2)
    ffn(NTC_OWN - 1)


_plan_pe2()

DQ_N = 10
def dq_sqhat(u):
    return DQ_N * u + 10

_DA_OPS = ["s", "b0", "b1", "b2", "b3", "aggr", "negmean", "veps", "rv", "y1"]
_DB_OPS = ["lh", "rowmax", "mask8", "hmask", "s2", "c0", "c1", "c2", "c3",
           "aggr2", "negmean2", "veps2", "rv2", "y2"]
d2abs = {}
_c2 = 0
def _seg(u, ops):
    global _c2
    for name in ops:
        _c2 += 1
        d2abs[(u, name)] = _c2
def _p2_sched():
    order = []
    for u in range(NTC_OWN):
        order.append(("A", u))
        if u >= 2:
            order.append(("B", u - 2))
    order.append(("B", NTC_OWN - 2))
    order.append(("B", NTC_OWN - 1))
    return order
P2_ORDER = _p2_sched()
for _kind, _u in P2_ORDER:
    _seg(_u, _DA_OPS if _kind == "A" else _DB_OPS)
a2abs = {}
_c3 = 0
for _kind, _u in P2_ORDER:
    if _kind == "A":
        _c3 += 1
        a2abs[(_u, "sqrt1")] = _c3
    else:
        for _r in range(4):
            _c3 += 1
            a2abs[(_u, "ey1t%d" % _r)] = _c3
        _c3 += 1
        a2abs[(_u, "ehmt")] = _c3
        _c3 += 1
        a2abs[(_u, "sqrt2")] = _c3

AQ_N = 4


def build_nc():
    nc = bass.Bass("TRN2", target_bir_lowering=False)

    xt_full = nc.dram_tensor("xt_full", [D, SEQ], F32, kind="ExternalInput").ap()
    xt_own = nc.dram_tensor("xt_own", [D, TOK], F32, kind="ExternalInput").ap()
    x_own = nc.dram_tensor("x_own", [TOK, D], F32, kind="ExternalInput").ap()
    wkv_t = nc.dram_tensor("wkv_t", [D, 1024], F32, kind="ExternalInput").ap()
    wq_t = nc.dram_tensor("wq_t", [D, D], F32, kind="ExternalInput").ap()
    wo_t = nc.dram_tensor("wo_t", [D, D], F32, kind="ExternalInput").ap()
    wbase_t = nc.dram_tensor("wbase_t", [D, D], F32, kind="ExternalInput").ap()
    ra_t = nc.dram_tensor("ra_t", [D, 136], F32, kind="ExternalInput").ap()
    bcat_t = nc.dram_tensor("bcat_t", [128, D], F32, kind="ExternalInput").ap()
    brows = nc.dram_tensor("brows", [1, 5120], F32, kind="ExternalInput").ap()
    ident = nc.dram_tensor("ident", [128, 128], F32, kind="ExternalInput").ap()
    y_out = nc.dram_tensor("y", [TOK, D], F32, kind="ExternalOutput").ap()
    attn_dram = nc.dram_tensor("attn_scratch", [NTC_OWN, 128, D], BF16, kind="Internal").ap()

    with ExitStack() as ctx:
        sb = lambda name, shape, dt: ctx.enter_context(nc.sbuf_tensor(name, shape, dt))
        sem = lambda name: ctx.enter_context(nc.semaphore(name))

        U0 = sb("u0", [128, 16, 2048], BF16)
        U1 = sb("u1", [128, 32768], BF16)
        wkv_sb = U1[:, 0:16384].rearrange("p (c n) -> p c n", c=16)
        slab = [U1[:, 16384:20480].rearrange("p (c n) -> p c n", c=16),
                U1[:, 20480:24576].rearrange("p (c n) -> p c n", c=16)]
        sk_t = [U1[:, 24576:25088], U1[:, 25088:25600]]
        vb_t = [U1[:, 25600:26112], U1[:, 26112:26624]]
        sq_p1 = U1[:, 26624:28672]
        scr1 = U1[:, 28672:30720].bitcast(F32)
        scr2 = U1[:, 30720:32768].bitcast(F32)
        wbase_sb = U1.rearrange("p (c n) -> p c n", c=16)

        fpA = sb("fpA", [128, D], F32)
        fpB = sb("fpB", [128, D], F32)
        prodb = fpB[:, :].bitcast(BF16)[:, 0:2048]
        kvb = sb("kvb", [128, 512], BF16)
        kvb_v = kvb[:, :].rearrange("p (h e) -> p h e", h=4)
        ksr512 = sb("ksr512", [1, 512], F32)
        ksum_bc = sb("ksum_bc", [128, D], BF16)
        I_bf = sb("I_bf", [128, 128], BF16)
        onesf = sb("onesf", [1, 128], F32)
        onesb = sb("onesb", [128, 1], BF16)
        onesr = sb("onesr", [1, 128], BF16)
        ra_sb = sb("ra_sb", [128, 16, 136], BF16)
        bcat_sb = sb("bcat_sb", [128, D], BF16)

        xt_p2 = [sb("xt_p2a", [128, D], F32), sb("xt_p2b", [128, D], F32),
                 sb("xt_p2c", [128, D], F32)]
        ksrow = xt_p2[0][0:1, :]
        sq_p2 = sb("sq_p2", [128, D], BF16)
        sq_p2b = sb("sq_p2b", [128, D], BF16)
        sqp = [sq_p2, sq_p2b]
        sqh = sb("sqh", [128, D], BF16)
        sqhT = sb("sqhT", [128, D], BF16)
        y1b = sb("y1b", [128, D], BF16)
        rowkv = sq_p2[0:1, 0:1024]
        rowq = y1b[0:1, 0:2048]
        rowbb = sqh[0:1, 0:2048]
        lh = sb("lh", [128, 136], F32)
        hmask = sb("hmask", [128, 128], BF16)
        hmaskT = sb("hmaskT", [128, 128], BF16)
        norm16 = sb("norm16", [128, 16], F32)
        normeps = sb("normeps", [128, 16], F32)
        rnorm = sb("rnorm", [128, 16], F32)
        rnbf = sb("rnbf", [128, 16], BF16)
        rowmax = sb("rowmax", [128, 1], F32)
        mask8 = sb("mask8", [128, 8], F32)
        negmean = [sb("negmean%d" % i, [128, 1], F32) for i in range(4)]
        veps = [sb("veps%d" % i, [128, 1], F32) for i in range(4)]
        stdv = [sb("stdv%d" % i, [128, 1], F32) for i in range(4)]
        rstd = [sb("rstd%d" % i, [128, 1], F32) for i in range(4)]
        bnstA = sb("bnstA", [128, 4, 6], F32)
        bnstB = sb("bnstB", [128, 4, 6], F32)
        mvA = sb("mvA", [128, 2], F32)
        mvB = sb("mvB", [128, 2], F32)

        psA = ctx.enter_context(nc.psum_tensor("psA", [128, 2048], F32))
        psB = ctx.enter_context(nc.psum_tensor("psB", [128, 2048], F32))
        psK = [psA[:, 0:512], psA[:, 512:1024]]
        psV = [psA[:, 1024:1536], psA[:, 1536:2048]]
        psKV = psB[:, 0:512]
        psKS = psB[0:1, 512:1024]
        psQ = [psA, psB]

        sWLa = sem("sWLa")
        sWLb = sem("sWLb")
        sXTe = sem("sXTe")
        sXTo = sem("sXTo")
        sWO = sem("sWO")
        sWB = sem("sWB")
        sStp = sem("sStp")
        sKVm = sem("sKVm")
        sAcc = sem("sAcc")
        sDVE1 = sem("sDVE1")
        sACT1 = sem("sACT1")
        sFin = sem("sFin")
        sPEq = sem("sPEq")
        sDVEq = sem("sDVEq")
        sACTq = sem("sACTq")
        sSQW = sem("sSQW")
        sXa = sem("sXa")
        sXb = sem("sXb")
        sXc = sem("sXc")
        sSQRa = sem("sSQRa")
        sSQRb = sem("sSQRb")
        sPE2 = sem("sPE2")
        sDVE2 = sem("sDVE2")
        sACT2 = sem("sACT2")
        sY1B = sem("sY1B")
        sOutS = sem("sOutS")

        sX3 = [sXa, sXb, sXc]
        block = ctx.enter_context(nc.Block())

        def slab_wait(eng, g):
            s = sXTe if g % 2 == 0 else sXTo
            eng.wait_ge(s, 16 * (g // 2 + 1))

        @block.tensor
        def _(tensor):
            tensor.wait_ge(sStp, 1)
            tensor.wait_ge(sWLa, 32)

            for t in range(NTC_ALL):
                par = t % 2
                s_idx = t // 2
                toff = (t - 2 * s_idx) * 128
                slab_wait(tensor, s_idx)
                if t >= 2:
                    tensor.wait_ge(sDVE1, 3 * (t - 1))
                for dc in range(16):
                    lhs = slab[s_idx % 2][:, dc, toff:toff + 128]
                    tensor.matmul(psK[par], lhs, wkv_sb[:, dc, 0:512],
                                  start=(dc == 0), stop=False,
                                  skip_group_check=True)
                    tensor.matmul(psV[par], lhs, wkv_sb[:, dc, 512:1024],
                                  start=(dc == 0), stop=False,
                                  skip_group_check=True)
                tensor.matmul(psK[par], onesr[:, :], rowkv[:, 0:512],
                              start=False, stop=True, skip_group_check=True)
                tensor.matmul(psV[par], onesr[:, :], rowkv[:, 512:1024],
                              start=False, stop=True, skip_group_check=True
                              ).then_inc(sKVm, 1)
                if t >= 1:
                    tp = t - 1
                    tensor.wait_ge(sDVE1, 3 * t)
                    for h4 in range(4):
                        tensor.matmul(psKV[:, h4 * 128:(h4 + 1) * 128],
                                      sk_t[tp % 2][:, h4 * 128:(h4 + 1) * 128],
                                      vb_t[tp % 2][:, h4 * 128:(h4 + 1) * 128],
                                      start=(tp == 0 and h4 == 0),
                                      stop=(tp == NTC_ALL - 1 and h4 == 3),
                                      skip_group_check=True)
                    tensor.matmul(psKS, onesb[:, :], sk_t[tp % 2][:, :],
                                  start=(tp == 0), stop=(tp == NTC_ALL - 1),
                                  skip_group_check=True).then_inc(sAcc, 1)
            tp = NTC_ALL - 1
            tensor.wait_ge(sDVE1, 3 * NTC_ALL)
            for h4 in range(4):
                tensor.matmul(psKV[:, h4 * 128:(h4 + 1) * 128],
                              sk_t[tp % 2][:, h4 * 128:(h4 + 1) * 128],
                              vb_t[tp % 2][:, h4 * 128:(h4 + 1) * 128],
                              start=False, stop=(h4 == 3), skip_group_check=True)
            tensor.matmul(psKS, onesb[:, :], sk_t[tp % 2][:, :],
                          start=False, stop=True, skip_group_check=True
                          ).then_inc(sAcc, 1)

            tensor.wait_ge(sFin, 3)
            for j in range(4):
                mm = tensor.matmul(psA[:, j * 512:(j + 1) * 512], onesf[:, :],
                                   ksrow[:, j * 512:(j + 1) * 512], start=True, stop=True)
            mm.then_inc(sFin, 1)

            tensor.wait_ge(sWLb, 16 * 12)
            tensor.wait_ge(sFin, 5)

            def q_group(u):
                par = u % 2
                s_idx = u // 2
                toff = (u - 2 * s_idx) * 128
                slab_wait(tensor, NSLAB_ALL + s_idx)
                if u >= 2:
                    tensor.wait_ge(sACTq, AQ_N * (u - 2) + 4)
                for dc in range(16):
                    lhs = slab[s_idx % 2][:, dc, toff:toff + 128]
                    for j in range(4):
                        tensor.matmul(psQ[par][:, j * 512:(j + 1) * 512], lhs,
                                      U0[:, dc, j * 512:(j + 1) * 512],
                                      start=(dc == 0), stop=False,
                                      skip_group_check=True)
                for j in range(4):
                    mm = tensor.matmul(psQ[par][:, j * 512:(j + 1) * 512],
                                       onesr[:, :],
                                       rowq[:, j * 512:(j + 1) * 512],
                                       start=False, stop=True, skip_group_check=True)
                mm.then_inc(sPEq, 1)

            def t_group(u):
                par = u % 2
                tensor.wait_ge(sDVEq, dq_sqhat(u))
                for h in range(16):
                    mm = tensor.matmul(psQ[par][:, h * 128:(h + 1) * 128],
                                       sqh[:, h * 128:(h + 1) * 128], I_bf[:, :],
                                       start=True, stop=True)
                mm.then_inc(sPEq, 1)

            def qkv_group(u):
                par = u % 2
                tensor.wait_ge(sACTq, AQ_N * u + 3)
                for h in range(16):
                    mm = tensor.matmul(psQ[par][:, h * 128:(h + 1) * 128],
                                       kvb_v[:, h // 4, :],
                                       sqhT[:, h * 128:(h + 1) * 128],
                                       start=True, stop=True)
                mm.then_inc(sPEq, 1)

            for u in range(NTC_OWN):
                q_group(u)
                if u >= 1:
                    t_group(u - 1)
                    qkv_group(u - 1)
            t_group(NTC_OWN - 1)
            qkv_group(NTC_OWN - 1)

            tensor.wait_ge(sACTq, AQ_N * NTC_OWN)

            def mm3_group(u):
                tensor.wait_ge(sSQRa if u % 2 == 0 else sSQRb, 16 * (u // 2 + 1))
                if u >= 1:
                    tensor.wait_ge(sDVE2, d2abs[(u - 1, "s")])
                if u >= 3:
                    tensor.wait_ge(sDVE2, d2abs[(u - 3, "lh")])
                    tensor.wait_ge(sACT2, a2abs[(u - 3, "ehmt")])
                if u == 0:
                    tensor.wait_ge(sWO, 16 * WO_CH)
                sq = sqp[u % 2]
                for h in range(16):
                    for j in range(4):
                        mm = tensor.matmul(psA[:, j * 512:(j + 1) * 512],
                                           sq[:, h * 128:(h + 1) * 128],
                                           U0[:, h, j * 512:(j + 1) * 512],
                                           start=(h == 0), stop=(h == 15))
                mm.then_inc(sPE2, 1)

            def ffn_group(u):
                for r in range(4):
                    if r == 0:
                        tensor.wait_ge(sY1B, u + 1)
                        if u >= 1:
                            tensor.wait_ge(sDVE2, d2abs[(u - 1, "s2")])
                    if r >= 2:
                        tensor.wait_ge(sACT2, a2abs[(u, "ey1t%d" % (r - 2))])
                    for k in range(4):
                        dc = 4 * r + k
                        base_col = (r % 2) * 512 + k * 128
                        mm = tensor.matmul(psB[:, base_col:base_col + 128],
                                           y1b[:, dc * 128:(dc + 1) * 128], I_bf[:, :],
                                           start=True, stop=True)
                    mm.then_inc(sPE2, 1)
                tensor.wait_ge(sACT2, a2abs[(u, "ey1t3")])
                tensor.wait_ge(sDVE2, d2abs[(min(u + 2, NTC_OWN - 1), "s")])
                for dc in range(16):
                    mm = tensor.matmul(psB[:, 0:136],
                                       sqhT[:, dc * 128:(dc + 1) * 128],
                                       ra_sb[:, dc, :],
                                       start=(dc == 0), stop=(dc == 15))
                mm.then_inc(sPE2, 1)
                tensor.wait_ge(sDVE2, d2abs[(u, "hmask")])
                tensor.matmul(psB[:, 512:640], hmask[:, :], I_bf[:, :],
                              start=True, stop=True).then_inc(sPE2, 1)
                tensor.wait_ge(sDVE2, d2abs[(u, "lh")])
                tensor.wait_ge(sACT2, a2abs[(u, "ehmt")])
                if u == 0:
                    tensor.wait_ge(sWB, 16 * (WB_CH + 1))
                for dc in range(16):
                    for j in range(4):
                        tensor.matmul(psB[:, j * 512:(j + 1) * 512],
                                      sqhT[:, dc * 128:(dc + 1) * 128],
                                      wbase_sb[:, dc, j * 512:(j + 1) * 512],
                                      start=(dc == 0), stop=False,
                                      skip_group_check=True)
                for j in range(4):
                    tensor.matmul(psB[:, j * 512:(j + 1) * 512], onesr[:, :],
                                  rowbb[:, j * 512:(j + 1) * 512],
                                  start=False, stop=False, skip_group_check=True)
                for j in range(4):
                    mm = tensor.matmul(psB[:, j * 512:(j + 1) * 512], hmaskT[:, :],
                                       bcat_sb[:, j * 512:(j + 1) * 512],
                                       start=False, stop=True,
                                       skip_group_check=True)
                mm.then_inc(sPE2, 1)

            for u in range(NTC_OWN):
                mm3_group(u)
                if u >= 2:
                    ffn_group(u - 2)
            ffn_group(NTC_OWN - 2)
            ffn_group(NTC_OWN - 1)

        @block.vector
        def _(vector):
            vector.memset(onesf[:, :], 1.0)
            vector.memset(onesb[:, :], 1.0)
            vector.memset(onesr[:, :], 1.0).then_inc(sStp, 1)

            dcnt = [0]

            def op(ins):
                ins.then_inc(sDVE1, 1)
                dcnt[0] += 1
                vector.wait_ge(sDVE1, dcnt[0])
                return ins

            for t in range(NTC_ALL):
                par = t % 2
                vector.wait_ge(sKVm, t + 1)
                if t >= 1:
                    vector.wait_ge(sACT1, t)
                op(vector.tensor_scalar(scr1[:, 0:512], psK[par], 0.0, None, ALU.min))
                vector.wait_ge(sACT1, t + 1)
                if t >= 2:
                    vector.wait_ge(sAcc, t - 1)
                op(vector.scalar_tensor_tensor(sk_t[par][:, :], psK[par], 0.0,
                                               scr2[:, 0:512], ALU.max, ALU.add))
                op(vector.tensor_copy(vb_t[par][:, :], psV[par]))

            vector.wait_ge(sAcc, NTC_ALL)
            vector.tensor_copy(kvb[:, :], psKV).then_inc(sFin, 1)
            vector.tensor_copy(ksr512[:, :], psKS).then_inc(sFin, 1)
            vector.wait_ge(sFin, 2)
            ks_in = ksr512[:, :].rearrange("a (h d) -> a h d", h=4).unsqueeze(2) \
                .broadcast_to([1, 4, 4, 128])
            vector.tensor_copy(ksrow.rearrange("a (h r d) -> a h r d", h=4, r=4),
                               ks_in).then_inc(sFin, 1)
            vector.wait_ge(sFin, 4)
            vector.tensor_copy(ksum_bc[:, :], psA[:, :]).then_inc(sFin, 1)

            dq = [0]

            def opq(ins):
                ins.then_inc(sDVEq, 1)
                dq[0] += 1
                vector.wait_ge(sDVEq, dq[0])
                return ins

            sq3 = sq_p1.rearrange("p (h d) -> p h d", h=16)
            prod3 = prodb.rearrange("p (h d) -> p h d", h=16)
            ks3 = ksum_bc[:, :].rearrange("p (h d) -> p h d", h=16)
            sqh3 = sqh[:, :].rearrange("p (h d) -> p h d", h=16)
            rn3 = rnbf[:, :].unsqueeze(2).broadcast_to([128, 16, 128])
            for u in range(NTC_OWN):
                par = u % 2
                vector.wait_ge(sPEq, peq_q[u])
                if u >= 1:
                    vector.wait_ge(sSQW, 16 * u)
                for j in range(2):
                    sl = slice(j * 1024, (j + 1) * 1024)
                    if u + j >= 1:
                        vector.wait_ge(sACTq, AQ_N * u + j)
                    opq(vector.tensor_scalar(scr1[:, :], psQ[par][:, sl], 0.0,
                                             None, ALU.min))
                    vector.wait_ge(sACTq, AQ_N * u + j + 1)
                    opq(vector.scalar_tensor_tensor(sq_p1[:, sl], psQ[par][:, sl], 0.0,
                                                    scr2[:, :], ALU.max, ALU.add))
                opq(vector.tensor_tensor(prod3, sq3, ks3, ALU.mult))
                opq(vector.tensor_reduce(norm16[:, :], prod3, AX.X, ALU.add))
                opq(vector.tensor_scalar(normeps[:, :], norm16[:, :], ATT_EPS,
                                         None, ALU.add))
                opq(vector.reciprocal(rnorm[:, :], normeps[:, :]))
                opq(vector.tensor_copy(rnbf[:, :], rnorm[:, :]))
                if u >= 1:
                    vector.wait_ge(sPEq, peq_T[u - 1])
                opq(vector.tensor_tensor(sqh3, sq3, rn3, ALU.mult))

            dd = [0]

            def op2(ins):
                ins.then_inc(sDVE2, 1)
                dd[0] += 1
                vector.wait_ge(sDVE2, dd[0])
                return ins

            lh3 = lh[:, 8:136].rearrange("p (e r) -> p e r", e=8)
            hm3 = hmask[:, :].rearrange("p (e r) -> p e r", e=8)
            m8b = mask8[:, :].unsqueeze(2).broadcast_to([128, 8, 16])

            def seg_a(u):
                p = u % 2
                vector.wait_ge(sPE2, pe2_mm3[u])
                vector.wait_ge(sX3[u % 3], 16 * (u // 3 + 1))
                op2(vector.scalar_tensor_tensor(fpA[:, :], psA[:, :], 1.0,
                                                xt_p2[u % 3][:, :], ALU.mult, ALU.add))
                for g in range(4):
                    op2(vector.bn_stats(bnstA[:, g, :], fpA[:, g * 512:(g + 1) * 512]))
                op2(vector.bn_aggr(mvA[:, :], bnstA[:, :, :].rearrange("p a b -> p (a b)")))
                op2(vector.tensor_scalar(negmean[p][:, :], mvA[:, 0:1], -1.0, None, ALU.mult))
                op2(vector.tensor_scalar(veps[p][:, :], mvA[:, 1:2], LN_EPS, None, ALU.add))
                op2(vector.reciprocal(stdv[p][:, :], veps[p][:, :]))
                vector.wait_ge(sACT2, a2abs[(u, "sqrt1")])
                op2(vector.tensor_scalar(xt_p2[u % 3][:, :], fpA[:, :], negmean[p][:, :],
                                         rstd[p][:, :], ALU.add, ALU.mult))

            def seg_b(u):
                p = 2 + u % 2
                vector.wait_ge(sPE2, pe2_misc[u])
                op2(vector.tensor_copy(lh[:, :], psB[:, 0:136]))
                op2(vector.reduce_max(rowmax[:, :], lh[:, 0:8], AX.X))
                op2(vector.tensor_scalar(mask8[:, :], lh[:, 0:8], rowmax[:, :],
                                         2.0, ALU.is_ge, ALU.mult))
                op2(vector.tensor_tensor(hm3, lh3, m8b, ALU.mult))
                vector.wait_ge(sPE2, pe2_base[u])
                vector.wait_ge(sY1B, u + 1)
                op2(vector.scalar_tensor_tensor(fpA[:, :], psB[:, :], 1.0,
                                                xt_p2[u % 3][:, :], ALU.mult, ALU.add))
                for g in range(4):
                    op2(vector.bn_stats(bnstB[:, g, :], fpA[:, g * 512:(g + 1) * 512]))
                op2(vector.bn_aggr(mvB[:, :], bnstB[:, :, :].rearrange("p a b -> p (a b)")))
                op2(vector.tensor_scalar(negmean[p][:, :], mvB[:, 0:1], -1.0, None, ALU.mult))
                op2(vector.tensor_scalar(veps[p][:, :], mvB[:, 1:2], LN_EPS, None, ALU.add))
                op2(vector.reciprocal(stdv[p][:, :], veps[p][:, :]))
                vector.wait_ge(sACT2, a2abs[(u, "sqrt2")])
                if u >= 1:
                    vector.wait_ge(sOutS, 16 * u)
                op2(vector.tensor_scalar(fpB[:, :], fpA[:, :], negmean[p][:, :],
                                         rstd[p][:, :], ALU.add, ALU.mult))

            for kind, u in P2_ORDER:
                if kind == "A":
                    seg_a(u)
                else:
                    seg_b(u)

        @block.scalar
        def _(scalar):
            scalar.preload_act_table(AF.Exp)
            for t in range(NTC_ALL):
                scalar.wait_ge(sDVE1, 3 * t + 1)
                scalar.activation(scr2[:, 0:512], scr1[:, 0:512], AF.Exp).then_inc(sACT1, 1)
            for u in range(NTC_OWN):
                par = u % 2
                for j in range(2):
                    scalar.wait_ge(sDVEq, DQ_N * u + 2 * j + 1)
                    scalar.activation(scr2[:, :], scr1[:, :], AF.Exp).then_inc(sACTq, 1)
                scalar.wait_ge(sPEq, peq_T[u])
                scalar.copy(sqhT[:, :], psQ[par][:, :]).then_inc(sACTq, 1)
                scalar.wait_ge(sPEq, peq_qkv[u])
                scalar.wait_ge(sDVEq, dq_sqhat(u))
                scalar.wait_ge(sSQW, 16 * u)
                scalar.copy(sq_p1, psQ[par][:, :]).then_inc(sACTq, 1)
            scalar.preload_act_table(AF.Sqrt)
            for kind, u in P2_ORDER:
                if kind == "A":
                    p = u % 2
                    scalar.wait_ge(sDVE2, d2abs[(u, "rv")])
                    scalar.activation(rstd[p][:, :], stdv[p][:, :], AF.Sqrt).then_inc(sACT2, 1)
                else:
                    p = 2 + u % 2
                    for r in range(4):
                        scalar.wait_ge(sPE2, pe2_y1t[(u, r)])
                        scalar.copy(sqhT[:, r * 512:(r + 1) * 512],
                                    psB[:, (r % 2) * 512:(r % 2) * 512 + 512]).then_inc(sACT2, 1)
                    scalar.wait_ge(sPE2, pe2_hmt[u])
                    scalar.copy(hmaskT[:, :], psB[:, 512:640]).then_inc(sACT2, 1)
                    scalar.wait_ge(sDVE2, d2abs[(u, "rv2")])
                    scalar.activation(rstd[p][:, :], stdv[p][:, :], AF.Sqrt).then_inc(sACT2, 1)
                    scalar.wait_ge(sDVE2, d2abs[(u, "y2")])
                    scalar.dma_start(out=y_out[u * 128:(u + 1) * 128, :],
                                     in_=fpB[:, :]).then_inc(sOutS, 16)

        @block.gpsimd
        def _(gpsimd):
            wq_src = wq_t.rearrange("(c p) n -> p c n", p=128)
            gpsimd.dma_start(out=wkv_sb, in_=wkv_t.rearrange("(c p) n -> p c n", p=128)
                             ).then_inc(sWLa, 16)
            gpsimd.dma_start(out=rowkv, in_=brows[:, 0:1024]).then_inc(sWLa, 16)
            gpsimd.wait_ge(sWLa, 32)
            wlb_small = [
                (rowq, brows[:, 1024:3072]),
                (I_bf[:, :], ident),
                (ra_sb[:, :, :], ra_t.rearrange("(c p) n -> p c n", p=128)),
                (bcat_sb[:, :], bcat_t),
            ]
            wlb_jobs = wlb_small + [
                (U0[:, 2 * i:2 * i + 2, :], wq_src[:, 2 * i:2 * i + 2, :])
                for i in range(8)]
            wlb_i = [0]

            def emit_wlb():
                if wlb_i[0] < len(wlb_jobs):
                    dst, s_ = wlb_jobs[wlb_i[0]]
                    gpsimd.dma_start(out=dst, in_=s_).then_inc(sWLb, 16)
                    wlb_i[0] += 1

            for s in range(NSLAB_ALL):
                if s >= 2:
                    gpsimd.wait_ge(sAcc, 2 * (s - 1))
                src = xt_full.rearrange("(c p) t -> p c t", p=128)[:, :, s * 256:(s + 1) * 256]
                gpsimd.dma_start(out=slab[s % 2], in_=src).then_inc(
                    sXTe if s % 2 == 0 else sXTo, 16)
                if 2 <= s < 6 or (s >= 8 and s % 2 == 0):
                    emit_wlb()
            while wlb_i[0] < len(wlb_jobs):
                emit_wlb()
            wo_src = wo_t.rearrange("(c p) n -> p c n", p=128)
            wb_src = wbase_t.rearrange("(c p) n -> p c n", p=128)
            for s in range(NSLAB_OWN):
                if s < 2:
                    gpsimd.wait_ge(sAcc, NTC_ALL)
                else:
                    gpsimd.wait_ge(sPEq, peq_q[2 * (s - 1)])
                g = NSLAB_ALL + s
                src = xt_own.rearrange("(c p) t -> p c t", p=128)[:, :, s * 256:(s + 1) * 256]
                gpsimd.dma_start(out=slab[g % 2], in_=src).then_inc(
                    sXTe if g % 2 == 0 else sXTo, 16)
                if 2 <= s < 6:
                    i = s - 2
                    if i == 0:
                        gpsimd.wait_ge(sKVm, NTC_ALL)
                    gpsimd.dma_start(out=wbase_sb[:, 2 * i:2 * i + 2, :],
                                     in_=wb_src[:, 2 * i:2 * i + 2, :]).then_inc(sWB, 16)
            gpsimd.wait_ge(sPEq, peq_q[NTC_OWN - 1])
            for i in range(WO_CH):
                gpsimd.dma_start(out=U0[:, 2 * i:2 * i + 2, :],
                                 in_=wo_src[:, 2 * i:2 * i + 2, :]).then_inc(sWO, 16)
            gpsimd.wait_ge(sDVEq, DQ_N * NTC_OWN)
            gpsimd.wait_ge(sACTq, AQ_N * NTC_OWN)
            gpsimd.wait_ge(sSQW, 16 * NTC_OWN)
            gpsimd.wait_ge(sPEq, peq_T[NTC_OWN - 1])
            for i in range(4, WB_CH):
                gpsimd.dma_start(out=wbase_sb[:, 2 * i:2 * i + 2, :],
                                 in_=wb_src[:, 2 * i:2 * i + 2, :]).then_inc(sWB, 16)
            gpsimd.dma_start(out=rowbb, in_=brows[:, 3072:5120]).then_inc(sWB, 16)

            def x_load_g(v):
                if v == 0:
                    gpsimd.wait_ge(sFin, 4)
                if v >= 3:
                    gpsimd.wait_ge(sDVE2, d2abs[(v - 3, "s2")])
                gpsimd.dma_start(out=xt_p2[v % 3][:, :],
                                 in_=x_own[v * 128:(v + 1) * 128, :]).then_inc(
                                     sX3[v % 3], 16)

            x_load_g(0)
            x_load_g(1)
            x_load_g(2)
            for u in range(NTC_OWN):
                gpsimd.wait_ge(sDVE2, d2abs[(u, "y1")])
                if u >= 1:
                    gpsimd.wait_ge(sPE2, pe2_y1t[(u - 1, 3)])
                gpsimd.tensor_copy(y1b[:, :], xt_p2[u % 3][:, :]).then_inc(sY1B, 1)
                if u + 3 < NTC_OWN:
                    x_load_g(u + 3)

        @block.sync
        def _(sync):
            for u in range(NTC_OWN):
                sync.wait_ge(sACTq, AQ_N * (u + 1))
                sync.dma_start(out=attn_dram[u], in_=sq_p1).then_inc(sSQW, 16)
            def sq_load(u):
                if u >= 2:
                    sync.wait_ge(sPE2, pe2_mm3[u - 2])
                sync.dma_start(out=sqp[u % 2][:, :], in_=attn_dram[u]).then_inc(
                    sSQRa if u % 2 == 0 else sSQRb, 16)

            for u in range(NTC_OWN):
                sq_load(u)
            sync.wait_ge(sOutS, 16 * NTC_OWN)


    return nc


def kernel(L0_attn_hidden_states, L0_attn_Wq, L0_attn_bq, L0_attn_Wk, L0_attn_bk,
           L0_attn_Wv, L0_attn_bv, L0_attn_Wo, L0_ffn_W_base, L0_ffn_b_base,
           L0_ffn_router_W, L0_ffn_A, L0_ffn_Bw):
    f32 = np.float32
    X = np.ascontiguousarray(np.asarray(L0_attn_hidden_states, dtype=f32))
    Wq = np.asarray(L0_attn_Wq, f32)
    Wk = np.asarray(L0_attn_Wk, f32)
    Wv = np.asarray(L0_attn_Wv, f32)
    Wo = np.asarray(L0_attn_Wo, f32)
    Wb = np.asarray(L0_ffn_W_base, f32)
    Rw = np.asarray(L0_ffn_router_W, f32)
    A = np.asarray(L0_ffn_A, f32)
    Bw = np.asarray(L0_ffn_Bw, f32)

    wkv_t = np.ascontiguousarray(np.concatenate([Wk, Wv], axis=0).T)
    wq_t = np.ascontiguousarray(Wq.T)
    wo_t = np.ascontiguousarray(Wo.T)
    wbase_t = np.ascontiguousarray(Wb.T)
    ra_t = np.ascontiguousarray(
        np.concatenate([Rw, A.reshape(128, D)], axis=0).T)
    bcat_t = np.ascontiguousarray(Bw.transpose(0, 2, 1).reshape(128, D))
    brows = np.concatenate([
        np.asarray(L0_attn_bk, f32), np.asarray(L0_attn_bv, f32),
        np.asarray(L0_attn_bq, f32), np.asarray(L0_ffn_b_base, f32)
    ]).reshape(1, 5120)
    ident = np.eye(128, dtype=f32)

    xt_b = [np.ascontiguousarray(X[b].T) for b in range(2)]

    nc = build_nc()
    in_maps = []
    for c in range(8):
        b, j = c // 4, c % 4
        in_maps.append(dict(
            xt_full=xt_b[b],
            xt_own=np.ascontiguousarray(xt_b[b][:, j * TOK:(j + 1) * TOK]),
            x_own=np.ascontiguousarray(X[b, j * TOK:(j + 1) * TOK, :]),
            wkv_t=wkv_t, wq_t=wq_t, wo_t=wo_t, wbase_t=wbase_t,
            ra_t=ra_t, bcat_t=bcat_t, brows=brows, ident=ident,
        ))
    res = run_bass_kernel_spmd(nc, in_maps, core_ids=list(range(8)))
    y = np.empty((2, SEQ, D), dtype=f32)
    for c in range(8):
        b, j = c // 4, c % 4
        y[b, j * TOK:(j + 1) * TOK, :] = res.results[c]["y"]
    return y


# version 12
# speedup vs baseline: 1.3543x; 1.0215x over previous
import numpy as np
from contextlib import ExitStack

import concourse.bass as bass
import concourse.mybir as mybir
from concourse.bass_utils import run_bass_kernel_spmd

F32 = mybir.dt.float32
BF16 = mybir.dt.bfloat16
ALU = mybir.AluOpType
AX = mybir.AxisListType
AF = mybir.ActivationFunctionType

D = 2048
SEQ = 8192
TOK = 2048
NTC_ALL = 64
NTC_OWN = 16
NSLAB_ALL = 32
NSLAB_OWN = 8
ATT_EPS = 1e-6
LN_EPS = 1e-5
WO_CH = 8
WB_CH = 8
N_P2W = WO_CH + WB_CH

peq_q, peq_T, peq_qkv = {}, {}, {}
_c = 0
for _u in range(NTC_OWN):
    _c += 1
    peq_q[_u] = _c
    if _u >= 1:
        _c += 1
        peq_T[_u - 1] = _c
        _c += 1
        peq_qkv[_u - 1] = _c
_c += 1
peq_T[NTC_OWN - 1] = _c
_c += 1
peq_qkv[NTC_OWN - 1] = _c

pe2_mm3 = {}
pe2_y1t = {}
pe2_misc, pe2_hmt, pe2_base = {}, {}, {}


def _plan_pe2():
    c = 0
    for v in range(NTC_OWN + 2):
        u = v - 2
        if v <= NTC_OWN - 1:
            if u >= 0:
                for r in range(4):
                    c += 1
                    pe2_y1t[(u, r)] = c
                c += 1
                pe2_misc[u] = c
            c += 1
            pe2_mm3[v] = c
        else:
            for r in range(4):
                c += 1
                pe2_y1t[(u, r)] = c
            c += 1
            pe2_misc[u] = c
        if u >= 0:
            c += 1
            pe2_hmt[u] = c
            c += 1
            pe2_base[u] = c


_plan_pe2()

DQ_N = 10
def dq_sqhat(u):
    return DQ_N * u + 10

_DSEG = {
    "A1": ["s"],
    "A2": ["b0", "b1", "b2", "b3", "aggr", "negmean", "veps", "rv", "y1"],
    "B1": ["lh", "rowmax", "mask8", "hmask"],
    "B2a": ["s2", "c0", "c1", "c2", "c3", "aggr2", "negmean2", "veps2", "rv2"],
    "B2y": ["y2"],
}
d2abs = {}
_c2 = 0
def _seg(u, ops):
    global _c2
    for name in ops:
        _c2 += 1
        d2abs[(u, name)] = _c2
def _p2_sched():
    order = []
    for v in range(NTC_OWN + 3):
        if 3 <= v <= NTC_OWN + 2:
            order.append(("B2a", v - 3))
        if 2 <= v <= NTC_OWN + 1:
            order.append(("B1", v - 2))
        if 3 <= v <= NTC_OWN + 2:
            order.append(("B2y", v - 3))
        if v <= NTC_OWN - 1:
            order.append(("A1", v))
            order.append(("A2", v))
    return order
P2_ORDER = _p2_sched()
for _kind, _u in P2_ORDER:
    _seg(_u, _DSEG[_kind])
def _p2_act_sched():
    order = []
    for v in range(NTC_OWN + 3):
        if 2 <= v <= NTC_OWN + 1:
            order.append(("Ey", v - 2))
        if 3 <= v <= NTC_OWN + 2:
            order.append(("Q2", v - 3))
        if 2 <= v <= NTC_OWN + 1:
            order.append(("Eh", v - 2))
        if v <= NTC_OWN - 1:
            order.append(("Aa", v))
    return order
P2_ACT = _p2_act_sched()
a2abs = {}
_c3 = 0
for _kind, _u in P2_ACT:
    if _kind == "Aa":
        _c3 += 1
        a2abs[(_u, "sqrt1")] = _c3
    elif _kind == "Ey":
        for _r in range(4):
            _c3 += 1
            a2abs[(_u, "ey1t%d" % _r)] = _c3
    elif _kind == "Eh":
        _c3 += 1
        a2abs[(_u, "ehmt")] = _c3
    else:
        _c3 += 1
        a2abs[(_u, "sqrt2")] = _c3

AQ_N = 4


def build_nc():
    nc = bass.Bass("TRN2", target_bir_lowering=False)

    xt_full = nc.dram_tensor("xt_full", [D, SEQ], F32, kind="ExternalInput").ap()
    xt_own = nc.dram_tensor("xt_own", [D, TOK], F32, kind="ExternalInput").ap()
    x_own = nc.dram_tensor("x_own", [TOK, D], F32, kind="ExternalInput").ap()
    wkv_t = nc.dram_tensor("wkv_t", [D, 1024], F32, kind="ExternalInput").ap()
    wq_t = nc.dram_tensor("wq_t", [D, D], F32, kind="ExternalInput").ap()
    wo_t = nc.dram_tensor("wo_t", [D, D], F32, kind="ExternalInput").ap()
    wbase_t = nc.dram_tensor("wbase_t", [D, D], F32, kind="ExternalInput").ap()
    ra_t = nc.dram_tensor("ra_t", [D, 136], F32, kind="ExternalInput").ap()
    bcat_t = nc.dram_tensor("bcat_t", [128, D], F32, kind="ExternalInput").ap()
    brows = nc.dram_tensor("brows", [1, 5120], F32, kind="ExternalInput").ap()
    ident = nc.dram_tensor("ident", [128, 128], F32, kind="ExternalInput").ap()
    y_out = nc.dram_tensor("y", [TOK, D], F32, kind="ExternalOutput").ap()
    attn_dram = nc.dram_tensor("attn_scratch", [NTC_OWN, 128, D], BF16, kind="Internal").ap()

    with ExitStack() as ctx:
        sb = lambda name, shape, dt: ctx.enter_context(nc.sbuf_tensor(name, shape, dt))
        sem = lambda name: ctx.enter_context(nc.semaphore(name))

        U0 = sb("u0", [128, 16, 2048], BF16)
        U1 = sb("u1", [128, 32768], BF16)
        wkv_sb = U1[:, 0:16384].rearrange("p (c n) -> p c n", c=16)
        slab = [U1[:, 16384:20480].rearrange("p (c n) -> p c n", c=16),
                U1[:, 20480:24576].rearrange("p (c n) -> p c n", c=16)]
        sk_t = [U1[:, 24576:25088], U1[:, 25088:25600]]
        vb_t = [U1[:, 25600:26112], U1[:, 26112:26624]]
        sq_p1 = U1[:, 26624:28672]
        scr1 = U1[:, 28672:30720].bitcast(F32)
        scr2 = U1[:, 30720:32768].bitcast(F32)
        wbase_sb = U1.rearrange("p (c n) -> p c n", c=16)

        fpA = sb("fpA", [128, D], F32)
        fpB = sb("fpB", [128, D], F32)
        prodb = fpB[:, :].bitcast(BF16)[:, 0:2048]
        kvb = sb("kvb", [128, 512], BF16)
        kvb_v = kvb[:, :].rearrange("p (h e) -> p h e", h=4)
        ksr512 = sb("ksr512", [1, 512], F32)
        ksum_bc = sb("ksum_bc", [128, D], BF16)
        I_bf = sb("I_bf", [128, 128], BF16)
        onesf = sb("onesf", [1, 128], F32)
        onesb = sb("onesb", [128, 1], BF16)
        onesr = sb("onesr", [1, 128], BF16)
        ra_sb = sb("ra_sb", [128, 16, 136], BF16)
        bcat_sb = sb("bcat_sb", [128, D], BF16)

        xt_p2 = [sb("xt_p2a", [128, D], F32), sb("xt_p2b", [128, D], F32),
                 sb("xt_p2c", [128, D], F32)]
        ksrow = xt_p2[0][0:1, :]
        sq_p2 = sb("sq_p2", [128, D], BF16)
        sq_p2b = sb("sq_p2b", [128, D], BF16)
        sqp = [sq_p2, sq_p2b]
        sqh = sb("sqh", [128, D], BF16)
        sqhT = sb("sqhT", [128, D], BF16)
        y1b = sb("y1b", [128, D], BF16)
        rowkv = sq_p2[0:1, 0:1024]
        rowq = y1b[0:1, 0:2048]
        rowbb = sqh[0:1, 0:2048]
        lh = sb("lh", [128, 136], F32)
        hmask = sb("hmask", [128, 128], BF16)
        hmaskT = sb("hmaskT", [128, 128], BF16)
        norm16 = sb("norm16", [128, 16], F32)
        normeps = sb("normeps", [128, 16], F32)
        rnorm = sb("rnorm", [128, 16], F32)
        rnbf = sb("rnbf", [128, 16], BF16)
        rowmax = sb("rowmax", [128, 1], F32)
        mask8 = sb("mask8", [128, 8], F32)
        negmean = [sb("negmean%d" % i, [128, 1], F32) for i in range(4)]
        veps = [sb("veps%d" % i, [128, 1], F32) for i in range(4)]
        stdv = [sb("stdv%d" % i, [128, 1], F32) for i in range(4)]
        rstd = [sb("rstd%d" % i, [128, 1], F32) for i in range(4)]
        bnstA = sb("bnstA", [128, 4, 6], F32)
        bnstB = sb("bnstB", [128, 4, 6], F32)
        mvA = sb("mvA", [128, 2], F32)
        mvB = sb("mvB", [128, 2], F32)

        psA = ctx.enter_context(nc.psum_tensor("psA", [128, 2048], F32))
        psB = ctx.enter_context(nc.psum_tensor("psB", [128, 2048], F32))
        psK = [psA[:, 0:512], psA[:, 512:1024]]
        psV = [psA[:, 1024:1536], psA[:, 1536:2048]]
        psKV = psB[:, 0:512]
        psKS = psB[0:1, 512:1024]
        psQ = [psA, psB]

        sWLa = sem("sWLa")
        sWLb = sem("sWLb")
        sXTe = sem("sXTe")
        sXTo = sem("sXTo")
        sWO = sem("sWO")
        sWB = sem("sWB")
        sStp = sem("sStp")
        sKVm = sem("sKVm")
        sAcc = sem("sAcc")
        sDVE1 = sem("sDVE1")
        sACT1 = sem("sACT1")
        sFin = sem("sFin")
        sPEq = sem("sPEq")
        sDVEq = sem("sDVEq")
        sACTq = sem("sACTq")
        sSQW = sem("sSQW")
        sXa = sem("sXa")
        sXb = sem("sXb")
        sXc = sem("sXc")
        sSQRa = sem("sSQRa")
        sSQRb = sem("sSQRb")
        sPE2 = sem("sPE2")
        sDVE2 = sem("sDVE2")
        sACT2 = sem("sACT2")
        sY1B = sem("sY1B")
        sOutS = sem("sOutS")

        sX3 = [sXa, sXb, sXc]
        block = ctx.enter_context(nc.Block())

        def slab_wait(eng, g):
            s = sXTe if g % 2 == 0 else sXTo
            eng.wait_ge(s, 16 * (g // 2 + 1))

        @block.tensor
        def _(tensor):
            tensor.wait_ge(sStp, 1)
            tensor.wait_ge(sWLa, 32)

            for t in range(NTC_ALL):
                par = t % 2
                s_idx = t // 2
                toff = (t - 2 * s_idx) * 128
                slab_wait(tensor, s_idx)
                if t >= 2:
                    tensor.wait_ge(sDVE1, 3 * (t - 1))
                for dc in range(16):
                    lhs = slab[s_idx % 2][:, dc, toff:toff + 128]
                    tensor.matmul(psK[par], lhs, wkv_sb[:, dc, 0:512],
                                  start=(dc == 0), stop=False,
                                  skip_group_check=True)
                    tensor.matmul(psV[par], lhs, wkv_sb[:, dc, 512:1024],
                                  start=(dc == 0), stop=False,
                                  skip_group_check=True)
                tensor.matmul(psK[par], onesr[:, :], rowkv[:, 0:512],
                              start=False, stop=True, skip_group_check=True)
                tensor.matmul(psV[par], onesr[:, :], rowkv[:, 512:1024],
                              start=False, stop=True, skip_group_check=True
                              ).then_inc(sKVm, 1)
                if t >= 1:
                    tp = t - 1
                    tensor.wait_ge(sDVE1, 3 * t)
                    for h4 in range(4):
                        tensor.matmul(psKV[:, h4 * 128:(h4 + 1) * 128],
                                      sk_t[tp % 2][:, h4 * 128:(h4 + 1) * 128],
                                      vb_t[tp % 2][:, h4 * 128:(h4 + 1) * 128],
                                      start=(tp == 0 and h4 == 0),
                                      stop=(tp == NTC_ALL - 1 and h4 == 3),
                                      skip_group_check=True)
                    tensor.matmul(psKS, onesb[:, :], sk_t[tp % 2][:, :],
                                  start=(tp == 0), stop=(tp == NTC_ALL - 1),
                                  skip_group_check=True).then_inc(sAcc, 1)
            tp = NTC_ALL - 1
            tensor.wait_ge(sDVE1, 3 * NTC_ALL)
            for h4 in range(4):
                tensor.matmul(psKV[:, h4 * 128:(h4 + 1) * 128],
                              sk_t[tp % 2][:, h4 * 128:(h4 + 1) * 128],
                              vb_t[tp % 2][:, h4 * 128:(h4 + 1) * 128],
                              start=False, stop=(h4 == 3), skip_group_check=True)
            tensor.matmul(psKS, onesb[:, :], sk_t[tp % 2][:, :],
                          start=False, stop=True, skip_group_check=True
                          ).then_inc(sAcc, 1)

            tensor.wait_ge(sFin, 3)
            for j in range(4):
                mm = tensor.matmul(psA[:, j * 512:(j + 1) * 512], onesf[:, :],
                                   ksrow[:, j * 512:(j + 1) * 512], start=True, stop=True)
            mm.then_inc(sFin, 1)

            tensor.wait_ge(sWLb, 16 * 12)
            tensor.wait_ge(sFin, 5)

            def q_group(u):
                par = u % 2
                s_idx = u // 2
                toff = (u - 2 * s_idx) * 128
                slab_wait(tensor, NSLAB_ALL + s_idx)
                if u >= 2:
                    tensor.wait_ge(sACTq, AQ_N * (u - 2) + 4)
                for dc in range(16):
                    lhs = slab[s_idx % 2][:, dc, toff:toff + 128]
                    for j in range(4):
                        tensor.matmul(psQ[par][:, j * 512:(j + 1) * 512], lhs,
                                      U0[:, dc, j * 512:(j + 1) * 512],
                                      start=(dc == 0), stop=False,
                                      skip_group_check=True)
                for j in range(4):
                    mm = tensor.matmul(psQ[par][:, j * 512:(j + 1) * 512],
                                       onesr[:, :],
                                       rowq[:, j * 512:(j + 1) * 512],
                                       start=False, stop=True, skip_group_check=True)
                mm.then_inc(sPEq, 1)

            def t_group(u):
                par = u % 2
                tensor.wait_ge(sDVEq, dq_sqhat(u))
                for h in range(16):
                    mm = tensor.matmul(psQ[par][:, h * 128:(h + 1) * 128],
                                       sqh[:, h * 128:(h + 1) * 128], I_bf[:, :],
                                       start=True, stop=True)
                mm.then_inc(sPEq, 1)

            def qkv_group(u):
                par = u % 2
                tensor.wait_ge(sACTq, AQ_N * u + 3)
                for h in range(16):
                    mm = tensor.matmul(psQ[par][:, h * 128:(h + 1) * 128],
                                       kvb_v[:, h // 4, :],
                                       sqhT[:, h * 128:(h + 1) * 128],
                                       start=True, stop=True)
                mm.then_inc(sPEq, 1)

            for u in range(NTC_OWN):
                q_group(u)
                if u >= 1:
                    t_group(u - 1)
                    qkv_group(u - 1)
            t_group(NTC_OWN - 1)
            qkv_group(NTC_OWN - 1)

            tensor.wait_ge(sACTq, AQ_N * NTC_OWN)

            def y1t_round(u, r):
                if r == 0:
                    tensor.wait_ge(sY1B, u + 1)
                    if u >= 1:
                        tensor.wait_ge(sDVE2, d2abs[(u - 1, "s2")])
                else:
                    if r >= 2:
                        tensor.wait_ge(sACT2, a2abs[(u, "ey1t%d" % (r - 2))])
                for k in range(4):
                    dc = 4 * r + k
                    base_col = (r % 2) * 512 + k * 128
                    mm = tensor.matmul(psB[:, base_col:base_col + 128],
                                       y1b[:, dc * 128:(dc + 1) * 128], I_bf[:, :],
                                       start=True, stop=True, skip_group_check=True)
                mm.then_inc(sPE2, 1)

            def misc_part(u, half):
                tensor.wait_ge(sACT2, a2abs[(u, "ey1t%d" % (half + 2))])
                for dc in range(8 * half, 8 * half + 8):
                    mm = tensor.matmul(psB[:, 0:136],
                                       sqhT[:, dc * 128:(dc + 1) * 128],
                                       ra_sb[:, dc, :],
                                       start=(dc == 0), stop=(dc == 15),
                                       skip_group_check=True)
                if half == 1:
                    mm.then_inc(sPE2, 1)

            def p2_iter(v):
                u = v - 2
                if v <= NTC_OWN - 1:
                    tensor.wait_ge(sSQRa if v % 2 == 0 else sSQRb, 16 * (v // 2 + 1))
                    if v >= 1:
                        tensor.wait_ge(sDVE2, d2abs[(v - 1, "s")])
                    if v == 0:
                        tensor.wait_ge(sWO, 16 * WO_CH)
                    sq = sqp[v % 2]
                    for h in range(16):
                        for j in range(4):
                            mm = tensor.matmul(psA[:, j * 512:(j + 1) * 512],
                                               sq[:, h * 128:(h + 1) * 128],
                                               U0[:, h, j * 512:(j + 1) * 512],
                                               start=(h == 0), stop=(h == 15),
                                               skip_group_check=True)
                        if u >= 0:
                            if h == 3:
                                y1t_round(u, 0)
                            elif h == 5:
                                y1t_round(u, 1)
                            elif h == 7:
                                y1t_round(u, 2)
                            elif h == 9:
                                y1t_round(u, 3)
                            elif h == 10:
                                misc_part(u, 0)
                            elif h == 12:
                                misc_part(u, 1)
                    mm.then_inc(sPE2, 1)
                else:
                    for r in range(4):
                        y1t_round(u, r)
                    misc_part(u, 0)
                    misc_part(u, 1)
                if u >= 0:
                    tensor.wait_ge(sDVE2, d2abs[(u, "hmask")])
                    tensor.matmul(psB[:, 512:640], hmask[:, :], I_bf[:, :],
                                  start=True, stop=True,
                                  skip_group_check=True).then_inc(sPE2, 1)
                    tensor.wait_ge(sDVE2, d2abs[(u, "lh")])
                    if u == 0:
                        tensor.wait_ge(sWB, 16 * (WB_CH + 1))
                    for j in range(4):
                        for dc in range(16):
                            tensor.matmul(psB[:, j * 512:(j + 1) * 512],
                                          sqhT[:, dc * 128:(dc + 1) * 128],
                                          wbase_sb[:, dc, j * 512:(j + 1) * 512],
                                          start=(dc == 0), stop=False,
                                          skip_group_check=True)
                        tensor.matmul(psB[:, j * 512:(j + 1) * 512], onesr[:, :],
                                      rowbb[:, j * 512:(j + 1) * 512],
                                      start=False, stop=False, skip_group_check=True)
                        if j == 0:
                            tensor.wait_ge(sACT2, a2abs[(u, "ehmt")])
                        mm = tensor.matmul(psB[:, j * 512:(j + 1) * 512], hmaskT[:, :],
                                           bcat_sb[:, j * 512:(j + 1) * 512],
                                           start=False, stop=True,
                                           skip_group_check=True)
                    mm.then_inc(sPE2, 1)

            for v in range(NTC_OWN + 2):
                p2_iter(v)

        @block.vector
        def _(vector):
            vector.memset(onesf[:, :], 1.0)
            vector.memset(onesb[:, :], 1.0)
            vector.memset(onesr[:, :], 1.0).then_inc(sStp, 1)

            dcnt = [0]

            def op(ins):
                ins.then_inc(sDVE1, 1)
                dcnt[0] += 1
                vector.wait_ge(sDVE1, dcnt[0])
                return ins

            for t in range(NTC_ALL):
                par = t % 2
                vector.wait_ge(sKVm, t + 1)
                if t >= 1:
                    vector.wait_ge(sACT1, t)
                op(vector.tensor_scalar(scr1[:, 0:512], psK[par], 0.0, None, ALU.min))
                vector.wait_ge(sACT1, t + 1)
                if t >= 2:
                    vector.wait_ge(sAcc, t - 1)
                op(vector.scalar_tensor_tensor(sk_t[par][:, :], psK[par], 0.0,
                                               scr2[:, 0:512], ALU.max, ALU.add))
                op(vector.tensor_copy(vb_t[par][:, :], psV[par]))

            vector.wait_ge(sAcc, NTC_ALL)
            vector.tensor_copy(kvb[:, :], psKV).then_inc(sFin, 1)
            vector.tensor_copy(ksr512[:, :], psKS).then_inc(sFin, 1)
            vector.wait_ge(sFin, 2)
            ks_in = ksr512[:, :].rearrange("a (h d) -> a h d", h=4).unsqueeze(2) \
                .broadcast_to([1, 4, 4, 128])
            vector.tensor_copy(ksrow.rearrange("a (h r d) -> a h r d", h=4, r=4),
                               ks_in).then_inc(sFin, 1)
            vector.wait_ge(sFin, 4)
            vector.tensor_copy(ksum_bc[:, :], psA[:, :]).then_inc(sFin, 1)

            dq = [0]

            def opq(ins):
                ins.then_inc(sDVEq, 1)
                dq[0] += 1
                vector.wait_ge(sDVEq, dq[0])
                return ins

            sq3 = sq_p1.rearrange("p (h d) -> p h d", h=16)
            prod3 = prodb.rearrange("p (h d) -> p h d", h=16)
            ks3 = ksum_bc[:, :].rearrange("p (h d) -> p h d", h=16)
            sqh3 = sqh[:, :].rearrange("p (h d) -> p h d", h=16)
            rn3 = rnbf[:, :].unsqueeze(2).broadcast_to([128, 16, 128])
            for u in range(NTC_OWN):
                par = u % 2
                vector.wait_ge(sPEq, peq_q[u])
                if u >= 1:
                    vector.wait_ge(sSQW, 16 * u)
                for j in range(2):
                    sl = slice(j * 1024, (j + 1) * 1024)
                    if u + j >= 1:
                        vector.wait_ge(sACTq, AQ_N * u + j)
                    opq(vector.tensor_scalar(scr1[:, :], psQ[par][:, sl], 0.0,
                                             None, ALU.min))
                    vector.wait_ge(sACTq, AQ_N * u + j + 1)
                    opq(vector.scalar_tensor_tensor(sq_p1[:, sl], psQ[par][:, sl], 0.0,
                                                    scr2[:, :], ALU.max, ALU.add))
                opq(vector.tensor_tensor(prod3, sq3, ks3, ALU.mult))
                opq(vector.tensor_reduce(norm16[:, :], prod3, AX.X, ALU.add))
                opq(vector.tensor_scalar(normeps[:, :], norm16[:, :], ATT_EPS,
                                         None, ALU.add))
                opq(vector.reciprocal(rnorm[:, :], normeps[:, :]))
                opq(vector.tensor_copy(rnbf[:, :], rnorm[:, :]))
                if u >= 1:
                    vector.wait_ge(sPEq, peq_T[u - 1])
                opq(vector.tensor_tensor(sqh3, sq3, rn3, ALU.mult))

            dd = [0]

            def op2(ins):
                ins.then_inc(sDVE2, 1)
                dd[0] += 1
                vector.wait_ge(sDVE2, dd[0])
                return ins

            lh3 = lh[:, 8:136].rearrange("p (e r) -> p e r", e=8)
            hm3 = hmask[:, :].rearrange("p (e r) -> p e r", e=8)
            m8b = mask8[:, :].unsqueeze(2).broadcast_to([128, 8, 16])

            def seg_a1(u):
                vector.wait_ge(sPE2, pe2_mm3[u])
                vector.wait_ge(sX3[u % 3], 16 * (u // 3 + 1))
                op2(vector.scalar_tensor_tensor(fpA[:, :], psA[:, :], 1.0,
                                                xt_p2[u % 3][:, :], ALU.mult, ALU.add))

            def seg_a2(u):
                p = u % 2
                for g in range(4):
                    op2(vector.bn_stats(bnstA[:, g, :], fpA[:, g * 512:(g + 1) * 512]))
                op2(vector.bn_aggr(mvA[:, :], bnstA[:, :, :].rearrange("p a b -> p (a b)")))
                op2(vector.tensor_scalar(negmean[p][:, :], mvA[:, 0:1], -1.0, None, ALU.mult))
                op2(vector.tensor_scalar(veps[p][:, :], mvA[:, 1:2], LN_EPS, None, ALU.add))
                op2(vector.reciprocal(stdv[p][:, :], veps[p][:, :]))
                vector.wait_ge(sACT2, a2abs[(u, "sqrt1")])
                op2(vector.tensor_scalar(xt_p2[u % 3][:, :], fpA[:, :], negmean[p][:, :],
                                         rstd[p][:, :], ALU.add, ALU.mult))

            def seg_b1(u):
                vector.wait_ge(sPE2, pe2_misc[u])
                op2(vector.tensor_copy(lh[:, :], psB[:, 0:136]))
                op2(vector.reduce_max(rowmax[:, :], lh[:, 0:8], AX.X))
                op2(vector.tensor_scalar(mask8[:, :], lh[:, 0:8], rowmax[:, :],
                                         2.0, ALU.is_ge, ALU.mult))
                op2(vector.tensor_tensor(hm3, lh3, m8b, ALU.mult))

            def seg_b2a(u):
                p = 2 + u % 2
                vector.wait_ge(sPE2, pe2_base[u])
                vector.wait_ge(sY1B, u + 1)
                op2(vector.scalar_tensor_tensor(fpA[:, :], psB[:, :], 1.0,
                                                xt_p2[u % 3][:, :], ALU.mult, ALU.add))
                for g in range(4):
                    op2(vector.bn_stats(bnstB[:, g, :], fpA[:, g * 512:(g + 1) * 512]))
                op2(vector.bn_aggr(mvB[:, :], bnstB[:, :, :].rearrange("p a b -> p (a b)")))
                op2(vector.tensor_scalar(negmean[p][:, :], mvB[:, 0:1], -1.0, None, ALU.mult))
                op2(vector.tensor_scalar(veps[p][:, :], mvB[:, 1:2], LN_EPS, None, ALU.add))
                op2(vector.reciprocal(stdv[p][:, :], veps[p][:, :]))

            def seg_b2y(u):
                p = 2 + u % 2
                vector.wait_ge(sACT2, a2abs[(u, "sqrt2")])
                if u >= 1:
                    vector.wait_ge(sOutS, 16 * u)
                op2(vector.tensor_scalar(fpB[:, :], fpA[:, :], negmean[p][:, :],
                                         rstd[p][:, :], ALU.add, ALU.mult))

            _dseg_fns = {"A1": seg_a1, "A2": seg_a2, "B1": seg_b1,
                         "B2a": seg_b2a, "B2y": seg_b2y}
            for kind, u in P2_ORDER:
                _dseg_fns[kind](u)

        @block.scalar
        def _(scalar):
            scalar.preload_act_table(AF.Exp)
            for t in range(NTC_ALL):
                scalar.wait_ge(sDVE1, 3 * t + 1)
                scalar.activation(scr2[:, 0:512], scr1[:, 0:512], AF.Exp).then_inc(sACT1, 1)
            for u in range(NTC_OWN):
                par = u % 2
                for j in range(2):
                    scalar.wait_ge(sDVEq, DQ_N * u + 2 * j + 1)
                    scalar.activation(scr2[:, :], scr1[:, :], AF.Exp).then_inc(sACTq, 1)
                scalar.wait_ge(sPEq, peq_T[u])
                scalar.copy(sqhT[:, :], psQ[par][:, :]).then_inc(sACTq, 1)
                scalar.wait_ge(sPEq, peq_qkv[u])
                scalar.wait_ge(sDVEq, dq_sqhat(u))
                scalar.wait_ge(sSQW, 16 * u)
                scalar.copy(sq_p1, psQ[par][:, :]).then_inc(sACTq, 1)
            scalar.preload_act_table(AF.Sqrt)
            for kind, u in P2_ACT:
                if kind == "Aa":
                    p = u % 2
                    scalar.wait_ge(sDVE2, d2abs[(u, "rv")])
                    scalar.activation(rstd[p][:, :], stdv[p][:, :], AF.Sqrt).then_inc(sACT2, 1)
                elif kind == "Ey":
                    for r in range(4):
                        scalar.wait_ge(sPE2, pe2_y1t[(u, r)])
                        scalar.copy(sqhT[:, r * 512:(r + 1) * 512],
                                    psB[:, (r % 2) * 512:(r % 2) * 512 + 512]).then_inc(sACT2, 1)
                elif kind == "Eh":
                    scalar.wait_ge(sPE2, pe2_hmt[u])
                    scalar.copy(hmaskT[:, :], psB[:, 512:640]).then_inc(sACT2, 1)
                else:
                    p = 2 + u % 2
                    scalar.wait_ge(sDVE2, d2abs[(u, "rv2")])
                    scalar.activation(rstd[p][:, :], stdv[p][:, :], AF.Sqrt).then_inc(sACT2, 1)

        @block.gpsimd
        def _(gpsimd):
            wq_src = wq_t.rearrange("(c p) n -> p c n", p=128)
            gpsimd.dma_start(out=wkv_sb, in_=wkv_t.rearrange("(c p) n -> p c n", p=128)
                             ).then_inc(sWLa, 16)
            gpsimd.dma_start(out=rowkv, in_=brows[:, 0:1024]).then_inc(sWLa, 16)
            gpsimd.wait_ge(sWLa, 32)
            wlb_small = [
                (rowq, brows[:, 1024:3072]),
                (I_bf[:, :], ident),
                (ra_sb[:, :, :], ra_t.rearrange("(c p) n -> p c n", p=128)),
                (bcat_sb[:, :], bcat_t),
            ]
            wlb_jobs = wlb_small + [
                (U0[:, 2 * i:2 * i + 2, :], wq_src[:, 2 * i:2 * i + 2, :])
                for i in range(8)]
            wlb_i = [0]

            def emit_wlb():
                if wlb_i[0] < len(wlb_jobs):
                    dst, s_ = wlb_jobs[wlb_i[0]]
                    gpsimd.dma_start(out=dst, in_=s_).then_inc(sWLb, 16)
                    wlb_i[0] += 1

            for s in range(NSLAB_ALL):
                if s >= 2:
                    gpsimd.wait_ge(sAcc, 2 * (s - 1))
                src = xt_full.rearrange("(c p) t -> p c t", p=128)[:, :, s * 256:(s + 1) * 256]
                gpsimd.dma_start(out=slab[s % 2], in_=src).then_inc(
                    sXTe if s % 2 == 0 else sXTo, 16)
                if 2 <= s < 6 or (s >= 8 and s % 2 == 0):
                    emit_wlb()
            while wlb_i[0] < len(wlb_jobs):
                emit_wlb()
            wo_src = wo_t.rearrange("(c p) n -> p c n", p=128)
            wb_src = wbase_t.rearrange("(c p) n -> p c n", p=128)
            for s in range(NSLAB_OWN):
                if s < 2:
                    gpsimd.wait_ge(sAcc, NTC_ALL)
                else:
                    gpsimd.wait_ge(sPEq, peq_q[2 * (s - 1)])
                g = NSLAB_ALL + s
                src = xt_own.rearrange("(c p) t -> p c t", p=128)[:, :, s * 256:(s + 1) * 256]
                gpsimd.dma_start(out=slab[g % 2], in_=src).then_inc(
                    sXTe if g % 2 == 0 else sXTo, 16)
                if 2 <= s < 6:
                    i = s - 2
                    if i == 0:
                        gpsimd.wait_ge(sKVm, NTC_ALL)
                    gpsimd.dma_start(out=wbase_sb[:, 2 * i:2 * i + 2, :],
                                     in_=wb_src[:, 2 * i:2 * i + 2, :]).then_inc(sWB, 16)
            gpsimd.wait_ge(sPEq, peq_q[NTC_OWN - 1])
            for i in range(WO_CH):
                gpsimd.dma_start(out=U0[:, 2 * i:2 * i + 2, :],
                                 in_=wo_src[:, 2 * i:2 * i + 2, :]).then_inc(sWO, 16)
            gpsimd.wait_ge(sDVEq, DQ_N * NTC_OWN)
            gpsimd.wait_ge(sACTq, AQ_N * NTC_OWN)
            gpsimd.wait_ge(sSQW, 16 * NTC_OWN)
            gpsimd.wait_ge(sPEq, peq_T[NTC_OWN - 1])
            for i in range(4, WB_CH):
                gpsimd.dma_start(out=wbase_sb[:, 2 * i:2 * i + 2, :],
                                 in_=wb_src[:, 2 * i:2 * i + 2, :]).then_inc(sWB, 16)
            gpsimd.dma_start(out=rowbb, in_=brows[:, 3072:5120]).then_inc(sWB, 16)

            def x_load_g(v):
                if v == 0:
                    gpsimd.wait_ge(sFin, 4)
                if v >= 3:
                    gpsimd.wait_ge(sDVE2, d2abs[(v - 3, "s2")])
                gpsimd.dma_start(out=xt_p2[v % 3][:, :],
                                 in_=x_own[v * 128:(v + 1) * 128, :]).then_inc(
                                     sX3[v % 3], 16)

            x_load_g(0)
            x_load_g(1)
            x_load_g(2)
            for u in range(NTC_OWN):
                gpsimd.wait_ge(sDVE2, d2abs[(u, "y1")])
                if u >= 1:
                    gpsimd.wait_ge(sPE2, pe2_y1t[(u - 1, 3)])
                gpsimd.tensor_copy(y1b[:, :], xt_p2[u % 3][:, :]).then_inc(sY1B, 1)
                if u + 3 < NTC_OWN:
                    x_load_g(u + 3)

        @block.sync
        def _(sync):
            for u in range(NTC_OWN):
                sync.wait_ge(sACTq, AQ_N * (u + 1))
                sync.dma_start(out=attn_dram[u], in_=sq_p1).then_inc(sSQW, 16)
            def sq_load(u):
                if u >= 2:
                    sync.wait_ge(sPE2, pe2_mm3[u - 2])
                sync.dma_start(out=sqp[u % 2][:, :], in_=attn_dram[u]).then_inc(
                    sSQRa if u % 2 == 0 else sSQRb, 16)

            def y_store(u):
                sync.wait_ge(sDVE2, d2abs[(u, "y2")])
                sync.dma_start(out=y_out[u * 128:(u + 1) * 128, :],
                               in_=fpB[:, :]).then_inc(sOutS, 16)

            for u in range(NTC_OWN):
                sq_load(u)
                if u >= 5:
                    y_store(u - 5)
            for u in range(NTC_OWN - 5, NTC_OWN):
                y_store(u)
            sync.wait_ge(sOutS, 16 * NTC_OWN)


    return nc


def kernel(L0_attn_hidden_states, L0_attn_Wq, L0_attn_bq, L0_attn_Wk, L0_attn_bk,
           L0_attn_Wv, L0_attn_bv, L0_attn_Wo, L0_ffn_W_base, L0_ffn_b_base,
           L0_ffn_router_W, L0_ffn_A, L0_ffn_Bw):
    f32 = np.float32
    X = np.ascontiguousarray(np.asarray(L0_attn_hidden_states, dtype=f32))
    Wq = np.asarray(L0_attn_Wq, f32)
    Wk = np.asarray(L0_attn_Wk, f32)
    Wv = np.asarray(L0_attn_Wv, f32)
    Wo = np.asarray(L0_attn_Wo, f32)
    Wb = np.asarray(L0_ffn_W_base, f32)
    Rw = np.asarray(L0_ffn_router_W, f32)
    A = np.asarray(L0_ffn_A, f32)
    Bw = np.asarray(L0_ffn_Bw, f32)

    wkv_t = np.ascontiguousarray(np.concatenate([Wk, Wv], axis=0).T)
    wq_t = np.ascontiguousarray(Wq.T)
    wo_t = np.ascontiguousarray(Wo.T)
    wbase_t = np.ascontiguousarray(Wb.T)
    ra_t = np.ascontiguousarray(
        np.concatenate([Rw, A.reshape(128, D)], axis=0).T)
    bcat_t = np.ascontiguousarray(Bw.transpose(0, 2, 1).reshape(128, D))
    brows = np.concatenate([
        np.asarray(L0_attn_bk, f32), np.asarray(L0_attn_bv, f32),
        np.asarray(L0_attn_bq, f32), np.asarray(L0_ffn_b_base, f32)
    ]).reshape(1, 5120)
    ident = np.eye(128, dtype=f32)

    xt_b = [np.ascontiguousarray(X[b].T) for b in range(2)]

    nc = build_nc()
    in_maps = []
    for c in range(8):
        b, j = c // 4, c % 4
        in_maps.append(dict(
            xt_full=xt_b[b],
            xt_own=np.ascontiguousarray(xt_b[b][:, j * TOK:(j + 1) * TOK]),
            x_own=np.ascontiguousarray(X[b, j * TOK:(j + 1) * TOK, :]),
            wkv_t=wkv_t, wq_t=wq_t, wo_t=wo_t, wbase_t=wbase_t,
            ra_t=ra_t, bcat_t=bcat_t, brows=brows, ident=ident,
        ))
    res = run_bass_kernel_spmd(nc, in_maps, core_ids=list(range(8)))
    y = np.empty((2, SEQ, D), dtype=f32)
    for c in range(8):
        b, j = c // 4, c % 4
        y[b, j * TOK:(j + 1) * TOK, :] = res.results[c]["y"]
    return y


# version 13
# speedup vs baseline: 1.4830x; 1.0950x over previous
import numpy as np
from contextlib import ExitStack

import concourse.bass as bass
import concourse.mybir as mybir
from concourse.bass_utils import run_bass_kernel_spmd

F32 = mybir.dt.float32
BF16 = mybir.dt.bfloat16
ALU = mybir.AluOpType
AX = mybir.AxisListType
AF = mybir.ActivationFunctionType

D = 2048
SEQ = 8192
TOK = 2048
NTC_ALL = 64
NTC_OWN = 16
NSLAB_ALL = 32
NSLAB_OWN = 8
ATT_EPS = 1e-6
LN_EPS = 1e-5
WO_CH = 8
WB_CH = 8
N_P2W = WO_CH + WB_CH

peq_q, peq_T, peq_qkv = {}, {}, {}
_c = 0
for _u in range(NTC_OWN):
    _c += 1
    peq_q[_u] = _c
    if _u >= 1:
        _c += 1
        peq_T[_u - 1] = _c
        _c += 1
        peq_qkv[_u - 1] = _c
_c += 1
peq_T[NTC_OWN - 1] = _c
_c += 1
peq_qkv[NTC_OWN - 1] = _c

pe2_mm3 = {}
pe2_y1t = {}
pe2_misc, pe2_hmt, pe2_base = {}, {}, {}


def _plan_pe2():
    c = 0
    for v in range(NTC_OWN + 2):
        u = v - 2
        if v <= NTC_OWN - 1:
            if u >= 0:
                for r in range(4):
                    c += 1
                    pe2_y1t[(u, r)] = c
                c += 1
                pe2_misc[u] = c
            c += 1
            pe2_mm3[v] = c
        else:
            for r in range(4):
                c += 1
                pe2_y1t[(u, r)] = c
            c += 1
            pe2_misc[u] = c
        if u >= 0:
            c += 1
            pe2_hmt[u] = c
            c += 1
            pe2_base[u] = c


_plan_pe2()

DQ_N = 10
def dq_sqhat(u):
    return DQ_N * u + 10

_DSEG = {
    "A1": ["s"],
    "A2": ["b0", "b1", "b2", "b3", "aggr", "negmean", "veps", "rv", "y1"],
    "B1": ["lh", "rowmax", "mask8", "hmask"],
    "B2a": ["s2", "c0", "c1", "c2", "c3", "aggr2", "negmean2", "veps2", "rv2"],
    "B2y": ["y2"],
    "YC": ["y1c"],
}
d2abs = {}
_c2 = 0
def _seg(u, ops):
    global _c2
    for name in ops:
        _c2 += 1
        d2abs[(u, name)] = _c2
def _p2_sched():
    order = []
    for v in range(NTC_OWN + 3):
        if 3 <= v <= NTC_OWN + 2:
            order.append(("B2a", v - 3))
        if 1 <= v <= NTC_OWN:
            order.append(("YC", v - 1))
        if 2 <= v <= NTC_OWN + 1:
            order.append(("B1", v - 2))
        if 3 <= v <= NTC_OWN + 2:
            order.append(("B2y", v - 3))
        if v <= NTC_OWN - 1:
            order.append(("A1", v))
            order.append(("A2", v))
    return order
P2_ORDER = _p2_sched()
for _kind, _u in P2_ORDER:
    _seg(_u, _DSEG[_kind])
def _p2_act_sched():
    order = []
    for v in range(NTC_OWN + 3):
        if 2 <= v <= NTC_OWN + 1:
            order.append(("Ey", v - 2))
        if 3 <= v <= NTC_OWN + 2:
            order.append(("Q2", v - 3))
        if 2 <= v <= NTC_OWN + 1:
            order.append(("Eh", v - 2))
        if v <= NTC_OWN - 1:
            order.append(("Aa", v))
    return order
P2_ACT = _p2_act_sched()
a2abs = {}
_c3 = 0
for _kind, _u in P2_ACT:
    if _kind == "Aa":
        _c3 += 1
        a2abs[(_u, "sqrt1")] = _c3
    elif _kind == "Ey":
        for _r in range(4):
            _c3 += 1
            a2abs[(_u, "ey1t%d" % _r)] = _c3
    elif _kind == "Eh":
        _c3 += 1
        a2abs[(_u, "ehmt")] = _c3
    else:
        _c3 += 1
        a2abs[(_u, "sqrt2")] = _c3

AQ_N = 4


def build_nc():
    nc = bass.Bass("TRN2", target_bir_lowering=False)

    xt_full = nc.dram_tensor("xt_full", [D, SEQ], F32, kind="ExternalInput").ap()
    xt_own = nc.dram_tensor("xt_own", [D, TOK], F32, kind="ExternalInput").ap()
    x_own = nc.dram_tensor("x_own", [TOK, D], F32, kind="ExternalInput").ap()
    wkv_t = nc.dram_tensor("wkv_t", [D, 1024], F32, kind="ExternalInput").ap()
    wq_t = nc.dram_tensor("wq_t", [D, D], F32, kind="ExternalInput").ap()
    wo_t = nc.dram_tensor("wo_t", [D, D], F32, kind="ExternalInput").ap()
    wbase_t = nc.dram_tensor("wbase_t", [D, D], F32, kind="ExternalInput").ap()
    ra_t = nc.dram_tensor("ra_t", [D, 136], F32, kind="ExternalInput").ap()
    bcat_t = nc.dram_tensor("bcat_t", [128, D], F32, kind="ExternalInput").ap()
    brows = nc.dram_tensor("brows", [1, 5120], F32, kind="ExternalInput").ap()
    ident = nc.dram_tensor("ident", [128, 128], F32, kind="ExternalInput").ap()
    y_out = nc.dram_tensor("y", [TOK, D], F32, kind="ExternalOutput").ap()
    attn_dram = nc.dram_tensor("attn_scratch", [NTC_OWN, 128, D], BF16, kind="Internal").ap()

    with ExitStack() as ctx:
        sb = lambda name, shape, dt: ctx.enter_context(nc.sbuf_tensor(name, shape, dt))
        sem = lambda name: ctx.enter_context(nc.semaphore(name))

        U0 = sb("u0", [128, 16, 2048], BF16)
        U1 = sb("u1", [128, 32768], BF16)
        wkv_sb = U1[:, 0:16384].rearrange("p (c n) -> p c n", c=16)
        slab = [U1[:, 16384:20480].rearrange("p (c n) -> p c n", c=16),
                U1[:, 20480:24576].rearrange("p (c n) -> p c n", c=16)]
        sk_t = [U1[:, 24576:25088], U1[:, 25088:25600]]
        vb_t = [U1[:, 25600:26112], U1[:, 26112:26624]]
        sq_p1 = U1[:, 26624:28672]
        scr1 = U1[:, 28672:30720].bitcast(F32)
        scr2 = U1[:, 30720:32768].bitcast(F32)
        wbase_sb = U1.rearrange("p (c n) -> p c n", c=16)

        fpA = sb("fpA", [128, D], F32)
        fpB = sb("fpB", [128, D], F32)
        prodb = fpB[:, :].bitcast(BF16)[:, 0:2048]
        kvb = sb("kvb", [128, 512], BF16)
        kvb_v = kvb[:, :].rearrange("p (h e) -> p h e", h=4)
        ksr512 = sb("ksr512", [1, 512], F32)
        ksum_bc = sb("ksum_bc", [128, D], BF16)
        I_bf = sb("I_bf", [128, 128], BF16)
        onesf = sb("onesf", [1, 128], F32)
        onesb = sb("onesb", [128, 1], BF16)
        onesr = sb("onesr", [1, 128], BF16)
        ra_sb = sb("ra_sb", [128, 16, 136], BF16)
        bcat_sb = sb("bcat_sb", [128, D], BF16)

        xt_p2 = [sb("xt_p2a", [128, D], F32), sb("xt_p2b", [128, D], F32),
                 sb("xt_p2c", [128, D], F32)]
        ksrow = xt_p2[0][0:1, :]
        sq_p2 = sb("sq_p2", [128, D], BF16)
        sq_p2b = sb("sq_p2b", [128, D], BF16)
        sqp = [sq_p2, sq_p2b]
        sqh = sb("sqh", [128, D], BF16)
        sqhT = sb("sqhT", [128, D], BF16)
        y1b = sb("y1b", [128, D], BF16)
        rowkv = sq_p2[0:1, 0:1024]
        rowq = y1b[0:1, 0:2048]
        rowbb = sqh[0:1, 0:2048]
        lh = sb("lh", [128, 136], F32)
        hmask = sb("hmask", [128, 128], BF16)
        hmaskT = sb("hmaskT", [128, 128], BF16)
        norm16 = sb("norm16", [128, 16], F32)
        normeps = sb("normeps", [128, 16], F32)
        rnorm = sb("rnorm", [128, 16], F32)
        rnbf = sb("rnbf", [128, 16], BF16)
        rowmax = sb("rowmax", [128, 1], F32)
        mask8 = sb("mask8", [128, 8], F32)
        negmean = [sb("negmean%d" % i, [128, 1], F32) for i in range(4)]
        veps = [sb("veps%d" % i, [128, 1], F32) for i in range(4)]
        stdv = [sb("stdv%d" % i, [128, 1], F32) for i in range(4)]
        rstd = [sb("rstd%d" % i, [128, 1], F32) for i in range(4)]
        bnstA = sb("bnstA", [128, 4, 6], F32)
        bnstB = sb("bnstB", [128, 4, 6], F32)
        mvA = sb("mvA", [128, 2], F32)
        mvB = sb("mvB", [128, 2], F32)

        psA = ctx.enter_context(nc.psum_tensor("psA", [128, 2048], F32))
        psB = ctx.enter_context(nc.psum_tensor("psB", [128, 2048], F32))
        psK = [psA[:, 0:512], psA[:, 512:1024]]
        psV = [psA[:, 1024:1536], psA[:, 1536:2048]]
        psKV = psB[:, 0:512]
        psKS = psB[0:1, 512:1024]
        psQ = [psA, psB]

        sWLa = sem("sWLa")
        sWLb = sem("sWLb")
        sXTe = sem("sXTe")
        sXTo = sem("sXTo")
        sWO = sem("sWO")
        sWB = sem("sWB")
        sStp = sem("sStp")
        sKVm = sem("sKVm")
        sAcc = sem("sAcc")
        sDVE1 = sem("sDVE1")
        sACT1 = sem("sACT1")
        sFin = sem("sFin")
        sPEq = sem("sPEq")
        sDVEq = sem("sDVEq")
        sACTq = sem("sACTq")
        sSQW = sem("sSQW")
        sXa = sem("sXa")
        sXb = sem("sXb")
        sXc = sem("sXc")
        sSQRa = sem("sSQRa")
        sSQRb = sem("sSQRb")
        sPE2 = sem("sPE2")
        sDVE2 = sem("sDVE2")
        sACT2 = sem("sACT2")
        sY1B = sem("sY1B")
        sOutS = sem("sOutS")

        sX3 = [sXa, sXb, sXc]
        block = ctx.enter_context(nc.Block())

        def slab_wait(eng, g):
            s = sXTe if g % 2 == 0 else sXTo
            eng.wait_ge(s, 16 * (g // 2 + 1))

        @block.tensor
        def _(tensor):
            tensor.wait_ge(sStp, 1)
            tensor.wait_ge(sWLa, 32)

            for t in range(NTC_ALL):
                par = t % 2
                s_idx = t // 2
                toff = (t - 2 * s_idx) * 128
                slab_wait(tensor, s_idx)
                if t >= 2:
                    tensor.wait_ge(sDVE1, 3 * (t - 1))
                for dc in range(16):
                    lhs = slab[s_idx % 2][:, dc, toff:toff + 128]
                    tensor.matmul(psK[par], lhs, wkv_sb[:, dc, 0:512],
                                  start=(dc == 0), stop=False,
                                  skip_group_check=True)
                    tensor.matmul(psV[par], lhs, wkv_sb[:, dc, 512:1024],
                                  start=(dc == 0), stop=False,
                                  skip_group_check=True)
                tensor.matmul(psK[par], onesr[:, :], rowkv[:, 0:512],
                              start=False, stop=True, skip_group_check=True)
                tensor.matmul(psV[par], onesr[:, :], rowkv[:, 512:1024],
                              start=False, stop=True, skip_group_check=True
                              ).then_inc(sKVm, 1)
                if t >= 1:
                    tp = t - 1
                    tensor.wait_ge(sDVE1, 3 * t)
                    for h4 in range(4):
                        tensor.matmul(psKV[:, h4 * 128:(h4 + 1) * 128],
                                      sk_t[tp % 2][:, h4 * 128:(h4 + 1) * 128],
                                      vb_t[tp % 2][:, h4 * 128:(h4 + 1) * 128],
                                      start=(tp == 0 and h4 == 0),
                                      stop=(tp == NTC_ALL - 1 and h4 == 3),
                                      skip_group_check=True)
                    tensor.matmul(psKS, onesb[:, :], sk_t[tp % 2][:, :],
                                  start=(tp == 0), stop=(tp == NTC_ALL - 1),
                                  skip_group_check=True).then_inc(sAcc, 1)
            tp = NTC_ALL - 1
            tensor.wait_ge(sDVE1, 3 * NTC_ALL)
            for h4 in range(4):
                tensor.matmul(psKV[:, h4 * 128:(h4 + 1) * 128],
                              sk_t[tp % 2][:, h4 * 128:(h4 + 1) * 128],
                              vb_t[tp % 2][:, h4 * 128:(h4 + 1) * 128],
                              start=False, stop=(h4 == 3), skip_group_check=True)
            tensor.matmul(psKS, onesb[:, :], sk_t[tp % 2][:, :],
                          start=False, stop=True, skip_group_check=True
                          ).then_inc(sAcc, 1)

            tensor.wait_ge(sFin, 3)
            for j in range(4):
                mm = tensor.matmul(psA[:, j * 512:(j + 1) * 512], onesf[:, :],
                                   ksrow[:, j * 512:(j + 1) * 512], start=True, stop=True)
            mm.then_inc(sFin, 1)

            tensor.wait_ge(sWLb, 16 * 12)
            tensor.wait_ge(sFin, 5)

            def q_group(u):
                par = u % 2
                s_idx = u // 2
                toff = (u - 2 * s_idx) * 128
                slab_wait(tensor, NSLAB_ALL + s_idx)
                if u >= 2:
                    tensor.wait_ge(sACTq, AQ_N * (u - 2) + 4)
                for dc in range(16):
                    lhs = slab[s_idx % 2][:, dc, toff:toff + 128]
                    for j in range(4):
                        tensor.matmul(psQ[par][:, j * 512:(j + 1) * 512], lhs,
                                      U0[:, dc, j * 512:(j + 1) * 512],
                                      start=(dc == 0), stop=False,
                                      skip_group_check=True)
                for j in range(4):
                    mm = tensor.matmul(psQ[par][:, j * 512:(j + 1) * 512],
                                       onesr[:, :],
                                       rowq[:, j * 512:(j + 1) * 512],
                                       start=False, stop=True, skip_group_check=True)
                mm.then_inc(sPEq, 1)

            def t_group(u):
                par = u % 2
                tensor.wait_ge(sDVEq, dq_sqhat(u))
                for h in range(16):
                    mm = tensor.matmul(psQ[par][:, h * 128:(h + 1) * 128],
                                       sqh[:, h * 128:(h + 1) * 128], I_bf[:, :],
                                       start=True, stop=True)
                mm.then_inc(sPEq, 1)

            def qkv_group(u):
                par = u % 2
                tensor.wait_ge(sACTq, AQ_N * u + 3)
                for h in range(16):
                    mm = tensor.matmul(psQ[par][:, h * 128:(h + 1) * 128],
                                       kvb_v[:, h // 4, :],
                                       sqhT[:, h * 128:(h + 1) * 128],
                                       start=True, stop=True)
                mm.then_inc(sPEq, 1)

            for u in range(NTC_OWN):
                q_group(u)
                if u >= 1:
                    t_group(u - 1)
                    qkv_group(u - 1)
            t_group(NTC_OWN - 1)
            qkv_group(NTC_OWN - 1)

            tensor.wait_ge(sACTq, AQ_N * NTC_OWN)

            def y1t_round(u, r):
                if r == 0:
                    tensor.wait_ge(sDVE2, d2abs[(u, "y1c")])
                    if u >= 1:
                        tensor.wait_ge(sDVE2, d2abs[(u - 1, "s2")])
                else:
                    if r >= 2:
                        tensor.wait_ge(sACT2, a2abs[(u, "ey1t%d" % (r - 2))])
                for k in range(4):
                    dc = 4 * r + k
                    base_col = (r % 2) * 512 + k * 128
                    mm = tensor.matmul(psB[:, base_col:base_col + 128],
                                       y1b[:, dc * 128:(dc + 1) * 128], I_bf[:, :],
                                       start=True, stop=True, skip_group_check=True)
                mm.then_inc(sPE2, 1)

            def misc_part(u, half):
                tensor.wait_ge(sACT2, a2abs[(u, "ey1t%d" % (half + 2))])
                for dc in range(8 * half, 8 * half + 8):
                    mm = tensor.matmul(psB[:, 0:136],
                                       sqhT[:, dc * 128:(dc + 1) * 128],
                                       ra_sb[:, dc, :],
                                       start=(dc == 0), stop=(dc == 15),
                                       skip_group_check=True)
                if half == 1:
                    mm.then_inc(sPE2, 1)

            def p2_iter(v):
                u = v - 2
                if v <= NTC_OWN - 1:
                    tensor.wait_ge(sSQRa if v % 2 == 0 else sSQRb, 16 * (v // 2 + 1))
                    if v >= 1:
                        tensor.wait_ge(sDVE2, d2abs[(v - 1, "s")])
                    if v == 0:
                        tensor.wait_ge(sWO, 16 * WO_CH)
                    sq = sqp[v % 2]
                    for h in range(16):
                        for j in range(4):
                            mm = tensor.matmul(psA[:, j * 512:(j + 1) * 512],
                                               sq[:, h * 128:(h + 1) * 128],
                                               U0[:, h, j * 512:(j + 1) * 512],
                                               start=(h == 0), stop=(h == 15),
                                               skip_group_check=True)
                        if u >= 0:
                            if h == 3:
                                y1t_round(u, 0)
                            elif h == 5:
                                y1t_round(u, 1)
                            elif h == 7:
                                y1t_round(u, 2)
                            elif h == 9:
                                y1t_round(u, 3)
                            elif h == 10:
                                misc_part(u, 0)
                            elif h == 12:
                                misc_part(u, 1)
                    mm.then_inc(sPE2, 1)
                else:
                    for r in range(4):
                        y1t_round(u, r)
                    misc_part(u, 0)
                    misc_part(u, 1)
                if u >= 0:
                    tensor.wait_ge(sDVE2, d2abs[(u, "hmask")])
                    tensor.matmul(psB[:, 512:640], hmask[:, :], I_bf[:, :],
                                  start=True, stop=True,
                                  skip_group_check=True).then_inc(sPE2, 1)
                    tensor.wait_ge(sDVE2, d2abs[(u, "lh")])
                    if u == 0:
                        tensor.wait_ge(sWB, 16 * (WB_CH + 1))
                    for j in range(4):
                        for dc in range(16):
                            tensor.matmul(psB[:, j * 512:(j + 1) * 512],
                                          sqhT[:, dc * 128:(dc + 1) * 128],
                                          wbase_sb[:, dc, j * 512:(j + 1) * 512],
                                          start=(dc == 0), stop=False,
                                          skip_group_check=True)
                        tensor.matmul(psB[:, j * 512:(j + 1) * 512], onesr[:, :],
                                      rowbb[:, j * 512:(j + 1) * 512],
                                      start=False, stop=False, skip_group_check=True)
                        if j == 0:
                            tensor.wait_ge(sACT2, a2abs[(u, "ehmt")])
                        mm = tensor.matmul(psB[:, j * 512:(j + 1) * 512], hmaskT[:, :],
                                           bcat_sb[:, j * 512:(j + 1) * 512],
                                           start=False, stop=True,
                                           skip_group_check=True)
                    mm.then_inc(sPE2, 1)

            for v in range(NTC_OWN + 2):
                p2_iter(v)

        @block.vector
        def _(vector):
            vector.memset(onesf[:, :], 1.0)
            vector.memset(onesb[:, :], 1.0)
            vector.memset(onesr[:, :], 1.0).then_inc(sStp, 1)

            dcnt = [0]

            def op(ins):
                ins.then_inc(sDVE1, 1)
                dcnt[0] += 1
                vector.wait_ge(sDVE1, dcnt[0])
                return ins

            for t in range(NTC_ALL):
                par = t % 2
                vector.wait_ge(sKVm, t + 1)
                if t >= 1:
                    vector.wait_ge(sACT1, t)
                op(vector.tensor_scalar(scr1[:, 0:512], psK[par], 0.0, None, ALU.min))
                vector.wait_ge(sACT1, t + 1)
                if t >= 2:
                    vector.wait_ge(sAcc, t - 1)
                op(vector.scalar_tensor_tensor(sk_t[par][:, :], psK[par], 0.0,
                                               scr2[:, 0:512], ALU.max, ALU.add))
                op(vector.tensor_copy(vb_t[par][:, :], psV[par]))

            vector.wait_ge(sAcc, NTC_ALL)
            vector.tensor_copy(kvb[:, :], psKV).then_inc(sFin, 1)
            vector.tensor_copy(ksr512[:, :], psKS).then_inc(sFin, 1)
            vector.wait_ge(sFin, 2)
            ks_in = ksr512[:, :].rearrange("a (h d) -> a h d", h=4).unsqueeze(2) \
                .broadcast_to([1, 4, 4, 128])
            vector.tensor_copy(ksrow.rearrange("a (h r d) -> a h r d", h=4, r=4),
                               ks_in).then_inc(sFin, 1)
            vector.wait_ge(sFin, 4)
            vector.tensor_copy(ksum_bc[:, :], psA[:, :]).then_inc(sFin, 1)

            dq = [0]

            def opq(ins):
                ins.then_inc(sDVEq, 1)
                dq[0] += 1
                vector.wait_ge(sDVEq, dq[0])
                return ins

            sq3 = sq_p1.rearrange("p (h d) -> p h d", h=16)
            prod3 = prodb.rearrange("p (h d) -> p h d", h=16)
            ks3 = ksum_bc[:, :].rearrange("p (h d) -> p h d", h=16)
            sqh3 = sqh[:, :].rearrange("p (h d) -> p h d", h=16)
            rn3 = rnbf[:, :].unsqueeze(2).broadcast_to([128, 16, 128])
            for u in range(NTC_OWN):
                par = u % 2
                vector.wait_ge(sPEq, peq_q[u])
                if u >= 1:
                    vector.wait_ge(sSQW, 16 * u)
                for j in range(2):
                    sl = slice(j * 1024, (j + 1) * 1024)
                    if u + j >= 1:
                        vector.wait_ge(sACTq, AQ_N * u + j)
                    opq(vector.tensor_scalar(scr1[:, :], psQ[par][:, sl], 0.0,
                                             None, ALU.min))
                    vector.wait_ge(sACTq, AQ_N * u + j + 1)
                    opq(vector.scalar_tensor_tensor(sq_p1[:, sl], psQ[par][:, sl], 0.0,
                                                    scr2[:, :], ALU.max, ALU.add))
                opq(vector.tensor_tensor(prod3, sq3, ks3, ALU.mult))
                opq(vector.tensor_reduce(norm16[:, :], prod3, AX.X, ALU.add))
                opq(vector.tensor_scalar(normeps[:, :], norm16[:, :], ATT_EPS,
                                         None, ALU.add))
                opq(vector.reciprocal(rnorm[:, :], normeps[:, :]))
                opq(vector.tensor_copy(rnbf[:, :], rnorm[:, :]))
                if u >= 1:
                    vector.wait_ge(sPEq, peq_T[u - 1])
                opq(vector.tensor_tensor(sqh3, sq3, rn3, ALU.mult))

            dd = [0]

            def op2(ins):
                ins.then_inc(sDVE2, 1)
                dd[0] += 1
                vector.wait_ge(sDVE2, dd[0])
                return ins

            lh3 = lh[:, 8:136].rearrange("p (e r) -> p e r", e=8)
            hm3 = hmask[:, :].rearrange("p (e r) -> p e r", e=8)
            m8b = mask8[:, :].unsqueeze(2).broadcast_to([128, 8, 16])

            def seg_a1(u):
                vector.wait_ge(sPE2, pe2_mm3[u])
                vector.wait_ge(sX3[u % 3], 16 * (u // 3 + 1))
                op2(vector.scalar_tensor_tensor(fpA[:, :], psA[:, :], 1.0,
                                                xt_p2[u % 3][:, :], ALU.mult, ALU.add))

            def seg_a2(u):
                p = u % 2
                for g in range(4):
                    op2(vector.bn_stats(bnstA[:, g, :], fpA[:, g * 512:(g + 1) * 512]))
                op2(vector.bn_aggr(mvA[:, :], bnstA[:, :, :].rearrange("p a b -> p (a b)")))
                op2(vector.tensor_scalar(negmean[p][:, :], mvA[:, 0:1], -1.0, None, ALU.mult))
                op2(vector.tensor_scalar(veps[p][:, :], mvA[:, 1:2], LN_EPS, None, ALU.add))
                op2(vector.reciprocal(stdv[p][:, :], veps[p][:, :]))
                vector.wait_ge(sACT2, a2abs[(u, "sqrt1")])
                op2(vector.tensor_scalar(xt_p2[u % 3][:, :], fpA[:, :], negmean[p][:, :],
                                         rstd[p][:, :], ALU.add, ALU.mult))

            def seg_b1(u):
                vector.wait_ge(sPE2, pe2_misc[u])
                op2(vector.tensor_copy(lh[:, :], psB[:, 0:136]))
                op2(vector.reduce_max(rowmax[:, :], lh[:, 0:8], AX.X))
                op2(vector.tensor_scalar(mask8[:, :], lh[:, 0:8], rowmax[:, :],
                                         2.0, ALU.is_ge, ALU.mult))
                op2(vector.tensor_tensor(hm3, lh3, m8b, ALU.mult))

            def seg_b2a(u):
                p = 2 + u % 2
                vector.wait_ge(sPE2, pe2_base[u])
                op2(vector.scalar_tensor_tensor(fpA[:, :], psB[:, :], 1.0,
                                                xt_p2[u % 3][:, :], ALU.mult, ALU.add))
                for g in range(4):
                    op2(vector.bn_stats(bnstB[:, g, :], fpA[:, g * 512:(g + 1) * 512]))
                op2(vector.bn_aggr(mvB[:, :], bnstB[:, :, :].rearrange("p a b -> p (a b)")))
                op2(vector.tensor_scalar(negmean[p][:, :], mvB[:, 0:1], -1.0, None, ALU.mult))
                op2(vector.tensor_scalar(veps[p][:, :], mvB[:, 1:2], LN_EPS, None, ALU.add))
                op2(vector.reciprocal(stdv[p][:, :], veps[p][:, :]))

            def seg_yc(u):
                if u >= 1:
                    vector.wait_ge(sPE2, pe2_y1t[(u - 1, 3)])
                op2(vector.tensor_copy(y1b[:, :], xt_p2[u % 3][:, :]))

            def seg_b2y(u):
                p = 2 + u % 2
                vector.wait_ge(sACT2, a2abs[(u, "sqrt2")])
                if u >= 1:
                    vector.wait_ge(sOutS, 16 * u)
                op2(vector.tensor_scalar(fpB[:, :], fpA[:, :], negmean[p][:, :],
                                         rstd[p][:, :], ALU.add, ALU.mult))

            _dseg_fns = {"A1": seg_a1, "A2": seg_a2, "B1": seg_b1,
                         "B2a": seg_b2a, "B2y": seg_b2y, "YC": seg_yc}
            for kind, u in P2_ORDER:
                _dseg_fns[kind](u)

        @block.scalar
        def _(scalar):
            scalar.preload_act_table(AF.Exp)
            for t in range(NTC_ALL):
                scalar.wait_ge(sDVE1, 3 * t + 1)
                scalar.activation(scr2[:, 0:512], scr1[:, 0:512], AF.Exp).then_inc(sACT1, 1)
            for u in range(NTC_OWN):
                par = u % 2
                for j in range(2):
                    scalar.wait_ge(sDVEq, DQ_N * u + 2 * j + 1)
                    scalar.activation(scr2[:, :], scr1[:, :], AF.Exp).then_inc(sACTq, 1)
                scalar.wait_ge(sPEq, peq_T[u])
                scalar.copy(sqhT[:, :], psQ[par][:, :]).then_inc(sACTq, 1)
                scalar.wait_ge(sPEq, peq_qkv[u])
                scalar.wait_ge(sDVEq, dq_sqhat(u))
                scalar.wait_ge(sSQW, 16 * u)
                scalar.copy(sq_p1, psQ[par][:, :]).then_inc(sACTq, 1)
            scalar.preload_act_table(AF.Sqrt)
            for kind, u in P2_ACT:
                if kind == "Aa":
                    p = u % 2
                    scalar.wait_ge(sDVE2, d2abs[(u, "rv")])
                    scalar.activation(rstd[p][:, :], stdv[p][:, :], AF.Sqrt).then_inc(sACT2, 1)
                elif kind == "Ey":
                    for r in range(4):
                        scalar.wait_ge(sPE2, pe2_y1t[(u, r)])
                        scalar.copy(sqhT[:, r * 512:(r + 1) * 512],
                                    psB[:, (r % 2) * 512:(r % 2) * 512 + 512]).then_inc(sACT2, 1)
                elif kind == "Eh":
                    scalar.wait_ge(sPE2, pe2_hmt[u])
                    scalar.copy(hmaskT[:, :], psB[:, 512:640]).then_inc(sACT2, 1)
                else:
                    p = 2 + u % 2
                    scalar.wait_ge(sDVE2, d2abs[(u, "rv2")])
                    scalar.activation(rstd[p][:, :], stdv[p][:, :], AF.Sqrt).then_inc(sACT2, 1)

        @block.gpsimd
        def _(gpsimd):
            wq_src = wq_t.rearrange("(c p) n -> p c n", p=128)
            gpsimd.dma_start(out=wkv_sb, in_=wkv_t.rearrange("(c p) n -> p c n", p=128)
                             ).then_inc(sWLa, 16)
            gpsimd.dma_start(out=rowkv, in_=brows[:, 0:1024]).then_inc(sWLa, 16)
            gpsimd.wait_ge(sWLa, 32)
            wlb_small = [
                (rowq, brows[:, 1024:3072]),
                (I_bf[:, :], ident),
                (ra_sb[:, :, :], ra_t.rearrange("(c p) n -> p c n", p=128)),
                (bcat_sb[:, :], bcat_t),
            ]
            wlb_jobs = wlb_small + [
                (U0[:, 2 * i:2 * i + 2, :], wq_src[:, 2 * i:2 * i + 2, :])
                for i in range(8)]
            wlb_i = [0]

            def emit_wlb():
                if wlb_i[0] < len(wlb_jobs):
                    dst, s_ = wlb_jobs[wlb_i[0]]
                    gpsimd.dma_start(out=dst, in_=s_).then_inc(sWLb, 16)
                    wlb_i[0] += 1

            for s in range(NSLAB_ALL):
                if s >= 2:
                    gpsimd.wait_ge(sAcc, 2 * (s - 1))
                src = xt_full.rearrange("(c p) t -> p c t", p=128)[:, :, s * 256:(s + 1) * 256]
                gpsimd.dma_start(out=slab[s % 2], in_=src).then_inc(
                    sXTe if s % 2 == 0 else sXTo, 16)
                if 2 <= s < 6 or (s >= 8 and s % 2 == 0):
                    emit_wlb()
            while wlb_i[0] < len(wlb_jobs):
                emit_wlb()
            wo_src = wo_t.rearrange("(c p) n -> p c n", p=128)
            wb_src = wbase_t.rearrange("(c p) n -> p c n", p=128)
            for s in range(NSLAB_OWN):
                if s < 2:
                    gpsimd.wait_ge(sAcc, NTC_ALL)
                else:
                    gpsimd.wait_ge(sPEq, peq_q[2 * (s - 1)])
                g = NSLAB_ALL + s
                src = xt_own.rearrange("(c p) t -> p c t", p=128)[:, :, s * 256:(s + 1) * 256]
                gpsimd.dma_start(out=slab[g % 2], in_=src).then_inc(
                    sXTe if g % 2 == 0 else sXTo, 16)
                if 2 <= s < 6:
                    i = s - 2
                    if i == 0:
                        gpsimd.wait_ge(sKVm, NTC_ALL)
                    gpsimd.dma_start(out=wbase_sb[:, 2 * i:2 * i + 2, :],
                                     in_=wb_src[:, 2 * i:2 * i + 2, :]).then_inc(sWB, 16)
            gpsimd.wait_ge(sPEq, peq_q[NTC_OWN - 1])
            for i in range(WO_CH):
                gpsimd.dma_start(out=U0[:, 2 * i:2 * i + 2, :],
                                 in_=wo_src[:, 2 * i:2 * i + 2, :]).then_inc(sWO, 16)
            gpsimd.wait_ge(sDVEq, DQ_N * NTC_OWN)
            gpsimd.wait_ge(sACTq, AQ_N * NTC_OWN)
            gpsimd.wait_ge(sSQW, 16 * NTC_OWN)
            gpsimd.wait_ge(sPEq, peq_T[NTC_OWN - 1])
            for i in range(4, WB_CH):
                gpsimd.dma_start(out=wbase_sb[:, 2 * i:2 * i + 2, :],
                                 in_=wb_src[:, 2 * i:2 * i + 2, :]).then_inc(sWB, 16)
            gpsimd.dma_start(out=rowbb, in_=brows[:, 3072:5120]).then_inc(sWB, 16)

            def x_load_g(v):
                if v == 0:
                    gpsimd.wait_ge(sFin, 4)
                if v >= 3:
                    gpsimd.wait_ge(sDVE2, d2abs[(v - 3, "s2")])
                gpsimd.dma_start(out=xt_p2[v % 3][:, :],
                                 in_=x_own[v * 128:(v + 1) * 128, :]).then_inc(
                                     sX3[v % 3], 16)

            for v in range(NTC_OWN):
                x_load_g(v)

        @block.sync
        def _(sync):
            for u in range(NTC_OWN):
                sync.wait_ge(sACTq, AQ_N * (u + 1))
                sync.dma_start(out=attn_dram[u], in_=sq_p1).then_inc(sSQW, 16)
            def sq_load(u):
                if u >= 2:
                    sync.wait_ge(sPE2, pe2_mm3[u - 2])
                sync.dma_start(out=sqp[u % 2][:, :], in_=attn_dram[u]).then_inc(
                    sSQRa if u % 2 == 0 else sSQRb, 16)

            def y_store(u):
                sync.wait_ge(sDVE2, d2abs[(u, "y2")])
                sync.dma_start(out=y_out[u * 128:(u + 1) * 128, :],
                               in_=fpB[:, :]).then_inc(sOutS, 16)

            for u in range(NTC_OWN):
                sq_load(u)
                if u >= 5:
                    y_store(u - 5)
            for u in range(NTC_OWN - 5, NTC_OWN):
                y_store(u)
            sync.wait_ge(sOutS, 16 * NTC_OWN)


    return nc


def kernel(L0_attn_hidden_states, L0_attn_Wq, L0_attn_bq, L0_attn_Wk, L0_attn_bk,
           L0_attn_Wv, L0_attn_bv, L0_attn_Wo, L0_ffn_W_base, L0_ffn_b_base,
           L0_ffn_router_W, L0_ffn_A, L0_ffn_Bw):
    f32 = np.float32
    X = np.ascontiguousarray(np.asarray(L0_attn_hidden_states, dtype=f32))
    Wq = np.asarray(L0_attn_Wq, f32)
    Wk = np.asarray(L0_attn_Wk, f32)
    Wv = np.asarray(L0_attn_Wv, f32)
    Wo = np.asarray(L0_attn_Wo, f32)
    Wb = np.asarray(L0_ffn_W_base, f32)
    Rw = np.asarray(L0_ffn_router_W, f32)
    A = np.asarray(L0_ffn_A, f32)
    Bw = np.asarray(L0_ffn_Bw, f32)

    wkv_t = np.ascontiguousarray(np.concatenate([Wk, Wv], axis=0).T)
    wq_t = np.ascontiguousarray(Wq.T)
    wo_t = np.ascontiguousarray(Wo.T)
    wbase_t = np.ascontiguousarray(Wb.T)
    ra_t = np.ascontiguousarray(
        np.concatenate([Rw, A.reshape(128, D)], axis=0).T)
    bcat_t = np.ascontiguousarray(Bw.transpose(0, 2, 1).reshape(128, D))
    brows = np.concatenate([
        np.asarray(L0_attn_bk, f32), np.asarray(L0_attn_bv, f32),
        np.asarray(L0_attn_bq, f32), np.asarray(L0_ffn_b_base, f32)
    ]).reshape(1, 5120)
    ident = np.eye(128, dtype=f32)

    xt_b = [np.ascontiguousarray(X[b].T) for b in range(2)]

    nc = build_nc()
    in_maps = []
    for c in range(8):
        b, j = c // 4, c % 4
        in_maps.append(dict(
            xt_full=xt_b[b],
            xt_own=np.ascontiguousarray(xt_b[b][:, j * TOK:(j + 1) * TOK]),
            x_own=np.ascontiguousarray(X[b, j * TOK:(j + 1) * TOK, :]),
            wkv_t=wkv_t, wq_t=wq_t, wo_t=wo_t, wbase_t=wbase_t,
            ra_t=ra_t, bcat_t=bcat_t, brows=brows, ident=ident,
        ))
    res = run_bass_kernel_spmd(nc, in_maps, core_ids=list(range(8)))
    y = np.empty((2, SEQ, D), dtype=f32)
    for c in range(8):
        b, j = c // 4, c % 4
        y[b, j * TOK:(j + 1) * TOK, :] = res.results[c]["y"]
    return y


# version 14
# speedup vs baseline: 1.5309x; 1.0323x over previous
import numpy as np
from contextlib import ExitStack

import concourse.bass as bass
import concourse.mybir as mybir
from concourse.bass_utils import run_bass_kernel_spmd

F32 = mybir.dt.float32
BF16 = mybir.dt.bfloat16
ALU = mybir.AluOpType
AX = mybir.AxisListType
AF = mybir.ActivationFunctionType

D = 2048
SEQ = 8192
TOK = 2048
NTC_ALL = 64
NTC_OWN = 16
NSLAB_ALL = 32
NSLAB_OWN = 8
ATT_EPS = 1e-6
LN_EPS = 1e-5
WO_CH = 8
WB_CH = 8
N_P2W = WO_CH + WB_CH

peq_q, peq_T, peq_qkv = {}, {}, {}
_c = 0
for _u in range(NTC_OWN):
    _c += 1
    peq_q[_u] = _c
    if _u >= 1:
        _c += 1
        peq_T[_u - 1] = _c
        _c += 1
        peq_qkv[_u - 1] = _c
_c += 1
peq_T[NTC_OWN - 1] = _c
_c += 1
peq_qkv[NTC_OWN - 1] = _c

pe2_mm3 = {}
pe2_y1t = {}
pe2_misc, pe2_hmt, pe2_base = {}, {}, {}


def _plan_pe2():
    c = 0
    for v in range(NTC_OWN + 2):
        u = v - 2
        if v <= NTC_OWN - 1:
            if u >= 0:
                for r in range(4):
                    c += 1
                    pe2_y1t[(u, r)] = c
                c += 1
                pe2_misc[u] = c
            c += 1
            pe2_mm3[v] = c
        else:
            for r in range(4):
                c += 1
                pe2_y1t[(u, r)] = c
            c += 1
            pe2_misc[u] = c
        if u >= 0:
            c += 1
            pe2_hmt[u] = c
            c += 1
            pe2_base[u] = c


_plan_pe2()

dqabs = {}
_cq = 0
def _dq_seg(u, names):
    global _cq
    for n in names:
        _cq += 1
        dqabs[(u, n)] = _cq
for _u in range(NTC_OWN):
    _dq_seg(_u, ["min0"])
    if _u >= 1:
        _dq_seg(_u - 1, ["E1", "E2"])
    _dq_seg(_u, ["stt0", "min1", "stt1", "prod", "red", "eps", "recip", "cast",
                 "sqhat0", "sqhat1"])
_dq_seg(NTC_OWN - 1, ["E1", "E2"])
DQ_TOTAL = _cq

_DSEG = {
    "A1": ["s"],
    "A2": ["b0", "b1", "b2", "b3", "aggr", "negmean", "veps", "rv", "y1"],
    "B1": ["lh", "rowmax", "mask8", "hmask"],
    "B2a": ["s2", "c0", "c1", "c2", "c3", "aggr2", "negmean2", "veps2", "rv2"],
    "B2y": ["y2"],
    "YC": ["y1c"],
}
d2abs = {}
_c2 = 0
def _seg(u, ops):
    global _c2
    for name in ops:
        _c2 += 1
        d2abs[(u, name)] = _c2
def _p2_sched():
    order = []
    for v in range(NTC_OWN + 3):
        if 3 <= v <= NTC_OWN + 2:
            order.append(("B2a", v - 3))
        if 1 <= v <= NTC_OWN:
            order.append(("YC", v - 1))
        if 2 <= v <= NTC_OWN + 1:
            order.append(("B1", v - 2))
        if 3 <= v <= NTC_OWN + 2:
            order.append(("B2y", v - 3))
        if v <= NTC_OWN - 1:
            order.append(("A1", v))
            order.append(("A2", v))
    return order
P2_ORDER = _p2_sched()
for _kind, _u in P2_ORDER:
    _seg(_u, _DSEG[_kind])
def _p2_act_sched():
    order = []
    for v in range(NTC_OWN + 3):
        if 2 <= v <= NTC_OWN + 1:
            order.append(("Ey", v - 2))
        if 3 <= v <= NTC_OWN + 2:
            order.append(("Q2", v - 3))
        if 2 <= v <= NTC_OWN + 1:
            order.append(("Eh", v - 2))
        if v <= NTC_OWN - 1:
            order.append(("Aa", v))
    return order
P2_ACT = _p2_act_sched()
a2abs = {}
_c3 = 0
for _kind, _u in P2_ACT:
    if _kind == "Aa":
        _c3 += 1
        a2abs[(_u, "sqrt1")] = _c3
    elif _kind == "Ey":
        for _r in range(4):
            _c3 += 1
            a2abs[(_u, "ey1t%d" % _r)] = _c3
    elif _kind == "Eh":
        _c3 += 1
        a2abs[(_u, "ehmt")] = _c3
    else:
        _c3 += 1
        a2abs[(_u, "sqrt2")] = _c3

AQ_N = 2


def build_nc():
    nc = bass.Bass("TRN2", target_bir_lowering=False)

    xt_full = nc.dram_tensor("xt_full", [D, SEQ], F32, kind="ExternalInput").ap()
    xt_own = nc.dram_tensor("xt_own", [D, TOK], F32, kind="ExternalInput").ap()
    x_own = nc.dram_tensor("x_own", [TOK, D], F32, kind="ExternalInput").ap()
    wkv_t = nc.dram_tensor("wkv_t", [D, 1024], F32, kind="ExternalInput").ap()
    wq_t = nc.dram_tensor("wq_t", [D, D], F32, kind="ExternalInput").ap()
    wo_t = nc.dram_tensor("wo_t", [D, D], F32, kind="ExternalInput").ap()
    wbase_t = nc.dram_tensor("wbase_t", [D, D], F32, kind="ExternalInput").ap()
    ra_t = nc.dram_tensor("ra_t", [D, 136], F32, kind="ExternalInput").ap()
    bcat_t = nc.dram_tensor("bcat_t", [128, D], F32, kind="ExternalInput").ap()
    brows = nc.dram_tensor("brows", [1, 5120], F32, kind="ExternalInput").ap()
    ident = nc.dram_tensor("ident", [128, 128], F32, kind="ExternalInput").ap()
    y_out = nc.dram_tensor("y", [TOK, D], F32, kind="ExternalOutput").ap()
    attn_dram = nc.dram_tensor("attn_scratch", [NTC_OWN, 128, D], BF16, kind="Internal").ap()

    with ExitStack() as ctx:
        sb = lambda name, shape, dt: ctx.enter_context(nc.sbuf_tensor(name, shape, dt))
        sem = lambda name: ctx.enter_context(nc.semaphore(name))

        U0 = sb("u0", [128, 16, 2048], BF16)
        U1 = sb("u1", [128, 32768], BF16)
        wkv_sb = U1[:, 0:16384].rearrange("p (c n) -> p c n", c=16)
        slab = [U1[:, 16384:20480].rearrange("p (c n) -> p c n", c=16),
                U1[:, 20480:24576].rearrange("p (c n) -> p c n", c=16)]
        sk_t = [U1[:, 24576:25088], U1[:, 25088:25600]]
        vb_t = [U1[:, 25600:26112], U1[:, 26112:26624]]
        sq_p1 = U1[:, 26624:28672]
        scr1 = U1[:, 28672:30720].bitcast(F32)
        scr2 = U1[:, 30720:32768].bitcast(F32)
        wbase_sb = U1.rearrange("p (c n) -> p c n", c=16)

        fpA = sb("fpA", [128, D], F32)
        fpB = sb("fpB", [128, D], F32)
        prodb = fpB[:, :].bitcast(BF16)[:, 0:2048]
        kvb = sb("kvb", [128, 512], BF16)
        kvb_v = kvb[:, :].rearrange("p (h e) -> p h e", h=4)
        ksr512 = sb("ksr512", [1, 512], F32)
        ksum_bc = sb("ksum_bc", [128, D], BF16)
        I_bf = sb("I_bf", [128, 128], BF16)
        onesf = sb("onesf", [1, 128], F32)
        onesb = sb("onesb", [128, 1], BF16)
        onesr = sb("onesr", [1, 128], BF16)
        ra_sb = sb("ra_sb", [128, 16, 136], BF16)
        bcat_sb = sb("bcat_sb", [128, D], BF16)

        xt_p2 = [sb("xt_p2a", [128, D], F32), sb("xt_p2b", [128, D], F32),
                 sb("xt_p2c", [128, D], F32)]
        ksrow = xt_p2[0][0:1, :]
        sq_p2 = sb("sq_p2", [128, D], BF16)
        sq_p2b = sb("sq_p2b", [128, D], BF16)
        sqp = [sq_p2, sq_p2b]
        sqh = sb("sqh", [128, D], BF16)
        sqhT = sb("sqhT", [128, D], BF16)
        y1b = sb("y1b", [128, D], BF16)
        rowkv = sq_p2[0:1, 0:1024]
        rowq = y1b[0:1, 0:2048]
        rowbb = sqh[0:1, 0:2048]
        lh = sb("lh", [128, 136], F32)
        hmask = sb("hmask", [128, 128], BF16)
        hmaskT = sb("hmaskT", [128, 128], BF16)
        norm16 = sb("norm16", [128, 16], F32)
        normeps = sb("normeps", [128, 16], F32)
        rnorm = sb("rnorm", [128, 16], F32)
        rnbf = sb("rnbf", [128, 16], BF16)
        rowmax = sb("rowmax", [128, 1], F32)
        mask8 = sb("mask8", [128, 8], F32)
        negmean = [sb("negmean%d" % i, [128, 1], F32) for i in range(4)]
        veps = [sb("veps%d" % i, [128, 1], F32) for i in range(4)]
        stdv = [sb("stdv%d" % i, [128, 1], F32) for i in range(4)]
        rstd = [sb("rstd%d" % i, [128, 1], F32) for i in range(4)]
        bnstA = sb("bnstA", [128, 4, 6], F32)
        bnstB = sb("bnstB", [128, 4, 6], F32)
        mvA = sb("mvA", [128, 2], F32)
        mvB = sb("mvB", [128, 2], F32)

        psA = ctx.enter_context(nc.psum_tensor("psA", [128, 2048], F32))
        psB = ctx.enter_context(nc.psum_tensor("psB", [128, 2048], F32))
        psK = [psA[:, 0:512], psA[:, 512:1024]]
        psV = [psA[:, 1024:1536], psA[:, 1536:2048]]
        psKV = psB[:, 0:512]
        psKS = psB[0:1, 512:1024]
        psQ = [psA, psB]

        sWLa = sem("sWLa")
        sWLb = sem("sWLb")
        sXTe = sem("sXTe")
        sXTo = sem("sXTo")
        sWO = sem("sWO")
        sWB = sem("sWB")
        sStp = sem("sStp")
        sKVm = sem("sKVm")
        sAcc = sem("sAcc")
        sDVE1 = sem("sDVE1")
        sACT1 = sem("sACT1")
        sFin = sem("sFin")
        sPEq = sem("sPEq")
        sDVEq = sem("sDVEq")
        sACTq = sem("sACTq")
        sSQW = sem("sSQW")
        sXa = sem("sXa")
        sXb = sem("sXb")
        sXc = sem("sXc")
        sSQRa = sem("sSQRa")
        sSQRb = sem("sSQRb")
        sPE2 = sem("sPE2")
        sDVE2 = sem("sDVE2")
        sACT2 = sem("sACT2")
        sY1B = sem("sY1B")
        sOutS = sem("sOutS")

        sX3 = [sXa, sXb, sXc]
        block = ctx.enter_context(nc.Block())

        def slab_wait(eng, g):
            s = sXTe if g % 2 == 0 else sXTo
            eng.wait_ge(s, 16 * (g // 2 + 1))

        @block.tensor
        def _(tensor):
            tensor.wait_ge(sStp, 1)
            tensor.wait_ge(sWLa, 32)

            for t in range(NTC_ALL):
                par = t % 2
                s_idx = t // 2
                toff = (t - 2 * s_idx) * 128
                slab_wait(tensor, s_idx)
                if t >= 2:
                    tensor.wait_ge(sDVE1, 3 * (t - 1))
                for dc in range(16):
                    lhs = slab[s_idx % 2][:, dc, toff:toff + 128]
                    tensor.matmul(psK[par], lhs, wkv_sb[:, dc, 0:512],
                                  start=(dc == 0), stop=False,
                                  skip_group_check=True)
                    tensor.matmul(psV[par], lhs, wkv_sb[:, dc, 512:1024],
                                  start=(dc == 0), stop=False,
                                  skip_group_check=True)
                tensor.matmul(psK[par], onesr[:, :], rowkv[:, 0:512],
                              start=False, stop=True, skip_group_check=True)
                tensor.matmul(psV[par], onesr[:, :], rowkv[:, 512:1024],
                              start=False, stop=True, skip_group_check=True
                              ).then_inc(sKVm, 1)
                if t >= 1:
                    tp = t - 1
                    tensor.wait_ge(sDVE1, 3 * t)
                    for h4 in range(4):
                        tensor.matmul(psKV[:, h4 * 128:(h4 + 1) * 128],
                                      sk_t[tp % 2][:, h4 * 128:(h4 + 1) * 128],
                                      vb_t[tp % 2][:, h4 * 128:(h4 + 1) * 128],
                                      start=(tp == 0 and h4 == 0),
                                      stop=(tp == NTC_ALL - 1 and h4 == 3),
                                      skip_group_check=True)
                    tensor.matmul(psKS, onesb[:, :], sk_t[tp % 2][:, :],
                                  start=(tp == 0), stop=(tp == NTC_ALL - 1),
                                  skip_group_check=True).then_inc(sAcc, 1)
            tp = NTC_ALL - 1
            tensor.wait_ge(sDVE1, 3 * NTC_ALL)
            for h4 in range(4):
                tensor.matmul(psKV[:, h4 * 128:(h4 + 1) * 128],
                              sk_t[tp % 2][:, h4 * 128:(h4 + 1) * 128],
                              vb_t[tp % 2][:, h4 * 128:(h4 + 1) * 128],
                              start=False, stop=(h4 == 3), skip_group_check=True)
            tensor.matmul(psKS, onesb[:, :], sk_t[tp % 2][:, :],
                          start=False, stop=True, skip_group_check=True
                          ).then_inc(sAcc, 1)

            tensor.wait_ge(sFin, 3)
            for j in range(4):
                mm = tensor.matmul(psA[:, j * 512:(j + 1) * 512], onesf[:, :],
                                   ksrow[:, j * 512:(j + 1) * 512], start=True, stop=True)
            mm.then_inc(sFin, 1)

            tensor.wait_ge(sWLb, 16 * 12)
            tensor.wait_ge(sFin, 5)

            def q_group(u):
                par = u % 2
                s_idx = u // 2
                toff = (u - 2 * s_idx) * 128
                slab_wait(tensor, NSLAB_ALL + s_idx)
                if u >= 2:
                    tensor.wait_ge(sDVEq, dqabs[(u - 2, "E2")])
                for dc in range(16):
                    lhs = slab[s_idx % 2][:, dc, toff:toff + 128]
                    for j in range(4):
                        tensor.matmul(psQ[par][:, j * 512:(j + 1) * 512], lhs,
                                      U0[:, dc, j * 512:(j + 1) * 512],
                                      start=(dc == 0), stop=False,
                                      skip_group_check=True)
                for j in range(4):
                    mm = tensor.matmul(psQ[par][:, j * 512:(j + 1) * 512],
                                       onesr[:, :],
                                       rowq[:, j * 512:(j + 1) * 512],
                                       start=False, stop=True, skip_group_check=True)
                mm.then_inc(sPEq, 1)

            def t_group(u):
                par = u % 2
                for h in range(16):
                    if h == 0:
                        tensor.wait_ge(sDVEq, dqabs[(u, "sqhat0")])
                    elif h == 8:
                        tensor.wait_ge(sDVEq, dqabs[(u, "sqhat1")])
                    mm = tensor.matmul(psQ[par][:, h * 128:(h + 1) * 128],
                                       sqh[:, h * 128:(h + 1) * 128], I_bf[:, :],
                                       start=True, stop=True)
                mm.then_inc(sPEq, 1)

            def qkv_group(u):
                par = u % 2
                tensor.wait_ge(sDVEq, dqabs[(u, "E1")])
                for h in range(16):
                    mm = tensor.matmul(psQ[par][:, h * 128:(h + 1) * 128],
                                       kvb_v[:, h // 4, :],
                                       sqhT[:, h * 128:(h + 1) * 128],
                                       start=True, stop=True)
                mm.then_inc(sPEq, 1)

            for u in range(NTC_OWN):
                q_group(u)
                if u >= 1:
                    t_group(u - 1)
                    qkv_group(u - 1)
            t_group(NTC_OWN - 1)
            qkv_group(NTC_OWN - 1)

            tensor.wait_ge(sDVEq, DQ_TOTAL)

            def y1t_round(u, r):
                if r == 0:
                    tensor.wait_ge(sDVE2, d2abs[(u, "y1c")])
                    if u >= 1:
                        tensor.wait_ge(sDVE2, d2abs[(u - 1, "s2")])
                else:
                    if r >= 2:
                        tensor.wait_ge(sACT2, a2abs[(u, "ey1t%d" % (r - 2))])
                for k in range(4):
                    dc = 4 * r + k
                    base_col = (r % 2) * 512 + k * 128
                    mm = tensor.matmul(psB[:, base_col:base_col + 128],
                                       y1b[:, dc * 128:(dc + 1) * 128], I_bf[:, :],
                                       start=True, stop=True, skip_group_check=True)
                mm.then_inc(sPE2, 1)

            def misc_part(u, half):
                tensor.wait_ge(sACT2, a2abs[(u, "ey1t%d" % (half + 2))])
                for dc in range(8 * half, 8 * half + 8):
                    mm = tensor.matmul(psB[:, 0:136],
                                       sqhT[:, dc * 128:(dc + 1) * 128],
                                       ra_sb[:, dc, :],
                                       start=(dc == 0), stop=(dc == 15),
                                       skip_group_check=True)
                if half == 1:
                    mm.then_inc(sPE2, 1)

            def p2_iter(v):
                u = v - 2
                if v <= NTC_OWN - 1:
                    tensor.wait_ge(sSQRa if v % 2 == 0 else sSQRb, 16 * (v // 2 + 1))
                    if v >= 1:
                        tensor.wait_ge(sDVE2, d2abs[(v - 1, "s")])
                    if v == 0:
                        tensor.wait_ge(sWO, 16 * WO_CH)
                    sq = sqp[v % 2]
                    for h in range(16):
                        for j in range(4):
                            mm = tensor.matmul(psA[:, j * 512:(j + 1) * 512],
                                               sq[:, h * 128:(h + 1) * 128],
                                               U0[:, h, j * 512:(j + 1) * 512],
                                               start=(h == 0), stop=(h == 15),
                                               skip_group_check=True)
                        if u >= 0:
                            if h == 3:
                                y1t_round(u, 0)
                            elif h == 5:
                                y1t_round(u, 1)
                            elif h == 7:
                                y1t_round(u, 2)
                            elif h == 9:
                                y1t_round(u, 3)
                            elif h == 10:
                                misc_part(u, 0)
                            elif h == 12:
                                misc_part(u, 1)
                    mm.then_inc(sPE2, 1)
                else:
                    for r in range(4):
                        y1t_round(u, r)
                    misc_part(u, 0)
                    misc_part(u, 1)
                if u >= 0:
                    tensor.wait_ge(sDVE2, d2abs[(u, "hmask")])
                    tensor.matmul(psB[:, 512:640], hmask[:, :], I_bf[:, :],
                                  start=True, stop=True,
                                  skip_group_check=True).then_inc(sPE2, 1)
                    tensor.wait_ge(sDVE2, d2abs[(u, "lh")])
                    if u == 0:
                        tensor.wait_ge(sWB, 16 * (WB_CH + 1))
                    for j in range(4):
                        for dc in range(16):
                            tensor.matmul(psB[:, j * 512:(j + 1) * 512],
                                          sqhT[:, dc * 128:(dc + 1) * 128],
                                          wbase_sb[:, dc, j * 512:(j + 1) * 512],
                                          start=(dc == 0), stop=False,
                                          skip_group_check=True)
                        tensor.matmul(psB[:, j * 512:(j + 1) * 512], onesr[:, :],
                                      rowbb[:, j * 512:(j + 1) * 512],
                                      start=False, stop=False, skip_group_check=True)
                        if j == 0:
                            tensor.wait_ge(sACT2, a2abs[(u, "ehmt")])
                        mm = tensor.matmul(psB[:, j * 512:(j + 1) * 512], hmaskT[:, :],
                                           bcat_sb[:, j * 512:(j + 1) * 512],
                                           start=False, stop=True,
                                           skip_group_check=True)
                    mm.then_inc(sPE2, 1)

            for v in range(NTC_OWN + 2):
                p2_iter(v)

        @block.vector
        def _(vector):
            vector.memset(onesf[:, :], 1.0)
            vector.memset(onesb[:, :], 1.0)
            vector.memset(onesr[:, :], 1.0).then_inc(sStp, 1)

            dcnt = [0]

            def op(ins):
                ins.then_inc(sDVE1, 1)
                dcnt[0] += 1
                vector.wait_ge(sDVE1, dcnt[0])
                return ins

            for t in range(NTC_ALL):
                par = t % 2
                vector.wait_ge(sKVm, t + 1)
                if t >= 1:
                    vector.wait_ge(sACT1, t)
                op(vector.tensor_scalar(scr1[:, 0:512], psK[par], 0.0, None, ALU.min))
                vector.wait_ge(sACT1, t + 1)
                if t >= 2:
                    vector.wait_ge(sAcc, t - 1)
                op(vector.scalar_tensor_tensor(sk_t[par][:, :], psK[par], 0.0,
                                               scr2[:, 0:512], ALU.max, ALU.add))
                op(vector.tensor_copy(vb_t[par][:, :], psV[par]))

            vector.wait_ge(sAcc, NTC_ALL)
            vector.tensor_copy(kvb[:, :], psKV).then_inc(sFin, 1)
            vector.tensor_copy(ksr512[:, :], psKS).then_inc(sFin, 1)
            vector.wait_ge(sFin, 2)
            ks_in = ksr512[:, :].rearrange("a (h d) -> a h d", h=4).unsqueeze(2) \
                .broadcast_to([1, 4, 4, 128])
            vector.tensor_copy(ksrow.rearrange("a (h r d) -> a h r d", h=4, r=4),
                               ks_in).then_inc(sFin, 1)
            vector.wait_ge(sFin, 4)
            vector.tensor_copy(ksum_bc[:, :], psA[:, :]).then_inc(sFin, 1)

            dq = [0]

            def opq(ins):
                ins.then_inc(sDVEq, 1)
                dq[0] += 1
                vector.wait_ge(sDVEq, dq[0])
                return ins

            sq3 = sq_p1.rearrange("p (h d) -> p h d", h=16)
            prod3 = prodb.rearrange("p (h d) -> p h d", h=16)
            ks3 = ksum_bc[:, :].rearrange("p (h d) -> p h d", h=16)
            sqh3 = sqh[:, :].rearrange("p (h d) -> p h d", h=16)
            rn3 = rnbf[:, :].unsqueeze(2).broadcast_to([128, 16, 128])

            def e1e2(pw):
                vector.wait_ge(sPEq, peq_T[pw])
                opq(vector.tensor_copy(sqhT[:, :], psQ[pw % 2][:, :]))
                vector.wait_ge(sPEq, peq_qkv[pw])
                if pw >= 1:
                    vector.wait_ge(sSQW, 16 * pw)
                opq(vector.tensor_copy(sqp[pw % 2][:, :], psQ[pw % 2][:, :]))

            for u in range(NTC_OWN):
                par = u % 2
                vector.wait_ge(sPEq, peq_q[u])
                if u >= 1:
                    vector.wait_ge(sACTq, 2 * u)
                opq(vector.tensor_scalar(scr1[:, :], psQ[par][:, 0:1024], 0.0,
                                         None, ALU.min))
                if u >= 1:
                    e1e2(u - 1)
                vector.wait_ge(sACTq, 2 * u + 1)
                opq(vector.scalar_tensor_tensor(sq_p1[:, 0:1024], psQ[par][:, 0:1024],
                                                0.0, scr2[:, :], ALU.max, ALU.add))
                opq(vector.tensor_scalar(scr1[:, :], psQ[par][:, 1024:2048], 0.0,
                                         None, ALU.min))
                vector.wait_ge(sACTq, 2 * u + 2)
                opq(vector.scalar_tensor_tensor(sq_p1[:, 1024:2048],
                                                psQ[par][:, 1024:2048],
                                                0.0, scr2[:, :], ALU.max, ALU.add))
                opq(vector.tensor_tensor(prod3, sq3, ks3, ALU.mult))
                opq(vector.tensor_reduce(norm16[:, :], prod3, AX.X, ALU.add))
                opq(vector.tensor_scalar(normeps[:, :], norm16[:, :], ATT_EPS,
                                         None, ALU.add))
                opq(vector.reciprocal(rnorm[:, :], normeps[:, :]))
                opq(vector.tensor_copy(rnbf[:, :], rnorm[:, :]))
                if u >= 1:
                    vector.wait_ge(sPEq, peq_T[u - 1])
                opq(vector.tensor_tensor(sqh3[:, 0:8, :], sq3[:, 0:8, :],
                                         rn3[:, 0:8, :], ALU.mult))
                opq(vector.tensor_tensor(sqh3[:, 8:16, :], sq3[:, 8:16, :],
                                         rn3[:, 8:16, :], ALU.mult))
            e1e2(NTC_OWN - 1)

            dd = [0]

            def op2(ins):
                ins.then_inc(sDVE2, 1)
                dd[0] += 1
                vector.wait_ge(sDVE2, dd[0])
                return ins

            lh3 = lh[:, 8:136].rearrange("p (e r) -> p e r", e=8)
            hm3 = hmask[:, :].rearrange("p (e r) -> p e r", e=8)
            m8b = mask8[:, :].unsqueeze(2).broadcast_to([128, 8, 16])

            def seg_a1(u):
                vector.wait_ge(sPE2, pe2_mm3[u])
                vector.wait_ge(sX3[u % 3], 16 * (u // 3 + 1))
                op2(vector.scalar_tensor_tensor(fpA[:, :], psA[:, :], 1.0,
                                                xt_p2[u % 3][:, :], ALU.mult, ALU.add))

            def seg_a2(u):
                p = u % 2
                for g in range(4):
                    op2(vector.bn_stats(bnstA[:, g, :], fpA[:, g * 512:(g + 1) * 512]))
                op2(vector.bn_aggr(mvA[:, :], bnstA[:, :, :].rearrange("p a b -> p (a b)")))
                op2(vector.tensor_scalar(negmean[p][:, :], mvA[:, 0:1], -1.0, None, ALU.mult))
                op2(vector.tensor_scalar(veps[p][:, :], mvA[:, 1:2], LN_EPS, None, ALU.add))
                op2(vector.reciprocal(stdv[p][:, :], veps[p][:, :]))
                vector.wait_ge(sACT2, a2abs[(u, "sqrt1")])
                op2(vector.tensor_scalar(xt_p2[u % 3][:, :], fpA[:, :], negmean[p][:, :],
                                         rstd[p][:, :], ALU.add, ALU.mult))

            def seg_b1(u):
                vector.wait_ge(sPE2, pe2_misc[u])
                op2(vector.tensor_copy(lh[:, :], psB[:, 0:136]))
                op2(vector.reduce_max(rowmax[:, :], lh[:, 0:8], AX.X))
                op2(vector.tensor_scalar(mask8[:, :], lh[:, 0:8], rowmax[:, :],
                                         2.0, ALU.is_ge, ALU.mult))
                op2(vector.tensor_tensor(hm3, lh3, m8b, ALU.mult))

            def seg_b2a(u):
                p = 2 + u % 2
                vector.wait_ge(sPE2, pe2_base[u])
                op2(vector.scalar_tensor_tensor(fpA[:, :], psB[:, :], 1.0,
                                                xt_p2[u % 3][:, :], ALU.mult, ALU.add))
                for g in range(4):
                    op2(vector.bn_stats(bnstB[:, g, :], fpA[:, g * 512:(g + 1) * 512]))
                op2(vector.bn_aggr(mvB[:, :], bnstB[:, :, :].rearrange("p a b -> p (a b)")))
                op2(vector.tensor_scalar(negmean[p][:, :], mvB[:, 0:1], -1.0, None, ALU.mult))
                op2(vector.tensor_scalar(veps[p][:, :], mvB[:, 1:2], LN_EPS, None, ALU.add))
                op2(vector.reciprocal(stdv[p][:, :], veps[p][:, :]))

            def seg_yc(u):
                if u >= 1:
                    vector.wait_ge(sPE2, pe2_y1t[(u - 1, 3)])
                op2(vector.tensor_copy(y1b[:, :], xt_p2[u % 3][:, :]))

            def seg_b2y(u):
                p = 2 + u % 2
                vector.wait_ge(sACT2, a2abs[(u, "sqrt2")])
                if u >= 1:
                    vector.wait_ge(sOutS, 16 * u)
                op2(vector.tensor_scalar(fpB[:, :], fpA[:, :], negmean[p][:, :],
                                         rstd[p][:, :], ALU.add, ALU.mult))

            _dseg_fns = {"A1": seg_a1, "A2": seg_a2, "B1": seg_b1,
                         "B2a": seg_b2a, "B2y": seg_b2y, "YC": seg_yc}
            for kind, u in P2_ORDER:
                _dseg_fns[kind](u)

        @block.scalar
        def _(scalar):
            scalar.preload_act_table(AF.Exp)
            for t in range(NTC_ALL):
                scalar.wait_ge(sDVE1, 3 * t + 1)
                scalar.activation(scr2[:, 0:512], scr1[:, 0:512], AF.Exp).then_inc(sACT1, 1)
            for u in range(NTC_OWN):
                for j in range(2):
                    scalar.wait_ge(sDVEq, dqabs[(u, "min%d" % j)])
                    scalar.activation(scr2[:, :], scr1[:, :], AF.Exp).then_inc(sACTq, 1)
            scalar.preload_act_table(AF.Sqrt)
            for kind, u in P2_ACT:
                if kind == "Aa":
                    p = u % 2
                    scalar.wait_ge(sDVE2, d2abs[(u, "rv")])
                    scalar.activation(rstd[p][:, :], stdv[p][:, :], AF.Sqrt).then_inc(sACT2, 1)
                elif kind == "Ey":
                    for r in range(4):
                        scalar.wait_ge(sPE2, pe2_y1t[(u, r)])
                        scalar.copy(sqhT[:, r * 512:(r + 1) * 512],
                                    psB[:, (r % 2) * 512:(r % 2) * 512 + 512]).then_inc(sACT2, 1)
                elif kind == "Eh":
                    scalar.wait_ge(sPE2, pe2_hmt[u])
                    scalar.copy(hmaskT[:, :], psB[:, 512:640]).then_inc(sACT2, 1)
                else:
                    p = 2 + u % 2
                    scalar.wait_ge(sDVE2, d2abs[(u, "rv2")])
                    scalar.activation(rstd[p][:, :], stdv[p][:, :], AF.Sqrt).then_inc(sACT2, 1)

        @block.gpsimd
        def _(gpsimd):
            wq_src = wq_t.rearrange("(c p) n -> p c n", p=128)
            gpsimd.dma_start(out=wkv_sb, in_=wkv_t.rearrange("(c p) n -> p c n", p=128)
                             ).then_inc(sWLa, 16)
            gpsimd.dma_start(out=rowkv, in_=brows[:, 0:1024]).then_inc(sWLa, 16)
            gpsimd.wait_ge(sWLa, 32)
            wlb_small = [
                (rowq, brows[:, 1024:3072]),
                (I_bf[:, :], ident),
                (ra_sb[:, :, :], ra_t.rearrange("(c p) n -> p c n", p=128)),
                (bcat_sb[:, :], bcat_t),
            ]
            wlb_jobs = wlb_small + [
                (U0[:, 2 * i:2 * i + 2, :], wq_src[:, 2 * i:2 * i + 2, :])
                for i in range(8)]
            wlb_i = [0]

            def emit_wlb():
                if wlb_i[0] < len(wlb_jobs):
                    dst, s_ = wlb_jobs[wlb_i[0]]
                    gpsimd.dma_start(out=dst, in_=s_).then_inc(sWLb, 16)
                    wlb_i[0] += 1

            for s in range(NSLAB_ALL):
                if s >= 2:
                    gpsimd.wait_ge(sAcc, 2 * (s - 1))
                src = xt_full.rearrange("(c p) t -> p c t", p=128)[:, :, s * 256:(s + 1) * 256]
                gpsimd.dma_start(out=slab[s % 2], in_=src).then_inc(
                    sXTe if s % 2 == 0 else sXTo, 16)
                if 2 <= s < 6 or (s >= 8 and s % 2 == 0):
                    emit_wlb()
            while wlb_i[0] < len(wlb_jobs):
                emit_wlb()
            wo_src = wo_t.rearrange("(c p) n -> p c n", p=128)
            wb_src = wbase_t.rearrange("(c p) n -> p c n", p=128)
            for s in range(NSLAB_OWN):
                if s < 2:
                    gpsimd.wait_ge(sAcc, NTC_ALL)
                else:
                    gpsimd.wait_ge(sPEq, peq_q[2 * (s - 1)])
                g = NSLAB_ALL + s
                src = xt_own.rearrange("(c p) t -> p c t", p=128)[:, :, s * 256:(s + 1) * 256]
                gpsimd.dma_start(out=slab[g % 2], in_=src).then_inc(
                    sXTe if g % 2 == 0 else sXTo, 16)
                if 2 <= s < 6:
                    i = s - 2
                    if i == 0:
                        gpsimd.wait_ge(sKVm, NTC_ALL)
                    gpsimd.dma_start(out=wbase_sb[:, 2 * i:2 * i + 2, :],
                                     in_=wb_src[:, 2 * i:2 * i + 2, :]).then_inc(sWB, 16)
            gpsimd.wait_ge(sPEq, peq_q[NTC_OWN - 1])
            for i in range(WO_CH):
                gpsimd.dma_start(out=U0[:, 2 * i:2 * i + 2, :],
                                 in_=wo_src[:, 2 * i:2 * i + 2, :]).then_inc(sWO, 16)
            gpsimd.wait_ge(sDVEq, DQ_TOTAL)
            gpsimd.wait_ge(sACTq, 2 * NTC_OWN)
            gpsimd.wait_ge(sSQW, 16 * NTC_OWN)
            gpsimd.wait_ge(sPEq, peq_T[NTC_OWN - 1])
            for i in range(4, WB_CH):
                gpsimd.dma_start(out=wbase_sb[:, 2 * i:2 * i + 2, :],
                                 in_=wb_src[:, 2 * i:2 * i + 2, :]).then_inc(sWB, 16)
            gpsimd.dma_start(out=rowbb, in_=brows[:, 3072:5120]).then_inc(sWB, 16)

            def x_load_g(v):
                if v == 0:
                    gpsimd.wait_ge(sFin, 4)
                if v >= 3:
                    gpsimd.wait_ge(sDVE2, d2abs[(v - 3, "s2")])
                gpsimd.dma_start(out=xt_p2[v % 3][:, :],
                                 in_=x_own[v * 128:(v + 1) * 128, :]).then_inc(
                                     sX3[v % 3], 16)

            for v in range(NTC_OWN):
                x_load_g(v)

        @block.sync
        def _(sync):
            for u in range(NTC_OWN):
                sync.wait_ge(sDVEq, dqabs[(u, "E2")])
                sync.dma_start(out=attn_dram[u], in_=sqp[u % 2][:, :]).then_inc(sSQW, 16)
            def sq_load(u):
                if u < 2:
                    sync.wait_ge(sSQW, 16 * NTC_OWN)
                else:
                    sync.wait_ge(sPE2, pe2_mm3[u - 2])
                sync.dma_start(out=sqp[u % 2][:, :], in_=attn_dram[u]).then_inc(
                    sSQRa if u % 2 == 0 else sSQRb, 16)

            def y_store(u):
                sync.wait_ge(sDVE2, d2abs[(u, "y2")])
                sync.dma_start(out=y_out[u * 128:(u + 1) * 128, :],
                               in_=fpB[:, :]).then_inc(sOutS, 16)

            for u in range(NTC_OWN):
                sq_load(u)
                if u >= 5:
                    y_store(u - 5)
            for u in range(NTC_OWN - 5, NTC_OWN):
                y_store(u)
            sync.wait_ge(sOutS, 16 * NTC_OWN)


    return nc


def kernel(L0_attn_hidden_states, L0_attn_Wq, L0_attn_bq, L0_attn_Wk, L0_attn_bk,
           L0_attn_Wv, L0_attn_bv, L0_attn_Wo, L0_ffn_W_base, L0_ffn_b_base,
           L0_ffn_router_W, L0_ffn_A, L0_ffn_Bw):
    f32 = np.float32
    X = np.ascontiguousarray(np.asarray(L0_attn_hidden_states, dtype=f32))
    Wq = np.asarray(L0_attn_Wq, f32)
    Wk = np.asarray(L0_attn_Wk, f32)
    Wv = np.asarray(L0_attn_Wv, f32)
    Wo = np.asarray(L0_attn_Wo, f32)
    Wb = np.asarray(L0_ffn_W_base, f32)
    Rw = np.asarray(L0_ffn_router_W, f32)
    A = np.asarray(L0_ffn_A, f32)
    Bw = np.asarray(L0_ffn_Bw, f32)

    wkv_t = np.ascontiguousarray(np.concatenate([Wk, Wv], axis=0).T)
    wq_t = np.ascontiguousarray(Wq.T)
    wo_t = np.ascontiguousarray(Wo.T)
    wbase_t = np.ascontiguousarray(Wb.T)
    ra_t = np.ascontiguousarray(
        np.concatenate([Rw, A.reshape(128, D)], axis=0).T)
    bcat_t = np.ascontiguousarray(Bw.transpose(0, 2, 1).reshape(128, D))
    brows = np.concatenate([
        np.asarray(L0_attn_bk, f32), np.asarray(L0_attn_bv, f32),
        np.asarray(L0_attn_bq, f32), np.asarray(L0_ffn_b_base, f32)
    ]).reshape(1, 5120)
    ident = np.eye(128, dtype=f32)

    xt_b = [np.ascontiguousarray(X[b].T) for b in range(2)]

    nc = build_nc()
    in_maps = []
    for c in range(8):
        b, j = c // 4, c % 4
        in_maps.append(dict(
            xt_full=xt_b[b],
            xt_own=np.ascontiguousarray(xt_b[b][:, j * TOK:(j + 1) * TOK]),
            x_own=np.ascontiguousarray(X[b, j * TOK:(j + 1) * TOK, :]),
            wkv_t=wkv_t, wq_t=wq_t, wo_t=wo_t, wbase_t=wbase_t,
            ra_t=ra_t, bcat_t=bcat_t, brows=brows, ident=ident,
        ))
    res = run_bass_kernel_spmd(nc, in_maps, core_ids=list(range(8)))
    y = np.empty((2, SEQ, D), dtype=f32)
    for c in range(8):
        b, j = c // 4, c % 4
        y[b, j * TOK:(j + 1) * TOK, :] = res.results[c]["y"]
    return y


# version 15
# speedup vs baseline: 1.7300x; 1.1301x over previous
import numpy as np
from contextlib import ExitStack

import concourse.bass as bass
import concourse.mybir as mybir
from concourse.bass_utils import run_bass_kernel_spmd

F32 = mybir.dt.float32
BF16 = mybir.dt.bfloat16
ALU = mybir.AluOpType
AX = mybir.AxisListType
AF = mybir.ActivationFunctionType

D = 2048
SEQ = 8192
TOK = 2048
NTC_ALL = 64
NTC_OWN = 16
NSLAB_ALL = 32
NSLAB_OWN = 8
ATT_EPS = 1e-6
LN_EPS = 1e-5
WO_CH = 8
WB_CH = 8
N_P2W = WO_CH + WB_CH

peq_q, peq_T, peq_qkv = {}, {}, {}
_c = 0
for _u in range(NTC_OWN):
    _c += 1
    peq_q[_u] = _c
    if _u >= 1:
        _c += 1
        peq_T[_u - 1] = _c
        _c += 1
        peq_qkv[_u - 1] = _c
_c += 1
peq_T[NTC_OWN - 1] = _c
_c += 1
peq_qkv[NTC_OWN - 1] = _c

pe2_mm3 = {}
pe2_y1t = {}
pe2_misc, pe2_hmt, pe2_base = {}, {}, {}


def _plan_pe2():
    c = 0
    for v in range(NTC_OWN + 2):
        u = v - 2
        if v <= NTC_OWN - 1:
            if u >= 0:
                for r in range(4):
                    c += 1
                    pe2_y1t[(u, r)] = c
                c += 1
                pe2_misc[u] = c
            c += 1
            pe2_mm3[v] = c
        else:
            for r in range(4):
                c += 1
                pe2_y1t[(u, r)] = c
            c += 1
            pe2_misc[u] = c
        if u >= 0:
            c += 1
            pe2_hmt[u] = c
            c += 1
            pe2_base[u] = c


_plan_pe2()

dqabs = {}
_cq = 0
def _dq_seg(u, names):
    global _cq
    for n in names:
        _cq += 1
        dqabs[(u, n)] = _cq
for _u in range(NTC_OWN):
    _dq_seg(_u, ["min0"])
    if _u >= 1:
        _dq_seg(_u - 1, ["E1", "E2"])
    _dq_seg(_u, ["stt0", "min1", "stt1", "prod", "red", "eps", "recip", "cast",
                 "sqhat0", "sqhat1"])
_dq_seg(NTC_OWN - 1, ["E1", "E2"])
DQ_TOTAL = _cq

_DSEG = {
    "A1": ["s"],
    "A2": ["b0", "b1", "b2", "b3", "aggr", "negmean", "veps", "rv", "y1"],
    "B1": ["lh", "rowmax", "mask8", "hmask"],
    "B2a": ["s2", "c0", "c1", "c2", "c3", "aggr2", "negmean2", "veps2", "rv2"],
    "B2y": ["y2"],
    "YC": ["y1c"],
}
d2abs = {}
_c2 = 0
def _seg(u, ops):
    global _c2
    for name in ops:
        _c2 += 1
        d2abs[(u, name)] = _c2
def _p2_sched():
    order = []
    for v in range(NTC_OWN + 3):
        if 3 <= v <= NTC_OWN + 2:
            order.append(("B2a", v - 3))
        if 1 <= v <= NTC_OWN:
            order.append(("YC", v - 1))
        if 2 <= v <= NTC_OWN + 1:
            order.append(("B1", v - 2))
        if 3 <= v <= NTC_OWN + 2:
            order.append(("B2y", v - 3))
        if v <= NTC_OWN - 1:
            order.append(("A1", v))
            order.append(("A2", v))
    return order
P2_ORDER = _p2_sched()
for _kind, _u in P2_ORDER:
    _seg(_u, _DSEG[_kind])
def _p2_act_sched():
    order = []
    for v in range(NTC_OWN + 3):
        if 2 <= v <= NTC_OWN + 1:
            order.append(("Ey", v - 2))
        if 3 <= v <= NTC_OWN + 2:
            order.append(("Q2", v - 3))
        if 2 <= v <= NTC_OWN + 1:
            order.append(("Eh", v - 2))
        if v <= NTC_OWN - 1:
            order.append(("Aa", v))
    return order
P2_ACT = _p2_act_sched()
a2abs = {}
_c3 = 0
for _kind, _u in P2_ACT:
    if _kind == "Aa":
        _c3 += 1
        a2abs[(_u, "sqrt1")] = _c3
    elif _kind == "Ey":
        for _r in range(4):
            _c3 += 1
            a2abs[(_u, "ey1t%d" % _r)] = _c3
    elif _kind == "Eh":
        _c3 += 1
        a2abs[(_u, "ehmt")] = _c3
    else:
        _c3 += 1
        a2abs[(_u, "sqrt2")] = _c3

AQ_N = 2


def build_nc():
    nc = bass.Bass("TRN2", target_bir_lowering=False)

    xt_full = nc.dram_tensor("xt_full", [D, SEQ], F32, kind="ExternalInput").ap()
    xt_own = nc.dram_tensor("xt_own", [D, TOK], F32, kind="ExternalInput").ap()
    x_own = nc.dram_tensor("x_own", [TOK, D], F32, kind="ExternalInput").ap()
    wkv_t = nc.dram_tensor("wkv_t", [D, 1024], F32, kind="ExternalInput").ap()
    wq_t = nc.dram_tensor("wq_t", [D, D], F32, kind="ExternalInput").ap()
    wo_t = nc.dram_tensor("wo_t", [D, D], F32, kind="ExternalInput").ap()
    wbase_t = nc.dram_tensor("wbase_t", [D, D], F32, kind="ExternalInput").ap()
    ra_t = nc.dram_tensor("ra_t", [D, 136], F32, kind="ExternalInput").ap()
    bcat_t = nc.dram_tensor("bcat_t", [128, D], F32, kind="ExternalInput").ap()
    brows = nc.dram_tensor("brows", [1, 5120], F32, kind="ExternalInput").ap()
    ident = nc.dram_tensor("ident", [128, 128], F32, kind="ExternalInput").ap()
    y_out = nc.dram_tensor("y", [TOK, D], F32, kind="ExternalOutput").ap()
    attn_dram = nc.dram_tensor("attn_scratch", [NTC_OWN, 128, D], BF16, kind="Internal").ap()

    with ExitStack() as ctx:
        sb = lambda name, shape, dt: ctx.enter_context(nc.sbuf_tensor(name, shape, dt))
        sem = lambda name: ctx.enter_context(nc.semaphore(name))

        U0 = sb("u0", [128, 16, 2048], BF16)
        U1 = sb("u1", [128, 32768], BF16)
        wkv_sb = U1[:, 0:16384].rearrange("p (c n) -> p c n", c=16)
        slab = [U1[:, 16384:20480].rearrange("p (c n) -> p c n", c=16),
                U1[:, 20480:24576].rearrange("p (c n) -> p c n", c=16)]
        sk_t = [U1[:, 24576:25088], U1[:, 25088:25600]]
        vb_t = [U1[:, 25600:26112], U1[:, 26112:26624]]
        sq_p1 = U1[:, 26624:28672]
        scr1 = U1[:, 28672:30720].bitcast(F32)
        scr2 = U1[:, 30720:32768].bitcast(F32)
        wbase_sb = U1.rearrange("p (c n) -> p c n", c=16)

        fpA = sb("fpA", [128, D], F32)
        fpB = sb("fpB", [128, D], F32)
        prodb = fpB[:, :].bitcast(BF16)[:, 0:2048]
        kvb = sb("kvb", [128, 512], BF16)
        kvb_v = kvb[:, :].rearrange("p (h e) -> p h e", h=4)
        ksr512 = sb("ksr512", [1, 512], F32)
        ksum_bc = sb("ksum_bc", [128, D], BF16)
        I_bf = sb("I_bf", [128, 128], BF16)
        onesf = sb("onesf", [1, 128], F32)
        onesb = sb("onesb", [128, 1], BF16)
        onesr = sb("onesr", [1, 128], BF16)
        ra_sb = sb("ra_sb", [128, 16, 136], BF16)
        bcat_sb = sb("bcat_sb", [128, D], BF16)

        xt_p2 = [sb("xt_p2a", [128, D], F32), sb("xt_p2b", [128, D], F32),
                 sb("xt_p2c", [128, D], F32)]
        ksrow = xt_p2[0][0:1, :]
        sq_p2 = sb("sq_p2", [128, D], BF16)
        sq_p2b = sb("sq_p2b", [128, D], BF16)
        sqp = [sq_p2, sq_p2b]
        sqh = sb("sqh", [128, D], BF16)
        sqhT = sb("sqhT", [128, D], BF16)
        y1b = sb("y1b", [128, D], BF16)
        rowkv = sq_p2[0:1, 0:1024]
        rowq = y1b[0:1, 0:2048]
        rowbb = sqh[0:1, 0:2048]
        lh = sb("lh", [128, 136], F32)
        hmask = sb("hmask", [128, 128], BF16)
        hmaskT = sb("hmaskT", [128, 128], BF16)
        norm16 = sb("norm16", [128, 16], F32)
        normeps = sb("normeps", [128, 16], F32)
        rnorm = sb("rnorm", [128, 16], F32)
        rnbf = sb("rnbf", [128, 16], BF16)
        rowmax = sb("rowmax", [128, 1], F32)
        mask8 = sb("mask8", [128, 8], F32)
        negmean = [sb("negmean%d" % i, [128, 1], F32) for i in range(4)]
        veps = [sb("veps%d" % i, [128, 1], F32) for i in range(4)]
        stdv = [sb("stdv%d" % i, [128, 1], F32) for i in range(4)]
        rstd = [sb("rstd%d" % i, [128, 1], F32) for i in range(4)]
        bnstA = sb("bnstA", [128, 4, 6], F32)
        bnstB = sb("bnstB", [128, 4, 6], F32)
        mvA = sb("mvA", [128, 2], F32)
        mvB = sb("mvB", [128, 2], F32)

        psA = ctx.enter_context(nc.psum_tensor("psA", [128, 2048], F32))
        psB = ctx.enter_context(nc.psum_tensor("psB", [128, 2048], F32))
        psK = [psA[:, 0:512], psA[:, 512:1024]]
        psV = [psA[:, 1024:1536], psA[:, 1536:2048]]
        psKV = psB[:, 0:512]
        psKS = psB[0:1, 512:1024]
        psQ = [psA, psB]

        sWLa = sem("sWLa")
        sWLb = sem("sWLb")
        sXTe = sem("sXTe")
        sXTo = sem("sXTo")
        sWO = [sem("sWOa"), sem("sWOb")]
        sWB = sem("sWB")
        sStp = sem("sStp")
        sKVm = sem("sKVm")
        sAcc = sem("sAcc")
        sDVE1 = sem("sDVE1")
        sACT1 = sem("sACT1")
        sFin = sem("sFin")
        sPEq = sem("sPEq")
        sDVEq = sem("sDVEq")
        sACTq = sem("sACTq")
        sSQW = sem("sSQW")
        sXa = sem("sXa")
        sXb = sem("sXb")
        sXc = sem("sXc")
        sSQRa = sem("sSQRa")
        sSQRb = sem("sSQRb")
        sPE2 = sem("sPE2")
        sDVE2 = sem("sDVE2")
        sACT2 = sem("sACT2")
        sY1B = sem("sY1B")
        sOutS = sem("sOutS")

        sX3 = [sXa, sXb, sXc]
        block = ctx.enter_context(nc.Block())

        def slab_wait(eng, g):
            s = sXTe if g % 2 == 0 else sXTo
            eng.wait_ge(s, 16 * (g // 2 + 1))

        @block.tensor
        def _(tensor):
            tensor.wait_ge(sStp, 1)
            tensor.wait_ge(sWLa, 32)

            for t in range(NTC_ALL):
                par = t % 2
                s_idx = t // 2
                toff = (t - 2 * s_idx) * 128
                slab_wait(tensor, s_idx)
                if t >= 2:
                    tensor.wait_ge(sDVE1, 3 * (t - 1))
                for dc in range(16):
                    lhs = slab[s_idx % 2][:, dc, toff:toff + 128]
                    tensor.matmul(psK[par], lhs, wkv_sb[:, dc, 0:512],
                                  start=(dc == 0), stop=False,
                                  skip_group_check=True)
                    tensor.matmul(psV[par], lhs, wkv_sb[:, dc, 512:1024],
                                  start=(dc == 0), stop=False,
                                  skip_group_check=True)
                tensor.matmul(psK[par], onesr[:, :], rowkv[:, 0:512],
                              start=False, stop=True, skip_group_check=True)
                tensor.matmul(psV[par], onesr[:, :], rowkv[:, 512:1024],
                              start=False, stop=True, skip_group_check=True
                              ).then_inc(sKVm, 1)
                if t >= 1:
                    tp = t - 1
                    tensor.wait_ge(sDVE1, 3 * t)
                    for h4 in range(4):
                        tensor.matmul(psKV[:, h4 * 128:(h4 + 1) * 128],
                                      sk_t[tp % 2][:, h4 * 128:(h4 + 1) * 128],
                                      vb_t[tp % 2][:, h4 * 128:(h4 + 1) * 128],
                                      start=(tp == 0 and h4 == 0),
                                      stop=(tp == NTC_ALL - 1 and h4 == 3),
                                      skip_group_check=True)
                    tensor.matmul(psKS, onesb[:, :], sk_t[tp % 2][:, :],
                                  start=(tp == 0), stop=(tp == NTC_ALL - 1),
                                  skip_group_check=True).then_inc(sAcc, 1)
            tp = NTC_ALL - 1
            tensor.wait_ge(sDVE1, 3 * NTC_ALL)
            for h4 in range(4):
                tensor.matmul(psKV[:, h4 * 128:(h4 + 1) * 128],
                              sk_t[tp % 2][:, h4 * 128:(h4 + 1) * 128],
                              vb_t[tp % 2][:, h4 * 128:(h4 + 1) * 128],
                              start=False, stop=(h4 == 3), skip_group_check=True)
            tensor.matmul(psKS, onesb[:, :], sk_t[tp % 2][:, :],
                          start=False, stop=True, skip_group_check=True
                          ).then_inc(sAcc, 1)

            tensor.wait_ge(sFin, 3)
            for j in range(4):
                mm = tensor.matmul(psA[:, j * 512:(j + 1) * 512], onesf[:, :],
                                   ksrow[:, j * 512:(j + 1) * 512], start=True, stop=True)
            mm.then_inc(sFin, 1)

            tensor.wait_ge(sWLb, 16 * 12)
            tensor.wait_ge(sFin, 5)

            def q_group(u):
                par = u % 2
                s_idx = u // 2
                toff = (u - 2 * s_idx) * 128
                slab_wait(tensor, NSLAB_ALL + s_idx)
                if u >= 2:
                    tensor.wait_ge(sDVEq, dqabs[(u - 2, "E2")])
                for dc in range(16):
                    lhs = slab[s_idx % 2][:, dc, toff:toff + 128]
                    for j in range(4):
                        tensor.matmul(psQ[par][:, j * 512:(j + 1) * 512], lhs,
                                      U0[:, dc, j * 512:(j + 1) * 512],
                                      start=(dc == 0), stop=False,
                                      skip_group_check=True)
                for j in range(4):
                    mm = tensor.matmul(psQ[par][:, j * 512:(j + 1) * 512],
                                       onesr[:, :],
                                       rowq[:, j * 512:(j + 1) * 512],
                                       start=False, stop=True, skip_group_check=True)
                mm.then_inc(sPEq, 1)

            def t_group(u):
                par = u % 2
                for h in range(16):
                    if h == 0:
                        tensor.wait_ge(sDVEq, dqabs[(u, "sqhat0")])
                    elif h == 8:
                        tensor.wait_ge(sDVEq, dqabs[(u, "sqhat1")])
                    mm = tensor.matmul(psQ[par][:, h * 128:(h + 1) * 128],
                                       sqh[:, h * 128:(h + 1) * 128], I_bf[:, :],
                                       start=True, stop=True)
                mm.then_inc(sPEq, 1)

            def qkv_group(u):
                par = u % 2
                tensor.wait_ge(sDVEq, dqabs[(u, "E1")])
                for h in range(16):
                    mm = tensor.matmul(psQ[par][:, h * 128:(h + 1) * 128],
                                       kvb_v[:, h // 4, :],
                                       sqhT[:, h * 128:(h + 1) * 128],
                                       start=True, stop=True)
                mm.then_inc(sPEq, 1)

            for u in range(NTC_OWN):
                q_group(u)
                if u >= 1:
                    t_group(u - 1)
                    qkv_group(u - 1)
            t_group(NTC_OWN - 1)
            qkv_group(NTC_OWN - 1)

            tensor.wait_ge(sDVEq, DQ_TOTAL)

            def y1t_round(u, r):
                if r == 0:
                    tensor.wait_ge(sDVE2, d2abs[(u, "y1c")])
                    if u >= 1:
                        tensor.wait_ge(sDVE2, d2abs[(u - 1, "s2")])
                else:
                    if r >= 2:
                        tensor.wait_ge(sACT2, a2abs[(u, "ey1t%d" % (r - 2))])
                for k in range(4):
                    dc = 4 * r + k
                    base_col = (r % 2) * 512 + k * 128
                    mm = tensor.matmul(psB[:, base_col:base_col + 128],
                                       y1b[:, dc * 128:(dc + 1) * 128], I_bf[:, :],
                                       start=True, stop=True, skip_group_check=True)
                mm.then_inc(sPE2, 1)

            def misc_part(u, half):
                tensor.wait_ge(sACT2, a2abs[(u, "ey1t%d" % (half + 2))])
                for dc in range(8 * half, 8 * half + 8):
                    mm = tensor.matmul(psB[:, 0:136],
                                       sqhT[:, dc * 128:(dc + 1) * 128],
                                       ra_sb[:, dc, :],
                                       start=(dc == 0), stop=(dc == 15),
                                       skip_group_check=True)
                if half == 1:
                    mm.then_inc(sPE2, 1)

            def p2_iter(v):
                u = v - 2
                if v <= NTC_OWN - 1:
                    tensor.wait_ge(sSQRa if v % 2 == 0 else sSQRb, 16 * (v // 2 + 1))
                    if v >= 1:
                        tensor.wait_ge(sDVE2, d2abs[(v - 1, "s")])
                    sq = sqp[v % 2]
                    for h in range(16):
                        if v == 0 and h % 2 == 0:
                            i = h // 2
                            tensor.wait_ge(sWO[i % 2], 16 * (i // 2 + 1))
                        elif v == 1 and h == 0:
                            tensor.wait_ge(sWO[0], 16 * (WO_CH // 2))
                            tensor.wait_ge(sWO[1], 16 * (WO_CH // 2))
                        for j in range(4):
                            mm = tensor.matmul(psA[:, j * 512:(j + 1) * 512],
                                               sq[:, h * 128:(h + 1) * 128],
                                               U0[:, h, j * 512:(j + 1) * 512],
                                               start=(h == 0), stop=(h == 15),
                                               skip_group_check=True)
                        if u >= 0:
                            if h == 3:
                                y1t_round(u, 0)
                            elif h == 5:
                                y1t_round(u, 1)
                            elif h == 7:
                                y1t_round(u, 2)
                            elif h == 9:
                                y1t_round(u, 3)
                            elif h == 10:
                                misc_part(u, 0)
                            elif h == 12:
                                misc_part(u, 1)
                    mm.then_inc(sPE2, 1)
                else:
                    for r in range(4):
                        y1t_round(u, r)
                    misc_part(u, 0)
                    misc_part(u, 1)
                if u >= 0:
                    tensor.wait_ge(sDVE2, d2abs[(u, "hmask")])
                    tensor.matmul(psB[:, 512:640], hmask[:, :], I_bf[:, :],
                                  start=True, stop=True,
                                  skip_group_check=True).then_inc(sPE2, 1)
                    tensor.wait_ge(sDVE2, d2abs[(u, "lh")])
                    if u == 0:
                        tensor.wait_ge(sWB, 16 * (WB_CH + 1))
                    for j in range(4):
                        for dc in range(16):
                            tensor.matmul(psB[:, j * 512:(j + 1) * 512],
                                          sqhT[:, dc * 128:(dc + 1) * 128],
                                          wbase_sb[:, dc, j * 512:(j + 1) * 512],
                                          start=(dc == 0), stop=False,
                                          skip_group_check=True)
                        tensor.matmul(psB[:, j * 512:(j + 1) * 512], onesr[:, :],
                                      rowbb[:, j * 512:(j + 1) * 512],
                                      start=False, stop=False, skip_group_check=True)
                        if j == 0:
                            tensor.wait_ge(sACT2, a2abs[(u, "ehmt")])
                        mm = tensor.matmul(psB[:, j * 512:(j + 1) * 512], hmaskT[:, :],
                                           bcat_sb[:, j * 512:(j + 1) * 512],
                                           start=False, stop=True,
                                           skip_group_check=True)
                    mm.then_inc(sPE2, 1)

            for v in range(NTC_OWN + 2):
                p2_iter(v)

        @block.vector
        def _(vector):
            vector.memset(onesf[:, :], 1.0)
            vector.memset(onesb[:, :], 1.0)
            vector.memset(onesr[:, :], 1.0).then_inc(sStp, 1)

            dcnt = [0]

            def op(ins):
                ins.then_inc(sDVE1, 1)
                dcnt[0] += 1
                vector.wait_ge(sDVE1, dcnt[0])
                return ins

            for t in range(NTC_ALL):
                par = t % 2
                vector.wait_ge(sKVm, t + 1)
                if t >= 1:
                    vector.wait_ge(sACT1, t)
                op(vector.tensor_scalar(scr1[:, 0:512], psK[par], 0.0, None, ALU.min))
                vector.wait_ge(sACT1, t + 1)
                if t >= 2:
                    vector.wait_ge(sAcc, t - 1)
                op(vector.scalar_tensor_tensor(sk_t[par][:, :], psK[par], 0.0,
                                               scr2[:, 0:512], ALU.max, ALU.add))
                op(vector.tensor_copy(vb_t[par][:, :], psV[par]))

            vector.wait_ge(sAcc, NTC_ALL)
            vector.tensor_copy(kvb[:, :], psKV).then_inc(sFin, 1)
            vector.tensor_copy(ksr512[:, :], psKS).then_inc(sFin, 1)
            vector.wait_ge(sFin, 2)
            ks_in = ksr512[:, :].rearrange("a (h d) -> a h d", h=4).unsqueeze(2) \
                .broadcast_to([1, 4, 4, 128])
            vector.tensor_copy(ksrow.rearrange("a (h r d) -> a h r d", h=4, r=4),
                               ks_in).then_inc(sFin, 1)
            vector.wait_ge(sFin, 4)
            vector.tensor_copy(ksum_bc[:, :], psA[:, :]).then_inc(sFin, 1)

            dq = [0]

            def opq(ins):
                ins.then_inc(sDVEq, 1)
                dq[0] += 1
                vector.wait_ge(sDVEq, dq[0])
                return ins

            sq3 = sq_p1.rearrange("p (h d) -> p h d", h=16)
            prod3 = prodb.rearrange("p (h d) -> p h d", h=16)
            ks3 = ksum_bc[:, :].rearrange("p (h d) -> p h d", h=16)
            sqh3 = sqh[:, :].rearrange("p (h d) -> p h d", h=16)
            rn3 = rnbf[:, :].unsqueeze(2).broadcast_to([128, 16, 128])

            def e1e2(pw):
                vector.wait_ge(sPEq, peq_T[pw])
                opq(vector.tensor_copy(sqhT[:, :], psQ[pw % 2][:, :]))
                vector.wait_ge(sPEq, peq_qkv[pw])
                if pw >= 1:
                    vector.wait_ge(sSQW, 16 * pw)
                opq(vector.tensor_copy(sqp[pw % 2][:, :], psQ[pw % 2][:, :]))

            for u in range(NTC_OWN):
                par = u % 2
                vector.wait_ge(sPEq, peq_q[u])
                if u >= 1:
                    vector.wait_ge(sACTq, 2 * u)
                opq(vector.tensor_scalar(scr1[:, :], psQ[par][:, 0:1024], 0.0,
                                         None, ALU.min))
                if u >= 1:
                    e1e2(u - 1)
                vector.wait_ge(sACTq, 2 * u + 1)
                opq(vector.scalar_tensor_tensor(sq_p1[:, 0:1024], psQ[par][:, 0:1024],
                                                0.0, scr2[:, :], ALU.max, ALU.add))
                opq(vector.tensor_scalar(scr1[:, :], psQ[par][:, 1024:2048], 0.0,
                                         None, ALU.min))
                vector.wait_ge(sACTq, 2 * u + 2)
                opq(vector.scalar_tensor_tensor(sq_p1[:, 1024:2048],
                                                psQ[par][:, 1024:2048],
                                                0.0, scr2[:, :], ALU.max, ALU.add))
                opq(vector.tensor_tensor(prod3, sq3, ks3, ALU.mult))
                opq(vector.tensor_reduce(norm16[:, :], prod3, AX.X, ALU.add))
                opq(vector.tensor_scalar(normeps[:, :], norm16[:, :], ATT_EPS,
                                         None, ALU.add))
                opq(vector.reciprocal(rnorm[:, :], normeps[:, :]))
                opq(vector.tensor_copy(rnbf[:, :], rnorm[:, :]))
                if u >= 1:
                    vector.wait_ge(sPEq, peq_T[u - 1])
                opq(vector.tensor_tensor(sqh3[:, 0:8, :], sq3[:, 0:8, :],
                                         rn3[:, 0:8, :], ALU.mult))
                opq(vector.tensor_tensor(sqh3[:, 8:16, :], sq3[:, 8:16, :],
                                         rn3[:, 8:16, :], ALU.mult))
            e1e2(NTC_OWN - 1)

            dd = [0]

            def op2(ins):
                ins.then_inc(sDVE2, 1)
                dd[0] += 1
                vector.wait_ge(sDVE2, dd[0])
                return ins

            lh3 = lh[:, 8:136].rearrange("p (e r) -> p e r", e=8)
            hm3 = hmask[:, :].rearrange("p (e r) -> p e r", e=8)
            m8b = mask8[:, :].unsqueeze(2).broadcast_to([128, 8, 16])

            def seg_a1(u):
                vector.wait_ge(sPE2, pe2_mm3[u])
                vector.wait_ge(sX3[u % 3], 16 * (u // 3 + 1))
                op2(vector.scalar_tensor_tensor(fpA[:, :], psA[:, :], 1.0,
                                                xt_p2[u % 3][:, :], ALU.mult, ALU.add))

            def seg_a2(u):
                p = u % 2
                for g in range(4):
                    op2(vector.bn_stats(bnstA[:, g, :], fpA[:, g * 512:(g + 1) * 512]))
                op2(vector.bn_aggr(mvA[:, :], bnstA[:, :, :].rearrange("p a b -> p (a b)")))
                op2(vector.tensor_scalar(negmean[p][:, :], mvA[:, 0:1], -1.0, None, ALU.mult))
                op2(vector.tensor_scalar(veps[p][:, :], mvA[:, 1:2], LN_EPS, None, ALU.add))
                op2(vector.reciprocal(stdv[p][:, :], veps[p][:, :]))
                vector.wait_ge(sACT2, a2abs[(u, "sqrt1")])
                op2(vector.tensor_scalar(xt_p2[u % 3][:, :], fpA[:, :], negmean[p][:, :],
                                         rstd[p][:, :], ALU.add, ALU.mult))

            def seg_b1(u):
                vector.wait_ge(sPE2, pe2_misc[u])
                op2(vector.tensor_copy(lh[:, :], psB[:, 0:136]))
                op2(vector.reduce_max(rowmax[:, :], lh[:, 0:8], AX.X))
                op2(vector.tensor_scalar(mask8[:, :], lh[:, 0:8], rowmax[:, :],
                                         2.0, ALU.is_ge, ALU.mult))
                op2(vector.tensor_tensor(hm3, lh3, m8b, ALU.mult))

            def seg_b2a(u):
                p = 2 + u % 2
                vector.wait_ge(sPE2, pe2_base[u])
                op2(vector.scalar_tensor_tensor(fpA[:, :], psB[:, :], 1.0,
                                                xt_p2[u % 3][:, :], ALU.mult, ALU.add))
                for g in range(4):
                    op2(vector.bn_stats(bnstB[:, g, :], fpA[:, g * 512:(g + 1) * 512]))
                op2(vector.bn_aggr(mvB[:, :], bnstB[:, :, :].rearrange("p a b -> p (a b)")))
                op2(vector.tensor_scalar(negmean[p][:, :], mvB[:, 0:1], -1.0, None, ALU.mult))
                op2(vector.tensor_scalar(veps[p][:, :], mvB[:, 1:2], LN_EPS, None, ALU.add))
                op2(vector.reciprocal(stdv[p][:, :], veps[p][:, :]))

            def seg_yc(u):
                if u >= 1:
                    vector.wait_ge(sPE2, pe2_y1t[(u - 1, 3)])
                op2(vector.tensor_copy(y1b[:, :], xt_p2[u % 3][:, :]))

            def seg_b2y(u):
                p = 2 + u % 2
                vector.wait_ge(sACT2, a2abs[(u, "sqrt2")])
                if u >= 1:
                    vector.wait_ge(sOutS, 16 * u)
                op2(vector.tensor_scalar(fpB[:, :], fpA[:, :], negmean[p][:, :],
                                         rstd[p][:, :], ALU.add, ALU.mult))

            _dseg_fns = {"A1": seg_a1, "A2": seg_a2, "B1": seg_b1,
                         "B2a": seg_b2a, "B2y": seg_b2y, "YC": seg_yc}
            for kind, u in P2_ORDER:
                _dseg_fns[kind](u)

        @block.scalar
        def _(scalar):
            scalar.preload_act_table(AF.Exp)
            for t in range(NTC_ALL):
                scalar.wait_ge(sDVE1, 3 * t + 1)
                scalar.activation(scr2[:, 0:512], scr1[:, 0:512], AF.Exp).then_inc(sACT1, 1)
            for u in range(NTC_OWN):
                for j in range(2):
                    scalar.wait_ge(sDVEq, dqabs[(u, "min%d" % j)])
                    scalar.activation(scr2[:, :], scr1[:, :], AF.Exp).then_inc(sACTq, 1)
            scalar.preload_act_table(AF.Sqrt)

            def x_load(v):
                if v == 0:
                    scalar.wait_ge(sFin, 4)
                scalar.dma_start(out=xt_p2[v % 3][:, :],
                                 in_=x_own[v * 128:(v + 1) * 128, :]).then_inc(
                                     sX3[v % 3], 16)

            x_load(0)
            x_load(1)
            x_load(2)
            for kind, u in P2_ACT:
                if kind == "Aa":
                    p = u % 2
                    scalar.wait_ge(sDVE2, d2abs[(u, "rv")])
                    scalar.activation(rstd[p][:, :], stdv[p][:, :], AF.Sqrt).then_inc(sACT2, 1)
                elif kind == "Ey":
                    for r in range(4):
                        scalar.wait_ge(sPE2, pe2_y1t[(u, r)])
                        scalar.copy(sqhT[:, r * 512:(r + 1) * 512],
                                    psB[:, (r % 2) * 512:(r % 2) * 512 + 512]).then_inc(sACT2, 1)
                elif kind == "Eh":
                    scalar.wait_ge(sPE2, pe2_hmt[u])
                    scalar.copy(hmaskT[:, :], psB[:, 512:640]).then_inc(sACT2, 1)
                else:
                    p = 2 + u % 2
                    scalar.wait_ge(sDVE2, d2abs[(u, "rv2")])
                    scalar.activation(rstd[p][:, :], stdv[p][:, :], AF.Sqrt).then_inc(sACT2, 1)
                    if u + 3 < NTC_OWN:
                        x_load(u + 3)

        @block.gpsimd
        def _(gpsimd):
            wq_src = wq_t.rearrange("(c p) n -> p c n", p=128)
            gpsimd.dma_start(out=wkv_sb, in_=wkv_t.rearrange("(c p) n -> p c n", p=128)
                             ).then_inc(sWLa, 16)
            gpsimd.dma_start(out=rowkv, in_=brows[:, 0:1024]).then_inc(sWLa, 16)
            gpsimd.wait_ge(sWLa, 32)
            wlb_small = [
                (rowq, brows[:, 1024:3072]),
                (I_bf[:, :], ident),
                (ra_sb[:, :, :], ra_t.rearrange("(c p) n -> p c n", p=128)),
                (bcat_sb[:, :], bcat_t),
            ]
            wlb_jobs = wlb_small + [
                (U0[:, 2 * i:2 * i + 2, :], wq_src[:, 2 * i:2 * i + 2, :])
                for i in range(8)]
            wlb_i = [0]

            def emit_wlb():
                if wlb_i[0] < len(wlb_jobs):
                    dst, s_ = wlb_jobs[wlb_i[0]]
                    gpsimd.dma_start(out=dst, in_=s_).then_inc(sWLb, 16)
                    wlb_i[0] += 1

            for s in range(NSLAB_ALL):
                if s >= 2:
                    gpsimd.wait_ge(sKVm, 2 * (s - 1))
                src = xt_full.rearrange("(c p) t -> p c t", p=128)[:, :, s * 256:(s + 1) * 256]
                gpsimd.dma_start(out=slab[s % 2], in_=src).then_inc(
                    sXTe if s % 2 == 0 else sXTo, 16)
                if 2 <= s < 6 or (s >= 8 and s % 2 == 0):
                    emit_wlb()
            while wlb_i[0] < len(wlb_jobs):
                emit_wlb()
            wo_src = wo_t.rearrange("(c p) n -> p c n", p=128)
            wb_src = wbase_t.rearrange("(c p) n -> p c n", p=128)
            for s in range(NSLAB_OWN):
                if s < 2:
                    gpsimd.wait_ge(sAcc, NTC_ALL)
                else:
                    gpsimd.wait_ge(sPEq, peq_q[2 * (s - 1)])
                g = NSLAB_ALL + s
                src = xt_own.rearrange("(c p) t -> p c t", p=128)[:, :, s * 256:(s + 1) * 256]
                gpsimd.dma_start(out=slab[g % 2], in_=src).then_inc(
                    sXTe if g % 2 == 0 else sXTo, 16)
                if 2 <= s < 6:
                    i = s - 2
                    if i == 0:
                        gpsimd.wait_ge(sKVm, NTC_ALL)
                    gpsimd.dma_start(out=wbase_sb[:, 2 * i:2 * i + 2, :],
                                     in_=wb_src[:, 2 * i:2 * i + 2, :]).then_inc(sWB, 16)
            gpsimd.wait_ge(sPEq, peq_q[NTC_OWN - 1])
            for i in range(WO_CH):
                if i >= 2:
                    gpsimd.wait_ge(sWO[i % 2], 16 * (i // 2))
                gpsimd.dma_start(out=U0[:, 2 * i:2 * i + 2, :],
                                 in_=wo_src[:, 2 * i:2 * i + 2, :]).then_inc(sWO[i % 2], 16)
            gpsimd.wait_ge(sDVEq, DQ_TOTAL)
            gpsimd.wait_ge(sACTq, 2 * NTC_OWN)
            gpsimd.wait_ge(sSQW, 16 * NTC_OWN)
            gpsimd.wait_ge(sPEq, peq_T[NTC_OWN - 1])
            for i in range(4, WB_CH):
                gpsimd.dma_start(out=wbase_sb[:, 2 * i:2 * i + 2, :],
                                 in_=wb_src[:, 2 * i:2 * i + 2, :]).then_inc(sWB, 16)
            gpsimd.dma_start(out=rowbb, in_=brows[:, 3072:5120]).then_inc(sWB, 16)


        @block.sync
        def _(sync):
            for u in range(NTC_OWN):
                sync.wait_ge(sDVEq, dqabs[(u, "E2")])
                sync.dma_start(out=attn_dram[u], in_=sqp[u % 2][:, :]).then_inc(sSQW, 16)
            def sq_load(u):
                if u < 2:
                    sync.wait_ge(sSQW, 16 * NTC_OWN)
                else:
                    sync.wait_ge(sPE2, pe2_mm3[u - 2])
                sync.dma_start(out=sqp[u % 2][:, :], in_=attn_dram[u]).then_inc(
                    sSQRa if u % 2 == 0 else sSQRb, 16)

            def y_store(u):
                sync.wait_ge(sDVE2, d2abs[(u, "y2")])
                sync.dma_start(out=y_out[u * 128:(u + 1) * 128, :],
                               in_=fpB[:, :]).then_inc(sOutS, 16)

            for u in range(NTC_OWN):
                sq_load(u)
                if u >= 5:
                    y_store(u - 5)
            for u in range(NTC_OWN - 5, NTC_OWN):
                y_store(u)
            sync.wait_ge(sOutS, 16 * NTC_OWN)


    return nc


def kernel(L0_attn_hidden_states, L0_attn_Wq, L0_attn_bq, L0_attn_Wk, L0_attn_bk,
           L0_attn_Wv, L0_attn_bv, L0_attn_Wo, L0_ffn_W_base, L0_ffn_b_base,
           L0_ffn_router_W, L0_ffn_A, L0_ffn_Bw):
    f32 = np.float32
    X = np.ascontiguousarray(np.asarray(L0_attn_hidden_states, dtype=f32))
    Wq = np.asarray(L0_attn_Wq, f32)
    Wk = np.asarray(L0_attn_Wk, f32)
    Wv = np.asarray(L0_attn_Wv, f32)
    Wo = np.asarray(L0_attn_Wo, f32)
    Wb = np.asarray(L0_ffn_W_base, f32)
    Rw = np.asarray(L0_ffn_router_W, f32)
    A = np.asarray(L0_ffn_A, f32)
    Bw = np.asarray(L0_ffn_Bw, f32)

    wkv_t = np.ascontiguousarray(np.concatenate([Wk, Wv], axis=0).T)
    wq_t = np.ascontiguousarray(Wq.T)
    wo_t = np.ascontiguousarray(Wo.T)
    wbase_t = np.ascontiguousarray(Wb.T)
    ra_t = np.ascontiguousarray(
        np.concatenate([Rw, A.reshape(128, D)], axis=0).T)
    bcat_t = np.ascontiguousarray(Bw.transpose(0, 2, 1).reshape(128, D))
    brows = np.concatenate([
        np.asarray(L0_attn_bk, f32), np.asarray(L0_attn_bv, f32),
        np.asarray(L0_attn_bq, f32), np.asarray(L0_ffn_b_base, f32)
    ]).reshape(1, 5120)
    ident = np.eye(128, dtype=f32)

    xt_b = [np.ascontiguousarray(X[b].T) for b in range(2)]

    nc = build_nc()
    in_maps = []
    for c in range(8):
        b, j = c // 4, c % 4
        in_maps.append(dict(
            xt_full=xt_b[b],
            xt_own=np.ascontiguousarray(xt_b[b][:, j * TOK:(j + 1) * TOK]),
            x_own=np.ascontiguousarray(X[b, j * TOK:(j + 1) * TOK, :]),
            wkv_t=wkv_t, wq_t=wq_t, wo_t=wo_t, wbase_t=wbase_t,
            ra_t=ra_t, bcat_t=bcat_t, brows=brows, ident=ident,
        ))
    res = run_bass_kernel_spmd(nc, in_maps, core_ids=list(range(8)))
    y = np.empty((2, SEQ, D), dtype=f32)
    for c in range(8):
        b, j = c // 4, c % 4
        y[b, j * TOK:(j + 1) * TOK, :] = res.results[c]["y"]
    return y


# version 16
# speedup vs baseline: 1.7846x; 1.0315x over previous
import numpy as np
from contextlib import ExitStack

import concourse.bass as bass
import concourse.mybir as mybir
from concourse.bass_utils import run_bass_kernel_spmd

F32 = mybir.dt.float32
BF16 = mybir.dt.bfloat16
ALU = mybir.AluOpType
AX = mybir.AxisListType
AF = mybir.ActivationFunctionType

D = 2048
SEQ = 8192
TOK = 2048
NTC_ALL = 64
NTC_OWN = 16
NSLAB_ALL = 32
NSLAB_OWN = 8
ATT_EPS = 1e-6
LN_EPS = 1e-5
WO_CH = 8
WB_CH = 8
N_P2W = WO_CH + WB_CH

peq_q, peq_qh, peq_T7, peq_T15, peq_qkvq = {}, {}, {}, {}, {}
_c = 0
for _u in range(NTC_OWN):
    _c += 1
    peq_qh[_u] = _c
    if _u >= 1:
        _c += 1
        peq_T7[_u - 1] = _c
        _c += 1
        peq_T15[_u - 1] = _c
    _c += 1
    peq_q[_u] = _c
    if _u >= 1:
        for _i in range(4):
            _c += 1
            peq_qkvq[(_u - 1, _i)] = _c
_c += 1
peq_T7[NTC_OWN - 1] = _c
_c += 1
peq_T15[NTC_OWN - 1] = _c
for _i in range(4):
    _c += 1
    peq_qkvq[(NTC_OWN - 1, _i)] = _c

pe2_mm3 = {}
pe2_y1t = {}
pe2_misc, pe2_hmt, pe2_base = {}, {}, {}


def _plan_pe2():
    c = 0
    for v in range(NTC_OWN + 2):
        u = v - 2
        if v <= NTC_OWN - 1:
            if u >= 0:
                for r in range(4):
                    c += 1
                    pe2_y1t[(u, r)] = c
                c += 1
                pe2_misc[u] = c
            c += 1
            pe2_mm3[v] = c
        else:
            for r in range(4):
                c += 1
                pe2_y1t[(u, r)] = c
            c += 1
            pe2_misc[u] = c
        if u >= 0:
            c += 1
            pe2_hmt[u] = c
            c += 1
            pe2_base[u] = c


_plan_pe2()

dqabs = {}
_cq = 0
def _dq_seg(u, names):
    global _cq
    for n in names:
        _cq += 1
        dqabs[(u, n)] = _cq
for _u in range(NTC_OWN):
    _dq_seg(_u, ["min0", "stt0"])
    if _u >= 1:
        _dq_seg(_u - 1, ["E1a", "E1b"])
    _dq_seg(_u, ["min1"])
    if _u >= 1:
        _dq_seg(_u - 1, ["E2c0", "E2c1", "E2c2", "E2c3"])
    _dq_seg(_u, ["stt1", "prod", "red", "eps", "recip", "cast",
                 "sqhat0", "sqhat1"])
_dq_seg(NTC_OWN - 1, ["E1a", "E1b", "E2c0", "E2c1", "E2c2", "E2c3"])
DQ_TOTAL = _cq

_DSEG = {
    "A1": ["s"],
    "A2": ["b0", "b1", "b2", "b3", "aggr", "negmean", "veps", "rv", "y1"],
    "B1": ["lh", "rowmax", "mask8", "hmask"],
    "B2a": ["s2", "c0", "c1", "c2", "c3", "aggr2", "negmean2", "veps2", "rv2"],
    "B2y": ["y2"],
    "YC": ["y1c"],
}
d2abs = {}
_c2 = 0
def _seg(u, ops):
    global _c2
    for name in ops:
        _c2 += 1
        d2abs[(u, name)] = _c2
def _p2_sched():
    order = []
    for v in range(NTC_OWN + 3):
        if 3 <= v <= NTC_OWN + 2:
            order.append(("B2a", v - 3))
        if 1 <= v <= NTC_OWN:
            order.append(("YC", v - 1))
        if 2 <= v <= NTC_OWN + 1:
            order.append(("B1", v - 2))
        if 3 <= v <= NTC_OWN + 2:
            order.append(("B2y", v - 3))
        if v <= NTC_OWN - 1:
            order.append(("A1", v))
            order.append(("A2", v))
    return order
P2_ORDER = _p2_sched()
for _kind, _u in P2_ORDER:
    _seg(_u, _DSEG[_kind])
def _p2_act_sched():
    order = []
    for v in range(NTC_OWN + 3):
        if 2 <= v <= NTC_OWN + 1:
            order.append(("Ey", v - 2))
        if 3 <= v <= NTC_OWN + 2:
            order.append(("Q2", v - 3))
        if 2 <= v <= NTC_OWN + 1:
            order.append(("Eh", v - 2))
        if v <= NTC_OWN - 1:
            order.append(("Aa", v))
    return order
P2_ACT = _p2_act_sched()
a2abs = {}
_c3 = 0
for _kind, _u in P2_ACT:
    if _kind == "Aa":
        _c3 += 1
        a2abs[(_u, "sqrt1")] = _c3
    elif _kind == "Ey":
        for _r in range(4):
            _c3 += 1
            a2abs[(_u, "ey1t%d" % _r)] = _c3
    elif _kind == "Eh":
        _c3 += 1
        a2abs[(_u, "ehmt")] = _c3
    else:
        _c3 += 1
        a2abs[(_u, "sqrt2")] = _c3

AQ_N = 2


def build_nc():
    nc = bass.Bass("TRN2", target_bir_lowering=False)

    xt_full = nc.dram_tensor("xt_full", [D, SEQ], F32, kind="ExternalInput").ap()
    xt_own = nc.dram_tensor("xt_own", [D, TOK], F32, kind="ExternalInput").ap()
    x_own = nc.dram_tensor("x_own", [TOK, D], F32, kind="ExternalInput").ap()
    wkv_t = nc.dram_tensor("wkv_t", [D, 1024], F32, kind="ExternalInput").ap()
    wq_t = nc.dram_tensor("wq_t", [D, D], F32, kind="ExternalInput").ap()
    wo_t = nc.dram_tensor("wo_t", [D, D], F32, kind="ExternalInput").ap()
    wbase_t = nc.dram_tensor("wbase_t", [D, D], F32, kind="ExternalInput").ap()
    ra_t = nc.dram_tensor("ra_t", [D, 136], F32, kind="ExternalInput").ap()
    bcat_t = nc.dram_tensor("bcat_t", [128, D], F32, kind="ExternalInput").ap()
    brows = nc.dram_tensor("brows", [1, 5120], F32, kind="ExternalInput").ap()
    ident = nc.dram_tensor("ident", [128, 128], F32, kind="ExternalInput").ap()
    y_out = nc.dram_tensor("y", [TOK, D], F32, kind="ExternalOutput").ap()
    attn_dram = nc.dram_tensor("attn_scratch", [NTC_OWN, 128, D], BF16, kind="Internal").ap()

    with ExitStack() as ctx:
        sb = lambda name, shape, dt: ctx.enter_context(nc.sbuf_tensor(name, shape, dt))
        sem = lambda name: ctx.enter_context(nc.semaphore(name))

        U0 = sb("u0", [128, 16, 2048], BF16)
        U1 = sb("u1", [128, 32768], BF16)
        wkv_sb = U1[:, 0:16384].rearrange("p (c n) -> p c n", c=16)
        slab = [U1[:, 16384:20480].rearrange("p (c n) -> p c n", c=16),
                U1[:, 20480:24576].rearrange("p (c n) -> p c n", c=16)]
        sk_t = [U1[:, 24576:25088], U1[:, 25088:25600]]
        vb_t = [U1[:, 25600:26112], U1[:, 26112:26624]]
        sq_p1 = U1[:, 26624:28672]
        scr1 = U1[:, 28672:30720].bitcast(F32)
        scr2 = U1[:, 30720:32768].bitcast(F32)
        wbase_sb = U1.rearrange("p (c n) -> p c n", c=16)

        fpA = sb("fpA", [128, D], F32)
        fpB = sb("fpB", [128, D], F32)
        prodb = fpB[:, :].bitcast(BF16)[:, 0:2048]
        kvb = sb("kvb", [128, 512], BF16)
        kvb_v = kvb[:, :].rearrange("p (h e) -> p h e", h=4)
        ksr512 = sb("ksr512", [1, 512], F32)
        ksum_bc = sb("ksum_bc", [128, D], BF16)
        I_bf = sb("I_bf", [128, 128], BF16)
        onesf = sb("onesf", [1, 128], F32)
        onesb = sb("onesb", [128, 1], BF16)
        onesr = sb("onesr", [1, 128], BF16)
        ra_sb = sb("ra_sb", [128, 16, 136], BF16)
        bcat_sb = sb("bcat_sb", [128, D], BF16)

        xt_p2 = [sb("xt_p2a", [128, D], F32), sb("xt_p2b", [128, D], F32),
                 sb("xt_p2c", [128, D], F32)]
        ksrow = xt_p2[0][0:1, :]
        sq_p2 = sb("sq_p2", [128, D], BF16)
        sq_p2b = sb("sq_p2b", [128, D], BF16)
        sqp = [sq_p2, sq_p2b]
        sqh = sb("sqh", [128, D], BF16)
        sqhT = sb("sqhT", [128, D], BF16)
        y1b = sb("y1b", [128, D], BF16)
        rowkv = sq_p2[0:1, 0:1024]
        rowq = y1b[0:1, 0:2048]
        rowbb = sqh[0:1, 0:2048]
        lh = sb("lh", [128, 136], F32)
        hmask = sb("hmask", [128, 128], BF16)
        hmaskT = sb("hmaskT", [128, 128], BF16)
        norm16 = sb("norm16", [128, 16], F32)
        normeps = sb("normeps", [128, 16], F32)
        rnorm = sb("rnorm", [128, 16], F32)
        rnbf = sb("rnbf", [128, 16], BF16)
        rowmax = sb("rowmax", [128, 1], F32)
        mask8 = sb("mask8", [128, 8], F32)
        negmean = [sb("negmean%d" % i, [128, 1], F32) for i in range(4)]
        veps = [sb("veps%d" % i, [128, 1], F32) for i in range(4)]
        stdv = [sb("stdv%d" % i, [128, 1], F32) for i in range(4)]
        rstd = [sb("rstd%d" % i, [128, 1], F32) for i in range(4)]
        bnstA = sb("bnstA", [128, 4, 6], F32)
        bnstB = sb("bnstB", [128, 4, 6], F32)
        mvA = sb("mvA", [128, 2], F32)
        mvB = sb("mvB", [128, 2], F32)

        psA = ctx.enter_context(nc.psum_tensor("psA", [128, 2048], F32))
        psB = ctx.enter_context(nc.psum_tensor("psB", [128, 2048], F32))
        psK = [psA[:, 0:512], psA[:, 512:1024]]
        psV = [psA[:, 1024:1536], psA[:, 1536:2048]]
        psKV = psB[:, 0:512]
        psKS = psB[0:1, 512:1024]
        psQ = [psA, psB]

        sWLa = sem("sWLa")
        sWLb = sem("sWLb")
        sXTe = sem("sXTe")
        sXTo = sem("sXTo")
        sWO = [sem("sWOa"), sem("sWOb")]
        sWB = sem("sWB")
        sStp = sem("sStp")
        sKVm = sem("sKVm")
        sAcc = sem("sAcc")
        sDVE1 = sem("sDVE1")
        sACT1 = sem("sACT1")
        sFin = sem("sFin")
        sPEq = sem("sPEq")
        sDVEq = sem("sDVEq")
        sACTq = sem("sACTq")
        sSQW = sem("sSQW")
        sXa = sem("sXa")
        sXb = sem("sXb")
        sXc = sem("sXc")
        sSQRa = sem("sSQRa")
        sSQRb = sem("sSQRb")
        sPE2 = sem("sPE2")
        sDVE2 = sem("sDVE2")
        sACT2 = sem("sACT2")
        sY1B = sem("sY1B")
        sOutS = sem("sOutS")

        sX3 = [sXa, sXb, sXc]
        block = ctx.enter_context(nc.Block())

        def slab_wait(eng, g):
            s = sXTe if g % 2 == 0 else sXTo
            eng.wait_ge(s, 16 * (g // 2 + 1))

        @block.tensor
        def _(tensor):
            tensor.wait_ge(sStp, 1)
            tensor.wait_ge(sWLa, 32)

            for t in range(NTC_ALL):
                par = t % 2
                s_idx = t // 2
                toff = (t - 2 * s_idx) * 128
                slab_wait(tensor, s_idx)
                if t >= 2:
                    tensor.wait_ge(sDVE1, 3 * (t - 1))
                for dc in range(16):
                    lhs = slab[s_idx % 2][:, dc, toff:toff + 128]
                    tensor.matmul(psK[par], lhs, wkv_sb[:, dc, 0:512],
                                  start=(dc == 0), stop=False,
                                  skip_group_check=True)
                    tensor.matmul(psV[par], lhs, wkv_sb[:, dc, 512:1024],
                                  start=(dc == 0), stop=False,
                                  skip_group_check=True)
                tensor.matmul(psK[par], onesr[:, :], rowkv[:, 0:512],
                              start=False, stop=True, skip_group_check=True)
                tensor.matmul(psV[par], onesr[:, :], rowkv[:, 512:1024],
                              start=False, stop=True, skip_group_check=True
                              ).then_inc(sKVm, 1)
                if t >= 1:
                    tp = t - 1
                    tensor.wait_ge(sDVE1, 3 * t)
                    for h4 in range(4):
                        tensor.matmul(psKV[:, h4 * 128:(h4 + 1) * 128],
                                      sk_t[tp % 2][:, h4 * 128:(h4 + 1) * 128],
                                      vb_t[tp % 2][:, h4 * 128:(h4 + 1) * 128],
                                      start=(tp == 0 and h4 == 0),
                                      stop=(tp == NTC_ALL - 1 and h4 == 3),
                                      skip_group_check=True)
                    tensor.matmul(psKS, onesb[:, :], sk_t[tp % 2][:, :],
                                  start=(tp == 0), stop=(tp == NTC_ALL - 1),
                                  skip_group_check=True).then_inc(sAcc, 1)
            tp = NTC_ALL - 1
            tensor.wait_ge(sDVE1, 3 * NTC_ALL)
            for h4 in range(4):
                tensor.matmul(psKV[:, h4 * 128:(h4 + 1) * 128],
                              sk_t[tp % 2][:, h4 * 128:(h4 + 1) * 128],
                              vb_t[tp % 2][:, h4 * 128:(h4 + 1) * 128],
                              start=False, stop=(h4 == 3), skip_group_check=True)
            tensor.matmul(psKS, onesb[:, :], sk_t[tp % 2][:, :],
                          start=False, stop=True, skip_group_check=True
                          ).then_inc(sAcc, 1)

            tensor.wait_ge(sFin, 3)
            for j in range(4):
                mm = tensor.matmul(psA[:, j * 512:(j + 1) * 512], onesf[:, :],
                                   ksrow[:, j * 512:(j + 1) * 512], start=True, stop=True)
            mm.then_inc(sFin, 1)

            tensor.wait_ge(sWLb, 16 * 12)
            tensor.wait_ge(sFin, 5)

            def t_group(u):
                par = u % 2
                for h in range(16):
                    if h == 0:
                        tensor.wait_ge(sDVEq, dqabs[(u, "sqhat0")])
                    elif h == 8:
                        tensor.wait_ge(sDVEq, dqabs[(u, "sqhat1")])
                    mm = tensor.matmul(psQ[par][:, h * 128:(h + 1) * 128],
                                       sqh[:, h * 128:(h + 1) * 128], I_bf[:, :],
                                       start=True, stop=True, skip_group_check=True)
                    if h == 7 or h == 15:
                        mm.then_inc(sPEq, 1)

            def qkv_group(u):
                par = u % 2
                for i in range(4):
                    if i == 0:
                        tensor.wait_ge(sDVEq, dqabs[(u, "E1a")])
                    elif i == 2:
                        tensor.wait_ge(sDVEq, dqabs[(u, "E1b")])
                    for h in range(4 * i, 4 * i + 4):
                        mm = tensor.matmul(psQ[par][:, h * 128:(h + 1) * 128],
                                           kvb_v[:, h // 4, :],
                                           sqhT[:, h * 128:(h + 1) * 128],
                                           start=True, stop=True,
                                           skip_group_check=True)
                    mm.then_inc(sPEq, 1)

            for u in range(NTC_OWN):
                par = u % 2
                s_idx = u // 2
                toff = (u - 2 * s_idx) * 128
                slab_wait(tensor, NSLAB_ALL + s_idx)
                for j in range(4):
                    if j == 3 and u >= 1:
                        t_group(u - 1)
                    if u >= 2:
                        tensor.wait_ge(sDVEq, dqabs[(u - 2, "E2c%d" % j)])
                    for dc in range(16):
                        lhs = slab[s_idx % 2][:, dc, toff:toff + 128]
                        tensor.matmul(psQ[par][:, j * 512:(j + 1) * 512], lhs,
                                      U0[:, dc, j * 512:(j + 1) * 512],
                                      start=(dc == 0), stop=False,
                                      skip_group_check=True)
                    mm = tensor.matmul(psQ[par][:, j * 512:(j + 1) * 512],
                                       onesr[:, :], rowq[:, j * 512:(j + 1) * 512],
                                       start=False, stop=True, skip_group_check=True)
                    if j == 1 or j == 3:
                        mm.then_inc(sPEq, 1)
                if u >= 1:
                    qkv_group(u - 1)
            t_group(NTC_OWN - 1)
            qkv_group(NTC_OWN - 1)

            tensor.wait_ge(sDVEq, DQ_TOTAL)

            def y1t_round(u, r):
                if r == 0:
                    tensor.wait_ge(sDVE2, d2abs[(u, "y1c")])
                    if u >= 1:
                        tensor.wait_ge(sDVE2, d2abs[(u - 1, "s2")])
                else:
                    if r >= 2:
                        tensor.wait_ge(sACT2, a2abs[(u, "ey1t%d" % (r - 2))])
                for k in range(4):
                    dc = 4 * r + k
                    base_col = (r % 2) * 512 + k * 128
                    mm = tensor.matmul(psB[:, base_col:base_col + 128],
                                       y1b[:, dc * 128:(dc + 1) * 128], I_bf[:, :],
                                       start=True, stop=True, skip_group_check=True)
                mm.then_inc(sPE2, 1)

            def misc_part(u, half):
                tensor.wait_ge(sACT2, a2abs[(u, "ey1t%d" % (half + 2))])
                for dc in range(8 * half, 8 * half + 8):
                    mm = tensor.matmul(psB[:, 0:136],
                                       sqhT[:, dc * 128:(dc + 1) * 128],
                                       ra_sb[:, dc, :],
                                       start=(dc == 0), stop=(dc == 15),
                                       skip_group_check=True)
                if half == 1:
                    mm.then_inc(sPE2, 1)

            def p2_iter(v):
                u = v - 2
                if v <= NTC_OWN - 1:
                    tensor.wait_ge(sSQRa if v % 2 == 0 else sSQRb, 16 * (v // 2 + 1))
                    if v >= 1:
                        tensor.wait_ge(sDVE2, d2abs[(v - 1, "s")])
                    sq = sqp[v % 2]
                    for h in range(16):
                        if v == 0 and h % 2 == 0:
                            i = h // 2
                            tensor.wait_ge(sWO[i % 2], 16 * (i // 2 + 1))
                        elif v == 1 and h == 0:
                            tensor.wait_ge(sWO[0], 16 * (WO_CH // 2))
                            tensor.wait_ge(sWO[1], 16 * (WO_CH // 2))
                        for j in range(4):
                            mm = tensor.matmul(psA[:, j * 512:(j + 1) * 512],
                                               sq[:, h * 128:(h + 1) * 128],
                                               U0[:, h, j * 512:(j + 1) * 512],
                                               start=(h == 0), stop=(h == 15),
                                               skip_group_check=True)
                        if u >= 0:
                            if h == 3:
                                y1t_round(u, 0)
                            elif h == 5:
                                y1t_round(u, 1)
                            elif h == 7:
                                y1t_round(u, 2)
                            elif h == 9:
                                y1t_round(u, 3)
                            elif h == 10:
                                misc_part(u, 0)
                            elif h == 12:
                                misc_part(u, 1)
                    mm.then_inc(sPE2, 1)
                else:
                    for r in range(4):
                        y1t_round(u, r)
                    misc_part(u, 0)
                    misc_part(u, 1)
                if u >= 0:
                    tensor.wait_ge(sDVE2, d2abs[(u, "hmask")])
                    tensor.matmul(psB[:, 512:640], hmask[:, :], I_bf[:, :],
                                  start=True, stop=True,
                                  skip_group_check=True).then_inc(sPE2, 1)
                    tensor.wait_ge(sDVE2, d2abs[(u, "lh")])
                    if u == 0:
                        tensor.wait_ge(sWB, 16 * (WB_CH + 1))
                    for j in range(4):
                        for dc in range(16):
                            tensor.matmul(psB[:, j * 512:(j + 1) * 512],
                                          sqhT[:, dc * 128:(dc + 1) * 128],
                                          wbase_sb[:, dc, j * 512:(j + 1) * 512],
                                          start=(dc == 0), stop=False,
                                          skip_group_check=True)
                        tensor.matmul(psB[:, j * 512:(j + 1) * 512], onesr[:, :],
                                      rowbb[:, j * 512:(j + 1) * 512],
                                      start=False, stop=False, skip_group_check=True)
                        if j == 0:
                            tensor.wait_ge(sACT2, a2abs[(u, "ehmt")])
                        mm = tensor.matmul(psB[:, j * 512:(j + 1) * 512], hmaskT[:, :],
                                           bcat_sb[:, j * 512:(j + 1) * 512],
                                           start=False, stop=True,
                                           skip_group_check=True)
                    mm.then_inc(sPE2, 1)

            for v in range(NTC_OWN + 2):
                p2_iter(v)

        @block.vector
        def _(vector):
            vector.memset(onesf[:, :], 1.0)
            vector.memset(onesb[:, :], 1.0)
            vector.memset(onesr[:, :], 1.0).then_inc(sStp, 1)

            dcnt = [0]

            def op(ins):
                ins.then_inc(sDVE1, 1)
                dcnt[0] += 1
                vector.wait_ge(sDVE1, dcnt[0])
                return ins

            for t in range(NTC_ALL):
                par = t % 2
                vector.wait_ge(sKVm, t + 1)
                if t >= 1:
                    vector.wait_ge(sACT1, t)
                op(vector.tensor_scalar(scr1[:, 0:512], psK[par], 0.0, None, ALU.min))
                vector.wait_ge(sACT1, t + 1)
                if t >= 2:
                    vector.wait_ge(sAcc, t - 1)
                op(vector.scalar_tensor_tensor(sk_t[par][:, :], psK[par], 0.0,
                                               scr2[:, 0:512], ALU.max, ALU.add))
                op(vector.tensor_copy(vb_t[par][:, :], psV[par]))

            vector.wait_ge(sAcc, NTC_ALL)
            vector.tensor_copy(kvb[:, :], psKV).then_inc(sFin, 1)
            vector.tensor_copy(ksr512[:, :], psKS).then_inc(sFin, 1)
            vector.wait_ge(sFin, 2)
            ks_in = ksr512[:, :].rearrange("a (h d) -> a h d", h=4).unsqueeze(2) \
                .broadcast_to([1, 4, 4, 128])
            vector.tensor_copy(ksrow.rearrange("a (h r d) -> a h r d", h=4, r=4),
                               ks_in).then_inc(sFin, 1)
            vector.wait_ge(sFin, 4)
            vector.tensor_copy(ksum_bc[:, :], psA[:, :]).then_inc(sFin, 1)

            dq = [0]

            def opq(ins):
                ins.then_inc(sDVEq, 1)
                dq[0] += 1
                vector.wait_ge(sDVEq, dq[0])
                return ins

            sq3 = sq_p1.rearrange("p (h d) -> p h d", h=16)
            prod3 = prodb.rearrange("p (h d) -> p h d", h=16)
            ks3 = ksum_bc[:, :].rearrange("p (h d) -> p h d", h=16)
            sqh3 = sqh[:, :].rearrange("p (h d) -> p h d", h=16)
            rn3 = rnbf[:, :].unsqueeze(2).broadcast_to([128, 16, 128])

            def e1seg(pw):
                vector.wait_ge(sPEq, peq_T7[pw])
                opq(vector.tensor_copy(sqhT[:, 0:1024], psQ[pw % 2][:, 0:1024]))
                vector.wait_ge(sPEq, peq_T15[pw])
                opq(vector.tensor_copy(sqhT[:, 1024:2048], psQ[pw % 2][:, 1024:2048]))

            def e2seg(pw):
                for i in range(4):
                    vector.wait_ge(sPEq, peq_qkvq[(pw, i)])
                    if i == 0 and pw >= 1:
                        vector.wait_ge(sSQW, 16 * pw)
                    opq(vector.tensor_copy(sqp[pw % 2][:, i * 512:(i + 1) * 512],
                                           psQ[pw % 2][:, i * 512:(i + 1) * 512]))

            for u in range(NTC_OWN):
                par = u % 2
                vector.wait_ge(sPEq, peq_qh[u])
                if u >= 1:
                    vector.wait_ge(sACTq, 2 * u)
                opq(vector.tensor_scalar(scr1[:, :], psQ[par][:, 0:1024], 0.0,
                                         None, ALU.min))
                vector.wait_ge(sACTq, 2 * u + 1)
                opq(vector.scalar_tensor_tensor(sq_p1[:, 0:1024], psQ[par][:, 0:1024],
                                                0.0, scr2[:, :], ALU.max, ALU.add))
                if u >= 1:
                    e1seg(u - 1)
                vector.wait_ge(sPEq, peq_q[u])
                opq(vector.tensor_scalar(scr1[:, :], psQ[par][:, 1024:2048], 0.0,
                                         None, ALU.min))
                if u >= 1:
                    e2seg(u - 1)
                vector.wait_ge(sACTq, 2 * u + 2)
                opq(vector.scalar_tensor_tensor(sq_p1[:, 1024:2048],
                                                psQ[par][:, 1024:2048],
                                                0.0, scr2[:, :], ALU.max, ALU.add))
                opq(vector.tensor_tensor(prod3, sq3, ks3, ALU.mult))
                opq(vector.tensor_reduce(norm16[:, :], prod3, AX.X, ALU.add))
                opq(vector.tensor_scalar(normeps[:, :], norm16[:, :], ATT_EPS,
                                         None, ALU.add))
                opq(vector.reciprocal(rnorm[:, :], normeps[:, :]))
                opq(vector.tensor_copy(rnbf[:, :], rnorm[:, :]))
                opq(vector.tensor_tensor(sqh3[:, 0:8, :], sq3[:, 0:8, :],
                                         rn3[:, 0:8, :], ALU.mult))
                opq(vector.tensor_tensor(sqh3[:, 8:16, :], sq3[:, 8:16, :],
                                         rn3[:, 8:16, :], ALU.mult))
            e1seg(NTC_OWN - 1)
            e2seg(NTC_OWN - 1)

            dd = [0]

            def op2(ins):
                ins.then_inc(sDVE2, 1)
                dd[0] += 1
                vector.wait_ge(sDVE2, dd[0])
                return ins

            lh3 = lh[:, 8:136].rearrange("p (e r) -> p e r", e=8)
            hm3 = hmask[:, :].rearrange("p (e r) -> p e r", e=8)
            m8b = mask8[:, :].unsqueeze(2).broadcast_to([128, 8, 16])

            def seg_a1(u):
                vector.wait_ge(sPE2, pe2_mm3[u])
                vector.wait_ge(sX3[u % 3], 16 * (u // 3 + 1))
                op2(vector.scalar_tensor_tensor(fpA[:, :], psA[:, :], 1.0,
                                                xt_p2[u % 3][:, :], ALU.mult, ALU.add))

            def seg_a2(u):
                p = u % 2
                for g in range(4):
                    op2(vector.bn_stats(bnstA[:, g, :], fpA[:, g * 512:(g + 1) * 512]))
                op2(vector.bn_aggr(mvA[:, :], bnstA[:, :, :].rearrange("p a b -> p (a b)")))
                op2(vector.tensor_scalar(negmean[p][:, :], mvA[:, 0:1], -1.0, None, ALU.mult))
                op2(vector.tensor_scalar(veps[p][:, :], mvA[:, 1:2], LN_EPS, None, ALU.add))
                op2(vector.reciprocal(stdv[p][:, :], veps[p][:, :]))
                vector.wait_ge(sACT2, a2abs[(u, "sqrt1")])
                op2(vector.tensor_scalar(xt_p2[u % 3][:, :], fpA[:, :], negmean[p][:, :],
                                         rstd[p][:, :], ALU.add, ALU.mult))

            def seg_b1(u):
                vector.wait_ge(sPE2, pe2_misc[u])
                op2(vector.tensor_copy(lh[:, :], psB[:, 0:136]))
                op2(vector.reduce_max(rowmax[:, :], lh[:, 0:8], AX.X))
                op2(vector.tensor_scalar(mask8[:, :], lh[:, 0:8], rowmax[:, :],
                                         2.0, ALU.is_ge, ALU.mult))
                op2(vector.tensor_tensor(hm3, lh3, m8b, ALU.mult))

            def seg_b2a(u):
                p = 2 + u % 2
                vector.wait_ge(sPE2, pe2_base[u])
                op2(vector.scalar_tensor_tensor(fpA[:, :], psB[:, :], 1.0,
                                                xt_p2[u % 3][:, :], ALU.mult, ALU.add))
                for g in range(4):
                    op2(vector.bn_stats(bnstB[:, g, :], fpA[:, g * 512:(g + 1) * 512]))
                op2(vector.bn_aggr(mvB[:, :], bnstB[:, :, :].rearrange("p a b -> p (a b)")))
                op2(vector.tensor_scalar(negmean[p][:, :], mvB[:, 0:1], -1.0, None, ALU.mult))
                op2(vector.tensor_scalar(veps[p][:, :], mvB[:, 1:2], LN_EPS, None, ALU.add))
                op2(vector.reciprocal(stdv[p][:, :], veps[p][:, :]))

            def seg_yc(u):
                if u >= 1:
                    vector.wait_ge(sPE2, pe2_y1t[(u - 1, 3)])
                op2(vector.tensor_copy(y1b[:, :], xt_p2[u % 3][:, :]))

            def seg_b2y(u):
                p = 2 + u % 2
                vector.wait_ge(sACT2, a2abs[(u, "sqrt2")])
                if u >= 1:
                    vector.wait_ge(sOutS, 16 * u)
                op2(vector.tensor_scalar(fpB[:, :], fpA[:, :], negmean[p][:, :],
                                         rstd[p][:, :], ALU.add, ALU.mult))

            _dseg_fns = {"A1": seg_a1, "A2": seg_a2, "B1": seg_b1,
                         "B2a": seg_b2a, "B2y": seg_b2y, "YC": seg_yc}
            for kind, u in P2_ORDER:
                _dseg_fns[kind](u)

        @block.scalar
        def _(scalar):
            scalar.preload_act_table(AF.Exp)
            for t in range(NTC_ALL):
                scalar.wait_ge(sDVE1, 3 * t + 1)
                scalar.activation(scr2[:, 0:512], scr1[:, 0:512], AF.Exp).then_inc(sACT1, 1)
            for u in range(NTC_OWN):
                for j in range(2):
                    scalar.wait_ge(sDVEq, dqabs[(u, "min%d" % j)])
                    scalar.activation(scr2[:, :], scr1[:, :], AF.Exp).then_inc(sACTq, 1)
            scalar.preload_act_table(AF.Sqrt)

            def x_load(v):
                if v == 0:
                    scalar.wait_ge(sFin, 4)
                scalar.dma_start(out=xt_p2[v % 3][:, :],
                                 in_=x_own[v * 128:(v + 1) * 128, :]).then_inc(
                                     sX3[v % 3], 16)

            x_load(0)
            x_load(1)
            x_load(2)
            for kind, u in P2_ACT:
                if kind == "Aa":
                    p = u % 2
                    scalar.wait_ge(sDVE2, d2abs[(u, "rv")])
                    scalar.activation(rstd[p][:, :], stdv[p][:, :], AF.Sqrt).then_inc(sACT2, 1)
                elif kind == "Ey":
                    for r in range(4):
                        scalar.wait_ge(sPE2, pe2_y1t[(u, r)])
                        scalar.copy(sqhT[:, r * 512:(r + 1) * 512],
                                    psB[:, (r % 2) * 512:(r % 2) * 512 + 512]).then_inc(sACT2, 1)
                elif kind == "Eh":
                    scalar.wait_ge(sPE2, pe2_hmt[u])
                    scalar.copy(hmaskT[:, :], psB[:, 512:640]).then_inc(sACT2, 1)
                else:
                    p = 2 + u % 2
                    scalar.wait_ge(sDVE2, d2abs[(u, "rv2")])
                    scalar.activation(rstd[p][:, :], stdv[p][:, :], AF.Sqrt).then_inc(sACT2, 1)
                    if u + 3 < NTC_OWN:
                        x_load(u + 3)

        @block.gpsimd
        def _(gpsimd):
            wq_src = wq_t.rearrange("(c p) n -> p c n", p=128)
            gpsimd.dma_start(out=wkv_sb, in_=wkv_t.rearrange("(c p) n -> p c n", p=128)
                             ).then_inc(sWLa, 16)
            gpsimd.dma_start(out=rowkv, in_=brows[:, 0:1024]).then_inc(sWLa, 16)
            gpsimd.wait_ge(sWLa, 32)
            wlb_small = [
                (rowq, brows[:, 1024:3072]),
                (I_bf[:, :], ident),
                (ra_sb[:, :, :], ra_t.rearrange("(c p) n -> p c n", p=128)),
                (bcat_sb[:, :], bcat_t),
            ]
            wlb_jobs = wlb_small + [
                (U0[:, 2 * i:2 * i + 2, :], wq_src[:, 2 * i:2 * i + 2, :])
                for i in range(8)]
            wlb_i = [0]

            def emit_wlb():
                if wlb_i[0] < len(wlb_jobs):
                    dst, s_ = wlb_jobs[wlb_i[0]]
                    gpsimd.dma_start(out=dst, in_=s_).then_inc(sWLb, 16)
                    wlb_i[0] += 1

            for s in range(NSLAB_ALL):
                if s >= 2:
                    gpsimd.wait_ge(sKVm, 2 * (s - 1))
                src = xt_full.rearrange("(c p) t -> p c t", p=128)[:, :, s * 256:(s + 1) * 256]
                gpsimd.dma_start(out=slab[s % 2], in_=src).then_inc(
                    sXTe if s % 2 == 0 else sXTo, 16)
                if 2 <= s < 6 or (s >= 8 and s % 2 == 0):
                    emit_wlb()
            while wlb_i[0] < len(wlb_jobs):
                emit_wlb()
            wo_src = wo_t.rearrange("(c p) n -> p c n", p=128)
            wb_src = wbase_t.rearrange("(c p) n -> p c n", p=128)
            for s in range(NSLAB_OWN):
                if s < 2:
                    gpsimd.wait_ge(sAcc, NTC_ALL)
                else:
                    gpsimd.wait_ge(sPEq, peq_q[2 * (s - 1)])
                g = NSLAB_ALL + s
                src = xt_own.rearrange("(c p) t -> p c t", p=128)[:, :, s * 256:(s + 1) * 256]
                gpsimd.dma_start(out=slab[g % 2], in_=src).then_inc(
                    sXTe if g % 2 == 0 else sXTo, 16)
                if 2 <= s < 6:
                    i = s - 2
                    if i == 0:
                        gpsimd.wait_ge(sKVm, NTC_ALL)
                    gpsimd.dma_start(out=wbase_sb[:, 2 * i:2 * i + 2, :],
                                     in_=wb_src[:, 2 * i:2 * i + 2, :]).then_inc(sWB, 16)
            gpsimd.wait_ge(sPEq, peq_q[NTC_OWN - 1])
            for i in range(WO_CH):
                if i >= 2:
                    gpsimd.wait_ge(sWO[i % 2], 16 * (i // 2))
                gpsimd.dma_start(out=U0[:, 2 * i:2 * i + 2, :],
                                 in_=wo_src[:, 2 * i:2 * i + 2, :]).then_inc(sWO[i % 2], 16)
            gpsimd.wait_ge(sDVEq, DQ_TOTAL)
            gpsimd.wait_ge(sACTq, 2 * NTC_OWN)
            gpsimd.wait_ge(sSQW, 16 * NTC_OWN)
            gpsimd.wait_ge(sPEq, peq_T15[NTC_OWN - 1])
            for i in range(4, WB_CH):
                gpsimd.dma_start(out=wbase_sb[:, 2 * i:2 * i + 2, :],
                                 in_=wb_src[:, 2 * i:2 * i + 2, :]).then_inc(sWB, 16)
            gpsimd.dma_start(out=rowbb, in_=brows[:, 3072:5120]).then_inc(sWB, 16)


        @block.sync
        def _(sync):
            for u in range(NTC_OWN):
                sync.wait_ge(sDVEq, dqabs[(u, "E2c3")])
                sync.dma_start(out=attn_dram[u], in_=sqp[u % 2][:, :]).then_inc(sSQW, 16)
            def sq_load(u):
                if u < 2:
                    sync.wait_ge(sSQW, 16 * NTC_OWN)
                else:
                    sync.wait_ge(sPE2, pe2_mm3[u - 2])
                sync.dma_start(out=sqp[u % 2][:, :], in_=attn_dram[u]).then_inc(
                    sSQRa if u % 2 == 0 else sSQRb, 16)

            def y_store(u):
                sync.wait_ge(sDVE2, d2abs[(u, "y2")])
                sync.dma_start(out=y_out[u * 128:(u + 1) * 128, :],
                               in_=fpB[:, :]).then_inc(sOutS, 16)

            for u in range(NTC_OWN):
                sq_load(u)
                if u >= 5:
                    y_store(u - 5)
            for u in range(NTC_OWN - 5, NTC_OWN):
                y_store(u)
            sync.wait_ge(sOutS, 16 * NTC_OWN)


    return nc


def kernel(L0_attn_hidden_states, L0_attn_Wq, L0_attn_bq, L0_attn_Wk, L0_attn_bk,
           L0_attn_Wv, L0_attn_bv, L0_attn_Wo, L0_ffn_W_base, L0_ffn_b_base,
           L0_ffn_router_W, L0_ffn_A, L0_ffn_Bw):
    f32 = np.float32
    X = np.ascontiguousarray(np.asarray(L0_attn_hidden_states, dtype=f32))
    Wq = np.asarray(L0_attn_Wq, f32)
    Wk = np.asarray(L0_attn_Wk, f32)
    Wv = np.asarray(L0_attn_Wv, f32)
    Wo = np.asarray(L0_attn_Wo, f32)
    Wb = np.asarray(L0_ffn_W_base, f32)
    Rw = np.asarray(L0_ffn_router_W, f32)
    A = np.asarray(L0_ffn_A, f32)
    Bw = np.asarray(L0_ffn_Bw, f32)

    wkv_t = np.ascontiguousarray(np.concatenate([Wk, Wv], axis=0).T)
    wq_t = np.ascontiguousarray(Wq.T)
    wo_t = np.ascontiguousarray(Wo.T)
    wbase_t = np.ascontiguousarray(Wb.T)
    ra_t = np.ascontiguousarray(
        np.concatenate([Rw, A.reshape(128, D)], axis=0).T)
    bcat_t = np.ascontiguousarray(Bw.transpose(0, 2, 1).reshape(128, D))
    brows = np.concatenate([
        np.asarray(L0_attn_bk, f32), np.asarray(L0_attn_bv, f32),
        np.asarray(L0_attn_bq, f32), np.asarray(L0_ffn_b_base, f32)
    ]).reshape(1, 5120)
    ident = np.eye(128, dtype=f32)

    xt_b = [np.ascontiguousarray(X[b].T) for b in range(2)]

    nc = build_nc()
    in_maps = []
    for c in range(8):
        b, j = c // 4, c % 4
        in_maps.append(dict(
            xt_full=xt_b[b],
            xt_own=np.ascontiguousarray(xt_b[b][:, j * TOK:(j + 1) * TOK]),
            x_own=np.ascontiguousarray(X[b, j * TOK:(j + 1) * TOK, :]),
            wkv_t=wkv_t, wq_t=wq_t, wo_t=wo_t, wbase_t=wbase_t,
            ra_t=ra_t, bcat_t=bcat_t, brows=brows, ident=ident,
        ))
    res = run_bass_kernel_spmd(nc, in_maps, core_ids=list(range(8)))
    y = np.empty((2, SEQ, D), dtype=f32)
    for c in range(8):
        b, j = c // 4, c % 4
        y[b, j * TOK:(j + 1) * TOK, :] = res.results[c]["y"]
    return y


# version 17
# speedup vs baseline: 1.7910x; 1.0036x over previous
import numpy as np
from contextlib import ExitStack

import concourse.bass as bass
import concourse.mybir as mybir
from concourse.bass_utils import run_bass_kernel_spmd

F32 = mybir.dt.float32
BF16 = mybir.dt.bfloat16
ALU = mybir.AluOpType
AX = mybir.AxisListType
AF = mybir.ActivationFunctionType

D = 2048
SEQ = 8192
TOK = 2048
NTC_ALL = 64
NTC_OWN = 16
NSLAB_ALL = 32
NSLAB_OWN = 8
ATT_EPS = 1e-6
LN_EPS = 1e-5
WO_CH = 8
WB_CH = 8
N_P2W = WO_CH + WB_CH

peq_q, peq_qh, peq_T7, peq_T15, peq_qkvq = {}, {}, {}, {}, {}
_c = 0
for _u in range(NTC_OWN):
    _c += 1
    peq_qh[_u] = _c
    if _u >= 1:
        _c += 1
        peq_T7[_u - 1] = _c
        _c += 1
        peq_T15[_u - 1] = _c
    _c += 1
    peq_q[_u] = _c
    if _u >= 1:
        for _i in range(4):
            _c += 1
            peq_qkvq[(_u - 1, _i)] = _c
_c += 1
peq_T7[NTC_OWN - 1] = _c
_c += 1
peq_T15[NTC_OWN - 1] = _c
for _i in range(4):
    _c += 1
    peq_qkvq[(NTC_OWN - 1, _i)] = _c

pe2_mm3 = {}
pe2_y1t = {}
pe2_misc, pe2_hmt, pe2_base = {}, {}, {}


def _plan_pe2():
    c = 0
    for v in range(NTC_OWN + 2):
        u = v - 2
        if v <= NTC_OWN - 1:
            if u >= 0:
                for r in range(4):
                    c += 1
                    pe2_y1t[(u, r)] = c
                c += 1
                pe2_misc[u] = c
            c += 1
            pe2_mm3[v] = c
        else:
            for r in range(4):
                c += 1
                pe2_y1t[(u, r)] = c
            c += 1
            pe2_misc[u] = c
        if u >= 0:
            c += 1
            pe2_hmt[u] = c
            c += 1
            pe2_base[u] = c


_plan_pe2()

dqabs = {}
_cq = 0
def _dq_seg(u, names):
    global _cq
    for n in names:
        _cq += 1
        dqabs[(u, n)] = _cq
for _u in range(NTC_OWN):
    _dq_seg(_u, ["min0", "stt0"])
    if _u >= 1:
        _dq_seg(_u - 1, ["E1a", "E1b", "E2c0"])
    _dq_seg(_u, ["min1"])
    if _u >= 1:
        _dq_seg(_u - 1, ["E2c1", "E2c2", "E2c3"])
    _dq_seg(_u, ["stt1", "prod", "red", "eps", "recip", "cast",
                 "sqhat0", "sqhat1"])
_dq_seg(NTC_OWN - 1, ["E1a", "E1b", "E2c0", "E2c1", "E2c2", "E2c3"])
DQ_TOTAL = _cq

_DSEG = {
    "A1": ["s"],
    "A2": ["b0", "b1", "b2", "b3", "aggr", "negmean", "veps", "rv", "y1"],
    "B1": ["lh", "rowmax", "mask8", "hmask"],
    "B2a": ["s2", "c0", "c1", "c2", "c3", "aggr2", "negmean2", "veps2", "rv2"],
    "B2y": ["y2"],
    "YC": ["y1c"],
}
d2abs = {}
_c2 = 0
def _seg(u, ops):
    global _c2
    for name in ops:
        _c2 += 1
        d2abs[(u, name)] = _c2
def _p2_sched():
    order = []
    for v in range(NTC_OWN + 3):
        if 3 <= v <= NTC_OWN + 2:
            order.append(("B2a", v - 3))
        if 1 <= v <= NTC_OWN:
            order.append(("YC", v - 1))
        if 2 <= v <= NTC_OWN + 1:
            order.append(("B1", v - 2))
        if 3 <= v <= NTC_OWN + 2:
            order.append(("B2y", v - 3))
        if v <= NTC_OWN - 1:
            order.append(("A1", v))
            order.append(("A2", v))
    return order
P2_ORDER = _p2_sched()
for _kind, _u in P2_ORDER:
    _seg(_u, _DSEG[_kind])
def _p2_act_sched():
    order = []
    for v in range(NTC_OWN + 3):
        if 2 <= v <= NTC_OWN + 1:
            order.append(("Ey", v - 2))
        if 3 <= v <= NTC_OWN + 2:
            order.append(("Q2", v - 3))
        if 2 <= v <= NTC_OWN + 1:
            order.append(("Eh", v - 2))
        if v <= NTC_OWN - 1:
            order.append(("Aa", v))
    return order
P2_ACT = _p2_act_sched()
a2abs = {}
_c3 = 0
for _kind, _u in P2_ACT:
    if _kind == "Aa":
        _c3 += 1
        a2abs[(_u, "sqrt1")] = _c3
    elif _kind == "Ey":
        for _r in range(4):
            _c3 += 1
            a2abs[(_u, "ey1t%d" % _r)] = _c3
    elif _kind == "Eh":
        _c3 += 1
        a2abs[(_u, "ehmt")] = _c3
    else:
        _c3 += 1
        a2abs[(_u, "sqrt2")] = _c3

AQ_N = 2


def build_nc():
    nc = bass.Bass("TRN2", target_bir_lowering=False)

    xt_full = nc.dram_tensor("xt_full", [D, SEQ], F32, kind="ExternalInput").ap()
    xt_own = nc.dram_tensor("xt_own", [D, TOK], F32, kind="ExternalInput").ap()
    x_own = nc.dram_tensor("x_own", [TOK, D], F32, kind="ExternalInput").ap()
    wkv_t = nc.dram_tensor("wkv_t", [D, 1024], F32, kind="ExternalInput").ap()
    wq_t = nc.dram_tensor("wq_t", [D, D], F32, kind="ExternalInput").ap()
    wo_t = nc.dram_tensor("wo_t", [D, D], F32, kind="ExternalInput").ap()
    wbase_t = nc.dram_tensor("wbase_t", [D, D], F32, kind="ExternalInput").ap()
    ra_t = nc.dram_tensor("ra_t", [D, 136], F32, kind="ExternalInput").ap()
    bcat_t = nc.dram_tensor("bcat_t", [128, D], F32, kind="ExternalInput").ap()
    brows = nc.dram_tensor("brows", [1, 5120], F32, kind="ExternalInput").ap()
    ident = nc.dram_tensor("ident", [128, 128], F32, kind="ExternalInput").ap()
    y_out = nc.dram_tensor("y", [TOK, D], F32, kind="ExternalOutput").ap()
    attn_dram = nc.dram_tensor("attn_scratch", [NTC_OWN, 128, D], BF16, kind="Internal").ap()

    with ExitStack() as ctx:
        sb = lambda name, shape, dt: ctx.enter_context(nc.sbuf_tensor(name, shape, dt))
        sem = lambda name: ctx.enter_context(nc.semaphore(name))

        U0 = sb("u0", [128, 16, 2048], BF16)
        U1 = sb("u1", [128, 32768], BF16)
        wkv_sb = U1[:, 0:16384].rearrange("p (c n) -> p c n", c=16)
        slab = [U1[:, 16384:20480].rearrange("p (c n) -> p c n", c=16),
                U1[:, 20480:24576].rearrange("p (c n) -> p c n", c=16)]
        sk_t = [U1[:, 24576:25088], U1[:, 25088:25600]]
        vb_t = [U1[:, 25600:26112], U1[:, 26112:26624]]
        sq_p1 = U1[:, 26624:28672]
        scr1 = U1[:, 28672:30720].bitcast(F32)
        scr2 = U1[:, 30720:32768].bitcast(F32)
        wbase_sb = U1.rearrange("p (c n) -> p c n", c=16)

        fpA = sb("fpA", [128, D], F32)
        fpB = sb("fpB", [128, D], F32)
        prodb = fpB[:, :].bitcast(BF16)[:, 0:2048]
        kvb = sb("kvb", [128, 512], BF16)
        kvb_v = kvb[:, :].rearrange("p (h e) -> p h e", h=4)
        ksr512 = sb("ksr512", [1, 512], F32)
        ksum_bc = sb("ksum_bc", [128, D], BF16)
        I_bf = sb("I_bf", [128, 128], BF16)
        onesf = sb("onesf", [1, 128], F32)
        onesb = sb("onesb", [128, 1], BF16)
        onesr = sb("onesr", [1, 128], BF16)
        ra_sb = sb("ra_sb", [128, 16, 136], BF16)
        bcat_sb = sb("bcat_sb", [128, D], BF16)

        xt_p2 = [sb("xt_p2a", [128, D], F32), sb("xt_p2b", [128, D], F32),
                 sb("xt_p2c", [128, D], F32)]
        ksrow = xt_p2[0][0:1, :]
        sq_p2 = sb("sq_p2", [128, D], BF16)
        sq_p2b = sb("sq_p2b", [128, D], BF16)
        sqp = [sq_p2, sq_p2b]
        sqh = sb("sqh", [128, D], BF16)
        sqhT = sb("sqhT", [128, D], BF16)
        y1b = sb("y1b", [128, D], BF16)
        rowkv = sq_p2[0:1, 0:1024]
        rowq = y1b[0:1, 0:2048]
        rowbb = sqh[0:1, 0:2048]
        lh = sb("lh", [128, 136], F32)
        hmask = sb("hmask", [128, 128], BF16)
        hmaskT = sb("hmaskT", [128, 128], BF16)
        norm16 = sb("norm16", [128, 16], F32)
        normeps = sb("normeps", [128, 16], F32)
        rnorm = sb("rnorm", [128, 16], F32)
        rnbf = sb("rnbf", [128, 16], BF16)
        rowmax = sb("rowmax", [128, 1], F32)
        mask8 = sb("mask8", [128, 8], F32)
        negmean = [sb("negmean%d" % i, [128, 1], F32) for i in range(4)]
        veps = [sb("veps%d" % i, [128, 1], F32) for i in range(4)]
        stdv = [sb("stdv%d" % i, [128, 1], F32) for i in range(4)]
        rstd = [sb("rstd%d" % i, [128, 1], F32) for i in range(4)]
        bnstA = sb("bnstA", [128, 4, 6], F32)
        bnstB = sb("bnstB", [128, 4, 6], F32)
        mvA = sb("mvA", [128, 2], F32)
        mvB = sb("mvB", [128, 2], F32)

        psA = ctx.enter_context(nc.psum_tensor("psA", [128, 2048], F32))
        psB = ctx.enter_context(nc.psum_tensor("psB", [128, 2048], F32))
        psK = [psA[:, 0:512], psA[:, 512:1024]]
        psV = [psA[:, 1024:1536], psA[:, 1536:2048]]
        psKV = psB[:, 0:512]
        psKS = psB[0:1, 512:1024]
        psQ = [psA, psB]

        sWLa = sem("sWLa")
        sWLb = sem("sWLb")
        sXTe = sem("sXTe")
        sXTo = sem("sXTo")
        sWO = [sem("sWOa"), sem("sWOb")]
        sWB = sem("sWB")
        sStp = sem("sStp")
        sKVm = sem("sKVm")
        sAcc = sem("sAcc")
        sDVE1 = sem("sDVE1")
        sACT1 = sem("sACT1")
        sFin = sem("sFin")
        sPEq = sem("sPEq")
        sDVEq = sem("sDVEq")
        sACTq = sem("sACTq")
        sSQW = sem("sSQW")
        sXa = sem("sXa")
        sXb = sem("sXb")
        sXc = sem("sXc")
        sSQRa = sem("sSQRa")
        sSQRb = sem("sSQRb")
        sPE2 = sem("sPE2")
        sDVE2 = sem("sDVE2")
        sACT2 = sem("sACT2")
        sY1B = sem("sY1B")
        sOutS = sem("sOutS")

        sX3 = [sXa, sXb, sXc]
        block = ctx.enter_context(nc.Block())

        def slab_wait(eng, g):
            s = sXTe if g % 2 == 0 else sXTo
            eng.wait_ge(s, 16 * (g // 2 + 1))

        @block.tensor
        def _(tensor):
            tensor.wait_ge(sStp, 1)
            tensor.wait_ge(sWLa, 32)

            for t in range(NTC_ALL):
                par = t % 2
                s_idx = t // 2
                toff = (t - 2 * s_idx) * 128
                slab_wait(tensor, s_idx)
                if t >= 2:
                    tensor.wait_ge(sDVE1, 3 * (t - 1))
                for dc in range(16):
                    lhs = slab[s_idx % 2][:, dc, toff:toff + 128]
                    tensor.matmul(psK[par], lhs, wkv_sb[:, dc, 0:512],
                                  start=(dc == 0), stop=False,
                                  skip_group_check=True)
                    tensor.matmul(psV[par], lhs, wkv_sb[:, dc, 512:1024],
                                  start=(dc == 0), stop=False,
                                  skip_group_check=True)
                tensor.matmul(psK[par], onesr[:, :], rowkv[:, 0:512],
                              start=False, stop=True, skip_group_check=True)
                tensor.matmul(psV[par], onesr[:, :], rowkv[:, 512:1024],
                              start=False, stop=True, skip_group_check=True
                              ).then_inc(sKVm, 1)
                if t >= 1:
                    tp = t - 1
                    tensor.wait_ge(sDVE1, 3 * t)
                    for h4 in range(4):
                        tensor.matmul(psKV[:, h4 * 128:(h4 + 1) * 128],
                                      sk_t[tp % 2][:, h4 * 128:(h4 + 1) * 128],
                                      vb_t[tp % 2][:, h4 * 128:(h4 + 1) * 128],
                                      start=(tp == 0 and h4 == 0),
                                      stop=(tp == NTC_ALL - 1 and h4 == 3),
                                      skip_group_check=True)
                    tensor.matmul(psKS, onesb[:, :], sk_t[tp % 2][:, :],
                                  start=(tp == 0), stop=(tp == NTC_ALL - 1),
                                  skip_group_check=True).then_inc(sAcc, 1)
            tp = NTC_ALL - 1
            tensor.wait_ge(sDVE1, 3 * NTC_ALL)
            for h4 in range(4):
                tensor.matmul(psKV[:, h4 * 128:(h4 + 1) * 128],
                              sk_t[tp % 2][:, h4 * 128:(h4 + 1) * 128],
                              vb_t[tp % 2][:, h4 * 128:(h4 + 1) * 128],
                              start=False, stop=(h4 == 3), skip_group_check=True)
            tensor.matmul(psKS, onesb[:, :], sk_t[tp % 2][:, :],
                          start=False, stop=True, skip_group_check=True
                          ).then_inc(sAcc, 1)

            tensor.wait_ge(sFin, 3)
            for j in range(4):
                mm = tensor.matmul(psA[:, j * 512:(j + 1) * 512], onesf[:, :],
                                   ksrow[:, j * 512:(j + 1) * 512], start=True, stop=True)
            mm.then_inc(sFin, 1)

            tensor.wait_ge(sWLb, 16 * 12)
            tensor.wait_ge(sFin, 5)

            def t_group(u):
                par = u % 2
                for h in range(16):
                    if h == 0:
                        tensor.wait_ge(sDVEq, dqabs[(u, "sqhat0")])
                    elif h == 8:
                        tensor.wait_ge(sDVEq, dqabs[(u, "sqhat1")])
                    mm = tensor.matmul(psQ[par][:, h * 128:(h + 1) * 128],
                                       sqh[:, h * 128:(h + 1) * 128], I_bf[:, :],
                                       start=True, stop=True, skip_group_check=True)
                    if h == 7 or h == 15:
                        mm.then_inc(sPEq, 1)

            def qkv_group(u):
                par = u % 2
                for i in range(4):
                    if i == 0:
                        tensor.wait_ge(sDVEq, dqabs[(u, "E1a")])
                    elif i == 2:
                        tensor.wait_ge(sDVEq, dqabs[(u, "E1b")])
                    for h in range(4 * i, 4 * i + 4):
                        mm = tensor.matmul(psQ[par][:, h * 128:(h + 1) * 128],
                                           kvb_v[:, h // 4, :],
                                           sqhT[:, h * 128:(h + 1) * 128],
                                           start=True, stop=True,
                                           skip_group_check=True)
                    mm.then_inc(sPEq, 1)

            for u in range(NTC_OWN):
                par = u % 2
                s_idx = u // 2
                toff = (u - 2 * s_idx) * 128
                slab_wait(tensor, NSLAB_ALL + s_idx)
                for j in range(4):
                    if j == 3 and u >= 1:
                        t_group(u - 1)
                    if u >= 2:
                        tensor.wait_ge(sDVEq, dqabs[(u - 2, "E2c%d" % j)])
                    for dc in range(16):
                        lhs = slab[s_idx % 2][:, dc, toff:toff + 128]
                        tensor.matmul(psQ[par][:, j * 512:(j + 1) * 512], lhs,
                                      U0[:, dc, j * 512:(j + 1) * 512],
                                      start=(dc == 0), stop=False,
                                      skip_group_check=True)
                    mm = tensor.matmul(psQ[par][:, j * 512:(j + 1) * 512],
                                       onesr[:, :], rowq[:, j * 512:(j + 1) * 512],
                                       start=False, stop=True, skip_group_check=True)
                    if j == 1 or j == 3:
                        mm.then_inc(sPEq, 1)
                if u >= 1:
                    qkv_group(u - 1)
            t_group(NTC_OWN - 1)
            qkv_group(NTC_OWN - 1)

            tensor.wait_ge(sDVEq, dqabs[(NTC_OWN - 2, "E2c3")])

            def y1t_round(u, r):
                if r == 0:
                    if u == 0:
                        tensor.wait_ge(sDVEq, DQ_TOTAL)
                    tensor.wait_ge(sDVE2, d2abs[(u, "y1c")])
                    if u >= 1:
                        tensor.wait_ge(sDVE2, d2abs[(u - 1, "s2")])
                else:
                    if r >= 2:
                        tensor.wait_ge(sACT2, a2abs[(u, "ey1t%d" % (r - 2))])
                for k in range(4):
                    dc = 4 * r + k
                    base_col = (r % 2) * 512 + k * 128
                    mm = tensor.matmul(psB[:, base_col:base_col + 128],
                                       y1b[:, dc * 128:(dc + 1) * 128], I_bf[:, :],
                                       start=True, stop=True, skip_group_check=True)
                mm.then_inc(sPE2, 1)

            def misc_part(u, half):
                tensor.wait_ge(sACT2, a2abs[(u, "ey1t%d" % (half + 2))])
                for dc in range(8 * half, 8 * half + 8):
                    mm = tensor.matmul(psB[:, 0:136],
                                       sqhT[:, dc * 128:(dc + 1) * 128],
                                       ra_sb[:, dc, :],
                                       start=(dc == 0), stop=(dc == 15),
                                       skip_group_check=True)
                if half == 1:
                    mm.then_inc(sPE2, 1)

            def p2_iter(v):
                u = v - 2
                if v <= NTC_OWN - 1:
                    tensor.wait_ge(sSQRa if v % 2 == 0 else sSQRb, 16 * (v // 2 + 1))
                    if v >= 1:
                        tensor.wait_ge(sDVE2, d2abs[(v - 1, "s")])
                    sq = sqp[v % 2]
                    for h in range(16):
                        if v == 0 and h % 2 == 0:
                            i = h // 2
                            tensor.wait_ge(sWO[i % 2], 16 * (i // 2 + 1))
                        elif v == 1 and h == 0:
                            tensor.wait_ge(sWO[0], 16 * (WO_CH // 2))
                            tensor.wait_ge(sWO[1], 16 * (WO_CH // 2))
                        for j in range(4):
                            mm = tensor.matmul(psA[:, j * 512:(j + 1) * 512],
                                               sq[:, h * 128:(h + 1) * 128],
                                               U0[:, h, j * 512:(j + 1) * 512],
                                               start=(h == 0), stop=(h == 15),
                                               skip_group_check=True)
                        if u >= 0:
                            if h == 3:
                                y1t_round(u, 0)
                            elif h == 5:
                                y1t_round(u, 1)
                            elif h == 7:
                                y1t_round(u, 2)
                            elif h == 9:
                                y1t_round(u, 3)
                            elif h == 10:
                                misc_part(u, 0)
                            elif h == 12:
                                misc_part(u, 1)
                    mm.then_inc(sPE2, 1)
                else:
                    for r in range(4):
                        y1t_round(u, r)
                    misc_part(u, 0)
                    misc_part(u, 1)
                if u >= 0:
                    tensor.wait_ge(sDVE2, d2abs[(u, "hmask")])
                    tensor.matmul(psB[:, 512:640], hmask[:, :], I_bf[:, :],
                                  start=True, stop=True,
                                  skip_group_check=True).then_inc(sPE2, 1)
                    tensor.wait_ge(sDVE2, d2abs[(u, "lh")])
                    if u == 0:
                        tensor.wait_ge(sWB, 16 * (WB_CH + 1))
                    for j in range(4):
                        for dc in range(16):
                            tensor.matmul(psB[:, j * 512:(j + 1) * 512],
                                          sqhT[:, dc * 128:(dc + 1) * 128],
                                          wbase_sb[:, dc, j * 512:(j + 1) * 512],
                                          start=(dc == 0), stop=False,
                                          skip_group_check=True)
                        tensor.matmul(psB[:, j * 512:(j + 1) * 512], onesr[:, :],
                                      rowbb[:, j * 512:(j + 1) * 512],
                                      start=False, stop=False, skip_group_check=True)
                        if j == 0:
                            tensor.wait_ge(sACT2, a2abs[(u, "ehmt")])
                        mm = tensor.matmul(psB[:, j * 512:(j + 1) * 512], hmaskT[:, :],
                                           bcat_sb[:, j * 512:(j + 1) * 512],
                                           start=False, stop=True,
                                           skip_group_check=True)
                    mm.then_inc(sPE2, 1)

            for v in range(NTC_OWN + 2):
                p2_iter(v)

        @block.vector
        def _(vector):
            vector.memset(onesf[:, :], 1.0)
            vector.memset(onesb[:, :], 1.0)
            vector.memset(onesr[:, :], 1.0).then_inc(sStp, 1)

            dcnt = [0]

            def op(ins):
                ins.then_inc(sDVE1, 1)
                dcnt[0] += 1
                vector.wait_ge(sDVE1, dcnt[0])
                return ins

            for t in range(NTC_ALL):
                par = t % 2
                vector.wait_ge(sKVm, t + 1)
                if t >= 1:
                    vector.wait_ge(sACT1, t)
                op(vector.tensor_scalar(scr1[:, 0:512], psK[par], 0.0, None, ALU.min))
                vector.wait_ge(sACT1, t + 1)
                if t >= 2:
                    vector.wait_ge(sAcc, t - 1)
                op(vector.scalar_tensor_tensor(sk_t[par][:, :], psK[par], 0.0,
                                               scr2[:, 0:512], ALU.max, ALU.add))
                op(vector.tensor_copy(vb_t[par][:, :], psV[par]))

            vector.wait_ge(sAcc, NTC_ALL)
            vector.tensor_copy(kvb[:, :], psKV).then_inc(sFin, 1)
            vector.tensor_copy(ksr512[:, :], psKS).then_inc(sFin, 1)
            vector.wait_ge(sFin, 2)
            ks_in = ksr512[:, :].rearrange("a (h d) -> a h d", h=4).unsqueeze(2) \
                .broadcast_to([1, 4, 4, 128])
            vector.tensor_copy(ksrow.rearrange("a (h r d) -> a h r d", h=4, r=4),
                               ks_in).then_inc(sFin, 1)
            vector.wait_ge(sFin, 4)
            vector.tensor_copy(ksum_bc[:, :], psA[:, :]).then_inc(sFin, 1)

            dq = [0]

            def opq(ins):
                ins.then_inc(sDVEq, 1)
                dq[0] += 1
                vector.wait_ge(sDVEq, dq[0])
                return ins

            sq3 = sq_p1.rearrange("p (h d) -> p h d", h=16)
            prod3 = prodb.rearrange("p (h d) -> p h d", h=16)
            ks3 = ksum_bc[:, :].rearrange("p (h d) -> p h d", h=16)
            sqh3 = sqh[:, :].rearrange("p (h d) -> p h d", h=16)
            rn3 = rnbf[:, :].unsqueeze(2).broadcast_to([128, 16, 128])

            def e1seg(pw):
                vector.wait_ge(sPEq, peq_T7[pw])
                opq(vector.tensor_copy(sqhT[:, 0:1024], psQ[pw % 2][:, 0:1024]))
                vector.wait_ge(sPEq, peq_T15[pw])
                opq(vector.tensor_copy(sqhT[:, 1024:2048], psQ[pw % 2][:, 1024:2048]))

            def e2seg(pw, quarters):
                for i in quarters:
                    vector.wait_ge(sPEq, peq_qkvq[(pw, i)])
                    if i == 0 and pw >= 1:
                        vector.wait_ge(sSQW, 16 * pw)
                    opq(vector.tensor_copy(sqp[pw % 2][:, i * 512:(i + 1) * 512],
                                           psQ[pw % 2][:, i * 512:(i + 1) * 512]))

            for u in range(NTC_OWN):
                par = u % 2
                vector.wait_ge(sPEq, peq_qh[u])
                if u >= 1:
                    vector.wait_ge(sACTq, 2 * u)
                opq(vector.tensor_scalar(scr1[:, :], psQ[par][:, 0:1024], 0.0,
                                         None, ALU.min))
                vector.wait_ge(sACTq, 2 * u + 1)
                opq(vector.scalar_tensor_tensor(sq_p1[:, 0:1024], psQ[par][:, 0:1024],
                                                0.0, scr2[:, :], ALU.max, ALU.add))
                if u >= 1:
                    e1seg(u - 1)
                    e2seg(u - 1, [0])
                vector.wait_ge(sPEq, peq_q[u])
                opq(vector.tensor_scalar(scr1[:, :], psQ[par][:, 1024:2048], 0.0,
                                         None, ALU.min))
                if u >= 1:
                    e2seg(u - 1, [1, 2, 3])
                vector.wait_ge(sACTq, 2 * u + 2)
                opq(vector.scalar_tensor_tensor(sq_p1[:, 1024:2048],
                                                psQ[par][:, 1024:2048],
                                                0.0, scr2[:, :], ALU.max, ALU.add))
                opq(vector.tensor_tensor(prod3, sq3, ks3, ALU.mult))
                opq(vector.tensor_reduce(norm16[:, :], prod3, AX.X, ALU.add))
                opq(vector.tensor_scalar(normeps[:, :], norm16[:, :], ATT_EPS,
                                         None, ALU.add))
                opq(vector.reciprocal(rnorm[:, :], normeps[:, :]))
                opq(vector.tensor_copy(rnbf[:, :], rnorm[:, :]))
                opq(vector.tensor_tensor(sqh3[:, 0:8, :], sq3[:, 0:8, :],
                                         rn3[:, 0:8, :], ALU.mult))
                opq(vector.tensor_tensor(sqh3[:, 8:16, :], sq3[:, 8:16, :],
                                         rn3[:, 8:16, :], ALU.mult))
            e1seg(NTC_OWN - 1)
            e2seg(NTC_OWN - 1, [0, 1, 2, 3])

            dd = [0]

            def op2(ins):
                ins.then_inc(sDVE2, 1)
                dd[0] += 1
                vector.wait_ge(sDVE2, dd[0])
                return ins

            lh3 = lh[:, 8:136].rearrange("p (e r) -> p e r", e=8)
            hm3 = hmask[:, :].rearrange("p (e r) -> p e r", e=8)
            m8b = mask8[:, :].unsqueeze(2).broadcast_to([128, 8, 16])

            def seg_a1(u):
                vector.wait_ge(sPE2, pe2_mm3[u])
                vector.wait_ge(sX3[u % 3], 16 * (u // 3 + 1))
                op2(vector.scalar_tensor_tensor(fpA[:, :], psA[:, :], 1.0,
                                                xt_p2[u % 3][:, :], ALU.mult, ALU.add))

            def seg_a2(u):
                p = u % 2
                for g in range(4):
                    op2(vector.bn_stats(bnstA[:, g, :], fpA[:, g * 512:(g + 1) * 512]))
                op2(vector.bn_aggr(mvA[:, :], bnstA[:, :, :].rearrange("p a b -> p (a b)")))
                op2(vector.tensor_scalar(negmean[p][:, :], mvA[:, 0:1], -1.0, None, ALU.mult))
                op2(vector.tensor_scalar(veps[p][:, :], mvA[:, 1:2], LN_EPS, None, ALU.add))
                op2(vector.reciprocal(stdv[p][:, :], veps[p][:, :]))
                vector.wait_ge(sACT2, a2abs[(u, "sqrt1")])
                op2(vector.tensor_scalar(xt_p2[u % 3][:, :], fpA[:, :], negmean[p][:, :],
                                         rstd[p][:, :], ALU.add, ALU.mult))

            def seg_b1(u):
                vector.wait_ge(sPE2, pe2_misc[u])
                op2(vector.tensor_copy(lh[:, :], psB[:, 0:136]))
                op2(vector.reduce_max(rowmax[:, :], lh[:, 0:8], AX.X))
                op2(vector.tensor_scalar(mask8[:, :], lh[:, 0:8], rowmax[:, :],
                                         2.0, ALU.is_ge, ALU.mult))
                op2(vector.tensor_tensor(hm3, lh3, m8b, ALU.mult))

            def seg_b2a(u):
                p = 2 + u % 2
                vector.wait_ge(sPE2, pe2_base[u])
                op2(vector.scalar_tensor_tensor(fpA[:, :], psB[:, :], 1.0,
                                                xt_p2[u % 3][:, :], ALU.mult, ALU.add))
                for g in range(4):
                    op2(vector.bn_stats(bnstB[:, g, :], fpA[:, g * 512:(g + 1) * 512]))
                op2(vector.bn_aggr(mvB[:, :], bnstB[:, :, :].rearrange("p a b -> p (a b)")))
                op2(vector.tensor_scalar(negmean[p][:, :], mvB[:, 0:1], -1.0, None, ALU.mult))
                op2(vector.tensor_scalar(veps[p][:, :], mvB[:, 1:2], LN_EPS, None, ALU.add))
                op2(vector.reciprocal(stdv[p][:, :], veps[p][:, :]))

            def seg_yc(u):
                if u >= 1:
                    vector.wait_ge(sPE2, pe2_y1t[(u - 1, 3)])
                op2(vector.tensor_copy(y1b[:, :], xt_p2[u % 3][:, :]))

            def seg_b2y(u):
                p = 2 + u % 2
                vector.wait_ge(sACT2, a2abs[(u, "sqrt2")])
                if u >= 1:
                    vector.wait_ge(sOutS, 16 * u)
                op2(vector.tensor_scalar(fpB[:, :], fpA[:, :], negmean[p][:, :],
                                         rstd[p][:, :], ALU.add, ALU.mult))

            _dseg_fns = {"A1": seg_a1, "A2": seg_a2, "B1": seg_b1,
                         "B2a": seg_b2a, "B2y": seg_b2y, "YC": seg_yc}
            for kind, u in P2_ORDER:
                _dseg_fns[kind](u)

        @block.scalar
        def _(scalar):
            scalar.preload_act_table(AF.Exp)
            for t in range(NTC_ALL):
                scalar.wait_ge(sDVE1, 3 * t + 1)
                scalar.activation(scr2[:, 0:512], scr1[:, 0:512], AF.Exp).then_inc(sACT1, 1)
            for u in range(NTC_OWN):
                for j in range(2):
                    scalar.wait_ge(sDVEq, dqabs[(u, "min%d" % j)])
                    scalar.activation(scr2[:, :], scr1[:, :], AF.Exp).then_inc(sACTq, 1)
            scalar.preload_act_table(AF.Sqrt)

            def x_load(v):
                if v == 0:
                    scalar.wait_ge(sFin, 4)
                scalar.dma_start(out=xt_p2[v % 3][:, :],
                                 in_=x_own[v * 128:(v + 1) * 128, :]).then_inc(
                                     sX3[v % 3], 16)

            x_load(0)
            x_load(1)
            x_load(2)
            for kind, u in P2_ACT:
                if kind == "Aa":
                    p = u % 2
                    scalar.wait_ge(sDVE2, d2abs[(u, "rv")])
                    scalar.activation(rstd[p][:, :], stdv[p][:, :], AF.Sqrt).then_inc(sACT2, 1)
                elif kind == "Ey":
                    for r in range(4):
                        scalar.wait_ge(sPE2, pe2_y1t[(u, r)])
                        scalar.copy(sqhT[:, r * 512:(r + 1) * 512],
                                    psB[:, (r % 2) * 512:(r % 2) * 512 + 512]).then_inc(sACT2, 1)
                elif kind == "Eh":
                    scalar.wait_ge(sPE2, pe2_hmt[u])
                    scalar.copy(hmaskT[:, :], psB[:, 512:640]).then_inc(sACT2, 1)
                else:
                    p = 2 + u % 2
                    scalar.wait_ge(sDVE2, d2abs[(u, "rv2")])
                    scalar.activation(rstd[p][:, :], stdv[p][:, :], AF.Sqrt).then_inc(sACT2, 1)
                    if u + 3 < NTC_OWN:
                        x_load(u + 3)

        @block.gpsimd
        def _(gpsimd):
            wq_src = wq_t.rearrange("(c p) n -> p c n", p=128)
            gpsimd.dma_start(out=wkv_sb, in_=wkv_t.rearrange("(c p) n -> p c n", p=128)
                             ).then_inc(sWLa, 16)
            gpsimd.dma_start(out=rowkv, in_=brows[:, 0:1024]).then_inc(sWLa, 16)
            gpsimd.wait_ge(sWLa, 32)
            wlb_small = [
                (rowq, brows[:, 1024:3072]),
                (I_bf[:, :], ident),
                (ra_sb[:, :, :], ra_t.rearrange("(c p) n -> p c n", p=128)),
                (bcat_sb[:, :], bcat_t),
            ]
            wlb_jobs = wlb_small + [
                (U0[:, 2 * i:2 * i + 2, :], wq_src[:, 2 * i:2 * i + 2, :])
                for i in range(8)]
            wlb_i = [0]

            def emit_wlb():
                if wlb_i[0] < len(wlb_jobs):
                    dst, s_ = wlb_jobs[wlb_i[0]]
                    gpsimd.dma_start(out=dst, in_=s_).then_inc(sWLb, 16)
                    wlb_i[0] += 1

            for s in range(NSLAB_ALL):
                if s >= 2:
                    gpsimd.wait_ge(sKVm, 2 * (s - 1))
                src = xt_full.rearrange("(c p) t -> p c t", p=128)[:, :, s * 256:(s + 1) * 256]
                gpsimd.dma_start(out=slab[s % 2], in_=src).then_inc(
                    sXTe if s % 2 == 0 else sXTo, 16)
                if 2 <= s < 6 or (s >= 8 and s % 2 == 0):
                    emit_wlb()
            while wlb_i[0] < len(wlb_jobs):
                emit_wlb()
            wo_src = wo_t.rearrange("(c p) n -> p c n", p=128)
            wb_src = wbase_t.rearrange("(c p) n -> p c n", p=128)
            for s in range(NSLAB_OWN):
                if s < 2:
                    gpsimd.wait_ge(sAcc, NTC_ALL)
                else:
                    gpsimd.wait_ge(sPEq, peq_q[2 * (s - 1)])
                g = NSLAB_ALL + s
                src = xt_own.rearrange("(c p) t -> p c t", p=128)[:, :, s * 256:(s + 1) * 256]
                gpsimd.dma_start(out=slab[g % 2], in_=src).then_inc(
                    sXTe if g % 2 == 0 else sXTo, 16)
                if 2 <= s < 6:
                    i = s - 2
                    if i == 0:
                        gpsimd.wait_ge(sKVm, NTC_ALL)
                    gpsimd.dma_start(out=wbase_sb[:, 2 * i:2 * i + 2, :],
                                     in_=wb_src[:, 2 * i:2 * i + 2, :]).then_inc(sWB, 16)
            gpsimd.wait_ge(sPEq, peq_q[NTC_OWN - 1])
            for i in range(WO_CH):
                if i >= 2:
                    gpsimd.wait_ge(sWO[i % 2], 16 * (i // 2))
                gpsimd.dma_start(out=U0[:, 2 * i:2 * i + 2, :],
                                 in_=wo_src[:, 2 * i:2 * i + 2, :]).then_inc(sWO[i % 2], 16)
            gpsimd.wait_ge(sDVEq, DQ_TOTAL)
            gpsimd.wait_ge(sACTq, 2 * NTC_OWN)
            gpsimd.wait_ge(sSQW, 16 * NTC_OWN)
            gpsimd.wait_ge(sPEq, peq_T15[NTC_OWN - 1])
            for i in range(4, WB_CH):
                gpsimd.dma_start(out=wbase_sb[:, 2 * i:2 * i + 2, :],
                                 in_=wb_src[:, 2 * i:2 * i + 2, :]).then_inc(sWB, 16)
            gpsimd.dma_start(out=rowbb, in_=brows[:, 3072:5120]).then_inc(sWB, 16)


        @block.sync
        def _(sync):
            def a_store(u):
                sync.wait_ge(sDVEq, dqabs[(u, "E2c3")])
                sync.dma_start(out=attn_dram[u], in_=sqp[u % 2][:, :]).then_inc(sSQW, 16)

            def sq_load(u):
                if u == 0:
                    sync.wait_ge(sSQW, 16 * (NTC_OWN - 1))
                elif u == 1:
                    sync.wait_ge(sSQW, 16 * NTC_OWN)
                else:
                    sync.wait_ge(sPE2, pe2_mm3[u - 2])
                sync.dma_start(out=sqp[u % 2][:, :], in_=attn_dram[u]).then_inc(
                    sSQRa if u % 2 == 0 else sSQRb, 16)

            def y_store(u):
                sync.wait_ge(sDVE2, d2abs[(u, "y2")])
                sync.dma_start(out=y_out[u * 128:(u + 1) * 128, :],
                               in_=fpB[:, :]).then_inc(sOutS, 16)

            for u in range(NTC_OWN - 1):
                a_store(u)
            sq_load(0)
            a_store(NTC_OWN - 1)
            for u in range(1, NTC_OWN):
                sq_load(u)
                if u >= 5:
                    y_store(u - 5)
            for u in range(NTC_OWN - 5, NTC_OWN):
                y_store(u)
            sync.wait_ge(sOutS, 16 * NTC_OWN)


    return nc


def kernel(L0_attn_hidden_states, L0_attn_Wq, L0_attn_bq, L0_attn_Wk, L0_attn_bk,
           L0_attn_Wv, L0_attn_bv, L0_attn_Wo, L0_ffn_W_base, L0_ffn_b_base,
           L0_ffn_router_W, L0_ffn_A, L0_ffn_Bw):
    f32 = np.float32
    X = np.ascontiguousarray(np.asarray(L0_attn_hidden_states, dtype=f32))
    Wq = np.asarray(L0_attn_Wq, f32)
    Wk = np.asarray(L0_attn_Wk, f32)
    Wv = np.asarray(L0_attn_Wv, f32)
    Wo = np.asarray(L0_attn_Wo, f32)
    Wb = np.asarray(L0_ffn_W_base, f32)
    Rw = np.asarray(L0_ffn_router_W, f32)
    A = np.asarray(L0_ffn_A, f32)
    Bw = np.asarray(L0_ffn_Bw, f32)

    wkv_t = np.ascontiguousarray(np.concatenate([Wk, Wv], axis=0).T)
    wq_t = np.ascontiguousarray(Wq.T)
    wo_t = np.ascontiguousarray(Wo.T)
    wbase_t = np.ascontiguousarray(Wb.T)
    ra_t = np.ascontiguousarray(
        np.concatenate([Rw, A.reshape(128, D)], axis=0).T)
    bcat_t = np.ascontiguousarray(Bw.transpose(0, 2, 1).reshape(128, D))
    brows = np.concatenate([
        np.asarray(L0_attn_bk, f32), np.asarray(L0_attn_bv, f32),
        np.asarray(L0_attn_bq, f32), np.asarray(L0_ffn_b_base, f32)
    ]).reshape(1, 5120)
    ident = np.eye(128, dtype=f32)

    xt_b = [np.ascontiguousarray(X[b].T) for b in range(2)]

    nc = build_nc()
    in_maps = []
    for c in range(8):
        b, j = c // 4, c % 4
        in_maps.append(dict(
            xt_full=xt_b[b],
            xt_own=np.ascontiguousarray(xt_b[b][:, j * TOK:(j + 1) * TOK]),
            x_own=np.ascontiguousarray(X[b, j * TOK:(j + 1) * TOK, :]),
            wkv_t=wkv_t, wq_t=wq_t, wo_t=wo_t, wbase_t=wbase_t,
            ra_t=ra_t, bcat_t=bcat_t, brows=brows, ident=ident,
        ))
    res = run_bass_kernel_spmd(nc, in_maps, core_ids=list(range(8)))
    y = np.empty((2, SEQ, D), dtype=f32)
    for c in range(8):
        b, j = c // 4, c % 4
        y[b, j * TOK:(j + 1) * TOK, :] = res.results[c]["y"]
    return y


# version 18
# speedup vs baseline: 1.8003x; 1.0052x over previous
import numpy as np
from contextlib import ExitStack

import concourse.bass as bass
import concourse.mybir as mybir
from concourse.bass_utils import run_bass_kernel_spmd

F32 = mybir.dt.float32
BF16 = mybir.dt.bfloat16
ALU = mybir.AluOpType
AX = mybir.AxisListType
AF = mybir.ActivationFunctionType

D = 2048
SEQ = 8192
TOK = 2048
NTC_ALL = 64
NTC_OWN = 16
NSLAB_ALL = 32
NSLAB_OWN = 8
ATT_EPS = 1e-6
LN_EPS = 1e-5
WO_CH = 5
WB_CH = 8
N_P2W = WO_CH + WB_CH

peq_q, peq_qh, peq_T7, peq_T15, peq_qkvq = {}, {}, {}, {}, {}
_c = 0
for _u in range(NTC_OWN):
    _c += 1
    peq_qh[_u] = _c
    if _u >= 1:
        _c += 1
        peq_T7[_u - 1] = _c
        _c += 1
        peq_T15[_u - 1] = _c
    _c += 1
    peq_q[_u] = _c
    if _u >= 1:
        for _i in range(4):
            _c += 1
            peq_qkvq[(_u - 1, _i)] = _c
_c += 1
peq_T7[NTC_OWN - 1] = _c
_c += 1
peq_T15[NTC_OWN - 1] = _c
for _i in range(4):
    _c += 1
    peq_qkvq[(NTC_OWN - 1, _i)] = _c

pe2_mm3 = {}
pe2_y1t = {}
pe2_misc, pe2_hmt, pe2_base = {}, {}, {}


def _plan_pe2():
    c = 0
    for v in range(NTC_OWN + 2):
        u = v - 2
        if v <= NTC_OWN - 1:
            if u >= 0:
                for r in range(4):
                    c += 1
                    pe2_y1t[(u, r)] = c
                c += 1
                pe2_misc[u] = c
            c += 1
            pe2_mm3[v] = c
        else:
            for r in range(4):
                c += 1
                pe2_y1t[(u, r)] = c
            c += 1
            pe2_misc[u] = c
        if u >= 0:
            c += 1
            pe2_hmt[u] = c
            c += 1
            pe2_base[u] = c


_plan_pe2()

dqabs = {}
_cq = 0
def _dq_seg(u, names):
    global _cq
    for n in names:
        _cq += 1
        dqabs[(u, n)] = _cq
for _u in range(NTC_OWN):
    _dq_seg(_u, ["min0", "stt0"])
    if _u >= 1:
        _dq_seg(_u - 1, ["E1a", "E1b", "E2c0"])
    _dq_seg(_u, ["min1"])
    if _u >= 1:
        _dq_seg(_u - 1, ["E2c1", "E2c2", "E2c3"])
    _dq_seg(_u, ["stt1", "prod", "red", "eps", "recip", "cast",
                 "sqhat0", "sqhat1"])
_dq_seg(NTC_OWN - 1, ["E1a", "E1b", "E2c0", "E2c1", "E2c2", "E2c3"])
DQ_TOTAL = _cq

_DSEG = {
    "A1": ["s"],
    "A2": ["b0", "b1", "b2", "b3", "aggr", "negmean", "veps", "rv", "y1"],
    "B1": ["lh", "rowmax", "mask8", "hmask"],
    "B2a": ["s2", "c0", "c1", "c2", "c3", "aggr2", "negmean2", "veps2", "rv2"],
    "B2y": ["y2"],
    "YC": ["y1c"],
}
d2abs = {}
_c2 = 0
def _seg(u, ops):
    global _c2
    for name in ops:
        _c2 += 1
        d2abs[(u, name)] = _c2
def _p2_sched():
    order = []
    for v in range(NTC_OWN + 3):
        if 3 <= v <= NTC_OWN + 2:
            order.append(("B2a", v - 3))
        if 1 <= v <= NTC_OWN:
            order.append(("YC", v - 1))
        if 2 <= v <= NTC_OWN + 1:
            order.append(("B1", v - 2))
        if 3 <= v <= NTC_OWN + 2:
            order.append(("B2y", v - 3))
        if v <= NTC_OWN - 1:
            order.append(("A1", v))
            order.append(("A2", v))
    return order
P2_ORDER = _p2_sched()
for _kind, _u in P2_ORDER:
    _seg(_u, _DSEG[_kind])
def _p2_act_sched():
    order = []
    for v in range(NTC_OWN + 3):
        if 2 <= v <= NTC_OWN + 1:
            order.append(("Ey", v - 2))
        if 3 <= v <= NTC_OWN + 2:
            order.append(("Q2", v - 3))
        if 2 <= v <= NTC_OWN + 1:
            order.append(("Eh", v - 2))
        if v <= NTC_OWN - 1:
            order.append(("Aa", v))
    return order
P2_ACT = _p2_act_sched()
a2abs = {}
_c3 = 0
for _kind, _u in P2_ACT:
    if _kind == "Aa":
        _c3 += 1
        a2abs[(_u, "sqrt1")] = _c3
    elif _kind == "Ey":
        for _r in range(4):
            _c3 += 1
            a2abs[(_u, "ey1t%d" % _r)] = _c3
    elif _kind == "Eh":
        _c3 += 1
        a2abs[(_u, "ehmt")] = _c3
    else:
        _c3 += 1
        a2abs[(_u, "sqrt2")] = _c3

AQ_N = 2


def build_nc():
    nc = bass.Bass("TRN2", target_bir_lowering=False)

    xt_full = nc.dram_tensor("xt_full", [D, SEQ], F32, kind="ExternalInput").ap()
    xt_own = nc.dram_tensor("xt_own", [D, TOK], F32, kind="ExternalInput").ap()
    x_own = nc.dram_tensor("x_own", [TOK, D], F32, kind="ExternalInput").ap()
    wkv_t = nc.dram_tensor("wkv_t", [D, 1024], F32, kind="ExternalInput").ap()
    wq_t = nc.dram_tensor("wq_t", [D, D], F32, kind="ExternalInput").ap()
    wo_t = nc.dram_tensor("wo_t", [D, D], F32, kind="ExternalInput").ap()
    wbase_t = nc.dram_tensor("wbase_t", [D, D], F32, kind="ExternalInput").ap()
    ra_t = nc.dram_tensor("ra_t", [D, 136], F32, kind="ExternalInput").ap()
    bcat_t = nc.dram_tensor("bcat_t", [128, D], F32, kind="ExternalInput").ap()
    brows = nc.dram_tensor("brows", [1, 5120], F32, kind="ExternalInput").ap()
    ident = nc.dram_tensor("ident", [128, 128], F32, kind="ExternalInput").ap()
    y_out = nc.dram_tensor("y", [TOK, D], F32, kind="ExternalOutput").ap()
    attn_dram = nc.dram_tensor("attn_scratch", [NTC_OWN, 128, D], BF16, kind="Internal").ap()

    with ExitStack() as ctx:
        sb = lambda name, shape, dt: ctx.enter_context(nc.sbuf_tensor(name, shape, dt))
        sem = lambda name: ctx.enter_context(nc.semaphore(name))

        U0 = sb("u0", [128, 16, 2048], BF16)
        U1 = sb("u1", [128, 32768], BF16)
        wkv_sb = U1[:, 0:16384].rearrange("p (c n) -> p c n", c=16)
        slab = [U1[:, 16384:20480].rearrange("p (c n) -> p c n", c=16),
                U1[:, 20480:24576].rearrange("p (c n) -> p c n", c=16)]
        sk_t = [U1[:, 24576:25088], U1[:, 25088:25600]]
        vb_t = [U1[:, 25600:26112], U1[:, 26112:26624]]
        sq_p1 = U1[:, 26624:28672]
        scr1 = U1[:, 28672:30720].bitcast(F32)
        scr2 = U1[:, 30720:32768].bitcast(F32)
        wbase_sb = U1.rearrange("p (c n) -> p c n", c=16)

        fpA = sb("fpA", [128, D], F32)
        fpB = sb("fpB", [128, D], F32)
        prodb = fpB[:, :].bitcast(BF16)[:, 0:2048]
        kvb = sb("kvb", [128, 512], BF16)
        kvb_v = kvb[:, :].rearrange("p (h e) -> p h e", h=4)
        ksr512 = sb("ksr512", [1, 512], F32)
        ksum_bc = sb("ksum_bc", [128, D], BF16)
        I_bf = sb("I_bf", [128, 128], BF16)
        onesf = sb("onesf", [1, 128], F32)
        onesb = sb("onesb", [128, 1], BF16)
        onesr = sb("onesr", [1, 128], BF16)
        ra_sb = sb("ra_sb", [128, 16, 136], BF16)
        bcat_sb = sb("bcat_sb", [128, D], BF16)

        xt_p2 = [sb("xt_p2a", [128, D], F32), sb("xt_p2b", [128, D], F32),
                 sb("xt_p2c", [128, D], F32)]
        ksrow = fpA[0:1, :]
        sq_p2 = sb("sq_p2", [128, D], BF16)
        sq_p2b = sb("sq_p2b", [128, D], BF16)
        sqp = [sq_p2, sq_p2b]
        sqh = sb("sqh", [128, D], BF16)
        sqhT = sb("sqhT", [128, D], BF16)
        y1b = sb("y1b", [128, D], BF16)
        rowkv = sq_p2[0:1, 0:1024]
        rowq = y1b[0:1, 0:2048]
        rowbb = sqh[0:1, 0:2048]
        lh = sb("lh", [128, 136], F32)
        hmask = sb("hmask", [128, 128], BF16)
        hmaskT = sb("hmaskT", [128, 128], BF16)
        norm16 = sb("norm16", [128, 16], F32)
        normeps = sb("normeps", [128, 16], F32)
        rnorm = sb("rnorm", [128, 16], F32)
        rnbf = sb("rnbf", [128, 16], BF16)
        rowmax = sb("rowmax", [128, 1], F32)
        mask8 = sb("mask8", [128, 8], F32)
        negmean = [sb("negmean%d" % i, [128, 1], F32) for i in range(4)]
        veps = [sb("veps%d" % i, [128, 1], F32) for i in range(4)]
        stdv = [sb("stdv%d" % i, [128, 1], F32) for i in range(4)]
        rstd = [sb("rstd%d" % i, [128, 1], F32) for i in range(4)]
        bnstA = sb("bnstA", [128, 4, 6], F32)
        bnstB = sb("bnstB", [128, 4, 6], F32)
        mvA = sb("mvA", [128, 2], F32)
        mvB = sb("mvB", [128, 2], F32)

        psA = ctx.enter_context(nc.psum_tensor("psA", [128, 2048], F32))
        psB = ctx.enter_context(nc.psum_tensor("psB", [128, 2048], F32))
        psK = [psA[:, 0:512], psA[:, 512:1024]]
        psV = [psA[:, 1024:1536], psA[:, 1536:2048]]
        psKV = psB[:, 0:512]
        psKS = psB[0:1, 512:1024]
        psQ = [psA, psB]

        sWLa = sem("sWLa")
        sWLb = sem("sWLb")
        sXTe = sem("sXTe")
        sXTo = sem("sXTo")
        sWO = [sem("sWOa"), sem("sWOb")]
        sWOC = sem("sWOC")
        sSTG = [sem("sSTGa"), sem("sSTGb"), sem("sSTGc")]
        sWB = sem("sWB")
        sStp = sem("sStp")
        sKVm = sem("sKVm")
        sAcc = sem("sAcc")
        sDVE1 = sem("sDVE1")
        sACT1 = sem("sACT1")
        sFin = sem("sFin")
        sPEq = sem("sPEq")
        sDVEq = sem("sDVEq")
        sACTq = sem("sACTq")
        sSQW = sem("sSQW")
        sXa = sem("sXa")
        sXb = sem("sXb")
        sXc = sem("sXc")
        sSQRa = sem("sSQRa")
        sSQRb = sem("sSQRb")
        sPE2 = sem("sPE2")
        sDVE2 = sem("sDVE2")
        sACT2 = sem("sACT2")
        sY1B = sem("sY1B")
        sOutS = sem("sOutS")

        sX3 = [sXa, sXb, sXc]
        block = ctx.enter_context(nc.Block())

        def slab_wait(eng, g):
            s = sXTe if g % 2 == 0 else sXTo
            eng.wait_ge(s, 16 * (g // 2 + 1))

        @block.tensor
        def _(tensor):
            tensor.wait_ge(sStp, 1)
            tensor.wait_ge(sWLa, 32)

            for t in range(NTC_ALL):
                par = t % 2
                s_idx = t // 2
                toff = (t - 2 * s_idx) * 128
                slab_wait(tensor, s_idx)
                if t >= 2:
                    tensor.wait_ge(sDVE1, 3 * (t - 1))
                for dc in range(16):
                    lhs = slab[s_idx % 2][:, dc, toff:toff + 128]
                    tensor.matmul(psK[par], lhs, wkv_sb[:, dc, 0:512],
                                  start=(dc == 0), stop=False,
                                  skip_group_check=True)
                    tensor.matmul(psV[par], lhs, wkv_sb[:, dc, 512:1024],
                                  start=(dc == 0), stop=False,
                                  skip_group_check=True)
                tensor.matmul(psK[par], onesr[:, :], rowkv[:, 0:512],
                              start=False, stop=True, skip_group_check=True)
                tensor.matmul(psV[par], onesr[:, :], rowkv[:, 512:1024],
                              start=False, stop=True, skip_group_check=True
                              ).then_inc(sKVm, 1)
                if t >= 1:
                    tp = t - 1
                    tensor.wait_ge(sDVE1, 3 * t)
                    for h4 in range(4):
                        tensor.matmul(psKV[:, h4 * 128:(h4 + 1) * 128],
                                      sk_t[tp % 2][:, h4 * 128:(h4 + 1) * 128],
                                      vb_t[tp % 2][:, h4 * 128:(h4 + 1) * 128],
                                      start=(tp == 0 and h4 == 0),
                                      stop=(tp == NTC_ALL - 1 and h4 == 3),
                                      skip_group_check=True)
                    tensor.matmul(psKS, onesb[:, :], sk_t[tp % 2][:, :],
                                  start=(tp == 0), stop=(tp == NTC_ALL - 1),
                                  skip_group_check=True).then_inc(sAcc, 1)
            tp = NTC_ALL - 1
            tensor.wait_ge(sDVE1, 3 * NTC_ALL)
            for h4 in range(4):
                tensor.matmul(psKV[:, h4 * 128:(h4 + 1) * 128],
                              sk_t[tp % 2][:, h4 * 128:(h4 + 1) * 128],
                              vb_t[tp % 2][:, h4 * 128:(h4 + 1) * 128],
                              start=False, stop=(h4 == 3), skip_group_check=True)
            tensor.matmul(psKS, onesb[:, :], sk_t[tp % 2][:, :],
                          start=False, stop=True, skip_group_check=True
                          ).then_inc(sAcc, 1)

            tensor.wait_ge(sFin, 3)
            for j in range(4):
                mm = tensor.matmul(psA[:, j * 512:(j + 1) * 512], onesf[:, :],
                                   ksrow[:, j * 512:(j + 1) * 512], start=True, stop=True)
            mm.then_inc(sFin, 1)

            tensor.wait_ge(sWLb, 16 * 12)
            tensor.wait_ge(sFin, 5)

            def t_group(u):
                par = u % 2
                for h in range(16):
                    if h == 0:
                        tensor.wait_ge(sDVEq, dqabs[(u, "sqhat0")])
                    elif h == 8:
                        tensor.wait_ge(sDVEq, dqabs[(u, "sqhat1")])
                    mm = tensor.matmul(psQ[par][:, h * 128:(h + 1) * 128],
                                       sqh[:, h * 128:(h + 1) * 128], I_bf[:, :],
                                       start=True, stop=True, skip_group_check=True)
                    if h == 7 or h == 15:
                        mm.then_inc(sPEq, 1)

            def qkv_group(u):
                par = u % 2
                for i in range(4):
                    if i == 0:
                        tensor.wait_ge(sDVEq, dqabs[(u, "E1a")])
                    elif i == 2:
                        tensor.wait_ge(sDVEq, dqabs[(u, "E1b")])
                    for h in range(4 * i, 4 * i + 4):
                        mm = tensor.matmul(psQ[par][:, h * 128:(h + 1) * 128],
                                           kvb_v[:, h // 4, :],
                                           sqhT[:, h * 128:(h + 1) * 128],
                                           start=True, stop=True,
                                           skip_group_check=True)
                    mm.then_inc(sPEq, 1)

            for u in range(NTC_OWN):
                par = u % 2
                s_idx = u // 2
                toff = (u - 2 * s_idx) * 128
                slab_wait(tensor, NSLAB_ALL + s_idx)
                for j in range(4):
                    if j == 3 and u >= 1:
                        t_group(u - 1)
                    if u >= 2:
                        tensor.wait_ge(sDVEq, dqabs[(u - 2, "E2c%d" % j)])
                    for dc in range(16):
                        lhs = slab[s_idx % 2][:, dc, toff:toff + 128]
                        tensor.matmul(psQ[par][:, j * 512:(j + 1) * 512], lhs,
                                      U0[:, dc, j * 512:(j + 1) * 512],
                                      start=(dc == 0), stop=False,
                                      skip_group_check=True)
                    mm = tensor.matmul(psQ[par][:, j * 512:(j + 1) * 512],
                                       onesr[:, :], rowq[:, j * 512:(j + 1) * 512],
                                       start=False, stop=True, skip_group_check=True)
                    if j == 1 or j == 3:
                        mm.then_inc(sPEq, 1)
                if u >= 1:
                    qkv_group(u - 1)
            t_group(NTC_OWN - 1)
            qkv_group(NTC_OWN - 1)

            tensor.wait_ge(sDVEq, dqabs[(NTC_OWN - 2, "E2c3")])

            def y1t_round(u, r):
                if r == 0:
                    if u == 0:
                        tensor.wait_ge(sDVEq, DQ_TOTAL)
                    tensor.wait_ge(sDVE2, d2abs[(u, "y1c")])
                    if u >= 1:
                        tensor.wait_ge(sDVE2, d2abs[(u - 1, "s2")])
                else:
                    if r >= 2:
                        tensor.wait_ge(sACT2, a2abs[(u, "ey1t%d" % (r - 2))])
                for k in range(4):
                    dc = 4 * r + k
                    base_col = (r % 2) * 512 + k * 128
                    mm = tensor.matmul(psB[:, base_col:base_col + 128],
                                       y1b[:, dc * 128:(dc + 1) * 128], I_bf[:, :],
                                       start=True, stop=True, skip_group_check=True)
                mm.then_inc(sPE2, 1)

            def misc_part(u, half):
                tensor.wait_ge(sACT2, a2abs[(u, "ey1t%d" % (half + 2))])
                for dc in range(8 * half, 8 * half + 8):
                    mm = tensor.matmul(psB[:, 0:136],
                                       sqhT[:, dc * 128:(dc + 1) * 128],
                                       ra_sb[:, dc, :],
                                       start=(dc == 0), stop=(dc == 15),
                                       skip_group_check=True)
                if half == 1:
                    mm.then_inc(sPE2, 1)

            def p2_iter(v):
                u = v - 2
                if v <= NTC_OWN - 1:
                    tensor.wait_ge(sSQRa if v % 2 == 0 else sSQRb, 16 * (v // 2 + 1))
                    if v >= 1:
                        tensor.wait_ge(sDVE2, d2abs[(v - 1, "s")])
                    sq = sqp[v % 2]
                    for h in range(16):
                        if v == 0:
                            if h < 10 and h % 2 == 0:
                                i = h // 2
                                tensor.wait_ge(sWO[i % 2], 16 * (i // 2 + 1))
                            elif h >= 10:
                                tensor.wait_ge(sWOC, h - 9)
                        elif v == 1 and h == 0:
                            tensor.wait_ge(sWO[0], 48)
                            tensor.wait_ge(sWO[1], 32)
                            tensor.wait_ge(sWOC, 6)
                        for j in range(4):
                            mm = tensor.matmul(psA[:, j * 512:(j + 1) * 512],
                                               sq[:, h * 128:(h + 1) * 128],
                                               U0[:, h, j * 512:(j + 1) * 512],
                                               start=(h == 0), stop=(h == 15),
                                               skip_group_check=True)
                        if u >= 0:
                            if h == 3:
                                y1t_round(u, 0)
                            elif h == 5:
                                y1t_round(u, 1)
                            elif h == 7:
                                y1t_round(u, 2)
                            elif h == 9:
                                y1t_round(u, 3)
                            elif h == 10:
                                misc_part(u, 0)
                            elif h == 12:
                                misc_part(u, 1)
                    mm.then_inc(sPE2, 1)
                else:
                    for r in range(4):
                        y1t_round(u, r)
                    misc_part(u, 0)
                    misc_part(u, 1)
                if u >= 0:
                    tensor.wait_ge(sDVE2, d2abs[(u, "hmask")])
                    tensor.matmul(psB[:, 512:640], hmask[:, :], I_bf[:, :],
                                  start=True, stop=True,
                                  skip_group_check=True).then_inc(sPE2, 1)
                    tensor.wait_ge(sDVE2, d2abs[(u, "lh")])
                    if u == 0:
                        tensor.wait_ge(sWB, 16 * (WB_CH + 1))
                    for j in range(4):
                        for dc in range(16):
                            tensor.matmul(psB[:, j * 512:(j + 1) * 512],
                                          sqhT[:, dc * 128:(dc + 1) * 128],
                                          wbase_sb[:, dc, j * 512:(j + 1) * 512],
                                          start=(dc == 0), stop=False,
                                          skip_group_check=True)
                        tensor.matmul(psB[:, j * 512:(j + 1) * 512], onesr[:, :],
                                      rowbb[:, j * 512:(j + 1) * 512],
                                      start=False, stop=False, skip_group_check=True)
                        if j == 0:
                            tensor.wait_ge(sACT2, a2abs[(u, "ehmt")])
                        mm = tensor.matmul(psB[:, j * 512:(j + 1) * 512], hmaskT[:, :],
                                           bcat_sb[:, j * 512:(j + 1) * 512],
                                           start=False, stop=True,
                                           skip_group_check=True)
                    mm.then_inc(sPE2, 1)

            for v in range(NTC_OWN + 2):
                p2_iter(v)

        @block.vector
        def _(vector):
            vector.memset(onesf[:, :], 1.0)
            vector.memset(onesb[:, :], 1.0)
            vector.memset(onesr[:, :], 1.0).then_inc(sStp, 1)

            dcnt = [0]

            def op(ins):
                ins.then_inc(sDVE1, 1)
                dcnt[0] += 1
                vector.wait_ge(sDVE1, dcnt[0])
                return ins

            for t in range(NTC_ALL):
                par = t % 2
                vector.wait_ge(sKVm, t + 1)
                if t >= 1:
                    vector.wait_ge(sACT1, t)
                op(vector.tensor_scalar(scr1[:, 0:512], psK[par], 0.0, None, ALU.min))
                vector.wait_ge(sACT1, t + 1)
                if t >= 2:
                    vector.wait_ge(sAcc, t - 1)
                op(vector.scalar_tensor_tensor(sk_t[par][:, :], psK[par], 0.0,
                                               scr2[:, 0:512], ALU.max, ALU.add))
                op(vector.tensor_copy(vb_t[par][:, :], psV[par]))

            vector.wait_ge(sAcc, NTC_ALL)
            vector.tensor_copy(kvb[:, :], psKV).then_inc(sFin, 1)
            vector.tensor_copy(ksr512[:, :], psKS).then_inc(sFin, 1)
            vector.wait_ge(sFin, 2)
            ks_in = ksr512[:, :].rearrange("a (h d) -> a h d", h=4).unsqueeze(2) \
                .broadcast_to([1, 4, 4, 128])
            vector.tensor_copy(ksrow.rearrange("a (h r d) -> a h r d", h=4, r=4),
                               ks_in).then_inc(sFin, 1)
            vector.wait_ge(sFin, 4)
            vector.tensor_copy(ksum_bc[:, :], psA[:, :]).then_inc(sFin, 1)

            dq = [0]

            def opq(ins):
                ins.then_inc(sDVEq, 1)
                dq[0] += 1
                vector.wait_ge(sDVEq, dq[0])
                return ins

            sq3 = sq_p1.rearrange("p (h d) -> p h d", h=16)
            prod3 = prodb.rearrange("p (h d) -> p h d", h=16)
            ks3 = ksum_bc[:, :].rearrange("p (h d) -> p h d", h=16)
            sqh3 = sqh[:, :].rearrange("p (h d) -> p h d", h=16)
            rn3 = rnbf[:, :].unsqueeze(2).broadcast_to([128, 16, 128])

            def e1seg(pw):
                vector.wait_ge(sPEq, peq_T7[pw])
                opq(vector.tensor_copy(sqhT[:, 0:1024], psQ[pw % 2][:, 0:1024]))
                vector.wait_ge(sPEq, peq_T15[pw])
                opq(vector.tensor_copy(sqhT[:, 1024:2048], psQ[pw % 2][:, 1024:2048]))

            def e2seg(pw, quarters):
                for i in quarters:
                    vector.wait_ge(sPEq, peq_qkvq[(pw, i)])
                    if i == 0 and pw >= 1:
                        vector.wait_ge(sSQW, 16 * pw)
                    opq(vector.tensor_copy(sqp[pw % 2][:, i * 512:(i + 1) * 512],
                                           psQ[pw % 2][:, i * 512:(i + 1) * 512]))

            for u in range(NTC_OWN):
                par = u % 2
                vector.wait_ge(sPEq, peq_qh[u])
                if u >= 1:
                    vector.wait_ge(sACTq, 2 * u)
                opq(vector.tensor_scalar(scr1[:, :], psQ[par][:, 0:1024], 0.0,
                                         None, ALU.min))
                vector.wait_ge(sACTq, 2 * u + 1)
                opq(vector.scalar_tensor_tensor(sq_p1[:, 0:1024], psQ[par][:, 0:1024],
                                                0.0, scr2[:, :], ALU.max, ALU.add))
                if u >= 1:
                    e1seg(u - 1)
                    e2seg(u - 1, [0])
                vector.wait_ge(sPEq, peq_q[u])
                opq(vector.tensor_scalar(scr1[:, :], psQ[par][:, 1024:2048], 0.0,
                                         None, ALU.min))
                if u >= 1:
                    e2seg(u - 1, [1, 2, 3])
                vector.wait_ge(sACTq, 2 * u + 2)
                opq(vector.scalar_tensor_tensor(sq_p1[:, 1024:2048],
                                                psQ[par][:, 1024:2048],
                                                0.0, scr2[:, :], ALU.max, ALU.add))
                opq(vector.tensor_tensor(prod3, sq3, ks3, ALU.mult))
                opq(vector.tensor_reduce(norm16[:, :], prod3, AX.X, ALU.add))
                opq(vector.tensor_scalar(normeps[:, :], norm16[:, :], ATT_EPS,
                                         None, ALU.add))
                opq(vector.reciprocal(rnorm[:, :], normeps[:, :]))
                opq(vector.tensor_copy(rnbf[:, :], rnorm[:, :]))
                opq(vector.tensor_tensor(sqh3[:, 0:8, :], sq3[:, 0:8, :],
                                         rn3[:, 0:8, :], ALU.mult))
                opq(vector.tensor_tensor(sqh3[:, 8:16, :], sq3[:, 8:16, :],
                                         rn3[:, 8:16, :], ALU.mult))
            e1seg(NTC_OWN - 1)
            e2seg(NTC_OWN - 1, [0, 1, 2, 3])
            for k, dc in enumerate([10, 11, 12, 13, 14, 15]):
                vector.wait_ge(sSTG[k % 3], 16 * (k // 3 + 1))
                vector.tensor_copy(U0[:, dc, :],
                                   xt_p2[k % 3][:, :]).then_inc(sWOC, 1)

            dd = [0]

            def op2(ins):
                ins.then_inc(sDVE2, 1)
                dd[0] += 1
                vector.wait_ge(sDVE2, dd[0])
                return ins

            lh3 = lh[:, 8:136].rearrange("p (e r) -> p e r", e=8)
            hm3 = hmask[:, :].rearrange("p (e r) -> p e r", e=8)
            m8b = mask8[:, :].unsqueeze(2).broadcast_to([128, 8, 16])

            def seg_a1(u):
                vector.wait_ge(sPE2, pe2_mm3[u])
                vector.wait_ge(sX3[u % 3], 16 * (u // 3 + 1))
                op2(vector.scalar_tensor_tensor(fpA[:, :], psA[:, :], 1.0,
                                                xt_p2[u % 3][:, :], ALU.mult, ALU.add))

            def seg_a2(u):
                p = u % 2
                for g in range(4):
                    op2(vector.bn_stats(bnstA[:, g, :], fpA[:, g * 512:(g + 1) * 512]))
                op2(vector.bn_aggr(mvA[:, :], bnstA[:, :, :].rearrange("p a b -> p (a b)")))
                op2(vector.tensor_scalar(negmean[p][:, :], mvA[:, 0:1], -1.0, None, ALU.mult))
                op2(vector.tensor_scalar(veps[p][:, :], mvA[:, 1:2], LN_EPS, None, ALU.add))
                op2(vector.reciprocal(stdv[p][:, :], veps[p][:, :]))
                vector.wait_ge(sACT2, a2abs[(u, "sqrt1")])
                op2(vector.tensor_scalar(xt_p2[u % 3][:, :], fpA[:, :], negmean[p][:, :],
                                         rstd[p][:, :], ALU.add, ALU.mult))

            def seg_b1(u):
                vector.wait_ge(sPE2, pe2_misc[u])
                op2(vector.tensor_copy(lh[:, :], psB[:, 0:136]))
                op2(vector.reduce_max(rowmax[:, :], lh[:, 0:8], AX.X))
                op2(vector.tensor_scalar(mask8[:, :], lh[:, 0:8], rowmax[:, :],
                                         2.0, ALU.is_ge, ALU.mult))
                op2(vector.tensor_tensor(hm3, lh3, m8b, ALU.mult))

            def seg_b2a(u):
                p = 2 + u % 2
                vector.wait_ge(sPE2, pe2_base[u])
                op2(vector.scalar_tensor_tensor(fpA[:, :], psB[:, :], 1.0,
                                                xt_p2[u % 3][:, :], ALU.mult, ALU.add))
                for g in range(4):
                    op2(vector.bn_stats(bnstB[:, g, :], fpA[:, g * 512:(g + 1) * 512]))
                op2(vector.bn_aggr(mvB[:, :], bnstB[:, :, :].rearrange("p a b -> p (a b)")))
                op2(vector.tensor_scalar(negmean[p][:, :], mvB[:, 0:1], -1.0, None, ALU.mult))
                op2(vector.tensor_scalar(veps[p][:, :], mvB[:, 1:2], LN_EPS, None, ALU.add))
                op2(vector.reciprocal(stdv[p][:, :], veps[p][:, :]))

            def seg_yc(u):
                if u >= 1:
                    vector.wait_ge(sPE2, pe2_y1t[(u - 1, 3)])
                op2(vector.tensor_copy(y1b[:, :], xt_p2[u % 3][:, :]))

            def seg_b2y(u):
                p = 2 + u % 2
                vector.wait_ge(sACT2, a2abs[(u, "sqrt2")])
                if u >= 1:
                    vector.wait_ge(sOutS, 16 * u)
                op2(vector.tensor_scalar(fpB[:, :], fpA[:, :], negmean[p][:, :],
                                         rstd[p][:, :], ALU.add, ALU.mult))

            _dseg_fns = {"A1": seg_a1, "A2": seg_a2, "B1": seg_b1,
                         "B2a": seg_b2a, "B2y": seg_b2y, "YC": seg_yc}
            for kind, u in P2_ORDER:
                _dseg_fns[kind](u)

        @block.scalar
        def _(scalar):
            scalar.preload_act_table(AF.Exp)
            for t in range(NTC_ALL):
                scalar.wait_ge(sDVE1, 3 * t + 1)
                scalar.activation(scr2[:, 0:512], scr1[:, 0:512], AF.Exp).then_inc(sACT1, 1)
            for u in range(NTC_OWN):
                for j in range(2):
                    scalar.wait_ge(sDVEq, dqabs[(u, "min%d" % j)])
                    scalar.activation(scr2[:, :], scr1[:, :], AF.Exp).then_inc(sACTq, 1)
            scalar.preload_act_table(AF.Sqrt)

            def x_load(v):
                if v == 0:
                    scalar.wait_ge(sFin, 4)
                if v < 3:
                    scalar.wait_ge(sWOC, 4 + v)
                scalar.dma_start(out=xt_p2[v % 3][:, :],
                                 in_=x_own[v * 128:(v + 1) * 128, :]).then_inc(
                                     sX3[v % 3], 16)

            x_load(0)
            x_load(1)
            x_load(2)
            for kind, u in P2_ACT:
                if kind == "Aa":
                    p = u % 2
                    scalar.wait_ge(sDVE2, d2abs[(u, "rv")])
                    scalar.activation(rstd[p][:, :], stdv[p][:, :], AF.Sqrt).then_inc(sACT2, 1)
                elif kind == "Ey":
                    for r in range(4):
                        scalar.wait_ge(sPE2, pe2_y1t[(u, r)])
                        scalar.copy(sqhT[:, r * 512:(r + 1) * 512],
                                    psB[:, (r % 2) * 512:(r % 2) * 512 + 512]).then_inc(sACT2, 1)
                elif kind == "Eh":
                    scalar.wait_ge(sPE2, pe2_hmt[u])
                    scalar.copy(hmaskT[:, :], psB[:, 512:640]).then_inc(sACT2, 1)
                else:
                    p = 2 + u % 2
                    scalar.wait_ge(sDVE2, d2abs[(u, "rv2")])
                    scalar.activation(rstd[p][:, :], stdv[p][:, :], AF.Sqrt).then_inc(sACT2, 1)
                    if u + 3 < NTC_OWN:
                        x_load(u + 3)

        @block.gpsimd
        def _(gpsimd):
            wq_src = wq_t.rearrange("(c p) n -> p c n", p=128)
            gpsimd.dma_start(out=wkv_sb, in_=wkv_t.rearrange("(c p) n -> p c n", p=128)
                             ).then_inc(sWLa, 16)
            gpsimd.dma_start(out=rowkv, in_=brows[:, 0:1024]).then_inc(sWLa, 16)
            gpsimd.wait_ge(sWLa, 32)
            wlb_small = [
                (rowq, brows[:, 1024:3072]),
                (I_bf[:, :], ident),
                (ra_sb[:, :, :], ra_t.rearrange("(c p) n -> p c n", p=128)),
                (bcat_sb[:, :], bcat_t),
            ]
            wlb_jobs = wlb_small + [
                (U0[:, 2 * i:2 * i + 2, :], wq_src[:, 2 * i:2 * i + 2, :])
                for i in range(8)]
            wlb_i = [0]

            def emit_wlb():
                if wlb_i[0] < len(wlb_jobs):
                    dst, s_ = wlb_jobs[wlb_i[0]]
                    gpsimd.dma_start(out=dst, in_=s_).then_inc(sWLb, 16)
                    wlb_i[0] += 1

            for s in range(NSLAB_ALL):
                if s >= 2:
                    gpsimd.wait_ge(sKVm, 2 * (s - 1))
                src = xt_full.rearrange("(c p) t -> p c t", p=128)[:, :, s * 256:(s + 1) * 256]
                gpsimd.dma_start(out=slab[s % 2], in_=src).then_inc(
                    sXTe if s % 2 == 0 else sXTo, 16)
                if 2 <= s < 6 or (s >= 8 and s % 2 == 0):
                    emit_wlb()
            while wlb_i[0] < len(wlb_jobs):
                emit_wlb()
            wo_src = wo_t.rearrange("(c p) n -> p c n", p=128)
            wb_src = wbase_t.rearrange("(c p) n -> p c n", p=128)
            for s in range(NSLAB_OWN):
                if s < 2:
                    gpsimd.wait_ge(sAcc, NTC_ALL)
                else:
                    gpsimd.wait_ge(sPEq, peq_q[2 * (s - 1)])
                g = NSLAB_ALL + s
                src = xt_own.rearrange("(c p) t -> p c t", p=128)[:, :, s * 256:(s + 1) * 256]
                gpsimd.dma_start(out=slab[g % 2], in_=src).then_inc(
                    sXTe if g % 2 == 0 else sXTo, 16)
                if 2 <= s < 6:
                    i = s - 2
                    if i == 0:
                        gpsimd.wait_ge(sKVm, NTC_ALL)
                    gpsimd.dma_start(out=wbase_sb[:, 2 * i:2 * i + 2, :],
                                     in_=wb_src[:, 2 * i:2 * i + 2, :]).then_inc(sWB, 16)
            gpsimd.wait_ge(sPEq, peq_q[NTC_OWN - 1])
            for i in range(WO_CH):
                if i >= 2:
                    gpsimd.wait_ge(sWO[i % 2], 16 * (i // 2))
                gpsimd.dma_start(out=U0[:, 2 * i:2 * i + 2, :],
                                 in_=wo_src[:, 2 * i:2 * i + 2, :]).then_inc(sWO[i % 2], 16)
            gpsimd.wait_ge(sDVEq, DQ_TOTAL)
            gpsimd.wait_ge(sACTq, 2 * NTC_OWN)
            gpsimd.wait_ge(sSQW, 16 * NTC_OWN)
            gpsimd.wait_ge(sPEq, peq_T15[NTC_OWN - 1])
            for i in range(4, WB_CH):
                gpsimd.dma_start(out=wbase_sb[:, 2 * i:2 * i + 2, :],
                                 in_=wb_src[:, 2 * i:2 * i + 2, :]).then_inc(sWB, 16)
            gpsimd.dma_start(out=rowbb, in_=brows[:, 3072:5120]).then_inc(sWB, 16)


        @block.sync
        def _(sync):
            def a_store(u):
                sync.wait_ge(sDVEq, dqabs[(u, "E2c3")])
                sync.dma_start(out=attn_dram[u], in_=sqp[u % 2][:, :]).then_inc(sSQW, 16)

            def sq_load(u):
                if u == 0:
                    sync.wait_ge(sSQW, 16 * (NTC_OWN - 1))
                elif u == 1:
                    sync.wait_ge(sSQW, 16 * NTC_OWN)
                else:
                    sync.wait_ge(sPE2, pe2_mm3[u - 2])
                sync.dma_start(out=sqp[u % 2][:, :], in_=attn_dram[u]).then_inc(
                    sSQRa if u % 2 == 0 else sSQRb, 16)

            def y_store(u):
                sync.wait_ge(sDVE2, d2abs[(u, "y2")])
                sync.dma_start(out=y_out[u * 128:(u + 1) * 128, :],
                               in_=fpB[:, :]).then_inc(sOutS, 16)

            wo_src_s = wo_t.rearrange("(c p) n -> p c n", p=128)
            for k in range(3):
                sync.dma_start(out=xt_p2[k][:, :],
                               in_=wo_src_s[:, 10 + k, :]).then_inc(sSTG[k], 16)
            for u in range(NTC_OWN - 1):
                a_store(u)
            sq_load(0)
            a_store(NTC_OWN - 1)
            sq_load(1)
            for k in range(3):
                sync.wait_ge(sWOC, k + 1)
                sync.dma_start(out=xt_p2[k][:, :],
                               in_=wo_src_s[:, 13 + k, :]).then_inc(sSTG[k], 16)
            for u in range(2, NTC_OWN):
                sq_load(u)
                if u >= 5:
                    y_store(u - 5)
            for u in range(NTC_OWN - 5, NTC_OWN):
                y_store(u)
            sync.wait_ge(sOutS, 16 * NTC_OWN)


    return nc


def kernel(L0_attn_hidden_states, L0_attn_Wq, L0_attn_bq, L0_attn_Wk, L0_attn_bk,
           L0_attn_Wv, L0_attn_bv, L0_attn_Wo, L0_ffn_W_base, L0_ffn_b_base,
           L0_ffn_router_W, L0_ffn_A, L0_ffn_Bw):
    f32 = np.float32
    X = np.ascontiguousarray(np.asarray(L0_attn_hidden_states, dtype=f32))
    Wq = np.asarray(L0_attn_Wq, f32)
    Wk = np.asarray(L0_attn_Wk, f32)
    Wv = np.asarray(L0_attn_Wv, f32)
    Wo = np.asarray(L0_attn_Wo, f32)
    Wb = np.asarray(L0_ffn_W_base, f32)
    Rw = np.asarray(L0_ffn_router_W, f32)
    A = np.asarray(L0_ffn_A, f32)
    Bw = np.asarray(L0_ffn_Bw, f32)

    wkv_t = np.ascontiguousarray(np.concatenate([Wk, Wv], axis=0).T)
    wq_t = np.ascontiguousarray(Wq.T)
    wo_t = np.ascontiguousarray(Wo.T)
    wbase_t = np.ascontiguousarray(Wb.T)
    ra_t = np.ascontiguousarray(
        np.concatenate([Rw, A.reshape(128, D)], axis=0).T)
    bcat_t = np.ascontiguousarray(Bw.transpose(0, 2, 1).reshape(128, D))
    brows = np.concatenate([
        np.asarray(L0_attn_bk, f32), np.asarray(L0_attn_bv, f32),
        np.asarray(L0_attn_bq, f32), np.asarray(L0_ffn_b_base, f32)
    ]).reshape(1, 5120)
    ident = np.eye(128, dtype=f32)

    xt_b = [np.ascontiguousarray(X[b].T) for b in range(2)]

    nc = build_nc()
    in_maps = []
    for c in range(8):
        b, j = c // 4, c % 4
        in_maps.append(dict(
            xt_full=xt_b[b],
            xt_own=np.ascontiguousarray(xt_b[b][:, j * TOK:(j + 1) * TOK]),
            x_own=np.ascontiguousarray(X[b, j * TOK:(j + 1) * TOK, :]),
            wkv_t=wkv_t, wq_t=wq_t, wo_t=wo_t, wbase_t=wbase_t,
            ra_t=ra_t, bcat_t=bcat_t, brows=brows, ident=ident,
        ))
    res = run_bass_kernel_spmd(nc, in_maps, core_ids=list(range(8)))
    y = np.empty((2, SEQ, D), dtype=f32)
    for c in range(8):
        b, j = c // 4, c % 4
        y[b, j * TOK:(j + 1) * TOK, :] = res.results[c]["y"]
    return y


# version 19
# speedup vs baseline: 1.8159x; 1.0087x over previous
import numpy as np
from contextlib import ExitStack

import concourse.bass as bass
import concourse.mybir as mybir
from concourse.bass_utils import run_bass_kernel_spmd

F32 = mybir.dt.float32
BF16 = mybir.dt.bfloat16
ALU = mybir.AluOpType
AX = mybir.AxisListType
AF = mybir.ActivationFunctionType

D = 2048
SEQ = 8192
TOK = 2048
NTC_ALL = 64
NTC_OWN = 16
NSLAB_ALL = 32
NSLAB_OWN = 8
ATT_EPS = 1e-6
LN_EPS = 1e-5
WO_CH = 5
WB_CH = 8
N_P2W = WO_CH + WB_CH

peq_q, peq_qh, peq_T7, peq_T15, peq_qkvq = {}, {}, {}, {}, {}
_c = 0
for _u in range(NTC_OWN):
    _c += 1
    peq_qh[_u] = _c
    if _u >= 1:
        _c += 1
        peq_T7[_u - 1] = _c
        _c += 1
        peq_T15[_u - 1] = _c
    _c += 1
    peq_q[_u] = _c
    if _u >= 1:
        for _i in range(4):
            _c += 1
            peq_qkvq[(_u - 1, _i)] = _c
_c += 1
peq_T7[NTC_OWN - 1] = _c
_c += 1
peq_T15[NTC_OWN - 1] = _c
for _i in range(4):
    _c += 1
    peq_qkvq[(NTC_OWN - 1, _i)] = _c

pe2_mm3 = {}
pe2_y1t = {}
pe2_misc, pe2_hmt, pe2_base = {}, {}, {}


def _plan_pe2():
    c = 0
    for v in range(NTC_OWN + 2):
        u = v - 2
        if v <= NTC_OWN - 1:
            if u >= 0:
                for r in range(4):
                    c += 1
                    pe2_y1t[(u, r)] = c
                c += 1
                pe2_misc[u] = c
            c += 1
            pe2_mm3[v] = c
        else:
            for r in range(4):
                c += 1
                pe2_y1t[(u, r)] = c
            c += 1
            pe2_misc[u] = c
        if u >= 0:
            c += 1
            pe2_hmt[u] = c
            c += 1
            pe2_base[u] = c


_plan_pe2()

dqabs = {}
_cq = 0
def _dq_seg(u, names):
    global _cq
    for n in names:
        _cq += 1
        dqabs[(u, n)] = _cq
for _u in range(NTC_OWN):
    _dq_seg(_u, ["min0", "stt0"])
    if _u >= 1:
        _dq_seg(_u - 1, ["E1a", "E1b", "E2c0"])
    _dq_seg(_u, ["min1"])
    if _u >= 1:
        _dq_seg(_u - 1, ["E2c1", "E2c2", "E2c3"])
    _dq_seg(_u, ["stt1", "prod", "red", "eps", "recip", "cast",
                 "sqhat0", "sqhat1"])
_dq_seg(NTC_OWN - 1, ["E1a", "E1b", "E2c0", "E2c1", "E2c2", "E2c3"])
DQ_TOTAL = _cq

_DSEG = {
    "A1": ["s"],
    "A2": ["b0", "b1", "b2", "b3", "aggr", "negmean", "veps", "rv", "y1"],
    "B1": ["lh", "rowmax", "mask8", "hmask"],
    "B2a": ["s2", "c0", "c1", "c2", "c3", "aggr2", "negmean2", "veps2", "rv2"],
    "B2y": ["y2"],
    "YC": ["y1c"],
}
d2abs = {}
_c2 = 0
def _seg(u, ops):
    global _c2
    for name in ops:
        _c2 += 1
        d2abs[(u, name)] = _c2
def _p2_sched():
    order = []
    for v in range(NTC_OWN + 3):
        if 3 <= v <= NTC_OWN + 2:
            order.append(("B2a", v - 3))
        if 1 <= v <= NTC_OWN:
            order.append(("YC", v - 1))
        if 2 <= v <= NTC_OWN + 1:
            order.append(("B1", v - 2))
        if 3 <= v <= NTC_OWN + 2:
            order.append(("B2y", v - 3))
        if v <= NTC_OWN - 1:
            order.append(("A1", v))
            order.append(("A2", v))
    return order
P2_ORDER = _p2_sched()
for _kind, _u in P2_ORDER:
    _seg(_u, _DSEG[_kind])
def _p2_act_sched():
    order = []
    for v in range(NTC_OWN + 3):
        if 2 <= v <= NTC_OWN + 1:
            order.append(("Ey", v - 2))
        if 3 <= v <= NTC_OWN + 2:
            order.append(("Q2", v - 3))
        if 2 <= v <= NTC_OWN + 1:
            order.append(("Eh", v - 2))
        if v <= NTC_OWN - 1:
            order.append(("Aa", v))
    return order
P2_ACT = _p2_act_sched()
a2abs = {}
_c3 = 0
for _kind, _u in P2_ACT:
    if _kind == "Aa":
        _c3 += 1
        a2abs[(_u, "sqrt1")] = _c3
    elif _kind == "Ey":
        for _r in range(4):
            _c3 += 1
            a2abs[(_u, "ey1t%d" % _r)] = _c3
    elif _kind == "Eh":
        _c3 += 1
        a2abs[(_u, "ehmt")] = _c3
    else:
        _c3 += 1
        a2abs[(_u, "sqrt2")] = _c3

AQ_N = 2


def build_nc():
    nc = bass.Bass("TRN2", target_bir_lowering=False)

    xt_full = nc.dram_tensor("xt_full", [D, SEQ], F32, kind="ExternalInput").ap()
    xt_own = nc.dram_tensor("xt_own", [D, TOK], F32, kind="ExternalInput").ap()
    x_own = nc.dram_tensor("x_own", [TOK, D], F32, kind="ExternalInput").ap()
    wkv_t = nc.dram_tensor("wkv_t", [D, 1024], F32, kind="ExternalInput").ap()
    wq_t = nc.dram_tensor("wq_t", [D, D], F32, kind="ExternalInput").ap()
    wo_t = nc.dram_tensor("wo_t", [D, D], F32, kind="ExternalInput").ap()
    wbase_t = nc.dram_tensor("wbase_t", [D, D], F32, kind="ExternalInput").ap()
    ra_t = nc.dram_tensor("ra_t", [D, 136], F32, kind="ExternalInput").ap()
    bcat_t = nc.dram_tensor("bcat_t", [128, D], F32, kind="ExternalInput").ap()
    brows = nc.dram_tensor("brows", [1, 5120], F32, kind="ExternalInput").ap()
    ident = nc.dram_tensor("ident", [128, 128], F32, kind="ExternalInput").ap()
    y_out = nc.dram_tensor("y", [TOK, D], F32, kind="ExternalOutput").ap()
    attn_dram = nc.dram_tensor("attn_scratch", [NTC_OWN, 128, D], BF16, kind="Internal").ap()

    with ExitStack() as ctx:
        sb = lambda name, shape, dt: ctx.enter_context(nc.sbuf_tensor(name, shape, dt))
        sem = lambda name: ctx.enter_context(nc.semaphore(name))

        U0 = sb("u0", [128, 16, 2048], BF16)
        U1 = sb("u1", [128, 32768], BF16)
        wkv_sb = U1[:, 0:16384].rearrange("p (c n) -> p c n", c=16)
        slab = [U1[:, 16384:20480].rearrange("p (c n) -> p c n", c=16),
                U1[:, 20480:24576].rearrange("p (c n) -> p c n", c=16)]
        sk_t = [U1[:, 24576:25088], U1[:, 25088:25600]]
        vb_t = [U1[:, 25600:26112], U1[:, 26112:26624]]
        sq_p1 = U1[:, 26624:28672]
        scr1 = U1[:, 28672:30720].bitcast(F32)
        scr2 = U1[:, 30720:32768].bitcast(F32)
        wbase_sb = U1.rearrange("p (c n) -> p c n", c=16)

        fpA = sb("fpA", [128, D], F32)
        fpB = sb("fpB", [128, D], F32)
        prodb = fpB[:, :].bitcast(BF16)[:, 0:2048]
        kvb = sb("kvb", [128, 512], BF16)
        kvb_v = kvb[:, :].rearrange("p (h e) -> p h e", h=4)
        ksr512 = sb("ksr512", [1, 512], F32)
        ksum_bc = sb("ksum_bc", [128, D], BF16)
        I_bf = sb("I_bf", [128, 128], BF16)
        onesf = sb("onesf", [1, 128], F32)
        onesb = sb("onesb", [128, 1], BF16)
        onesr = sb("onesr", [1, 128], BF16)
        ra_sb = sb("ra_sb", [128, 16, 136], BF16)
        bcat_sb = sb("bcat_sb", [128, D], BF16)

        xt_p2 = [sb("xt_p2a", [128, D], F32), sb("xt_p2b", [128, D], F32),
                 sb("xt_p2c", [128, D], F32)]
        ksrow = fpA[0:1, :]
        sq_p2 = sb("sq_p2", [128, D], BF16)
        sq_p2b = sb("sq_p2b", [128, D], BF16)
        sqp = [sq_p2, sq_p2b]
        sqh = sb("sqh", [128, D], BF16)
        sqhT = sb("sqhT", [128, D], BF16)
        y1b = sb("y1b", [128, D], BF16)
        rowkv = sq_p2[0:1, 0:1024]
        rowq = y1b[0:1, 0:2048]
        rowbb = sqh[0:1, 0:2048]
        lh = sb("lh", [128, 136], F32)
        hmask = sb("hmask", [128, 128], BF16)
        hmaskT = sb("hmaskT", [128, 128], BF16)
        norm16 = sb("norm16", [128, 16], F32)
        normeps = sb("normeps", [128, 16], F32)
        rnorm = sb("rnorm", [128, 16], F32)
        rnbf = sb("rnbf", [128, 16], BF16)
        rowmax = sb("rowmax", [128, 1], F32)
        mask8 = sb("mask8", [128, 8], F32)
        negmean = [sb("negmean%d" % i, [128, 1], F32) for i in range(4)]
        veps = [sb("veps%d" % i, [128, 1], F32) for i in range(4)]
        stdv = [sb("stdv%d" % i, [128, 1], F32) for i in range(4)]
        rstd = [sb("rstd%d" % i, [128, 1], F32) for i in range(4)]
        bnstA = sb("bnstA", [128, 4, 6], F32)
        bnstB = sb("bnstB", [128, 4, 6], F32)
        mvA = sb("mvA", [128, 2], F32)
        mvB = sb("mvB", [128, 2], F32)

        psA = ctx.enter_context(nc.psum_tensor("psA", [128, 2048], F32))
        psB = ctx.enter_context(nc.psum_tensor("psB", [128, 2048], F32))
        psK = [psA[:, 0:512], psA[:, 512:1024]]
        psV = [psA[:, 1024:1536], psA[:, 1536:2048]]
        psKV = psB[:, 0:512]
        psKS = psB[0:1, 512:1024]
        psQ = [psA, psB]

        sWLa = sem("sWLa")
        sWLb = sem("sWLb")
        sXTe = sem("sXTe")
        sXTo = sem("sXTo")
        sWO = [sem("sWOa"), sem("sWOb")]
        sWKV = [sem("sWKVa"), sem("sWKVb")]
        sKVC = sem("sKVC")
        sSG = [sem("sSGa"), sem("sSGb"), sem("sSGc"), sem("sSGd")]
        sWOC = sem("sWOC")
        sSTG = [sem("sSTGa"), sem("sSTGb"), sem("sSTGc")]
        sWB = sem("sWB")
        sStp = sem("sStp")
        sKVm = sem("sKVm")
        sAcc = sem("sAcc")
        sDVE1 = sem("sDVE1")
        sACT1 = sem("sACT1")
        sFin = sem("sFin")
        sPEq = sem("sPEq")
        sDVEq = sem("sDVEq")
        sACTq = sem("sACTq")
        sSQW = sem("sSQW")
        sXa = sem("sXa")
        sXb = sem("sXb")
        sXc = sem("sXc")
        sSQRa = sem("sSQRa")
        sSQRb = sem("sSQRb")
        sPE2 = sem("sPE2")
        sDVE2 = sem("sDVE2")
        sACT2 = sem("sACT2")
        sY1B = sem("sY1B")
        sOutS = sem("sOutS")

        sX3 = [sXa, sXb, sXc]
        block = ctx.enter_context(nc.Block())

        def slab_wait(eng, g):
            if g <= 1:
                eng.wait_ge(sKVC, 2 * (g + 1))
            else:
                s = sXTe if g % 2 == 0 else sXTo
                eng.wait_ge(s, 16 * (g // 2))

        @block.tensor
        def _(tensor):
            tensor.wait_ge(sStp, 1)
            tensor.wait_ge(sWLa, 16)

            for t in range(NTC_ALL):
                par = t % 2
                s_idx = t // 2
                toff = (t - 2 * s_idx) * 128
                slab_wait(tensor, s_idx)
                if t >= 2:
                    tensor.wait_ge(sDVE1, 3 * (t - 1))
                for dc in range(16):
                    if t == 0 and dc % 4 == 0:
                        i = dc // 4
                        tensor.wait_ge(sWKV[i % 2], 16 * (i // 2 + 1))
                    elif t == 1 and dc == 0:
                        tensor.wait_ge(sWKV[0], 32)
                        tensor.wait_ge(sWKV[1], 32)
                    lhs = slab[s_idx % 2][:, dc, toff:toff + 128]
                    tensor.matmul(psK[par], lhs, wkv_sb[:, dc, 0:512],
                                  start=(dc == 0), stop=False,
                                  skip_group_check=True)
                    tensor.matmul(psV[par], lhs, wkv_sb[:, dc, 512:1024],
                                  start=(dc == 0), stop=False,
                                  skip_group_check=True)
                tensor.matmul(psK[par], onesr[:, :], rowkv[:, 0:512],
                              start=False, stop=True, skip_group_check=True)
                tensor.matmul(psV[par], onesr[:, :], rowkv[:, 512:1024],
                              start=False, stop=True, skip_group_check=True
                              ).then_inc(sKVm, 1)
                if t >= 1:
                    tp = t - 1
                    tensor.wait_ge(sDVE1, 3 * t)
                    for h4 in range(4):
                        tensor.matmul(psKV[:, h4 * 128:(h4 + 1) * 128],
                                      sk_t[tp % 2][:, h4 * 128:(h4 + 1) * 128],
                                      vb_t[tp % 2][:, h4 * 128:(h4 + 1) * 128],
                                      start=(tp == 0 and h4 == 0),
                                      stop=(tp == NTC_ALL - 1 and h4 == 3),
                                      skip_group_check=True)
                    tensor.matmul(psKS, onesb[:, :], sk_t[tp % 2][:, :],
                                  start=(tp == 0), stop=(tp == NTC_ALL - 1),
                                  skip_group_check=True).then_inc(sAcc, 1)
            tp = NTC_ALL - 1
            tensor.wait_ge(sDVE1, 3 * NTC_ALL)
            for h4 in range(4):
                tensor.matmul(psKV[:, h4 * 128:(h4 + 1) * 128],
                              sk_t[tp % 2][:, h4 * 128:(h4 + 1) * 128],
                              vb_t[tp % 2][:, h4 * 128:(h4 + 1) * 128],
                              start=False, stop=(h4 == 3), skip_group_check=True)
            tensor.matmul(psKS, onesb[:, :], sk_t[tp % 2][:, :],
                          start=False, stop=True, skip_group_check=True
                          ).then_inc(sAcc, 1)

            tensor.wait_ge(sFin, 3)
            for j in range(4):
                mm = tensor.matmul(psA[:, j * 512:(j + 1) * 512], onesf[:, :],
                                   ksrow[:, j * 512:(j + 1) * 512], start=True, stop=True)
            mm.then_inc(sFin, 1)

            tensor.wait_ge(sWLb, 16 * 12)
            tensor.wait_ge(sFin, 5)

            def t_group(u):
                par = u % 2
                for h in range(16):
                    if h == 0:
                        tensor.wait_ge(sDVEq, dqabs[(u, "sqhat0")])
                    elif h == 8:
                        tensor.wait_ge(sDVEq, dqabs[(u, "sqhat1")])
                    mm = tensor.matmul(psQ[par][:, h * 128:(h + 1) * 128],
                                       sqh[:, h * 128:(h + 1) * 128], I_bf[:, :],
                                       start=True, stop=True, skip_group_check=True)
                    if h == 7 or h == 15:
                        mm.then_inc(sPEq, 1)

            def qkv_group(u):
                par = u % 2
                for i in range(4):
                    if i == 0:
                        tensor.wait_ge(sDVEq, dqabs[(u, "E1a")])
                    elif i == 2:
                        tensor.wait_ge(sDVEq, dqabs[(u, "E1b")])
                    for h in range(4 * i, 4 * i + 4):
                        mm = tensor.matmul(psQ[par][:, h * 128:(h + 1) * 128],
                                           kvb_v[:, h // 4, :],
                                           sqhT[:, h * 128:(h + 1) * 128],
                                           start=True, stop=True,
                                           skip_group_check=True)
                    mm.then_inc(sPEq, 1)

            for u in range(NTC_OWN):
                par = u % 2
                s_idx = u // 2
                toff = (u - 2 * s_idx) * 128
                slab_wait(tensor, NSLAB_ALL + s_idx)
                for j in range(4):
                    if j == 3 and u >= 1:
                        t_group(u - 1)
                    if u >= 2:
                        tensor.wait_ge(sDVEq, dqabs[(u - 2, "E2c%d" % j)])
                    for dc in range(16):
                        lhs = slab[s_idx % 2][:, dc, toff:toff + 128]
                        tensor.matmul(psQ[par][:, j * 512:(j + 1) * 512], lhs,
                                      U0[:, dc, j * 512:(j + 1) * 512],
                                      start=(dc == 0), stop=False,
                                      skip_group_check=True)
                    mm = tensor.matmul(psQ[par][:, j * 512:(j + 1) * 512],
                                       onesr[:, :], rowq[:, j * 512:(j + 1) * 512],
                                       start=False, stop=True, skip_group_check=True)
                    if j == 1 or j == 3:
                        mm.then_inc(sPEq, 1)
                if u >= 1:
                    qkv_group(u - 1)
            t_group(NTC_OWN - 1)
            qkv_group(NTC_OWN - 1)

            tensor.wait_ge(sDVEq, dqabs[(NTC_OWN - 2, "E2c3")])

            def y1t_round(u, r):
                if r == 0:
                    if u == 0:
                        tensor.wait_ge(sDVEq, DQ_TOTAL)
                    tensor.wait_ge(sDVE2, d2abs[(u, "y1c")])
                    if u >= 1:
                        tensor.wait_ge(sDVE2, d2abs[(u - 1, "s2")])
                else:
                    if r >= 2:
                        tensor.wait_ge(sACT2, a2abs[(u, "ey1t%d" % (r - 2))])
                for k in range(4):
                    dc = 4 * r + k
                    base_col = (r % 2) * 512 + k * 128
                    mm = tensor.matmul(psB[:, base_col:base_col + 128],
                                       y1b[:, dc * 128:(dc + 1) * 128], I_bf[:, :],
                                       start=True, stop=True, skip_group_check=True)
                mm.then_inc(sPE2, 1)

            def misc_part(u, half):
                tensor.wait_ge(sACT2, a2abs[(u, "ey1t%d" % (half + 2))])
                for dc in range(8 * half, 8 * half + 8):
                    mm = tensor.matmul(psB[:, 0:136],
                                       sqhT[:, dc * 128:(dc + 1) * 128],
                                       ra_sb[:, dc, :],
                                       start=(dc == 0), stop=(dc == 15),
                                       skip_group_check=True)
                if half == 1:
                    mm.then_inc(sPE2, 1)

            def p2_iter(v):
                u = v - 2
                if v <= NTC_OWN - 1:
                    tensor.wait_ge(sSQRa if v % 2 == 0 else sSQRb, 16 * (v // 2 + 1))
                    if v >= 1:
                        tensor.wait_ge(sDVE2, d2abs[(v - 1, "s")])
                    sq = sqp[v % 2]
                    for h in range(16):
                        if v == 0:
                            if h < 10 and h % 2 == 0:
                                i = h // 2
                                tensor.wait_ge(sWO[i % 2], 16 * (i // 2 + 1))
                            elif h >= 10:
                                tensor.wait_ge(sWOC, h - 9)
                        elif v == 1 and h == 0:
                            tensor.wait_ge(sWO[0], 48)
                            tensor.wait_ge(sWO[1], 32)
                            tensor.wait_ge(sWOC, 6)
                        for j in range(4):
                            mm = tensor.matmul(psA[:, j * 512:(j + 1) * 512],
                                               sq[:, h * 128:(h + 1) * 128],
                                               U0[:, h, j * 512:(j + 1) * 512],
                                               start=(h == 0), stop=(h == 15),
                                               skip_group_check=True)
                        if u >= 0:
                            if h == 3:
                                y1t_round(u, 0)
                            elif h == 5:
                                y1t_round(u, 1)
                            elif h == 7:
                                y1t_round(u, 2)
                            elif h == 9:
                                y1t_round(u, 3)
                            elif h == 10:
                                misc_part(u, 0)
                            elif h == 12:
                                misc_part(u, 1)
                    mm.then_inc(sPE2, 1)
                else:
                    for r in range(4):
                        y1t_round(u, r)
                    misc_part(u, 0)
                    misc_part(u, 1)
                if u >= 0:
                    tensor.wait_ge(sDVE2, d2abs[(u, "hmask")])
                    tensor.matmul(psB[:, 512:640], hmask[:, :], I_bf[:, :],
                                  start=True, stop=True,
                                  skip_group_check=True).then_inc(sPE2, 1)
                    tensor.wait_ge(sDVE2, d2abs[(u, "lh")])
                    if u == 0:
                        tensor.wait_ge(sWB, 16 * (WB_CH + 1))
                    for j in range(4):
                        for dc in range(16):
                            tensor.matmul(psB[:, j * 512:(j + 1) * 512],
                                          sqhT[:, dc * 128:(dc + 1) * 128],
                                          wbase_sb[:, dc, j * 512:(j + 1) * 512],
                                          start=(dc == 0), stop=False,
                                          skip_group_check=True)
                        tensor.matmul(psB[:, j * 512:(j + 1) * 512], onesr[:, :],
                                      rowbb[:, j * 512:(j + 1) * 512],
                                      start=False, stop=False, skip_group_check=True)
                        if j == 0:
                            tensor.wait_ge(sACT2, a2abs[(u, "ehmt")])
                        mm = tensor.matmul(psB[:, j * 512:(j + 1) * 512], hmaskT[:, :],
                                           bcat_sb[:, j * 512:(j + 1) * 512],
                                           start=False, stop=True,
                                           skip_group_check=True)
                    mm.then_inc(sPE2, 1)

            for v in range(NTC_OWN + 2):
                p2_iter(v)

        @block.vector
        def _(vector):
            vector.memset(onesf[:, :], 1.0)
            vector.memset(onesb[:, :], 1.0)
            vector.memset(onesr[:, :], 1.0).then_inc(sStp, 1)
            stage_srcs = [fpA, fpB, xt_p2[0], xt_p2[1]]
            for k in range(4):
                vector.wait_ge(sSG[k], 16)
                vector.tensor_copy(slab[k // 2][:, 8 * (k % 2):8 * (k % 2) + 8, :],
                                   stage_srcs[k][:, :].rearrange(
                                       "p (c n) -> p c n", c=8)).then_inc(sKVC, 1)

            dcnt = [0]

            def op(ins):
                ins.then_inc(sDVE1, 1)
                dcnt[0] += 1
                vector.wait_ge(sDVE1, dcnt[0])
                return ins

            for t in range(NTC_ALL):
                par = t % 2
                vector.wait_ge(sKVm, t + 1)
                if t >= 1:
                    vector.wait_ge(sACT1, t)
                op(vector.tensor_scalar(scr1[:, 0:512], psK[par], 0.0, None, ALU.min))
                vector.wait_ge(sACT1, t + 1)
                if t >= 2:
                    vector.wait_ge(sAcc, t - 1)
                op(vector.scalar_tensor_tensor(sk_t[par][:, :], psK[par], 0.0,
                                               scr2[:, 0:512], ALU.max, ALU.add))
                op(vector.tensor_copy(vb_t[par][:, :], psV[par]))

            vector.wait_ge(sAcc, NTC_ALL)
            vector.tensor_copy(kvb[:, :], psKV).then_inc(sFin, 1)
            vector.tensor_copy(ksr512[:, :], psKS).then_inc(sFin, 1)
            vector.wait_ge(sFin, 2)
            ks_in = ksr512[:, :].rearrange("a (h d) -> a h d", h=4).unsqueeze(2) \
                .broadcast_to([1, 4, 4, 128])
            vector.tensor_copy(ksrow.rearrange("a (h r d) -> a h r d", h=4, r=4),
                               ks_in).then_inc(sFin, 1)
            vector.wait_ge(sFin, 4)
            vector.tensor_copy(ksum_bc[:, :], psA[:, :]).then_inc(sFin, 1)

            dq = [0]

            def opq(ins):
                ins.then_inc(sDVEq, 1)
                dq[0] += 1
                vector.wait_ge(sDVEq, dq[0])
                return ins

            sq3 = sq_p1.rearrange("p (h d) -> p h d", h=16)
            prod3 = prodb.rearrange("p (h d) -> p h d", h=16)
            ks3 = ksum_bc[:, :].rearrange("p (h d) -> p h d", h=16)
            sqh3 = sqh[:, :].rearrange("p (h d) -> p h d", h=16)
            rn3 = rnbf[:, :].unsqueeze(2).broadcast_to([128, 16, 128])

            def e1seg(pw):
                vector.wait_ge(sPEq, peq_T7[pw])
                opq(vector.tensor_copy(sqhT[:, 0:1024], psQ[pw % 2][:, 0:1024]))
                vector.wait_ge(sPEq, peq_T15[pw])
                opq(vector.tensor_copy(sqhT[:, 1024:2048], psQ[pw % 2][:, 1024:2048]))

            def e2seg(pw, quarters):
                for i in quarters:
                    vector.wait_ge(sPEq, peq_qkvq[(pw, i)])
                    if i == 0 and pw >= 1:
                        vector.wait_ge(sSQW, 16 * pw)
                    opq(vector.tensor_copy(sqp[pw % 2][:, i * 512:(i + 1) * 512],
                                           psQ[pw % 2][:, i * 512:(i + 1) * 512]))

            for u in range(NTC_OWN):
                par = u % 2
                vector.wait_ge(sPEq, peq_qh[u])
                if u >= 1:
                    vector.wait_ge(sACTq, 2 * u)
                opq(vector.tensor_scalar(scr1[:, :], psQ[par][:, 0:1024], 0.0,
                                         None, ALU.min))
                vector.wait_ge(sACTq, 2 * u + 1)
                opq(vector.scalar_tensor_tensor(sq_p1[:, 0:1024], psQ[par][:, 0:1024],
                                                0.0, scr2[:, :], ALU.max, ALU.add))
                if u >= 1:
                    e1seg(u - 1)
                    e2seg(u - 1, [0])
                vector.wait_ge(sPEq, peq_q[u])
                opq(vector.tensor_scalar(scr1[:, :], psQ[par][:, 1024:2048], 0.0,
                                         None, ALU.min))
                if u >= 1:
                    e2seg(u - 1, [1, 2, 3])
                vector.wait_ge(sACTq, 2 * u + 2)
                opq(vector.scalar_tensor_tensor(sq_p1[:, 1024:2048],
                                                psQ[par][:, 1024:2048],
                                                0.0, scr2[:, :], ALU.max, ALU.add))
                opq(vector.tensor_tensor(prod3, sq3, ks3, ALU.mult))
                opq(vector.tensor_reduce(norm16[:, :], prod3, AX.X, ALU.add))
                opq(vector.tensor_scalar(normeps[:, :], norm16[:, :], ATT_EPS,
                                         None, ALU.add))
                opq(vector.reciprocal(rnorm[:, :], normeps[:, :]))
                opq(vector.tensor_copy(rnbf[:, :], rnorm[:, :]))
                opq(vector.tensor_tensor(sqh3[:, 0:8, :], sq3[:, 0:8, :],
                                         rn3[:, 0:8, :], ALU.mult))
                opq(vector.tensor_tensor(sqh3[:, 8:16, :], sq3[:, 8:16, :],
                                         rn3[:, 8:16, :], ALU.mult))
            e1seg(NTC_OWN - 1)
            e2seg(NTC_OWN - 1, [0, 1, 2, 3])
            for k, dc in enumerate([10, 11, 12, 13, 14, 15]):
                vector.wait_ge(sSTG[k % 3], 16 * (k // 3 + 1))
                vector.tensor_copy(U0[:, dc, :],
                                   xt_p2[k % 3][:, :]).then_inc(sWOC, 1)

            dd = [0]

            def op2(ins):
                ins.then_inc(sDVE2, 1)
                dd[0] += 1
                vector.wait_ge(sDVE2, dd[0])
                return ins

            lh3 = lh[:, 8:136].rearrange("p (e r) -> p e r", e=8)
            hm3 = hmask[:, :].rearrange("p (e r) -> p e r", e=8)
            m8b = mask8[:, :].unsqueeze(2).broadcast_to([128, 8, 16])

            def seg_a1(u):
                vector.wait_ge(sPE2, pe2_mm3[u])
                vector.wait_ge(sX3[u % 3], 16 * (u // 3 + 1))
                op2(vector.scalar_tensor_tensor(fpA[:, :], psA[:, :], 1.0,
                                                xt_p2[u % 3][:, :], ALU.mult, ALU.add))

            def seg_a2(u):
                p = u % 2
                for g in range(4):
                    op2(vector.bn_stats(bnstA[:, g, :], fpA[:, g * 512:(g + 1) * 512]))
                op2(vector.bn_aggr(mvA[:, :], bnstA[:, :, :].rearrange("p a b -> p (a b)")))
                op2(vector.tensor_scalar(negmean[p][:, :], mvA[:, 0:1], -1.0, None, ALU.mult))
                op2(vector.tensor_scalar(veps[p][:, :], mvA[:, 1:2], LN_EPS, None, ALU.add))
                op2(vector.reciprocal(stdv[p][:, :], veps[p][:, :]))
                vector.wait_ge(sACT2, a2abs[(u, "sqrt1")])
                op2(vector.tensor_scalar(xt_p2[u % 3][:, :], fpA[:, :], negmean[p][:, :],
                                         rstd[p][:, :], ALU.add, ALU.mult))

            def seg_b1(u):
                vector.wait_ge(sPE2, pe2_misc[u])
                op2(vector.tensor_copy(lh[:, :], psB[:, 0:136]))
                op2(vector.reduce_max(rowmax[:, :], lh[:, 0:8], AX.X))
                op2(vector.tensor_scalar(mask8[:, :], lh[:, 0:8], rowmax[:, :],
                                         2.0, ALU.is_ge, ALU.mult))
                op2(vector.tensor_tensor(hm3, lh3, m8b, ALU.mult))

            def seg_b2a(u):
                p = 2 + u % 2
                vector.wait_ge(sPE2, pe2_base[u])
                op2(vector.scalar_tensor_tensor(fpA[:, :], psB[:, :], 1.0,
                                                xt_p2[u % 3][:, :], ALU.mult, ALU.add))
                for g in range(4):
                    op2(vector.bn_stats(bnstB[:, g, :], fpA[:, g * 512:(g + 1) * 512]))
                op2(vector.bn_aggr(mvB[:, :], bnstB[:, :, :].rearrange("p a b -> p (a b)")))
                op2(vector.tensor_scalar(negmean[p][:, :], mvB[:, 0:1], -1.0, None, ALU.mult))
                op2(vector.tensor_scalar(veps[p][:, :], mvB[:, 1:2], LN_EPS, None, ALU.add))
                op2(vector.reciprocal(stdv[p][:, :], veps[p][:, :]))

            def seg_yc(u):
                if u >= 1:
                    vector.wait_ge(sPE2, pe2_y1t[(u - 1, 3)])
                op2(vector.tensor_copy(y1b[:, :], xt_p2[u % 3][:, :]))

            def seg_b2y(u):
                p = 2 + u % 2
                vector.wait_ge(sACT2, a2abs[(u, "sqrt2")])
                if u >= 1:
                    vector.wait_ge(sOutS, 16 * u)
                op2(vector.tensor_scalar(fpB[:, :], fpA[:, :], negmean[p][:, :],
                                         rstd[p][:, :], ALU.add, ALU.mult))

            _dseg_fns = {"A1": seg_a1, "A2": seg_a2, "B1": seg_b1,
                         "B2a": seg_b2a, "B2y": seg_b2y, "YC": seg_yc}
            for kind, u in P2_ORDER:
                _dseg_fns[kind](u)

        @block.scalar
        def _(scalar):
            scalar.preload_act_table(AF.Exp)
            for t in range(NTC_ALL):
                scalar.wait_ge(sDVE1, 3 * t + 1)
                scalar.activation(scr2[:, 0:512], scr1[:, 0:512], AF.Exp).then_inc(sACT1, 1)
            for u in range(NTC_OWN):
                for j in range(2):
                    scalar.wait_ge(sDVEq, dqabs[(u, "min%d" % j)])
                    scalar.activation(scr2[:, :], scr1[:, :], AF.Exp).then_inc(sACTq, 1)
            scalar.preload_act_table(AF.Sqrt)

            def x_load(v):
                if v == 0:
                    scalar.wait_ge(sFin, 4)
                if v < 3:
                    scalar.wait_ge(sWOC, 4 + v)
                scalar.dma_start(out=xt_p2[v % 3][:, :],
                                 in_=x_own[v * 128:(v + 1) * 128, :]).then_inc(
                                     sX3[v % 3], 16)

            x_load(0)
            x_load(1)
            x_load(2)
            for kind, u in P2_ACT:
                if kind == "Aa":
                    p = u % 2
                    scalar.wait_ge(sDVE2, d2abs[(u, "rv")])
                    scalar.activation(rstd[p][:, :], stdv[p][:, :], AF.Sqrt).then_inc(sACT2, 1)
                elif kind == "Ey":
                    for r in range(4):
                        scalar.wait_ge(sPE2, pe2_y1t[(u, r)])
                        scalar.copy(sqhT[:, r * 512:(r + 1) * 512],
                                    psB[:, (r % 2) * 512:(r % 2) * 512 + 512]).then_inc(sACT2, 1)
                elif kind == "Eh":
                    scalar.wait_ge(sPE2, pe2_hmt[u])
                    scalar.copy(hmaskT[:, :], psB[:, 512:640]).then_inc(sACT2, 1)
                else:
                    p = 2 + u % 2
                    scalar.wait_ge(sDVE2, d2abs[(u, "rv2")])
                    scalar.activation(rstd[p][:, :], stdv[p][:, :], AF.Sqrt).then_inc(sACT2, 1)
                    if u + 3 < NTC_OWN:
                        x_load(u + 3)

        @block.gpsimd
        def _(gpsimd):
            wq_src = wq_t.rearrange("(c p) n -> p c n", p=128)
            wkv_src = wkv_t.rearrange("(c p) n -> p c n", p=128)
            gpsimd.dma_start(out=rowkv, in_=brows[:, 0:1024]).then_inc(sWLa, 16)
            for i in range(4):
                if i >= 2:
                    gpsimd.wait_ge(sWKV[i % 2], 16 * (i // 2))
                gpsimd.dma_start(out=wkv_sb[:, 4 * i:4 * i + 4, :],
                                 in_=wkv_src[:, 4 * i:4 * i + 4, :]
                                 ).then_inc(sWKV[i % 2], 16)
            gpsimd.wait_ge(sWKV[0], 32)
            gpsimd.wait_ge(sWKV[1], 32)
            gpsimd.wait_ge(sWLa, 16)
            wlb_small = [
                (rowq, brows[:, 1024:3072]),
                (I_bf[:, :], ident),
                (ra_sb[:, :, :], ra_t.rearrange("(c p) n -> p c n", p=128)),
                (bcat_sb[:, :], bcat_t),
            ]
            wlb_jobs = wlb_small + [
                (U0[:, 2 * i:2 * i + 2, :], wq_src[:, 2 * i:2 * i + 2, :])
                for i in range(8)]
            wlb_i = [0]

            def emit_wlb():
                if wlb_i[0] < len(wlb_jobs):
                    dst, s_ = wlb_jobs[wlb_i[0]]
                    gpsimd.dma_start(out=dst, in_=s_).then_inc(sWLb, 16)
                    wlb_i[0] += 1

            for s in range(2, NSLAB_ALL):
                gpsimd.wait_ge(sKVm, 2 * (s - 1))
                src = xt_full.rearrange("(c p) t -> p c t", p=128)[:, :, s * 256:(s + 1) * 256]
                gpsimd.dma_start(out=slab[s % 2], in_=src).then_inc(
                    sXTe if s % 2 == 0 else sXTo, 16)
                if 2 <= s < 6 or (s >= 8 and s % 2 == 0):
                    emit_wlb()
            while wlb_i[0] < len(wlb_jobs):
                emit_wlb()
            wo_src = wo_t.rearrange("(c p) n -> p c n", p=128)
            wb_src = wbase_t.rearrange("(c p) n -> p c n", p=128)
            for s in range(NSLAB_OWN):
                if s < 2:
                    gpsimd.wait_ge(sAcc, NTC_ALL)
                else:
                    gpsimd.wait_ge(sPEq, peq_q[2 * (s - 1)])
                g = NSLAB_ALL + s
                src = xt_own.rearrange("(c p) t -> p c t", p=128)[:, :, s * 256:(s + 1) * 256]
                gpsimd.dma_start(out=slab[g % 2], in_=src).then_inc(
                    sXTe if g % 2 == 0 else sXTo, 16)
                if 2 <= s < 6:
                    i = s - 2
                    if i == 0:
                        gpsimd.wait_ge(sKVm, NTC_ALL)
                    gpsimd.dma_start(out=wbase_sb[:, 2 * i:2 * i + 2, :],
                                     in_=wb_src[:, 2 * i:2 * i + 2, :]).then_inc(sWB, 16)
            gpsimd.wait_ge(sPEq, peq_q[NTC_OWN - 1])
            for i in range(WO_CH):
                if i >= 2:
                    gpsimd.wait_ge(sWO[i % 2], 16 * (i // 2))
                gpsimd.dma_start(out=U0[:, 2 * i:2 * i + 2, :],
                                 in_=wo_src[:, 2 * i:2 * i + 2, :]).then_inc(sWO[i % 2], 16)
            gpsimd.wait_ge(sDVEq, DQ_TOTAL)
            gpsimd.wait_ge(sACTq, 2 * NTC_OWN)
            gpsimd.wait_ge(sSQW, 16 * NTC_OWN)
            gpsimd.wait_ge(sPEq, peq_T15[NTC_OWN - 1])
            for i in range(4, WB_CH):
                gpsimd.dma_start(out=wbase_sb[:, 2 * i:2 * i + 2, :],
                                 in_=wb_src[:, 2 * i:2 * i + 2, :]).then_inc(sWB, 16)
            gpsimd.dma_start(out=rowbb, in_=brows[:, 3072:5120]).then_inc(sWB, 16)


        @block.sync
        def _(sync):
            def a_store(u):
                sync.wait_ge(sDVEq, dqabs[(u, "E2c3")])
                sync.dma_start(out=attn_dram[u], in_=sqp[u % 2][:, :]).then_inc(sSQW, 16)

            def sq_load(u):
                if u == 0:
                    sync.wait_ge(sSQW, 16 * (NTC_OWN - 1))
                elif u == 1:
                    sync.wait_ge(sSQW, 16 * NTC_OWN)
                else:
                    sync.wait_ge(sPE2, pe2_mm3[u - 2])
                sync.dma_start(out=sqp[u % 2][:, :], in_=attn_dram[u]).then_inc(
                    sSQRa if u % 2 == 0 else sSQRb, 16)

            def y_store(u):
                sync.wait_ge(sDVE2, d2abs[(u, "y2")])
                sync.dma_start(out=y_out[u * 128:(u + 1) * 128, :],
                               in_=fpB[:, :]).then_inc(sOutS, 16)

            wo_src_s = wo_t.rearrange("(c p) n -> p c n", p=128)
            xt_src = xt_full.rearrange("(c p) t -> p c t", p=128)
            stage_dsts = [fpA, fpB, xt_p2[0], xt_p2[1]]
            for k in range(4):
                half = xt_src[:, 8 * (k % 2):8 * (k % 2) + 8,
                              256 * (k // 2):256 * (k // 2) + 256]
                sync.dma_start(out=stage_dsts[k][:, :].rearrange(
                    "p (c n) -> p c n", c=8), in_=half).then_inc(sSG[k], 16)
            for k in range(3):
                if k < 2:
                    sync.wait_ge(sKVC, 3 + k)
                sync.dma_start(out=xt_p2[k][:, :],
                               in_=wo_src_s[:, 10 + k, :]).then_inc(sSTG[k], 16)
            for u in range(NTC_OWN - 1):
                a_store(u)
            sq_load(0)
            a_store(NTC_OWN - 1)
            sq_load(1)
            for k in range(3):
                sync.wait_ge(sWOC, k + 1)
                sync.dma_start(out=xt_p2[k][:, :],
                               in_=wo_src_s[:, 13 + k, :]).then_inc(sSTG[k], 16)
            for u in range(2, NTC_OWN):
                sq_load(u)
                if u >= 5:
                    y_store(u - 5)
            for u in range(NTC_OWN - 5, NTC_OWN):
                y_store(u)
            sync.wait_ge(sOutS, 16 * NTC_OWN)


    return nc


def kernel(L0_attn_hidden_states, L0_attn_Wq, L0_attn_bq, L0_attn_Wk, L0_attn_bk,
           L0_attn_Wv, L0_attn_bv, L0_attn_Wo, L0_ffn_W_base, L0_ffn_b_base,
           L0_ffn_router_W, L0_ffn_A, L0_ffn_Bw):
    f32 = np.float32
    X = np.ascontiguousarray(np.asarray(L0_attn_hidden_states, dtype=f32))
    Wq = np.asarray(L0_attn_Wq, f32)
    Wk = np.asarray(L0_attn_Wk, f32)
    Wv = np.asarray(L0_attn_Wv, f32)
    Wo = np.asarray(L0_attn_Wo, f32)
    Wb = np.asarray(L0_ffn_W_base, f32)
    Rw = np.asarray(L0_ffn_router_W, f32)
    A = np.asarray(L0_ffn_A, f32)
    Bw = np.asarray(L0_ffn_Bw, f32)

    wkv_t = np.ascontiguousarray(np.concatenate([Wk, Wv], axis=0).T)
    wq_t = np.ascontiguousarray(Wq.T)
    wo_t = np.ascontiguousarray(Wo.T)
    wbase_t = np.ascontiguousarray(Wb.T)
    ra_t = np.ascontiguousarray(
        np.concatenate([Rw, A.reshape(128, D)], axis=0).T)
    bcat_t = np.ascontiguousarray(Bw.transpose(0, 2, 1).reshape(128, D))
    brows = np.concatenate([
        np.asarray(L0_attn_bk, f32), np.asarray(L0_attn_bv, f32),
        np.asarray(L0_attn_bq, f32), np.asarray(L0_ffn_b_base, f32)
    ]).reshape(1, 5120)
    ident = np.eye(128, dtype=f32)

    xt_b = [np.ascontiguousarray(X[b].T) for b in range(2)]

    nc = build_nc()
    in_maps = []
    for c in range(8):
        b, j = c // 4, c % 4
        in_maps.append(dict(
            xt_full=xt_b[b],
            xt_own=np.ascontiguousarray(xt_b[b][:, j * TOK:(j + 1) * TOK]),
            x_own=np.ascontiguousarray(X[b, j * TOK:(j + 1) * TOK, :]),
            wkv_t=wkv_t, wq_t=wq_t, wo_t=wo_t, wbase_t=wbase_t,
            ra_t=ra_t, bcat_t=bcat_t, brows=brows, ident=ident,
        ))
    res = run_bass_kernel_spmd(nc, in_maps, core_ids=list(range(8)))
    y = np.empty((2, SEQ, D), dtype=f32)
    for c in range(8):
        b, j = c // 4, c % 4
        y[b, j * TOK:(j + 1) * TOK, :] = res.results[c]["y"]
    return y
